# version 01
# baseline (speedup 1.0000x reference)
import numpy as np
import ml_dtypes
from contextlib import ExitStack

import concourse.bass as bass
import concourse.mybir as mybir
from concourse.bass_utils import run_bass_kernel_spmd
from concourse.bass_core_rust import IndirectOffsetOnAxis

F32 = mybir.dt.float32
BF16 = mybir.dt.bfloat16
I32 = mybir.dt.int32
ALU = mybir.AluOpType
AF = mybir.ActivationFunctionType
AX = mybir.AxisListType.X

MAGIC = float(np.float32(1.5 * 2 ** 23))
NCORE = 8
TOK = 2048
NT = 16384
D = 1024
LN_EPS = 1e-5
PAD = 512

ENGS = ("sync", "scalar", "vector", "tensor", "gpsimd")


class Prog:
    def __init__(self, nc):
        self.nc = nc
        self.calls = {e: [] for e in ENGS}
        self.cnt = {e: 0 for e in ENGS}
        self.dma_cnt = {e: 0 for e in ENGS}
        self.sems = ExitStack()
        self.sem = {e: self.sems.enter_context(nc.semaphore(f"s_{e}")) for e in ENGS}
        self.NDSEM = 6
        self.dsem = {e: [self.sems.enter_context(nc.semaphore(f"d_{e}{i}"))
                         for i in range(self.NDSEM)] for e in ENGS}
        self.duses = {e: [0] * self.NDSEM for e in ENGS}
        self.drr = {e: 0 for e in ENGS}
        self.last_w = {}
        self.readers = {}
        self.wm = {e: {} for e in ENGS}

    @staticmethod
    def _key(x):
        if isinstance(x, str):
            return x
        t = getattr(x, "tensor", x)
        return getattr(t, "name", str(t))

    def _emit_wait(self, eng, tok):
        kind, e2, val = tok
        if kind == "E":
            sem, name = self.sem[e2], ("E", e2)
        else:
            q, i = e2
            sem, name = self.dsem[q][i], ("D", q, i)
        if self.wm[eng].get(name, 0) >= val:
            return
        self.wm[eng][name] = val
        self.calls[eng].append(lambda E, s=sem, v=val: E.wait_ge(s, v))

    def _deps(self, eng, reads, writes):
        toks = set()
        for r in reads:
            t = self.last_w.get(self._key(r))
            if t is not None:
                toks.add(t)
        for w in writes:
            k = self._key(w)
            t = self.last_w.get(k)
            if t is not None:
                toks.add(t)
            toks.update(self.readers.get(k, ()))
        for t in sorted(toks):
            self._emit_wait(eng, t)

    def _post(self, reads, writes, tok):
        for w in writes:
            k = self._key(w)
            self.last_w[k] = tok
            self.readers[k] = []
        for r in reads:
            self.readers.setdefault(self._key(r), []).append(tok)

    def op(self, eng, fn, reads=(), writes=()):
        self._deps(eng, reads, writes)
        self.cnt[eng] += 1
        tok = ("E", eng, self.cnt[eng])
        sem = self.sem[eng]
        self.calls[eng].append(lambda E, fn=fn, sem=sem: fn(E).then_inc(sem, 1))
        self._post(reads, writes, tok)
        return tok

    def _dma_slot(self, eng):
        i = self.drr[eng] % self.NDSEM
        self.drr[eng] += 1
        if self.duses[eng][i]:
            self._emit_wait(eng, ("D", (eng, i), 16 * self.duses[eng][i]))
        self.duses[eng][i] += 1
        return i, self.dsem[eng][i], 16 * self.duses[eng][i]

    def dma(self, eng, out, in_, reads=(), writes=(), **kw):
        reads = list(reads) + [in_]
        writes = list(writes) + [out]
        self._deps(eng, reads, writes)
        i, sem, val = self._dma_slot(eng)
        tok = ("D", (eng, i), val)
        self.calls[eng].append(
            lambda E, out=out, in_=in_, sem=sem, kw=kw:
            E.dma_start(out=out, in_=in_, **kw).then_inc(sem, 16))
        self._post(reads, writes, tok)
        return tok

    def dma_ind(self, out, out_off, in_, in_off, bounds, reads=(), writes=()):
        eng = "gpsimd"
        extra = [x.ap for x in (out_off, in_off) if x is not None]
        reads = list(reads) + [in_] + extra
        writes = list(writes) + [out]
        self._deps(eng, reads, writes)
        i, sem, val = self._dma_slot(eng)
        tok = ("D", (eng, i), val)
        self.calls[eng].append(
            lambda E, out=out, in_=in_, sem=sem:
            E.indirect_dma_start(out, out_off, in_, in_off, bounds_check=bounds,
                                 oob_is_err=False).then_inc(sem, 16))
        self._post(reads, writes, tok)
        return tok

    def barrier(self):
        toks = []
        for e in ENGS:
            if self.cnt[e]:
                toks.append(("E", e, self.cnt[e]))
            for i in range(self.NDSEM):
                if self.duses[e][i]:
                    toks.append(("D", (e, i), 16 * self.duses[e][i]))
        for eng in ENGS:
            for t in toks:
                if t[0] == "E" and t[1] == eng:
                    continue
                self._emit_wait(eng, t)

    def finish(self):
        nc = self.nc
        for e in ENGS:
            if e != "sync" and self.cnt[e]:
                self.calls["sync"].append(
                    lambda E, s=self.sem[e], v=self.cnt[e]: E.wait_ge(s, v))
            for i in range(self.NDSEM):
                if self.duses[e][i]:
                    self.calls["sync"].append(
                        lambda E, s=self.dsem[e][i], v=16 * self.duses[e][i]:
                        E.wait_ge(s, v))
        with self.sems, nc.Block() as block:
            @block.sync
            def _(E):
                for c in self.calls["sync"]:
                    c(E)

            @block.scalar
            def _(E):
                for c in self.calls["scalar"]:
                    c(E)

            @block.vector
            def _(E):
                for c in self.calls["vector"]:
                    c(E)

            @block.tensor
            def _(E):
                for c in self.calls["tensor"]:
                    c(E)

            @block.gpsimd
            def _(E):
                for c in self.calls["gpsimd"]:
                    c(E)


class Ctx:
    def __init__(self):
        self.nc = bass.Bass("TRN2", target_bir_lowering=False)
        self.es = ExitStack()
        self.p = Prog(self.nc)
        self.banks = [self.es.enter_context(
            self.nc.psum_tensor(f"bank{i}", [128, 512], F32)) for i in range(8)]
        self._bank_i = 0
        self.ones1 = self.sb("ones1", [1, 128])
        self.onescol = self.sb("onescol", [128, 1])
        self.onescolb = self.sb("onescolb", [128, 1], BF16)
        self.p.op("vector", lambda E: E.memset(self.ones1[:], 1.0), writes=[self.ones1])
        self.p.op("vector", lambda E: E.memset(self.onescol[:], 1.0), writes=[self.onescol])
        self.p.op("vector", lambda E: E.memset(self.onescolb[:], 1.0), writes=[self.onescolb])

    def sb(self, name, shape, dtype=F32, es=None):
        return (es or self.es).enter_context(self.nc.sbuf_tensor(name, shape, dtype))

    def bank(self):
        n = len(self.rot) if hasattr(self, "rot") and self.rot else 8
        pool = self.rot if (hasattr(self, "rot") and self.rot) else self.banks
        b = pool[self._bank_i % n]
        self._bank_i += 1
        return b

    def bcast_row(self, row_ap, out_sb, n, out_off=0):
        p = self.p
        for j in range((n + 511) // 512):
            c0, c1 = j * 512, min(n, (j + 1) * 512)
            pt = self.bank()
            p.op("tensor", lambda E, pt=pt, c0=c0, c1=c1: E.matmul(
                pt[:, 0:c1 - c0], self.ones1[0:1, :], row_ap[0:1, c0:c1],
                start=True, stop=True), reads=[row_ap, self.ones1], writes=[pt])
            p.op("scalar", lambda E, pt=pt, c0=c0, c1=c1:
                 E.copy(out_sb[:, out_off + c0:out_off + c1], pt[:, 0:c1 - c0]),
                 reads=[pt], writes=[out_sb])

    def transpose(self, out_sb, in_ap, k, m, ident, out_off=0):
        p = self.p
        pt = self.bank()
        p.op("tensor", lambda E, pt=pt: E.transpose(pt[0:m, 0:k], in_ap, ident[0:k, 0:k]),
             reads=[in_ap, ident], writes=[pt])
        p.op("scalar", lambda E, pt=pt:
             E.copy(out_sb[0:m, out_off:out_off + k], pt[0:m, 0:k]),
             reads=[pt], writes=[out_sb])


def trig_consts(cx, es):
    c = {}
    c["pio2"] = cx.sb("c_pio2", [128, 1], es=es)
    c["neg1"] = cx.sb("c_neg1", [128, 1], es=es)
    cx.p.op("vector", lambda E: E.memset(c["pio2"][:], float(np.pi / 2)), writes=[c["pio2"]])
    cx.p.op("vector", lambda E: E.memset(c["neg1"][:], -1.0), writes=[c["neg1"]])
    return c


def trig_block(cx, phi, r, sin_o, cos_o, c):
    p = cx.p
    inv2pi = float(np.float32(1.0 / (2 * np.pi)))
    twopi = float(np.float32(2 * np.pi))
    p.op("vector", lambda E: E.tensor_scalar(r[:], phi[:], inv2pi, MAGIC, ALU.mult, ALU.add),
         reads=[phi], writes=[r])
    p.op("vector", lambda E: E.tensor_scalar(r[:], r[:], -MAGIC, -twopi, ALU.add, ALU.mult),
         reads=[r], writes=[r])
    p.op("vector", lambda E: E.tensor_add(r[:], r[:], phi[:]), reads=[r, phi], writes=[r])
    p.op("scalar", lambda E: E.activation(sin_o[:], r[:], AF.Sin), reads=[r], writes=[sin_o])
    p.op("scalar", lambda E: E.activation(cos_o[:], r[:], AF.Abs), reads=[r], writes=[cos_o])
    p.op("scalar", lambda E: E.activation(cos_o[:], cos_o[:], AF.Sin,
                                          bias=c["pio2"][:], scale=c["neg1"][:]),
         reads=[cos_o, c["pio2"], c["neg1"]], writes=[cos_o])


def build_launch_A():
    cx = Ctx()
    nc, p = cx.nc, cx.p

    xT = nc.dram_tensor("xT", [D, NT], F32, kind="ExternalInput")
    xTo = nc.dram_tensor("xTo", [D, TOK], F32, kind="ExternalInput")
    Wr16 = nc.dram_tensor("Wr16", [D, 16], F32, kind="ExternalInput")
    WomT = nc.dram_tensor("WomT", [D, D], F32, kind="ExternalInput")
    b_om = nc.dram_tensor("b_om", [128, 8], F32, kind="ExternalInput")
    lsc = nc.dram_tensor("lsc", [128, 8], F32, kind="ExternalInput")
    possc = nc.dram_tensor("possc", [1, TOK], F32, kind="ExternalInput")
    moeW1 = nc.dram_tensor("moeW1", [8, D, 4096], F32, kind="ExternalInput")
    moeW2 = nc.dram_tensor("moeW2", [8, 4096, D], F32, kind="ExternalInput")
    e9iota = nc.dram_tensor("e9iota", [1, 1152], F32, kind="ExternalInput")
    cmask = nc.dram_tensor("cmask", [1, 8], F32, kind="ExternalInput")
    bidx = nc.dram_tensor("bidx", [16, 3], F32, kind="ExternalInput")
    ridx = nc.dram_tensor("ridx", [2, 1], F32, kind="ExternalInput")
    e8colD = nc.dram_tensor("e8colD", [8, 1], F32, kind="ExternalInput")
    identD = nc.dram_tensor("identD", [128, 128], F32, kind="ExternalInput")

    o_phi = nc.dram_tensor("o_phi", [D, TOK], F32, kind="ExternalOutput")
    o_phic = nc.dram_tensor("o_phic", [128, 8], F32, kind="ExternalOutput")
    o_moe = nc.dram_tensor("o_moe", [TOK, D], F32, kind="ExternalOutput")
    o_stat = nc.dram_tensor("o_stat", [2, TOK], F32, kind="ExternalOutput")
    o_rows = nc.dram_tensor("o_rows", [3, NT], F32, kind="ExternalOutput")
    o_cmc = nc.dram_tensor("o_cmc", [128, 16], F32, kind="ExternalOutput")
    scr_row = nc.dram_tensor("scr_row", [5, NT], F32, kind="Internal")
    scr_xe = nc.dram_tensor("scr_xe", [8 * PAD, D], F32, kind="Internal")
    scr_ye = nc.dram_tensor("scr_ye", [8 * PAD, D], F32, kind="Internal")
    scr_b = nc.dram_tensor("scr_b", [8, 1], F32, kind="Internal")

    ident = cx.sb("ident", [128, 128])
    p.dma("sync", ident[:], identD.ap())
    rowview = scr_row.ap().rearrange("r (b j) -> (r b) j", j=128)
    rowview8 = scr_row.ap().rearrange("r (c t) -> (r c) t", t=TOK)

    eT = cx.sb("eT", [128, 16])
    pT = cx.sb("pT", [128, 16])
    sidx_i = cx.sb("sidx_i", [128, 16], I32)
    gidx_i = cx.sb("gidx_i", [128, 16], I32)
    baserow = cx.sb("baserow", [1, 8])
    xn0 = [cx.sb(f"xn0_{k}", [128, TOK]) for k in range(8)]

    esR = ExitStack()
    xt = [cx.sb(f"xt{k}", [128, 512], es=esR) for k in range(8)]
    sq2 = [cx.sb(f"sq{k}", [128, 512], es=esR) for k in range(2)]
    wr16 = cx.sb("wr16", [128, 16 * 8], es=esR)
    logit = cx.sb("logit", [128, 1152], es=esR)
    rb1 = cx.sb("rb1", [1, 512], es=esR)
    rb2 = cx.sb("rb2", [1, 512], es=esR)
    for k in range(8):
        p.dma("scalar", wr16[:, 16 * k:16 * (k + 1)], Wr16.ap()[128 * k:128 * (k + 1), :])
    for ch in range(32):
        for k in range(8):
            p.dma("sync", xt[k][:], xT.ap()[128 * k:128 * (k + 1), 512 * ch:512 * (ch + 1)])
        pS, pQ = cx.bank(), cx.bank()
        for k in range(8):
            sq = sq2[k % 2]
            p.op("scalar", lambda E, k=k, sq=sq: E.activation(sq[:], xt[k][:], AF.Square),
                 reads=[xt[k]], writes=[sq])
            p.op("tensor", lambda E, k=k, pS=pS: E.matmul(
                pS[0:1, :], cx.onescol[:], xt[k][:], start=(k == 0), stop=(k == 7)),
                reads=[xt[k], cx.onescol], writes=[pS])
            p.op("tensor", lambda E, sq=sq, k=k, pQ=pQ: E.matmul(
                pQ[0:1, :], cx.onescol[:], sq[:], start=(k == 0), stop=(k == 7)),
                reads=[sq, cx.onescol], writes=[pQ])
        p.op("vector", lambda E, pS=pS: E.tensor_copy(rb1[:], pS[0:1, :]),
             reads=[pS], writes=[rb1])
        p.op("vector", lambda E, pQ=pQ: E.tensor_copy(rb2[:], pQ[0:1, :]),
             reads=[pQ], writes=[rb2])
        p.dma("sync", scr_row.ap()[3:4, 512 * ch:512 * (ch + 1)], rb1[:])
        p.dma("sync", scr_row.ap()[4:5, 512 * ch:512 * (ch + 1)], rb2[:])
        for w in range(4):
            pL = cx.bank()
            for k in range(8):
                p.op("tensor", lambda E, k=k, w=w, pL=pL: E.matmul(
                    pL[:, 0:16], xt[k][:, 128 * w:128 * (w + 1)],
                    wr16[:, 16 * k:16 * (k + 1)], start=(k == 0), stop=(k == 7)),
                    reads=[xt[k], wr16], writes=[pL])
            W = ch * 4 + w
            p.op("scalar", lambda E, W=W, pL=pL: E.copy(logit[:, 9 * W:9 * W + 9], pL[:, 0:9]),
                 reads=[pL], writes=[logit])

    muw = cx.sb("muw", [128, 128], es=esR)
    rsw = cx.sb("rsw", [128, 128], es=esR)
    vtmp = cx.sb("vtmp", [128, 128], es=esR)
    blk = cx.sb("blk", [128, 128], es=esR)
    p.dma("sync", blk[:], rowview[3 * 128:4 * 128, :])
    cx.transpose(muw, blk[:], 128, 128, ident)
    p.dma("sync", blk[:], rowview[4 * 128:5 * 128, :])
    cx.transpose(vtmp, blk[:], 128, 128, ident)
    p.op("vector", lambda E: E.tensor_scalar(muw[:], muw[:], 1.0 / D, None, ALU.mult),
         reads=[muw], writes=[muw])
    p.op("vector", lambda E: E.tensor_scalar(vtmp[:], vtmp[:], 1.0 / D, None, ALU.mult),
         reads=[vtmp], writes=[vtmp])
    p.op("scalar", lambda E: E.activation(rsw[:], muw[:], AF.Square),
         reads=[muw], writes=[rsw])
    p.op("vector", lambda E: E.tensor_sub(vtmp[:], vtmp[:], rsw[:]),
         reads=[vtmp, rsw], writes=[vtmp])
    p.op("vector", lambda E: E.tensor_scalar(vtmp[:], vtmp[:], float(LN_EPS), None, ALU.add),
         reads=[vtmp], writes=[vtmp])
    varw = cx.sb("varw", [128, 128], es=esR)
    p.op("vector", lambda E: E.tensor_scalar(varw[:], vtmp[:], -float(LN_EPS), None, ALU.add),
         reads=[vtmp], writes=[varw])
    p.op("scalar", lambda E: E.activation(vtmp[:], vtmp[:], AF.Sqrt),
         reads=[vtmp], writes=[vtmp])
    p.op("vector", lambda E: E.reciprocal(rsw[:], vtmp[:]), reads=[vtmp], writes=[rsw])
    sxn2w = cx.sb("sxn2w", [128, 128], es=esR)
    p.op("scalar", lambda E: E.activation(sxn2w[:], rsw[:], AF.Square),
         reads=[rsw], writes=[sxn2w])
    p.op("vector", lambda E: E.tensor_mul(sxn2w[:], sxn2w[:], varw[:]),
         reads=[sxn2w, varw], writes=[sxn2w])
    p.op("vector", lambda E: E.tensor_scalar(sxn2w[:], sxn2w[:], float(D), None, ALU.mult),
         reads=[sxn2w], writes=[sxn2w])

    cw = cx.sb("cw", [1, 16], es=esR)
    cwrep = cx.sb("cwrep", [128, 16], es=esR)
    t1 = cx.sb("t1", [128, 1152], es=esR)
    pC = cx.bank()
    for k in range(8):
        p.op("tensor", lambda E, k=k, pC=pC: E.matmul(
            pC[0:1, 0:16], cx.onescol[:], wr16[:, 16 * k:16 * (k + 1)],
            start=(k == 0), stop=(k == 7)), reads=[wr16, cx.onescol], writes=[pC])
    p.op("vector", lambda E, pC=pC: E.tensor_copy(cw[:], pC[0:1, 0:16]),
         reads=[pC], writes=[cw])
    cx.bcast_row(cw, cwrep, 16)

    def we(x):
        return x.rearrange("p w -> p w ()").broadcast_to((128, 128, 9))

    L3 = logit[:].rearrange("p (w e) -> p w e", e=9)
    T3 = t1[:].rearrange("p (w e) -> p w e", e=9)
    p.op("vector", lambda E: E.tensor_tensor(
        t1[:], we(muw[:]),
        cwrep[:, 0:9].rearrange("p e -> p () e").broadcast_to((128, 128, 9)), ALU.mult),
        reads=[muw, cwrep], writes=[t1])
    p.op("vector", lambda E: E.tensor_sub(logit[:], logit[:], t1[:]),
         reads=[logit, t1], writes=[logit])
    p.op("vector", lambda E: E.tensor_tensor(logit[:], logit[:], we(rsw[:]), ALU.mult),
         reads=[logit, rsw], writes=[logit])

    pmx = cx.sb("pmx", [128, 128], es=esR)
    ptk = cx.sb("ptk", [128, 128], es=esR)
    eix = cx.sb("eix", [128, 128], es=esR)
    e9 = cx.sb("e9", [128, 1152], es=esR)
    e9row = cx.sb("e9row", [1, 1152], es=esR)
    p.dma("scalar", e9row[:], e9iota.ap())
    cx.bcast_row(e9row, e9, 1152)
    p.op("vector", lambda E: E.tensor_reduce(pmx[:], L3[:, :, 0:8], AX, ALU.max),
         reads=[logit], writes=[pmx])
    p.op("vector", lambda E: E.tensor_tensor(t1[:], logit[:], we(pmx[:]), ALU.subtract),
         reads=[logit, pmx], writes=[t1])
    p.op("scalar", lambda E: E.activation(t1[:], t1[:], AF.Exp), reads=[t1], writes=[t1])
    p.op("vector", lambda E: E.tensor_reduce(ptk[:], T3[:, :, 0:8], AX, ALU.add),
         reads=[t1], writes=[ptk])
    p.op("vector", lambda E: E.reciprocal(ptk[:], ptk[:]), reads=[ptk], writes=[ptk])
    p.op("vector", lambda E: E.tensor_tensor(t1[:], logit[:], we(pmx[:]), ALU.is_ge),
         reads=[logit, pmx], writes=[t1])
    p.op("vector", lambda E: E.tensor_mul(t1[:], t1[:], e9[:]), reads=[t1, e9], writes=[t1])
    p.op("vector", lambda E: E.tensor_reduce(eix[:], T3[:, :, 0:8], AX, ALU.add),
         reads=[t1], writes=[eix])

    for rid, t_src in ((0, eix), (1, ptk), (3, muw), (4, rsw)):
        cx.transpose(blk, t_src[:], 128, 128, ident)
        p.dma("sync", rowview[rid * 128:(rid + 1) * 128, :], blk[:])
        if rid == 3:
            p.dma("sync", o_rows.ap()[0:1, :].rearrange("a (w q) -> (a w) q", q=128), blk[:])
        if rid == 4:
            p.dma("sync", o_rows.ap()[1:2, :].rearrange("a (w q) -> (a w) q", q=128), blk[:])
    cx.transpose(blk, sxn2w[:], 128, 128, ident)
    p.dma("sync", o_rows.ap()[2:3, :].rearrange("a (w q) -> (a w) q", q=128), blk[:])

    SEG = 4096
    erow = cx.sb("erow", [1, SEG], es=esR)
    e8 = cx.sb("e8", [8, SEG], es=esR)
    oh = cx.sb("oh", [8, SEG], es=esR)
    rk8 = cx.sb("rk8", [8, SEG], es=esR)
    e8col = cx.sb("e8col", [8, 1], es=esR)
    ones8 = cx.sb("ones8", [8, 1], es=esR)
    carry8 = cx.sb("carry8", [8, 1], es=esR)
    rrow = cx.sb("rrow", [1, SEG], es=esR)
    base8 = cx.sb("base8", [8, 8], es=esR)
    p.op("vector", lambda E: E.memset(ones8[:], 1.0), writes=[ones8])
    p.op("vector", lambda E: E.memset(carry8[:], 0.0), writes=[carry8])
    p.dma("scalar", e8col[:], e8colD.ap())
    for sg in range(NT // SEG):
        p.dma("scalar", erow[:], scr_row.ap()[0:1, SEG * sg:SEG * (sg + 1)])
        for j in range(SEG // 512):
            pt = cx.bank()
            p.op("tensor", lambda E, j=j, pt=pt: E.matmul(
                pt[0:8, :], cx.ones1[0:1, 0:8], erow[0:1, 512 * j:512 * (j + 1)],
                start=True, stop=True), reads=[erow, cx.ones1], writes=[pt])
            p.op("scalar", lambda E, j=j, pt=pt: E.copy(e8[:, 512 * j:512 * (j + 1)], pt[0:8, :]),
                 reads=[pt], writes=[e8])
        p.op("vector", lambda E: E.tensor_scalar(oh[:], e8[:], e8col[:], None, ALU.is_equal),
             reads=[e8, e8col], writes=[oh])
        p.op("vector", lambda E: E.tensor_tensor_scan(
            rk8[:], ones8[:].broadcast_to((8, SEG)), oh[:], carry8[:], ALU.mult, ALU.add),
            reads=[oh, ones8, carry8], writes=[rk8])
        p.op("vector", lambda E: E.tensor_copy(carry8[:], rk8[:, SEG - 1:SEG]),
             reads=[rk8], writes=[carry8])
        p.op("vector", lambda E: E.tensor_sub(rk8[:], rk8[:], oh[:]),
             reads=[rk8, oh], writes=[rk8])
        p.op("vector", lambda E, sg=sg: E.tensor_copy(base8[:, 2 * sg:2 * sg + 1], rk8[:, 0:1]),
             reads=[rk8], writes=[base8])
        p.op("vector", lambda E, sg=sg: E.tensor_copy(
            base8[:, 2 * sg + 1:2 * sg + 2], rk8[:, TOK:TOK + 1]),
            reads=[rk8], writes=[base8])
        p.op("vector", lambda E: E.tensor_mul(oh[:], oh[:], rk8[:]), reads=[oh, rk8], writes=[oh])
        for j in range(SEG // 512):
            pt = cx.bank()
            p.op("tensor", lambda E, j=j, pt=pt: E.matmul(
                pt[0:1, :], ones8[:], oh[:, 512 * j:512 * (j + 1)],
                start=True, stop=True), reads=[oh, ones8], writes=[pt])
            p.op("vector", lambda E, j=j, pt=pt: E.tensor_copy(
                rrow[0:1, 512 * j:512 * (j + 1)], pt[0:1, :]), reads=[pt], writes=[rrow])
        p.dma("sync", scr_row.ap()[2:3, SEG * sg:SEG * (sg + 1)], rrow[:])

    cmk = cx.sb("cmk", [1, 8], es=esR)
    cmk8 = cx.sb("cmk8", [8, 8], es=esR)
    bsel = cx.sb("bsel", [8, 1], es=esR)
    p.dma("scalar", cmk[:], cmask.ap())
    pt = cx.bank()
    p.op("tensor", lambda E, pt=pt: E.matmul(pt[0:8, 0:8], cx.ones1[0:1, 0:8], cmk[0:1, :],
                                             start=True, stop=True),
         reads=[cmk, cx.ones1], writes=[pt])
    p.op("scalar", lambda E, pt=pt: E.copy(cmk8[:], pt[0:8, 0:8]), reads=[pt], writes=[cmk8])
    p.op("vector", lambda E: E.tensor_mul(cmk8[:], cmk8[:], base8[:]),
         reads=[cmk8, base8], writes=[cmk8])
    p.op("vector", lambda E: E.tensor_reduce(bsel[:], cmk8[:], AX, ALU.add),
         reads=[cmk8], writes=[bsel])
    p.dma("sync", scr_b.ap(), bsel[:])
    p.dma("sync", baserow[:], scr_b.ap().rearrange("a b -> b a"))
    esR.close()
    p.barrier()

    esM = ExitStack()
    own3 = [cx.sb(f"own{i}", [16, 128], es=esM) for i in range(3)]
    bidx_sb = cx.sb("bidx_sb", [16, 3], es=esM)
    bidx_i = cx.sb("bidx_i", [16, 3], I32, es=esM)
    p.dma("scalar", bidx_sb[:], bidx.ap())
    p.op("gpsimd", lambda E: E.tensor_copy(bidx_i[:], bidx_sb[:]),
         reads=[bidx_sb], writes=[bidx_i])
    rT = cx.sb("rT", [128, 16], es=esM)
    for i, dst in ((0, eT), (1, pT), (2, rT)):
        p.dma_ind(own3[i][:], None, rowview, IndirectOffsetOnAxis(bidx_i[:, i:i + 1], 0), 5 * 128 - 1)
        cx.transpose(dst, own3[i][:], 16, 128, ident)

    baserep = cx.sb("baserep", [128, 8], es=esM)
    cx.bcast_row(baserow, baserep, 8)
    msk = cx.sb("msk", [128, 16], es=esM)
    btok = cx.sb("btok", [128, 16], es=esM)
    loc = cx.sb("loc", [128, 16], es=esM)
    sidx_f = cx.sb("sidx_f", [128, 16], es=esM)
    gidx_f = cx.sb("gidx_f", [128, 16], es=esM)
    p.op("vector", lambda E: E.memset(btok[:], 0.0), writes=[btok])
    for e in range(8):
        p.op("vector", lambda E, e=e: E.tensor_scalar(msk[:], eT[:], float(e), None, ALU.is_equal),
             reads=[eT], writes=[msk])
        p.op("vector", lambda E, e=e: E.scalar_tensor_tensor(
            btok[:], msk[:], baserep[:, e:e + 1], btok[:], ALU.mult, ALU.add),
            reads=[msk, baserep, btok], writes=[btok])
    p.op("vector", lambda E: E.tensor_sub(loc[:], rT[:], btok[:]), reads=[rT, btok], writes=[loc])
    p.op("vector", lambda E: E.tensor_scalar(sidx_f[:], eT[:], float(PAD), None, ALU.mult),
         reads=[eT], writes=[sidx_f])
    p.op("vector", lambda E: E.tensor_add(sidx_f[:], sidx_f[:], loc[:]),
         reads=[sidx_f, loc], writes=[sidx_f])
    p.op("vector", lambda E: E.tensor_scalar(msk[:], loc[:], float(PAD), None, ALU.is_ge),
         reads=[loc], writes=[msk])
    p.op("vector", lambda E: E.scalar_tensor_tensor(
        sidx_f[:], msk[:], 1.0e6, sidx_f[:], ALU.mult, ALU.add),
        reads=[msk, sidx_f], writes=[sidx_f])
    p.op("vector", lambda E: E.tensor_scalar(msk[:], rT[:], 2048.0, None, ALU.is_ge),
         reads=[rT], writes=[msk])
    p.op("vector", lambda E: E.scalar_tensor_tensor(
        gidx_f[:], msk[:], 1.0e6, sidx_f[:], ALU.mult, ALU.add),
        reads=[msk, sidx_f], writes=[gidx_f])
    p.op("gpsimd", lambda E: E.tensor_copy(sidx_i[:], sidx_f[:]), reads=[sidx_f], writes=[sidx_i])
    p.op("gpsimd", lambda E: E.tensor_copy(gidx_i[:], gidx_f[:]), reads=[gidx_f], writes=[gidx_i])

    ridx_sb = cx.sb("ridx_sb", [2, 1], es=esM)
    ridx_i = cx.sb("ridx_i", [2, 1], I32, es=esM)
    mu_own = cx.sb("mu_own", [1, TOK], es=esM)
    rs_own = cx.sb("rs_own", [1, TOK], es=esM)
    mu_rep = cx.sb("mu_rep", [128, TOK], es=esM)
    rs_rep = cx.sb("rs_rep", [128, TOK], es=esM)
    p.dma("scalar", ridx_sb[:], ridx.ap())
    p.op("gpsimd", lambda E: E.tensor_copy(ridx_i[:], ridx_sb[:]), reads=[ridx_sb], writes=[ridx_i])
    stat2 = cx.sb("stat2", [2, TOK], es=esM)
    p.dma_ind(stat2[:], None, rowview8, IndirectOffsetOnAxis(ridx_i[:], 0), 5 * 8 - 1)
    p.dma("sync", o_stat.ap(), stat2[:])
    p.dma("scalar", mu_own[:], stat2[0:1, :], reads=[stat2])
    p.dma("scalar", rs_own[:], stat2[1:2, :], reads=[stat2])
    cx.bcast_row(mu_own, mu_rep, TOK)
    cx.bcast_row(rs_own, rs_rep, TOK)
    for k in range(8):
        p.dma("sync", xn0[k][:], xTo.ap()[128 * k:128 * (k + 1), :])
        p.op("vector", lambda E, k=k: E.tensor_sub(xn0[k][:], xn0[k][:], mu_rep[:]),
             reads=[xn0[k], mu_rep], writes=[xn0[k]])
        p.op("vector", lambda E, k=k: E.tensor_mul(xn0[k][:], xn0[k][:], rs_rep[:]),
             reads=[xn0[k], rs_rep], writes=[xn0[k]])
    esM.close()
    p.barrier()

    esS = ExitStack()
    zero_t = cx.sb("zero_t", [128, D], es=esS)
    rowt = cx.sb("rowt", [128, D], es=esS)
    p.op("vector", lambda E: E.memset(zero_t[:], 0.0), writes=[zero_t])
    for j in range(32):
        p.dma("scalar", scr_xe.ap()[128 * j:128 * (j + 1), :], zero_t[:])
    for j in range(16):
        for k in range(8):
            cx.transpose(rowt, xn0[k][:, 128 * j:128 * (j + 1)], 128, 128, ident,
                         out_off=128 * k)
        p.dma_ind(scr_xe.ap(), IndirectOffsetOnAxis(sidx_i[:, j:j + 1], 0),
                  rowt[:], None, 8 * PAD - 1)
    esS.close()
    p.barrier()

    esO = ExitStack()
    womk = [cx.sb(f"womk{k}", [128, D], es=esO) for k in range(8)]
    b_om_t = cx.sb("b_om_t", [128, 8], es=esO)
    lsce = cx.sb("lsce", [128, 8], es=esO)
    possr = cx.sb("possr", [1, TOK], es=esO)
    possc_r = cx.sb("possc_r", [128, TOK], es=esO)
    onesT = cx.sb("onesT", [128, TOK], es=esO)
    om_t = cx.sb("om_t", [128, TOK], es=esO)
    phi_t = cx.sb("phi_t", [128, TOK], es=esO)
    phic_t = cx.sb("phic_t", [128, 8], es=esO)
    cmc_t = cx.sb("cmc_t", [128, 16], es=esO)
    trg1 = cx.sb("trg1", [128, TOK], es=esO)
    trg2 = cx.sb("trg2", [128, TOK], es=esO)
    consts = trig_consts(cx, esO)
    p.dma("scalar", b_om_t[:], b_om.ap())
    p.dma("scalar", lsce[:], lsc.ap())
    p.op("scalar", lambda E: E.activation(lsce[:], lsce[:], AF.Exp), reads=[lsce], writes=[lsce])
    p.dma("scalar", possr[:], possc.ap())
    cx.bcast_row(possr, possc_r, TOK)
    p.op("vector", lambda E: E.memset(onesT[:], 1.0), writes=[onesT])
    for k in range(8):
        p.dma("sync", womk[k][:], WomT.ap()[128 * k:128 * (k + 1), :])
    for et in range(8):
        for ch in range(4):
            pt = cx.bank()
            for k in range(8):
                p.op("tensor", lambda E, k=k, et=et, ch=ch, pt=pt: E.matmul(
                    pt[:], womk[k][:, 128 * et:128 * (et + 1)],
                    xn0[k][:, 512 * ch:512 * (ch + 1)], start=(k == 0), stop=(k == 7)),
                    reads=[womk[k], xn0[k]], writes=[pt])
            p.op("vector", lambda E, et=et, ch=ch, pt=pt: E.tensor_scalar(
                om_t[:, 512 * ch:512 * (ch + 1)], pt[:], b_om_t[:, et:et + 1],
                lsce[:, et:et + 1], ALU.add, ALU.mult),
                reads=[pt, b_om_t, lsce], writes=[om_t])
        p.op("vector", lambda E, et=et: E.tensor_mul(om_t[:], om_t[:], possc_r[:]),
             reads=[om_t, possc_r], writes=[om_t])
        p.op("vector", lambda E, et=et: E.tensor_tensor_scan(
            phi_t[:], onesT[:], om_t[:], 0.0, ALU.mult, ALU.add),
            reads=[om_t, onesT], writes=[phi_t])
        p.op("vector", lambda E, et=et: E.tensor_copy(phic_t[:, et:et + 1], phi_t[:, TOK - 1:TOK]),
             reads=[phi_t], writes=[phic_t])
        p.dma("sync", o_phi.ap()[128 * et:128 * (et + 1), :], phi_t[:])
        trig_block(cx, phi_t, om_t, trg1, trg2, consts)
        p.op("vector", lambda E, et=et: E.tensor_mul(trg2[:], trg2[:], xn0[et][:]),
             reads=[trg2, xn0[et]], writes=[trg2])
        p.op("vector", lambda E, et=et: E.tensor_mul(trg1[:], trg1[:], xn0[et][:]),
             reads=[trg1, xn0[et]], writes=[trg1])
        p.op("vector", lambda E, et=et: E.tensor_reduce(
            cmc_t[:, et:et + 1], trg2[:], AX, ALU.add), reads=[trg2], writes=[cmc_t])
        p.op("vector", lambda E, et=et: E.tensor_reduce(
            cmc_t[:, 8 + et:8 + et + 1], trg1[:], AX, ALU.add), reads=[trg1], writes=[cmc_t])
    p.dma("sync", o_phic.ap(), phic_t[:])
    p.dma("sync", o_cmc.ap(), cmc_t[:])
    esO.close()
    p.barrier()

    esE = ExitStack()
    mw = [cx.sb(f"mw{k}", [128, 1024], es=esE) for k in range(8)]
    w2b = [cx.sb(f"w2b{k}", [128, 1024], BF16, es=esE) for k in range(2)]
    mxeT = [cx.sb(f"mxeT{k}", [128, PAD], es=esE) for k in range(8)]
    mxb = [cx.sb(f"mxb{k}", [128, PAD], BF16, es=esE) for k in range(8)]
    mh = [cx.sb(f"mh{f}", [128, PAD], BF16, es=esE) for f in range(32)]
    myeo = [cx.sb(f"myeo{o}", [128, PAD], es=esE) for o in range(8)]
    rowe = cx.sb("rowe", [128, D], es=esE)
    for e in range(8):
        for j in range(4):
            p.dma("sync", rowe[:], scr_xe.ap()[e * PAD + 128 * j:e * PAD + 128 * (j + 1), :])
            for k in range(8):
                cx.transpose(mxeT[k], rowe[:, 128 * k:128 * (k + 1)], 128, 128, ident,
                             out_off=128 * j)
        for k in range(8):
            p.op("vector", lambda E, k=k: E.tensor_copy(mxb[k][:], mxeT[k][:]),
                 reads=[mxeT[k]], writes=[mxb[k]])
        for fb in range(4):
            for k in range(8):
                p.dma("gpsimd", mw[k][:],
                      moeW1.ap()[e, 128 * k:128 * (k + 1), 1024 * fb:1024 * (fb + 1)])
            for k in range(8):
                for fi in range(8):
                    bk = cx.banks[fi]
                    p.op("tensor", lambda E, k=k, fi=fi, bk=bk: E.matmul(
                        bk[:], mw[k][:, 128 * fi:128 * (fi + 1)], mxeT[k][:],
                        start=(k == 0), stop=(k == 7)),
                        reads=[mw[k], mxeT[k]], writes=[bk])
            for fi in range(8):
                f = fb * 8 + fi
                bk = cx.banks[fi]
                p.op("scalar", lambda E, f=f, bk=bk: E.activation(mh[f][:], bk[:], AF.Relu),
                     reads=[bk], writes=[mh[f]])
        for f in range(32):
            p.dma("gpsimd", w2b[f % 2][:], moeW2.ap()[e, 128 * f:128 * (f + 1), :])
            for oi in range(8):
                bk = cx.banks[oi]
                p.op("tensor", lambda E, f=f, oi=oi, bk=bk: E.matmul(
                    bk[:], w2b[f % 2][:, 128 * oi:128 * (oi + 1)], mh[f][:],
                    start=(f == 0), stop=(f == 31)),
                    reads=[w2b[f % 2], mh[f]], writes=[bk])
        for oi in range(8):
            bk = cx.banks[oi]
            p.op("vector", lambda E, oi=oi, bk=bk: E.tensor_copy(myeo[oi][:], bk[:]),
                 reads=[bk], writes=[myeo[oi]])
        for cb in range(4):
            for oi in range(8):
                cx.transpose(rowe, myeo[oi][:, 128 * cb:128 * (cb + 1)], 128, 128, ident,
                             out_off=128 * oi)
            p.dma("sync", scr_ye.ap()[e * PAD + 128 * cb:e * PAD + 128 * (cb + 1), :], rowe[:])
    esE.close()
    p.barrier()

    esC = ExitStack()
    rowc = cx.sb("rowc", [128, D], es=esC)
    for j in range(16):
        p.op("vector", lambda E: E.memset(rowc[:], 0.0), reads=[], writes=[rowc])
        p.dma_ind(rowc[:], None, scr_ye.ap(), IndirectOffsetOnAxis(gidx_i[:, j:j + 1], 0),
                  8 * PAD - 1)
        p.op("vector", lambda E, j=j: E.tensor_scalar(
            rowc[:], rowc[:], pT[:, j:j + 1], None, ALU.mult),
            reads=[rowc, pT], writes=[rowc])
        p.dma("sync", o_moe.ap()[128 * j:128 * (j + 1), :], rowc[:])
    esC.close()
    p.barrier()

    p.finish()
    return nc



def build_launch_B():
    cx = Ctx()
    nc, p = cx.nc, cx.p
    CH = 512
    NCH = TOK // CH

    xTo = nc.dram_tensor("xTo", [D, TOK], F32, kind="ExternalInput")
    stato = nc.dram_tensor("stato", [2, TOK], F32, kind="ExternalInput")
    phiD = nc.dram_tensor("phiD", [D, TOK], F32, kind="ExternalInput")
    phicI = nc.dram_tensor("phicI", [128, 8], F32, kind="ExternalInput")
    cmcI = nc.dram_tensor("cmcI", [128, 16], F32, kind="ExternalInput")
    sxn2o = nc.dram_tensor("sxn2o", [1, TOK], F32, kind="ExternalInput")
    moeD = nc.dram_tensor("moeD", [TOK, D], F32, kind="ExternalInput")
    posinv = nc.dram_tensor("posinv", [1, TOK], F32, kind="ExternalInput")
    posscD = nc.dram_tensor("posscD", [1, TOK], F32, kind="ExternalInput")
    W1TD = nc.dram_tensor("W1TD", [4096, 2048], F32, kind="ExternalInput")
    g4 = nc.dram_tensor("g4", [128, 32], F32, kind="ExternalInput")
    b4 = nc.dram_tensor("b4", [128, 32], F32, kind="ExternalInput")
    b1r = nc.dram_tensor("b1r", [1, 2048], F32, kind="ExternalInput")
    W2TD = nc.dram_tensor("W2TD", [2048, 1024], F32, kind="ExternalInput")
    b2c = nc.dram_tensor("b2c", [128, 8], F32, kind="ExternalInput")
    gb1 = nc.dram_tensor("gb1", [128, 16], F32, kind="ExternalInput")
    inwD = nc.dram_tensor("inwD", [D, 4096], F32, kind="ExternalInput")
    identD = nc.dram_tensor("identD", [128, 128], F32, kind="ExternalInput")

    o_xz = nc.dram_tensor("o_xz", [4096, TOK], F32, kind="ExternalOutput")
    o_x1 = nc.dram_tensor("o_x1", [D, TOK], F32, kind="ExternalOutput")
    o_xn1 = nc.dram_tensor("o_xn1", [D, TOK], F32, kind="ExternalOutput")
    o_hl = nc.dram_tensor("o_hl", [4096, 3], F32, kind="ExternalOutput")
    o_hr = nc.dram_tensor("o_hr", [4096, 3], F32, kind="ExternalOutput")
    o_mx = nc.dram_tensor("o_mx", [1, 1], F32, kind="ExternalOutput")
    o_dbgc = nc.dram_tensor("o_dbgc", [32, 128, 512], BF16, kind="ExternalOutput")
    o_dbgh = nc.dram_tensor("o_dbgh", [16, 128, 512], BF16, kind="ExternalOutput")
    o_dbgs = nc.dram_tensor("o_dbgs", [4, 512], F32, kind="ExternalOutput")
    o_dbgt = nc.dram_tensor("o_dbgt", [6, 128, 512], F32, kind="ExternalOutput")
    scr = nc.dram_tensor("scrB", [2, 2048], F32, kind="Internal")

    ident = cx.sb("ident", [128, 128])
    p.dma("sync", ident[:], identD.ap())
    consts = trig_consts(cx, cx.es)

    mu_rep = cx.sb("mu_rep", [128, TOK])
    rs_rep = cx.sb("rs_rep", [128, TOK])
    pos_rep = cx.sb("pos_rep", [128, TOK])
    rowT = cx.sb("rowT", [1, TOK])
    p.dma("scalar", rowT[:], stato.ap()[0:1, :])
    cx.bcast_row(rowT, mu_rep, TOK)
    p.dma("scalar", rowT[:], stato.ap()[1:2, :], reads=[mu_rep])
    cx.bcast_row(rowT, rs_rep, TOK)
    p.dma("scalar", rowT[:], posinv.ap(), reads=[rs_rep])
    cx.bcast_row(rowT, pos_rep, TOK)
    g4t = cx.sb("g4t", [128, 32])
    b4t = cx.sb("b4t", [128, 32])
    b2t = cx.sb("b2t", [128, 8])
    gb1t = cx.sb("gb1t", [128, 16])
    phict = cx.sb("phict", [128, 8])
    cmct = cx.sb("cmct", [128, 16])
    p.dma("scalar", g4t[:], g4.ap())
    p.dma("scalar", b4t[:], b4.ap())
    p.dma("scalar", b2t[:], b2c.ap())
    p.dma("scalar", gb1t[:], gb1.ap())
    p.dma("scalar", phict[:], phicI.ap())
    p.dma("scalar", cmct[:], cmcI.ap())
    G1c = cx.sb("G1c", [128, 16])
    Bvc = cx.sb("Bvc", [128, 16])

    esG = ExitStack()
    w1kt = cx.sb("w1kt", [128, 2048], es=esG)
    w1s = cx.sb("w1s", [128, 2048], es=esG)
    growp = [cx.banks[i] for i in range(4)]
    browp = [cx.banks[4 + i] for i in range(4)]
    for k in range(32):
        p.dma("sync", w1kt[:], W1TD.ap()[128 * k:128 * (k + 1), :])
        p.op("vector", lambda E, k=k: E.tensor_scalar(
            w1s[:], w1kt[:], g4t[:, k:k + 1], None, ALU.mult),
            reads=[w1kt, g4t], writes=[w1s])
        for j in range(4):
            p.op("tensor", lambda E, k=k, j=j: E.matmul(
                growp[j][0:1, :], cx.onescol[:], w1s[:, 512 * j:512 * (j + 1)],
                start=(k == 0), stop=(k == 31)),
                reads=[w1s, cx.onescol], writes=[growp[j]])
        p.op("vector", lambda E, k=k: E.tensor_scalar(
            w1s[:], w1kt[:], b4t[:, k:k + 1], None, ALU.mult),
            reads=[w1kt, b4t], writes=[w1s])
        for j in range(4):
            p.op("tensor", lambda E, k=k, j=j: E.matmul(
                browp[j][0:1, :], cx.onescol[:], w1s[:, 512 * j:512 * (j + 1)],
                start=(k == 0), stop=(k == 31)),
                reads=[w1s, cx.onescol], writes=[browp[j]])
    grow = cx.sb("grow", [1, 2048], es=esG)
    brow = cx.sb("brow", [1, 2048], es=esG)
    b1row = cx.sb("b1row", [1, 2048], es=esG)
    p.dma("scalar", b1row[:], b1r.ap())
    for j in range(4):
        p.op("vector", lambda E, j=j: E.tensor_copy(
            grow[0:1, 512 * j:512 * (j + 1)], growp[j][0:1, :]),
            reads=[growp[j]], writes=[grow])
        p.op("vector", lambda E, j=j: E.tensor_copy(
            brow[0:1, 512 * j:512 * (j + 1)], browp[j][0:1, :]),
            reads=[browp[j]], writes=[brow])
    p.op("vector", lambda E: E.tensor_add(brow[:], brow[:], b1row[:]),
         reads=[brow, b1row], writes=[brow])
    blkg = cx.sb("blkg", [16, 128], es=esG)
    p.dma("sync", scr.ap()[0:1, :], grow[:])
    p.dma("sync", scr.ap()[1:2, :], brow[:])
    p.dma("sync", blkg[:], scr.ap()[0:1, :].rearrange("a (j q) -> (a j) q", q=128))
    cx.transpose(G1c, blkg[:], 16, 128, ident)
    p.dma("sync", blkg[:], scr.ap()[1:2, :].rearrange("a (j q) -> (a j) q", q=128),
          reads=[G1c])
    cx.transpose(Bvc, blkg[:], 16, 128, ident)
    esG.close()
    p.barrier()

    esC = ExitStack()
    xn0c = [cx.sb(f"xn0c{k}", [128, CH], es=esC) for k in range(8)]
    phit1 = cx.sb("phit1", [128, CH], es=esC)
    ctx = [cx.sb(f"ctx{k}", [128, CH], BF16, es=esC) for k in range(32)]
    sint = cx.sb("sint", [128, CH], es=esC)
    cost = cx.sb("cost", [128, CH], es=esC)
    smr = cx.sb("smr", [128, CH], es=esC)
    smi = cx.sb("smi", [128, CH], es=esC)
    memr = cx.sb("memr", [128, CH], es=esC)
    memi = cx.sb("memi", [128, CH], es=esC)
    sqs = cx.sb("sqs", [128, CH], es=esC)
    carry = cx.sb("carryB", [128, 16], es=esC)
    h1g = [cx.sb(f"h1g{t}", [128, CH], BF16, es=esC) for t in range(16)]
    w1c = cx.sb("w1c", [128, 128], BF16, es=esC)
    w2c = cx.sb("w2c", [128, 128], BF16, es=esC)
    statrow = cx.sb("statrow", [1, CH], es=esC)
    muc_rep = cx.sb("muc_rep", [128, CH], es=esC)
    rsc_rep = cx.sb("rsc_rep", [128, CH], es=esC)
    tas = cx.sb("tas", [128, CH], es=esC)
    tbs = cx.sb("tbs", [128, CH], es=esC)
    x1t = [cx.sb(f"x1t{k}", [128, CH], es=esC) for k in range(8)]
    hlnb = [cx.sb(f"hlnb{k}", [128, CH], BF16, es=esC) for k in range(8)]
    mxacc = cx.sb("mxacc", [128, 1], es=esC)
    mrow = cx.sb("mrow", [1, CH], es=esC)
    m2row = cx.sb("m2row", [1, CH], es=esC)
    p.op("vector", lambda E: E.memset(mxacc[:], 0.0), writes=[mxacc])
    p.op("vector", lambda E: E.tensor_copy(carry[:], cmct[:]), reads=[cmct], writes=[carry])

    for tch in range(NCH):
        sl = slice(CH * tch, CH * (tch + 1))
        cx.rot = [cx.banks[i] for i in range(6)]
        for k in range(8):
            p.dma("sync", xn0c[k][:], xTo.ap()[128 * k:128 * (k + 1), sl])
            p.op("vector", lambda E, k=k, sl=sl: E.tensor_sub(xn0c[k][:], xn0c[k][:], mu_rep[:, sl]),
                 reads=[xn0c[k], mu_rep], writes=[xn0c[k]])
            p.op("vector", lambda E, k=k, sl=sl: E.tensor_mul(xn0c[k][:], xn0c[k][:], rs_rep[:, sl]),
                 reads=[xn0c[k], rs_rep], writes=[xn0c[k]])
        pm2 = cx.banks[7]
        for et in range(8):
            p.dma("sync", phit1[:], phiD.ap()[128 * et:128 * (et + 1), sl])
            p.op("vector", lambda E, et=et: E.tensor_scalar(
                phit1[:], phit1[:], phict[:, et:et + 1], None, ALU.add),
                reads=[phit1, phict], writes=[phit1])
            trig_block(cx, phit1, sqs, sint, cost, consts)
            p.op("vector", lambda E, et=et: E.tensor_tensor(
                ctx[et][:], xn0c[et][:], cost[:], ALU.mult),
                reads=[xn0c[et], cost], writes=[ctx[et]])
            p.op("vector", lambda E, et=et: E.tensor_tensor(
                ctx[8 + et][:], xn0c[et][:], sint[:], ALU.mult),
                reads=[xn0c[et], sint], writes=[ctx[8 + et]])
            p.op("vector", lambda E, et=et: E.tensor_tensor_scan(
                smr[:], cx.onescol[:].broadcast_to((128, CH)), ctx[et][:],
                carry[:, et:et + 1], ALU.mult, ALU.add),
                reads=[ctx[et], carry, cx.onescol], writes=[smr])
            p.op("vector", lambda E, et=et: E.tensor_copy(carry[:, et:et + 1], smr[:, CH - 1:CH]),
                 reads=[smr], writes=[carry])
            p.op("vector", lambda E, et=et: E.tensor_tensor_scan(
                smi[:], cx.onescol[:].broadcast_to((128, CH)), ctx[8 + et][:],
                carry[:, 8 + et:8 + et + 1], ALU.mult, ALU.add),
                reads=[ctx[8 + et], carry, cx.onescol], writes=[smi])
            p.op("vector", lambda E, et=et: E.tensor_copy(
                carry[:, 8 + et:8 + et + 1], smi[:, CH - 1:CH]),
                reads=[smi], writes=[carry])
            p.op("vector", lambda E, sl=sl: E.tensor_mul(memr[:], smr[:], pos_rep[:, sl]),
                 reads=[smr, pos_rep], writes=[memr])
            p.op("vector", lambda E, sl=sl: E.tensor_mul(memi[:], smi[:], pos_rep[:, sl]),
                 reads=[smi, pos_rep], writes=[memi])
            if tch == 0 and et == 0:
                p.dma("sync", o_dbgt.ap()[4], xn0c[0][:], reads=[xn0c[0]])
            p.op("vector", lambda E: E.tensor_mul(smr[:], memr[:], cost[:]),
                 reads=[memr, cost], writes=[smr])
            p.op("vector", lambda E: E.tensor_mul(smi[:], memi[:], sint[:]),
                 reads=[memi, sint], writes=[smi])
            p.op("vector", lambda E, et=et: E.tensor_add(ctx[16 + et][:], smr[:], smi[:]),
                 reads=[smr, smi], writes=[ctx[16 + et]])
            p.op("vector", lambda E: E.tensor_mul(smr[:], memi[:], cost[:]),
                 reads=[memi, cost], writes=[smr])
            p.op("vector", lambda E: E.tensor_mul(smi[:], memr[:], sint[:]),
                 reads=[memr, sint], writes=[smi])
            p.op("vector", lambda E, et=et: E.tensor_sub(ctx[24 + et][:], smr[:], smi[:]),
                 reads=[smr, smi], writes=[ctx[24 + et]])
            if tch == 0 and et == 0:
                p.dma("sync", o_dbgt.ap()[0], sint[:], reads=[sint])
                p.dma("sync", o_dbgt.ap()[1], cost[:], reads=[cost])
                p.dma("sync", o_dbgt.ap()[2], rs_rep[:, 0:512], reads=[rs_rep])
                p.dma("sync", o_dbgt.ap()[3], phit1[:], reads=[phit1])
                p.dma("sync", o_dbgt.ap()[5], mu_rep[:, 0:512], reads=[mu_rep])
            p.op("scalar", lambda E: E.activation(sqs[:], memr[:], AF.Square),
                 reads=[memr], writes=[sqs])
            p.op("tensor", lambda E, et=et: E.matmul(
                pm2[0:1, 0:CH], cx.onescol[:], sqs[:],
                start=(et == 0), stop=False),
                reads=[sqs, cx.onescol], writes=[pm2])
            p.op("scalar", lambda E: E.activation(sqs[:], memi[:], AF.Square),
                 reads=[memi], writes=[sqs])
            p.op("tensor", lambda E, et=et: E.matmul(
                pm2[0:1, 0:CH], cx.onescol[:], sqs[:],
                start=False, stop=(et == 7)),
                reads=[sqs, cx.onescol], writes=[pm2])
        if tch == 0:
            for k in range(32):
                p.dma("sync", o_dbgc.ap()[k], ctx[k][:])
        pcs = cx.banks[6]
        for k in range(32):
            p.op("tensor", lambda E, k=k: E.matmul(
                pcs[0:1, 0:CH], cx.onescolb[:], ctx[k][:],
                start=(k == 0), stop=(k == 31)),
                reads=[ctx[k], cx.onescolb], writes=[pcs])
        p.op("vector", lambda E: E.tensor_scalar(
            mrow[:], pcs[0:1, 0:CH], 1.0 / 4096.0, None, ALU.mult),
            reads=[pcs], writes=[mrow])
        p.dma("scalar", statrow[:], sxn2o.ap()[0:1, sl])
        p.op("vector", lambda E: E.tensor_add(m2row[:], pm2[0:1, 0:CH], statrow[:]),
             reads=[pm2, statrow], writes=[m2row])
        p.op("vector", lambda E: E.tensor_scalar(
            m2row[:], m2row[:], 1.0 / 4096.0, None, ALU.mult),
            reads=[m2row], writes=[m2row])
        p.op("scalar", lambda E: E.activation(statrow[:], mrow[:], AF.Square),
             reads=[mrow], writes=[statrow])
        p.op("vector", lambda E: E.tensor_sub(m2row[:], m2row[:], statrow[:]),
             reads=[m2row, statrow], writes=[m2row])
        p.op("vector", lambda E: E.tensor_scalar(
            m2row[:], m2row[:], float(LN_EPS), None, ALU.add),
            reads=[m2row], writes=[m2row])
        p.op("scalar", lambda E: E.activation(m2row[:], m2row[:], AF.Sqrt),
             reads=[m2row], writes=[m2row])
        p.op("vector", lambda E: E.reciprocal(m2row[:], m2row[:]), reads=[m2row], writes=[m2row])
        cx.bcast_row(m2row, rsc_rep, CH)
        p.op("vector", lambda E: E.tensor_mul(mrow[:], mrow[:], m2row[:]),
             reads=[mrow, m2row], writes=[mrow])
        p.op("vector", lambda E: E.tensor_scalar(mrow[:], mrow[:], -1.0, None, ALU.mult),
             reads=[mrow], writes=[mrow])
        cx.bcast_row(mrow, muc_rep, CH)

        for ot in range(16):
            pw = cx.bank()
            for k in range(32):
                p.dma("gpsimd", w1c[:], W1TD.ap()[128 * k:128 * (k + 1),
                                                  128 * ot:128 * (ot + 1)])
                p.op("vector", lambda E, k=k: E.tensor_scalar(
                    w1c[:], w1c[:], g4t[:, k:k + 1], None, ALU.mult),
                    reads=[w1c, g4t], writes=[w1c])
                p.op("tensor", lambda E, k=k, pw=pw: E.matmul(
                    pw[:, 0:CH], w1c[:], ctx[k][:], start=(k == 0), stop=(k == 31)),
                    reads=[w1c, ctx[k]], writes=[pw])
            p.op("vector", lambda E, pw=pw: E.tensor_tensor(tas[:], pw[:, 0:CH], rsc_rep[:], ALU.mult),
                 reads=[pw, rsc_rep], writes=[tas])
            p.op("vector", lambda E, ot=ot: E.tensor_scalar(
                tbs[:], muc_rep[:], G1c[:, ot:ot + 1], None, ALU.mult),
                reads=[muc_rep, G1c], writes=[tbs])
            p.op("vector", lambda E: E.tensor_add(tas[:], tas[:], tbs[:]),
                 reads=[tas, tbs], writes=[tas])
            p.op("scalar", lambda E, ot=ot: E.activation(
                h1g[ot][:], tas[:], AF.Gelu, bias=Bvc[:, ot:ot + 1]),
                reads=[tas, Bvc], writes=[h1g[ot]])
        if tch == 0:
            for k in range(16):
                p.dma("sync", o_dbgh.ap()[k], h1g[k][:])
            p.dma("sync", o_dbgs.ap()[0:1, :], mrow[:])
            p.dma("sync", o_dbgs.ap()[1:2, :], m2row[:])
        for o2 in range(8):
            pw = cx.bank()
            for k in range(16):
                p.dma("gpsimd", w2c[:], W2TD.ap()[128 * k:128 * (k + 1),
                                                  128 * o2:128 * (o2 + 1)])
                p.op("tensor", lambda E, k=k, pw=pw: E.matmul(
                    pw[:, 0:CH], w2c[:], h1g[k][:], start=(k == 0), stop=(k == 15)),
                    reads=[w2c, h1g[k]], writes=[pw])
            p.op("vector", lambda E, pw=pw, o2=o2: E.tensor_scalar(
                memr[:], pw[:, 0:CH], b2t[:, o2:o2 + 1], None, ALU.add),
                reads=[pw, b2t], writes=[memr])
            p.op("vector", lambda E, o2=o2: E.tensor_add(memr[:], memr[:], xn0c[o2][:]),
                 reads=[memr, xn0c[o2]], writes=[memr])
            for jj in range(4):
                p.dma("sync", smi[:, 0:128],
                      moeD.ap()[CH * tch + 128 * jj:CH * tch + 128 * (jj + 1),
                                128 * o2:128 * (o2 + 1)], writes=[])
                cx.transpose(tbs, smi[:, 0:128], 128, 128, ident, out_off=128 * jj)
            p.op("vector", lambda E, o2=o2: E.tensor_add(memr[:], memr[:], tbs[:]),
                 reads=[memr, tbs], writes=[memr])
            p.dma("sync", xn0c[o2][:], xTo.ap()[128 * o2:128 * (o2 + 1), sl],
                  reads=[], writes=[])
            p.op("vector", lambda E, o2=o2: E.tensor_add(x1t[o2][:], memr[:], xn0c[o2][:]),
                 reads=[memr, xn0c[o2]], writes=[x1t[o2]])
            p.dma("sync", o_x1.ap()[128 * o2:128 * (o2 + 1), sl], x1t[o2][:])
        ps1, ps2 = cx.bank(), cx.bank()
        for k in range(8):
            p.op("tensor", lambda E, k=k, ps1=ps1: E.matmul(
                ps1[0:1, 0:CH], cx.onescol[:], x1t[k][:], start=(k == 0), stop=(k == 7)),
                reads=[x1t[k], cx.onescol], writes=[ps1])
            p.op("scalar", lambda E, k=k: E.activation(sqs[:], x1t[k][:], AF.Square),
                 reads=[x1t[k]], writes=[sqs])
            p.op("tensor", lambda E, k=k, ps2=ps2: E.matmul(
                ps2[0:1, 0:CH], cx.onescol[:], sqs[:], start=(k == 0), stop=(k == 7)),
                reads=[sqs, cx.onescol], writes=[ps2])
        p.op("vector", lambda E, ps1=ps1: E.tensor_scalar(
            mrow[:], ps1[0:1, 0:CH], 1.0 / D, None, ALU.mult), reads=[ps1], writes=[mrow])
        p.op("vector", lambda E, ps2=ps2: E.tensor_scalar(
            m2row[:], ps2[0:1, 0:CH], 1.0 / D, None, ALU.mult), reads=[ps2], writes=[m2row])
        p.op("scalar", lambda E: E.activation(statrow[:], mrow[:], AF.Square),
             reads=[mrow], writes=[statrow])
        p.op("vector", lambda E: E.tensor_sub(m2row[:], m2row[:], statrow[:]),
             reads=[m2row, statrow], writes=[m2row])
        p.op("vector", lambda E: E.tensor_scalar(
            m2row[:], m2row[:], float(LN_EPS), None, ALU.add), reads=[m2row], writes=[m2row])
        p.op("scalar", lambda E: E.activation(m2row[:], m2row[:], AF.Sqrt),
             reads=[m2row], writes=[m2row])
        p.op("vector", lambda E: E.reciprocal(m2row[:], m2row[:]), reads=[m2row], writes=[m2row])
        cx.bcast_row(m2row, rsc_rep, CH)
        cx.bcast_row(mrow, muc_rep, CH)
        for k in range(8):
            p.op("vector", lambda E, k=k: E.tensor_sub(x1t[k][:], x1t[k][:], muc_rep[:]),
                 reads=[x1t[k], muc_rep], writes=[x1t[k]])
            p.op("vector", lambda E, k=k: E.tensor_mul(x1t[k][:], x1t[k][:], rsc_rep[:]),
                 reads=[x1t[k], rsc_rep], writes=[x1t[k]])
            p.dma("sync", o_xn1.ap()[128 * k:128 * (k + 1), sl], x1t[k][:])
            p.op("vector", lambda E, k=k: E.tensor_reduce(
                sqs[:, 0:1], x1t[k][:], AX, ALU.max, apply_absolute_value=True),
                reads=[x1t[k]], writes=[sqs])
            p.op("vector", lambda E, k=k: E.tensor_max(mxacc[:], mxacc[:], sqs[:, 0:1]),
                 reads=[sqs, mxacc], writes=[mxacc])
            p.op("vector", lambda E, k=k: E.tensor_scalar(
                sqs[:], x1t[k][:], gb1t[:, k:k + 1], gb1t[:, 8 + k:8 + k + 1],
                ALU.mult, ALU.add), reads=[f"xn1b_{k}", gb1t], writes=[sqs])
            p.op("vector", lambda E, k=k: E.tensor_copy(hlnb[k][:], sqs[:]),
                 reads=[sqs], writes=[hlnb[k]])
        for ct in range(32):
            pw = cx.bank()
            for k in range(8):
                p.dma("gpsimd", w2c[:], inwD.ap()[128 * k:128 * (k + 1),
                                                  128 * ct:128 * (ct + 1)])
                p.op("tensor", lambda E, k=k, ct=ct, pw=pw: E.matmul(
                    pw[:, 0:CH], w2c[:], hlnb[k][:],
                    start=(k == 0), stop=(k == 7)),
                    reads=[w2c, hlnb[k]], writes=[pw])
            p.op("vector", lambda E, pw=pw: E.tensor_copy(sqs[:], pw[:, 0:CH]),
                 reads=[pw], writes=[sqs])
            p.dma("sync", o_xz.ap()[128 * ct:128 * (ct + 1), sl], sqs[:],
                  reads=[sqs])
            if tch == 0:
                p.dma("sync", o_hl.ap()[128 * ct:128 * (ct + 1), :], sqs[:, 0:3],
                      reads=[sqs])
            if tch == NCH - 1:
                p.dma("sync", o_hr.ap()[128 * ct:128 * (ct + 1), :], sqs[:, CH - 3:CH],
                      reads=[sqs])
        cx.rot = None
    esC.close()
    p.barrier()
    fin = cx.sb("finmax", [128, 1])
    p.op("vector", lambda E: E.tensor_copy(fin[:], mxacc[:]) if False else E.nop(),
         reads=[], writes=[])
    pmx = cx.bank()
    p.op("tensor", lambda E, pmx=pmx: E.transpose(pmx[0:1, 0:128], mxacc[:], ident[:]),
         reads=[mxacc, ident], writes=[pmx])
    mrow2 = cx.sb("mrow2", [1, 128])
    p.op("vector", lambda E, pmx=pmx: E.tensor_copy(mrow2[:], pmx[0:1, 0:128]),
         reads=[pmx], writes=[mrow2])
    p.op("vector", lambda E: E.tensor_reduce(mrow2[:, 0:1], mrow2[:], AX, ALU.max),
         reads=[mrow2], writes=[mrow2])
    p.dma("sync", o_mx.ap(), mrow2[0:1, 0:1], reads=[mrow2])

    p.finish()
    return nc


_NC_CACHE = {}


def _get_nc(name, builder):
    if name not in _NC_CACHE:
        _NC_CACHE[name] = builder()
    return _NC_CACHE[name]


def _f32(a):
    return np.ascontiguousarray(a, dtype=np.float32)


def host_inputs_A(ins):
    x = _f32(ins["L0_attn_x"]).reshape(NT, D)
    xT = np.ascontiguousarray(x.T)
    Wr16 = np.zeros((D, 16), np.float32)
    Wr16[:, 0:8] = _f32(ins["L0_ffn_Wr"])
    Wr16[:, 8] = 1.0
    WomT = np.ascontiguousarray(_f32(ins["L0_attn_W_omega"]).T)
    b_om = np.ascontiguousarray(_f32(ins["L0_attn_b_omega"]).reshape(8, 128).T)
    lsc = np.ascontiguousarray(_f32(ins["L0_attn_log_scale"]).reshape(8, 128).T)
    e9 = np.zeros((1, 1152), np.float32)
    e9[0, :] = np.tile(np.arange(9, dtype=np.float32), 128)
    ident = np.eye(128, dtype=np.float32)
    moeW1 = _f32(ins["L0_ffn_W1"])
    moeW2 = _f32(ins["L0_ffn_W2"])
    maps = []
    for c in range(NCORE):
        pos = np.arange(c % 2 * TOK + 1, c % 2 * TOK + TOK + 1, dtype=np.float64)
        possc = (1.0 / np.sqrt(pos)).astype(np.float32).reshape(1, TOK)
        cmask = np.zeros((1, 8), np.float32)
        cmask[0, c] = 1.0
        bidx = np.zeros((16, 3), np.float32)
        for j in range(16):
            for rid in range(3):
                bidx[j, rid] = rid * 128 + c * 16 + j
        ridxv = np.array([[3 * 8 + c], [4 * 8 + c]], np.float32)
        maps.append({
            "xT": xT, "xTo": np.ascontiguousarray(xT[:, c * TOK:(c + 1) * TOK]),
            "Wr16": Wr16, "WomT": WomT, "b_om": b_om, "lsc": lsc,
            "possc": possc, "moeW1": moeW1, "moeW2": moeW2, "e9iota": e9,
            "cmask": cmask, "bidx": bidx, "ridx": ridxv, "identD": ident,
            "e8colD": np.arange(8, dtype=np.float32).reshape(8, 1),
        })
    return maps




def build_launch_C():
    cx = Ctx()
    nc, p = cx.nc, cx.p

    xzD = nc.dram_tensor("xzD", [4096, TOK], F32, kind="ExternalInput")
    hlI = nc.dram_tensor("hlI", [2048, 3], F32, kind="ExternalInput")
    hrI = nc.dram_tensor("hrI", [2048, 3], F32, kind="ExternalInput")
    cwf = nc.dram_tensor("cwf", [128, 64], F32, kind="ExternalInput")
    cwb = nc.dram_tensor("cwb", [128, 64], F32, kind="ExternalInput")
    cbf = nc.dram_tensor("cbf", [128, 16], F32, kind="ExternalInput")
    cbb = nc.dram_tensor("cbb", [128, 16], F32, kind="ExternalInput")
    xpf = nc.dram_tensor("xpf", [2048, 96], F32, kind="ExternalInput")
    xpb = nc.dram_tensor("xpb", [2048, 96], F32, kind="ExternalInput")
    dtwf = nc.dram_tensor("dtwf", [64, 2048], F32, kind="ExternalInput")
    dtwb = nc.dram_tensor("dtwb", [64, 2048], F32, kind="ExternalInput")
    dtbf = nc.dram_tensor("dtbf", [128, 16], F32, kind="ExternalInput")
    dtbb = nc.dram_tensor("dtbb", [128, 16], F32, kind="ExternalInput")
    dskf = nc.dram_tensor("dskf", [128, 16], F32, kind="ExternalInput")
    dskb = nc.dram_tensor("dskb", [128, 16], F32, kind="ExternalInput")

    o_dt = nc.dram_tensor("o_dt", [2, 2048, TOK], BF16, kind="ExternalOutput")
    o_dtx = nc.dram_tensor("o_dtx", [2, 2048, TOK], BF16, kind="ExternalOutput")
    o_y0 = nc.dram_tensor("o_y0", [2, 2048, TOK], BF16, kind="ExternalOutput")
    o_bc = nc.dram_tensor("o_bc", [2, 32, TOK], F32, kind="ExternalOutput")

    onebias = cx.sb("onebias", [128, 1])
    p.op("vector", lambda E: E.memset(onebias[:], 1.0), writes=[onebias])
    neg1ap = consts_neg1(cx)

    for d, (cw, cb, xp, dtw, dtb, dsk, hal) in enumerate(
            (((cwf, cbf, xpf, dtwf, dtbf, dskf, hlI)),
             ((cwb, cbb, xpb, dtwb, dtbb, dskb, hrI)))):
        esD = ExitStack()
        cwt = cx.sb(f"cwt{d}", [128, 64], es=esD)
        cbt = cx.sb(f"cbt{d}", [128, 16], es=esD)
        dtbt = cx.sb(f"dtbt{d}", [128, 16], es=esD)
        dskt = cx.sb(f"dskt{d}", [128, 16], es=esD)
        p.dma("scalar", cwt[:], cw.ap())
        p.dma("scalar", cbt[:], cb.ap())
        p.dma("scalar", dtbt[:], dtb.ap())
        p.dma("scalar", dskt[:], dsk.ap())
        ext = cx.sb(f"ext{d}", [128, TOK + 8], es=esD)
        acc = cx.sb(f"acc{d}", [128, TOK], es=esD)
        xcb = [cx.sb(f"xcb{d}_{t}", [128, TOK], BF16, es=esD) for t in range(16)]
        xpt = cx.sb(f"xpt{d}", [128, 96], BF16, es=esD)
        xdbl = cx.sb(f"xdbl{d}", [96, TOK], es=esD)
        sp1 = cx.sb(f"sp1{d}", [128, 512], es=esD)
        sp2 = cx.sb(f"sp2{d}", [128, 512], es=esD)
        dtt = cx.sb(f"dtt{d}", [128, 512], BF16, es=esD)
        dxt = cx.sb(f"dxt{d}", [128, 512], BF16, es=esD)
        bct = cx.sb(f"bct{d}", [32, TOK], es=esD)
        for ct in range(16):
            p.op("vector", lambda E, ct=ct: E.memset(ext[:], 0.0), writes=[ext])
            p.dma("sync", ext[:, 3:3 + TOK], xzD.ap()[128 * ct:128 * (ct + 1), :])
            if d == 0:
                p.dma("sync", ext[:, 0:3], hal.ap()[128 * ct:128 * (ct + 1), :])
            else:
                p.dma("sync", ext[:, 3 + TOK:6 + TOK], hal.ap()[128 * ct:128 * (ct + 1), :])
            offs = [0, 1, 2, 3] if d == 0 else [6, 5, 4, 3]
            p.op("vector", lambda E, ct=ct, o=offs[0]: E.tensor_scalar(
                acc[:], ext[:, o:o + TOK], cwt[:, 16 * 0 + ct:16 * 0 + ct + 1], None, ALU.mult),
                reads=[ext, cwt], writes=[acc])
            for k in (1, 2, 3):
                p.op("vector", lambda E, ct=ct, k=k, o=offs[k]: E.scalar_tensor_tensor(
                    acc[:], ext[:, o:o + TOK], cwt[:, 16 * k + ct:16 * k + ct + 1],
                    acc[:], ALU.mult, ALU.add),
                    reads=[ext, cwt, acc], writes=[acc])
            p.op("scalar", lambda E, ct=ct: E.activation(
                xcb[ct][:], acc[:], AF.Silu, bias=cbt[:, ct:ct + 1]),
                reads=[acc, cbt], writes=[xcb[ct]])
        for ct in range(16):
            p.op("vector", lambda E, ct=ct: E.tensor_scalar(
                acc[:], xcb[ct][:], dskt[:, ct:ct + 1], None, ALU.mult),
                reads=[xcb[ct], dskt], writes=[acc])
            p.dma("gpsimd", o_y0.ap()[d, 128 * ct:128 * (ct + 1), :], acc[:], reads=[acc])
        for k in range(16):
            p.dma("gpsimd", xpt[:], xp.ap()[128 * k:128 * (k + 1), :])
            for ch in range(4):
                pt = cx.banks[ch]
                p.op("tensor", lambda E, k=k, ch=ch, pt=pt: E.matmul(
                    pt[0:96, :], xpt[:], xcb[k][:, 512 * ch:512 * (ch + 1)],
                    start=(k == 0), stop=(k == 15)),
                    reads=[xpt, xcb[k]], writes=[pt])
        for ch in range(4):
            pt = cx.banks[ch]
            p.op("vector", lambda E, ch=ch, pt=pt: E.tensor_copy(
                xdbl[:, 512 * ch:512 * (ch + 1)], pt[0:96, :]), reads=[pt], writes=[xdbl])
        p.op("vector", lambda E: E.tensor_copy(bct[:], xdbl[64:96, :]),
             reads=[xdbl], writes=[bct])
        p.dma("sync", o_bc.ap()[d], bct[:])
        dtw_t = cx.sb(f"dtw_t{d}", [64, 2048], es=esD)
        p.dma("sync", dtw_t[:], dtw.ap())
        for dt_i in range(16):
            for ch in range(4):
                pt = cx.banks[4 + (ch % 2)]
                p.op("tensor", lambda E, dt_i=dt_i, ch=ch, pt=pt: E.matmul(
                    pt[:, 0:512], dtw_t[:, 128 * dt_i:128 * (dt_i + 1)],
                    xdbl[0:64, 512 * ch:512 * (ch + 1)], start=True, stop=True),
                    reads=[dtw_t, xdbl], writes=[pt])
                p.op("vector", lambda E, dt_i=dt_i, pt=pt: E.tensor_scalar(
                    sp1[:], pt[:, 0:512], dtbt[:, dt_i:dt_i + 1], None, ALU.add),
                    reads=[pt, dtbt], writes=[sp1])
                p.op("scalar", lambda E: E.activation(sp2[:], sp1[:], AF.Abs),
                     reads=[sp1], writes=[sp2])
                p.op("scalar", lambda E: E.activation(sp2[:], sp2[:], AF.Exp,
                                                      scale=neg1ap),
                     reads=[sp2, "negone_c"], writes=[sp2])
                p.op("scalar", lambda E: E.activation(sp2[:], sp2[:], AF.Ln,
                                                      bias=onebias[:]),
                     reads=[sp2, onebias], writes=[sp2])
                p.op("scalar", lambda E: E.activation(sp1[:], sp1[:], AF.Relu),
                     reads=[sp1], writes=[sp1])
                p.op("vector", lambda E: E.tensor_add(sp1[:], sp1[:], sp2[:]),
                     reads=[sp1, sp2], writes=[sp1])
                p.op("vector", lambda E: E.tensor_copy(dtt[:], sp1[:]),
                     reads=[sp1], writes=[dtt])
                p.dma("sync", o_dt.ap()[d, 128 * dt_i:128 * (dt_i + 1),
                                        512 * ch:512 * (ch + 1)], dtt[:], reads=[dtt])
                p.op("vector", lambda E, dt_i=dt_i, ch=ch: E.tensor_tensor(
                    dxt[:], dtt[:], xcb[dt_i][:, 512 * ch:512 * (ch + 1)], ALU.mult),
                    reads=[dtt, xcb[dt_i]], writes=[dxt])
                p.dma("sync", o_dtx.ap()[d, 128 * dt_i:128 * (dt_i + 1),
                                         512 * ch:512 * (ch + 1)], dxt[:], reads=[dxt])
        esD.close()
        p.barrier()
    p.finish()
    return nc


_NEG1 = {}


def consts_neg1(cx):
    if id(cx) not in _NEG1:
        t = cx.sb("negone_c", [128, 1])
        cx.p.op("vector", lambda E: E.memset(t[:], -1.0), writes=[t])
        _NEG1[id(cx)] = t
    return _NEG1[id(cx)][:]


def build_launch_scan():
    cx = Ctx()
    nc, p = cx.nc, cx.p
    CH = 512
    NCH = TOK // CH

    dtD = nc.dram_tensor("dtD", [2048, TOK], BF16, kind="ExternalInput")
    dtxD = nc.dram_tensor("dtxD", [2048, TOK], BF16, kind="ExternalInput")
    y0D = nc.dram_tensor("y0D", [2048, TOK], BF16, kind="ExternalInput")
    bcflat = nc.dram_tensor("bcflat", [2, 16 * TOK], F32, kind="ExternalInput")
    alogD = nc.dram_tensor("alogD", [128, 256], F32, kind="ExternalInput")
    hinD = nc.dram_tensor("hinD", [128, 256], F32, kind="ExternalInput")

    o_y = nc.dram_tensor("o_y", [2048, TOK], BF16, kind="ExternalOutput")
    o_hf = nc.dram_tensor("o_hf", [128, 256], F32, kind="ExternalOutput")

    negA = cx.sb("negA", [128, 256])
    hst = cx.sb("hst", [128, 256])
    p.dma("scalar", negA[:], alogD.ap())
    p.op("scalar", lambda E: E.activation(negA[:], negA[:], AF.Exp), reads=[negA], writes=[negA])
    p.op("vector", lambda E: E.tensor_scalar(negA[:], negA[:], -1.0, None, ALU.mult),
         reads=[negA], writes=[negA])
    p.dma("scalar", hst[:], hinD.ap())

    brep = cx.sb("brep", [128, 16 * CH], BF16)
    crep = cx.sb("crep", [128, 16 * CH], BF16)
    bcrow = cx.sb("bcrow", [1, 16 * CH])
    dtt = [cx.sb(f"dtt{i}", [128, CH], BF16) for i in range(2)]
    dxx = [cx.sb(f"dxx{i}", [128, CH], BF16) for i in range(2)]
    y00 = [cx.sb(f"y00{i}", [128, CH], BF16) for i in range(2)]
    dA = cx.sb("dA", [128, CH], BF16)
    dBu = cx.sb("dBu", [128, CH], BF16)
    hh = cx.sb("hh", [128, CH], BF16)
    yC = cx.sb("yC", [128, CH], BF16)
    acc = cx.sb("accS", [128, CH], BF16)

    for ch in range(NCH):
        sl = slice(CH * ch, CH * (ch + 1))
        for row, rep in ((0, brep), (1, crep)):
            src_ap = bcflat.ap()[row:row + 1, :].rearrange(
                "a (n t) -> a n t", t=TOK)[:, :, sl]
            p.dma("scalar", bcrow[:].rearrange("a (n t) -> a n t", t=CH), src_ap,
                  reads=[rep])
            for j in range(16):
                pt = cx.bank()
                p.op("tensor", lambda E, j=j, pt=pt: E.matmul(
                    pt[:, 0:CH], cx.ones1[0:1, :], bcrow[0:1, CH * j:CH * (j + 1)],
                    start=True, stop=True), reads=[bcrow, cx.ones1], writes=[pt])
                p.op("scalar", lambda E, j=j, pt=pt, rep=rep: E.copy(
                    rep[:, CH * j:CH * (j + 1)], pt[:, 0:CH]), reads=[pt], writes=[rep])
        for dt_i in range(16):
            pb = dtt[dt_i % 2]
            px = dxx[dt_i % 2]
            py = y00[dt_i % 2]
            p.dma("sync", pb[:], dtD.ap()[128 * dt_i:128 * (dt_i + 1), sl])
            p.dma("sync", px[:], dtxD.ap()[128 * dt_i:128 * (dt_i + 1), sl])
            p.dma("sync", py[:], y0D.ap()[128 * dt_i:128 * (dt_i + 1), sl])
            p.op("vector", lambda E, py=py: E.tensor_copy(acc[:], py[:]),
                 reads=[py], writes=[acc])
            for n in range(16):
                col = dt_i * 16 + n
                p.op("scalar", lambda E, n=n, col=col, pb=pb: E.activation(
                    dA[:], pb[:], AF.Exp, scale=negA[:, col:col + 1]),
                    reads=[pb, negA], writes=[dA])
                p.op("vector", lambda E, n=n, px=px: E.tensor_tensor(
                    dBu[:], px[:], brep[:, CH * n:CH * (n + 1)], ALU.mult),
                    reads=[px, brep], writes=[dBu])
                p.op("vector", lambda E, col=col: E.tensor_tensor_scan(
                    hh[:], dA[:], dBu[:], hst[:, col:col + 1], ALU.mult, ALU.add),
                    reads=[dA, dBu, hst], writes=[hh])
                p.op("vector", lambda E, col=col: E.tensor_copy(
                    hst[:, col:col + 1], hh[:, CH - 1:CH]), reads=[hh], writes=[hst])
                p.op("vector", lambda E, n=n: E.tensor_tensor(
                    yC[:], hh[:], crep[:, CH * n:CH * (n + 1)], ALU.mult),
                    reads=[hh, crep], writes=[yC])
                p.op("vector", lambda E: E.tensor_add(acc[:], acc[:], yC[:]),
                     reads=[acc, yC], writes=[acc])
            p.dma("sync", o_y.ap()[128 * dt_i:128 * (dt_i + 1), sl], acc[:], reads=[acc])
    p.dma("sync", o_hf.ap(), hst[:])
    p.finish()
    return nc



def emit_lpbq_dequant(cx, wD, rows, cols, outT, ident, nbl):
    nc, p = cx.nc, cx.p
    es = ExitStack()
    wt = cx.sb(f"dq_wt{id(wD)}", [128, cols], es=es)
    bm = cx.sb(f"dq_bm{id(wD)}", [128, nbl], es=es)
    sinv = cx.sb(f"dq_si{id(wD)}", [128, nbl], es=es)
    rr = cx.sb(f"dq_rr{id(wD)}", [128, cols], es=es)
    wq = cx.sb(f"dq_wq{id(wD)}", [128, cols], es=es)
    w3 = wt[:].rearrange("p (b q) -> p b q", q=32)
    for ti in range(rows // 128):
        p.dma("sync", wt[:], wD.ap()[128 * ti:128 * (ti + 1), :])
        p.op("vector", lambda E: E.tensor_reduce(bm[:], w3, AX, ALU.max,
                                                 apply_absolute_value=True),
             reads=[wt], writes=[bm])
        p.op("vector", lambda E: E.tensor_scalar(bm[:], bm[:], 1.0 / 7.0, 1e-12,
                                                 ALU.mult, ALU.max),
             reads=[bm], writes=[bm])
        p.op("vector", lambda E: E.reciprocal(sinv[:], bm[:]), reads=[bm], writes=[sinv])
        p.op("vector", lambda E: E.tensor_tensor(
            rr[:], wt[:], sinv[:].rearrange("p b -> p b ()").broadcast_to((128, nbl, 32)),
            ALU.mult), reads=[wt, sinv], writes=[rr])
        p.op("vector", lambda E: E.tensor_scalar(rr[:], rr[:], MAGIC, -MAGIC,
                                                 ALU.add, ALU.add),
             reads=[rr], writes=[rr])
        p.op("vector", lambda E: E.tensor_tensor(
            wq[:], rr[:], bm[:].rearrange("p b -> p b ()").broadcast_to((128, nbl, 32)),
            ALU.mult), reads=[rr, bm], writes=[wq])
        for k in range(cols // 128):
            cx.transpose(outT[k], wq[:, 128 * k:128 * (k + 1)], 128, 128, ident,
                         out_off=128 * ti)
    es.close()
    p.barrier()


def emit_qdq(cx, src_tiles, sptr, siptr, out_tiles, n):
    p = cx.p
    for t_s, t_o in zip(src_tiles, out_tiles):
        p.op("vector", lambda E, t_s=t_s, t_o=t_o: E.tensor_scalar(
            t_o[:], t_s[:], siptr, MAGIC, ALU.mult, ALU.add),
            reads=[t_s, "qscale"], writes=[t_o])
        p.op("vector", lambda E, t_o=t_o: E.tensor_scalar(
            t_o[:], t_o[:], -MAGIC, sptr, ALU.add, ALU.mult),
            reads=[t_o, "qscale"], writes=[t_o])


def build_launch_G():
    cx = Ctx()
    nc, p = cx.nc, cx.p
    CH = 512

    yfD = nc.dram_tensor("yfD", [2048, TOK], BF16, kind="ExternalInput")
    ybD = nc.dram_tensor("ybD", [2048, TOK], BF16, kind="ExternalInput")
    zD = nc.dram_tensor("zD", [2048, TOK], F32, kind="ExternalInput")
    xn1D = nc.dram_tensor("xn1D", [D, TOK], F32, kind="ExternalInput")
    x1D = nc.dram_tensor("x1D", [D, TOK], F32, kind="ExternalInput")
    mrgD = nc.dram_tensor("mrgD", [4096, 2048], F32, kind="ExternalInput")
    outwD = nc.dram_tensor("outwD", [2048, D], F32, kind="ExternalInput")
    wupD = nc.dram_tensor("wupD", [3072, D], F32, kind="ExternalInput")
    wgtD = nc.dram_tensor("wgtD", [3072, D], F32, kind="ExternalInput")
    gmaxD = nc.dram_tensor("gmaxD", [1, 8], F32, kind="ExternalInput")
    identD = nc.dram_tensor("identD", [128, 128], F32, kind="ExternalInput")

    o_x2p = nc.dram_tensor("o_x2p", [D, TOK], F32, kind="ExternalOutput")
    o_up = nc.dram_tensor("o_up", [3072, TOK], F32, kind="ExternalOutput")
    o_gate = nc.dram_tensor("o_gate", [3072, TOK], F32, kind="ExternalOutput")
    o_mx2 = nc.dram_tensor("o_mx2", [1, 2], F32, kind="ExternalOutput")

    ident = cx.sb("ident", [128, 128])
    p.dma("sync", ident[:], identD.ap())

    esM = ExitStack()
    yk = [cx.sb(f"yk{k}", [128, CH], BF16, es=esM) for k in range(32)]
    mw = cx.sb("mwG", [128, 128], BF16, es=esM)
    gat = [cx.sb(f"gat{t}", [128, CH], BF16, es=esM) for t in range(16)]
    zt = cx.sb("ztG", [128, CH], es=esM)
    szt = cx.sb("sztG", [128, CH], BF16, es=esM)
    mrgb = cx.sb("mrgb", [128, CH], BF16, es=esM)
    t_a = cx.sb("t_aG", [128, CH], es=esM)
    for ch in range(4):
        sl = slice(CH * ch, CH * (ch + 1))
        for k in range(16):
            p.dma("sync", yk[k][:], yfD.ap()[128 * k:128 * (k + 1), sl])
            p.dma("sync", yk[16 + k][:], ybD.ap()[128 * k:128 * (k + 1), sl])
        for ot in range(16):
            pw = cx.bank()
            for k in range(32):
                p.dma("gpsimd", mw[:], mrgD.ap()[128 * k:128 * (k + 1),
                                                 128 * ot:128 * (ot + 1)])
                p.op("tensor", lambda E, k=k, pw=pw: E.matmul(
                    pw[:, 0:CH], mw[:], yk[k][:], start=(k == 0), stop=(k == 31)),
                    reads=[mw, yk[k]], writes=[pw])
            p.dma("sync", zt[:], zD.ap()[128 * ot:128 * (ot + 1), sl])
            p.op("scalar", lambda E: E.activation(szt[:], zt[:], AF.Silu),
                 reads=[zt], writes=[szt])
            p.op("scalar", lambda E, pw=pw: E.copy(mrgb[:], pw[:, 0:CH]),
                 reads=[pw], writes=[mrgb])
            p.op("vector", lambda E, ot=ot: E.tensor_tensor(gat[ot][:], mrgb[:], szt[:],
                                                            ALU.mult),
                 reads=[mrgb, szt], writes=[gat[ot]])
        for o2 in range(8):
            pw = cx.bank()
            for k in range(16):
                p.dma("gpsimd", mw[:], outwD.ap()[128 * k:128 * (k + 1),
                                                  128 * o2:128 * (o2 + 1)])
                p.op("tensor", lambda E, k=k, pw=pw: E.matmul(
                    pw[:, 0:CH], mw[:], gat[k][:], start=(k == 0), stop=(k == 15)),
                    reads=[mw, gat[k]], writes=[pw])
            p.dma("sync", zt[:], x1D.ap()[128 * o2:128 * (o2 + 1), sl])
            p.dma("sync", t_a[:], xn1D.ap()[128 * o2:128 * (o2 + 1), sl])
            p.op("vector", lambda E: E.tensor_add(t_a[:], t_a[:], zt[:]),
                 reads=[t_a, zt], writes=[t_a])
            p.op("vector", lambda E, pw=pw: E.tensor_add(t_a[:], t_a[:], pw[:, 0:CH]),
                 reads=[t_a, pw], writes=[t_a])
            p.dma("sync", o_x2p.ap()[128 * o2:128 * (o2 + 1), sl], t_a[:], reads=[t_a])
    esM.close()
    p.barrier()

    esQ = ExitStack()
    grow = cx.sb("growG", [1, 8], es=esQ)
    srow = cx.sb("srowG", [1, 1], es=esQ)
    sptr = cx.sb("sptrG", [128, 1], es=esQ)
    siptr = cx.sb("siptrG", [128, 1], es=esQ)
    p.dma("scalar", grow[:], gmaxD.ap())
    p.op("vector", lambda E: E.tensor_reduce(srow[:], grow[:], AX, ALU.max),
         reads=[grow], writes=[srow])
    p.op("vector", lambda E: E.tensor_scalar(srow[:], srow[:], 1.0 / 32767.0, 1e-12,
                                             ALU.mult, ALU.max),
         reads=[srow], writes=[srow])
    pt = cx.bank()
    p.op("tensor", lambda E, pt=pt: E.matmul(pt[:, 0:1], cx.ones1[0:1, :], srow[0:1, 0:1],
                                             start=True, stop=True),
         reads=[srow, cx.ones1], writes=[pt])
    p.op("scalar", lambda E, pt=pt: E.copy(sptr[:], pt[:, 0:1]), reads=[pt], writes=[sptr])
    p.op("vector", lambda E: E.reciprocal(siptr[:], sptr[:]), reads=[sptr], writes=[siptr])
    p.op("vector", lambda E: E.engine_nop(), reads=[sptr, siptr], writes=["qscale"])

    xqb = [cx.sb(f"xqb{k}", [128, TOK], BF16, es=esQ) for k in range(8)]
    xqt = cx.sb("xqt", [128, TOK], es=esQ)
    for k in range(8):
        p.dma("sync", xqt[:], xn1D.ap()[128 * k:128 * (k + 1), :])
        p.op("vector", lambda E: E.tensor_scalar(
            xqt[:], xqt[:], siptr[:, 0:1], MAGIC, ALU.mult, ALU.add),
            reads=[xqt, "qscale"], writes=[xqt])
        p.op("vector", lambda E: E.tensor_scalar(
            xqt[:], xqt[:], -MAGIC, sptr[:, 0:1], ALU.add, ALU.mult),
            reads=[xqt, "qscale"], writes=[xqt])
        p.op("vector", lambda E, k=k: E.tensor_copy(xqb[k][:], xqt[:]),
             reads=[xqt], writes=[xqb[k]])

    wT = [cx.sb(f"wTq{k}", [128, 3072], BF16, es=esQ) for k in range(8)]
    upt = cx.sb("uptG", [128, CH], es=esQ)
    mxa = cx.sb("mxaG", [128, 2], es=esQ)
    mrow2 = cx.sb("mrow2G", [1, 128], es=esQ)
    p.op("vector", lambda E: E.memset(mxa[:], 0.0), writes=[mxa])
    for gi, (wD, oD) in enumerate(((wupD, o_up), (wgtD, o_gate))):
        emit_lpbq_dequant(cx, wD, 3072, 1024, wT, ident, 32)
        for ot in range(24):
            for ch in range(4):
                pw = cx.bank()
                for k in range(8):
                    p.op("tensor", lambda E, k=k, ot=ot, ch=ch, pw=pw: E.matmul(
                        pw[:, 0:CH], wT[k][:, 128 * ot:128 * (ot + 1)],
                        xqb[k][:, CH * ch:CH * (ch + 1)], start=(k == 0), stop=(k == 7)),
                        reads=[wT[k], xqb[k]], writes=[pw])
                p.op("vector", lambda E, pw=pw: E.tensor_copy(upt[:], pw[:, 0:CH]),
                     reads=[pw], writes=[upt])
                p.dma("sync", oD.ap()[128 * ot:128 * (ot + 1), CH * ch:CH * (ch + 1)],
                      upt[:], reads=[upt])
                p.op("vector", lambda E: E.tensor_reduce(
                    xqt[:, 0:1], upt[:], AX, ALU.max, apply_absolute_value=True),
                    reads=[upt], writes=[xqt])
                p.op("vector", lambda E, gi=gi: E.tensor_max(
                    mxa[:, gi:gi + 1], mxa[:, gi:gi + 1], xqt[:, 0:1]),
                    reads=[xqt, mxa], writes=[mxa])
    pmx = cx.bank()
    p.op("tensor", lambda E, pmx=pmx: E.transpose(pmx[0:2, 0:128], mxa[:], ident[:]),
         reads=[mxa, ident], writes=[pmx])
    mrow22 = cx.sb("mrow22", [2, 128], es=esQ)
    p.op("vector", lambda E, pmx=pmx: E.tensor_copy(mrow22[:], pmx[0:2, 0:128]),
         reads=[pmx], writes=[mrow22])
    p.op("vector", lambda E: E.tensor_reduce(mrow22[:, 0:1], mrow22[:], AX, ALU.max),
         reads=[mrow22], writes=[mrow22])
    p.dma("sync", o_mx2.ap().rearrange("a b -> b a"), mrow22[0:2, 0:1], reads=[mrow22])
    esQ.close()
    p.finish()
    return nc


def build_launch_H():
    cx = Ctx()
    nc, p = cx.nc, cx.p
    upD = nc.dram_tensor("upD", [3072, TOK], F32, kind="ExternalInput")
    gtD = nc.dram_tensor("gtD", [3072, TOK], F32, kind="ExternalInput")
    gmaxD = nc.dram_tensor("gmaxD", [1, 16], F32, kind="ExternalInput")
    identD = nc.dram_tensor("identD", [128, 128], F32, kind="ExternalInput")
    o_upq = nc.dram_tensor("o_upq", [3072, TOK], F32, kind="ExternalOutput")
    o_t3 = nc.dram_tensor("o_t3", [3072, TOK], F32, kind="ExternalOutput")
    o_mx = nc.dram_tensor("o_mx", [1, 1], F32, kind="ExternalOutput")
    ident = cx.sb("ident", [128, 128])
    p.dma("sync", ident[:], identD.ap())
    grow = cx.sb("growH", [1, 16])
    sr = cx.sb("srH", [1, 2])
    sp_u = cx.sb("spu", [128, 2])
    si_u = cx.sb("siu", [128, 2])
    p.dma("scalar", grow[:], gmaxD.ap())
    p.op("vector", lambda E: E.tensor_reduce(
        sr[:], grow[:].rearrange("a (c q) -> a q c", q=2), AX, ALU.max),
        reads=[grow], writes=[sr])
    p.op("vector", lambda E: E.tensor_scalar(sr[:], sr[:], 1.0 / 32767.0, 1e-12,
                                             ALU.mult, ALU.max), reads=[sr], writes=[sr])
    pt = cx.bank()
    p.op("tensor", lambda E, pt=pt: E.matmul(pt[:, 0:2], cx.ones1[0:1, :], sr[0:1, :],
                                             start=True, stop=True),
         reads=[sr, cx.ones1], writes=[pt])
    p.op("scalar", lambda E, pt=pt: E.copy(sp_u[:], pt[:, 0:2]), reads=[pt], writes=[sp_u])
    p.op("vector", lambda E: E.reciprocal(si_u[:], sp_u[:]), reads=[sp_u], writes=[si_u])
    p.op("vector", lambda E: E.engine_nop(), reads=[sp_u, si_u], writes=["qscale"])

    ut = cx.sb("utH", [128, TOK])
    gt = cx.sb("gtH", [128, TOK])
    sg = cx.sb("sgH", [128, TOK])
    mxa = cx.sb("mxaH", [128, 1])
    p.op("vector", lambda E: E.memset(mxa[:], 0.0), writes=[mxa])
    for t in range(24):
        p.dma("sync", ut[:], upD.ap()[128 * t:128 * (t + 1), :])
        p.dma("sync", gt[:], gtD.ap()[128 * t:128 * (t + 1), :])
        p.op("vector", lambda E: E.tensor_scalar(ut[:], ut[:], si_u[:, 0:1], MAGIC,
                                                 ALU.mult, ALU.add),
             reads=[ut, "qscale"], writes=[ut])
        p.op("vector", lambda E: E.tensor_scalar(ut[:], ut[:], -MAGIC, sp_u[:, 0:1],
                                                 ALU.add, ALU.mult),
             reads=[ut], writes=[ut])
        p.dma("sync", o_upq.ap()[128 * t:128 * (t + 1), :], ut[:], reads=[ut])
        p.op("vector", lambda E: E.tensor_scalar(gt[:], gt[:], si_u[:, 1:2], MAGIC,
                                                 ALU.mult, ALU.add),
             reads=[gt, "qscale"], writes=[gt])
        p.op("vector", lambda E: E.tensor_scalar(gt[:], gt[:], -MAGIC, sp_u[:, 1:2],
                                                 ALU.add, ALU.mult),
             reads=[gt], writes=[gt])
        p.op("scalar", lambda E: E.activation(sg[:], gt[:], AF.Sigmoid),
             reads=[gt], writes=[sg])
        p.op("vector", lambda E: E.tensor_scalar(sg[:], sg[:], 65536.0, MAGIC,
                                                 ALU.mult, ALU.add),
             reads=[sg], writes=[sg])
        p.op("vector", lambda E: E.tensor_scalar(sg[:], sg[:], -MAGIC, 65535.0,
                                                 ALU.add, ALU.min),
             reads=[sg], writes=[sg])
        p.op("vector", lambda E: E.tensor_scalar(sg[:], sg[:], 1.0 / 65536.0, None,
                                                 ALU.mult),
             reads=[sg], writes=[sg])
        p.op("vector", lambda E: E.tensor_mul(gt[:], gt[:], sg[:]),
             reads=[gt, sg], writes=[gt])
        p.dma("sync", o_t3.ap()[128 * t:128 * (t + 1), :], gt[:], reads=[gt])
        p.op("vector", lambda E: E.tensor_reduce(sg[:, 0:1], gt[:], AX, ALU.max,
                                                 apply_absolute_value=True),
             reads=[gt], writes=[sg])
        p.op("vector", lambda E: E.tensor_max(mxa[:], mxa[:], sg[:, 0:1]),
             reads=[sg, mxa], writes=[mxa])
    pmx = cx.bank()
    p.op("tensor", lambda E, pmx=pmx: E.transpose(pmx[0:1, 0:128], mxa[:], ident[:]),
         reads=[mxa, ident], writes=[pmx])
    mr = cx.sb("mrH", [1, 128])
    p.op("vector", lambda E, pmx=pmx: E.tensor_copy(mr[:], pmx[0:1, 0:128]),
         reads=[pmx], writes=[mr])
    p.op("vector", lambda E: E.tensor_reduce(mr[:, 0:1], mr[:], AX, ALU.max),
         reads=[mr], writes=[mr])
    p.dma("sync", o_mx.ap(), mr[0:1, 0:1], reads=[mr])
    p.finish()
    return nc


def build_launch_I():
    cx = Ctx()
    nc, p = cx.nc, cx.p
    t3D = nc.dram_tensor("t3D", [3072, TOK], F32, kind="ExternalInput")
    upqD = nc.dram_tensor("upqD", [3072, TOK], F32, kind="ExternalInput")
    gmaxD = nc.dram_tensor("gmaxD", [1, 8], F32, kind="ExternalInput")
    identD = nc.dram_tensor("identD", [128, 128], F32, kind="ExternalInput")
    o_t4 = nc.dram_tensor("o_t4", [3072, TOK], F32, kind="ExternalOutput")
    o_mx = nc.dram_tensor("o_mx", [1, 1], F32, kind="ExternalOutput")
    ident = cx.sb("ident", [128, 128])
    p.dma("sync", ident[:], identD.ap())
    grow = cx.sb("growI", [1, 8])
    sr = cx.sb("srI", [1, 1])
    sp_u = cx.sb("spuI", [128, 1])
    si_u = cx.sb("siuI", [128, 1])
    p.dma("scalar", grow[:], gmaxD.ap())
    p.op("vector", lambda E: E.tensor_reduce(sr[:], grow[:], AX, ALU.max),
         reads=[grow], writes=[sr])
    p.op("vector", lambda E: E.tensor_scalar(sr[:], sr[:], 1.0 / 32767.0, 1e-12,
                                             ALU.mult, ALU.max), reads=[sr], writes=[sr])
    pt = cx.bank()
    p.op("tensor", lambda E, pt=pt: E.matmul(pt[:, 0:1], cx.ones1[0:1, :], sr[0:1, :],
                                             start=True, stop=True),
         reads=[sr, cx.ones1], writes=[pt])
    p.op("scalar", lambda E, pt=pt: E.copy(sp_u[:], pt[:, 0:1]), reads=[pt], writes=[sp_u])
    p.op("vector", lambda E: E.reciprocal(si_u[:], sp_u[:]), reads=[sp_u], writes=[si_u])
    p.op("vector", lambda E: E.engine_nop(), reads=[sp_u, si_u], writes=["qscale"])
    ut = cx.sb("utI", [128, TOK])
    gt = cx.sb("gtI", [128, TOK])
    mxa = cx.sb("mxaI", [128, 1])
    p.op("vector", lambda E: E.memset(mxa[:], 0.0), writes=[mxa])
    for t in range(24):
        p.dma("sync", gt[:], t3D.ap()[128 * t:128 * (t + 1), :])
        p.dma("sync", ut[:], upqD.ap()[128 * t:128 * (t + 1), :])
        p.op("vector", lambda E: E.tensor_scalar(gt[:], gt[:], si_u[:, 0:1], MAGIC,
                                                 ALU.mult, ALU.add),
             reads=[gt, "qscale"], writes=[gt])
        p.op("vector", lambda E: E.tensor_scalar(gt[:], gt[:], -MAGIC, sp_u[:, 0:1],
                                                 ALU.add, ALU.mult),
             reads=[gt], writes=[gt])
        p.op("vector", lambda E: E.tensor_mul(gt[:], gt[:], ut[:]),
             reads=[gt, ut], writes=[gt])
        p.dma("sync", o_t4.ap()[128 * t:128 * (t + 1), :], gt[:], reads=[gt])
        p.op("vector", lambda E: E.tensor_reduce(ut[:, 0:1], gt[:], AX, ALU.max,
                                                 apply_absolute_value=True),
             reads=[gt], writes=[ut])
        p.op("vector", lambda E: E.tensor_max(mxa[:], mxa[:], ut[:, 0:1]),
             reads=[ut, mxa], writes=[mxa])
    pmx = cx.bank()
    p.op("tensor", lambda E, pmx=pmx: E.transpose(pmx[0:1, 0:128], mxa[:], ident[:]),
         reads=[mxa, ident], writes=[pmx])
    mr = cx.sb("mrI", [1, 128])
    p.op("vector", lambda E, pmx=pmx: E.tensor_copy(mr[:], pmx[0:1, 0:128]),
         reads=[pmx], writes=[mr])
    p.op("vector", lambda E: E.tensor_reduce(mr[:, 0:1], mr[:], AX, ALU.max),
         reads=[mr], writes=[mr])
    p.dma("sync", o_mx.ap(), mr[0:1, 0:1], reads=[mr])
    p.finish()
    return nc


def build_launch_J():
    cx = Ctx()
    nc, p = cx.nc, cx.p
    CH = 512
    t4D = nc.dram_tensor("t4D", [3072, TOK], F32, kind="ExternalInput")
    x2pD = nc.dram_tensor("x2pD", [D, TOK], F32, kind="ExternalInput")
    wdD = nc.dram_tensor("wdD", [D, 3072], F32, kind="ExternalInput")
    gmaxD = nc.dram_tensor("gmaxD", [1, 8], F32, kind="ExternalInput")
    identD = nc.dram_tensor("identD", [128, 128], F32, kind="ExternalInput")
    o_out = nc.dram_tensor("o_out", [TOK, D], F32, kind="ExternalOutput")
    ident = cx.sb("ident", [128, 128])
    p.dma("sync", ident[:], identD.ap())
    grow = cx.sb("growJ", [1, 8])
    sr = cx.sb("srJ", [1, 1])
    sp_u = cx.sb("spuJ", [128, 1])
    si_u = cx.sb("siuJ", [128, 1])
    p.dma("scalar", grow[:], gmaxD.ap())
    p.op("vector", lambda E: E.tensor_reduce(sr[:], grow[:], AX, ALU.max),
         reads=[grow], writes=[sr])
    p.op("vector", lambda E: E.tensor_scalar(sr[:], sr[:], 1.0 / 32767.0, 1e-12,
                                             ALU.mult, ALU.max), reads=[sr], writes=[sr])
    pt = cx.bank()
    p.op("tensor", lambda E, pt=pt: E.matmul(pt[:, 0:1], cx.ones1[0:1, :], sr[0:1, :],
                                             start=True, stop=True),
         reads=[sr, cx.ones1], writes=[pt])
    p.op("scalar", lambda E, pt=pt: E.copy(sp_u[:], pt[:, 0:1]), reads=[pt], writes=[sp_u])
    p.op("vector", lambda E: E.reciprocal(si_u[:], sp_u[:]), reads=[sp_u], writes=[si_u])
    p.op("vector", lambda E: E.engine_nop(), reads=[sp_u, si_u], writes=["qscale"])

    wdT = [cx.sb(f"wdT{k}", [128, 1024], BF16) for k in range(24)]
    emit_lpbq_dequant(cx, wdD, 1024, 3072, wdT, ident, 96)

    oqb = [cx.sb(f"oqb{k}", [128, TOK], BF16) for k in range(24)]
    qt = cx.sb("qtJ", [128, TOK])
    for t in range(24):
        p.dma("sync", qt[:], t4D.ap()[128 * t:128 * (t + 1), :])
        p.op("vector", lambda E: E.tensor_scalar(qt[:], qt[:], si_u[:, 0:1], MAGIC,
                                                 ALU.mult, ALU.add),
             reads=[qt, "qscale"], writes=[qt])
        p.op("vector", lambda E: E.tensor_scalar(qt[:], qt[:], -MAGIC, sp_u[:, 0:1],
                                                 ALU.add, ALU.mult),
             reads=[qt], writes=[qt])
        p.op("vector", lambda E, t=t: E.tensor_copy(oqb[t][:], qt[:]),
             reads=[qt], writes=[oqb[t]])
    xf = cx.sb("xfJ", [128, CH])
    rowo = cx.sb("rowoJ", [128, D])
    for ch in range(4):
        sl = slice(CH * ch, CH * (ch + 1))
        for o2 in range(8):
            pw = cx.bank()
            for k in range(24):
                p.op("tensor", lambda E, k=k, o2=o2, sl=sl, pw=pw: E.matmul(
                    pw[:, 0:CH], wdT[k][:, 128 * o2:128 * (o2 + 1)],
                    oqb[k][:, sl], start=(k == 0), stop=(k == 23)),
                    reads=[wdT[k], oqb[k]], writes=[pw])
            p.dma("sync", xf[:], x2pD.ap()[128 * o2:128 * (o2 + 1), sl])
            p.op("vector", lambda E, pw=pw: E.tensor_add(xf[:], xf[:], pw[:, 0:CH]),
                 reads=[xf, pw], writes=[xf])
            for jj in range(4):
                cx.transpose(rowo, xf[:, 128 * jj:128 * (jj + 1)], 128, 128, ident,
                             out_off=128 * o2)
                p.dma("sync", o_out.ap()[CH * ch + 128 * jj:CH * ch + 128 * (jj + 1),
                                         128 * o2:128 * (o2 + 1)],
                      rowo[:, 128 * o2:128 * (o2 + 1)], reads=[rowo])
    p.finish()
    return nc


def host_inputs_B(ins, outA):
    x = _f32(ins["L0_attn_x"]).reshape(NT, D)
    xT = np.ascontiguousarray(x.T)
    W1T = np.ascontiguousarray(_f32(ins["L0_attn_W1"]).T)
    W2T = np.ascontiguousarray(_f32(ins["L0_attn_W2"]).T)
    g4 = np.ascontiguousarray(_f32(ins["L0_attn_ln_gamma"]).reshape(32, 128).T)
    b4 = np.ascontiguousarray(_f32(ins["L0_attn_ln_beta"]).reshape(32, 128).T)
    b1row = _f32(ins["L0_attn_b1"]).reshape(1, 2048)
    b2c = np.ascontiguousarray(_f32(ins["L0_attn_b2"]).reshape(8, 128).T)
    gb1 = np.zeros((128, 16), np.float32)
    gb1[:, 0:8] = _f32(ins["L1_attn_gamma"]).reshape(8, 128).T
    gb1[:, 8:16] = _f32(ins["L1_attn_beta"]).reshape(8, 128).T
    inw = _f32(ins["L1_attn_in_w"])
    ident = np.eye(128, dtype=np.float32)
    maps = []
    for c in range(NCORE):
        pos = np.arange(c % 2 * TOK + 1, c % 2 * TOK + TOK + 1, dtype=np.float64)
        pinv = (1.0 / pos).astype(np.float32).reshape(1, TOK)
        possc = (1.0 / np.sqrt(pos)).astype(np.float32).reshape(1, TOK)
        if c % 2 == 1:
            phic = outA[c - 1]["o_phic"]
            cmc = outA[c - 1]["o_cmc"]
        else:
            phic = np.zeros((128, 8), np.float32)
            cmc = np.zeros((128, 16), np.float32)
        maps.append({
            "xTo": np.ascontiguousarray(xT[:, c * TOK:(c + 1) * TOK]),
            "stato": outA[c]["o_stat"], "phiD": outA[c]["o_phi"],
            "phicI": np.ascontiguousarray(phic), "cmcI": np.ascontiguousarray(cmc),
            "sxn2o": np.ascontiguousarray(outA[c]["o_rows"][2:3, c * TOK:(c + 1) * TOK]),
            "moeD": outA[c]["o_moe"], "posinv": pinv, "posscD": possc,
            "W1TD": W1T, "g4": g4, "b4": b4, "b1r": b1row, "W2TD": W2T,
            "b2c": b2c, "gb1": gb1, "inwD": inw, "identD": ident,
        })
    return maps


def host_inputs_C(ins, outB):
    def tile16(v):
        return np.ascontiguousarray(_f32(v).reshape(16, 128).T)
    cwf = np.ascontiguousarray(_f32(ins["L1_attn_conv_w"]).reshape(16, 128, 4)
                               .transpose(1, 2, 0).reshape(128, 64))
    cwb = np.ascontiguousarray(_f32(ins["L1_attn_conv_w_b"]).reshape(16, 128, 4)
                               .transpose(1, 2, 0).reshape(128, 64))
    com = {
        "cwf": cwf, "cwb": cwb,
        "cbf": tile16(ins["L1_attn_conv_b"]), "cbb": tile16(ins["L1_attn_conv_b_b"]),
        "xpf": _f32(ins["L1_attn_xproj_w"]), "xpb": _f32(ins["L1_attn_xproj_w_b"]),
        "dtwf": _f32(ins["L1_attn_dt_w"]), "dtwb": _f32(ins["L1_attn_dt_w_b"]),
        "dtbf": tile16(ins["L1_attn_dt_b"]), "dtbb": tile16(ins["L1_attn_dt_b_b"]),
        "dskf": tile16(ins["L1_attn_D_skip"]), "dskb": tile16(ins["L1_attn_D_skip_b"]),
    }
    maps = []
    z3 = np.zeros((2048, 3), np.float32)
    for c in range(NCORE):
        hl = outB[c - 1]["o_hr"][0:2048] if c % 2 == 1 else z3
        hr = outB[c + 1]["o_hl"][0:2048] if c % 2 == 0 else z3
        m = dict(com)
        m["xzD"] = outB[c]["o_xz"]
        m["hlI"] = np.ascontiguousarray(hl)
        m["hrI"] = np.ascontiguousarray(hr)
        maps.append(m)
    return maps


def _alog_layout(a):
    return np.ascontiguousarray(_f32(a).reshape(16, 128, 16).transpose(1, 0, 2)
                                .reshape(128, 256))


def host_inputs_scan(ins, outC, outD=None):
    bf = ml_dtypes.bfloat16
    maps = []
    for c in range(NCORE):
        if outD is None:
            di = 0 if c % 2 == 0 else 1
            flip = (c % 2 == 1)
            hin = np.zeros((128, 256), np.float32)
        else:
            di = 1 if c % 2 == 0 else 0
            flip = (c % 2 == 0)
            sib = c + 1 if c % 2 == 0 else c - 1
            hin = outD[sib]["o_hf"]
        dt = outC[c]["o_dt"][di]
        dtx = outC[c]["o_dtx"][di]
        y0 = outC[c]["o_y0"][di]
        bc = _f32(outC[c]["o_bc"][di])
        if flip:
            dt = dt[:, ::-1]; dtx = dtx[:, ::-1]; y0 = y0[:, ::-1]; bc = bc[:, ::-1]
        alog = ins["L1_attn_A_log"] if di == 0 else ins["L1_attn_A_log_b"]
        maps.append({
            "dtD": np.ascontiguousarray(dt), "dtxD": np.ascontiguousarray(dtx),
            "y0D": np.ascontiguousarray(y0),
            "bcflat": np.ascontiguousarray(bc.reshape(2, 16 * TOK)),
            "alogD": _alog_layout(alog),
            "hinD": np.ascontiguousarray(_f32(hin)),
        })
    return maps


def host_inputs_G(ins, outB, outD, outE):
    mrg = _f32(ins["L1_attn_merge_w"])
    outw = _f32(ins["L1_attn_out_w"])
    wup = _f32(ins["L1_ffn_w_up"])
    wgt = _f32(ins["L1_ffn_w_gate"])
    gmax = np.concatenate([outB[c]["o_mx"] for c in range(NCORE)], axis=1)
    ident = np.eye(128, dtype=np.float32)
    maps = []
    for c in range(NCORE):
        if c % 2 == 0:
            yf = outD[c]["o_y"]
            yb = np.ascontiguousarray(outE[c]["o_y"][:, ::-1])
        else:
            yf = outE[c]["o_y"]
            yb = np.ascontiguousarray(outD[c]["o_y"][:, ::-1])
        maps.append({
            "yfD": yf, "ybD": yb,
            "zD": np.ascontiguousarray(outB[c]["o_xz"][2048:4096]),
            "xn1D": outB[c]["o_xn1"], "x1D": outB[c]["o_x1"],
            "mrgD": mrg, "outwD": outw, "wupD": wup, "wgtD": wgt,
            "gmaxD": gmax, "identD": ident,
        })
    return maps


def host_inputs_H(outG):
    gmax = np.concatenate([outG[c]["o_mx2"] for c in range(NCORE)], axis=1)
    ident = np.eye(128, dtype=np.float32)
    return [{"upD": outG[c]["o_up"], "gtD": outG[c]["o_gate"],
             "gmaxD": gmax, "identD": ident} for c in range(NCORE)]


def host_inputs_I(outH):
    gmax = np.concatenate([outH[c]["o_mx"] for c in range(NCORE)], axis=1)
    ident = np.eye(128, dtype=np.float32)
    return [{"t3D": outH[c]["o_t3"], "upqD": outH[c]["o_upq"],
             "gmaxD": gmax, "identD": ident} for c in range(NCORE)]


def host_inputs_J(ins, outG, outI):
    gmax = np.concatenate([outI[c]["o_mx"] for c in range(NCORE)], axis=1)
    ident = np.eye(128, dtype=np.float32)
    wd = _f32(ins["L1_ffn_w_down"])
    return [{"t4D": outI[c]["o_t4"], "x2pD": outG[c]["o_x2p"],
             "wdD": wd, "gmaxD": gmax, "identD": ident} for c in range(NCORE)]


def kernel(**inputs):
    cores = list(range(NCORE))

    def go(name, builder, maps):
        nc = _get_nc(name, builder)
        res = run_bass_kernel_spmd(nc, maps, core_ids=cores)
        return [dict(r) for r in res.results]

    outA = go("A", build_launch_A, host_inputs_A(inputs))
    outB = go("B", build_launch_B, host_inputs_B(inputs, outA))
    outC = go("C", build_launch_C, host_inputs_C(inputs, outB))
    outD = go("S", build_launch_scan, host_inputs_scan(inputs, outC))
    outE = go("S", build_launch_scan, host_inputs_scan(inputs, outC, outD))
    outG = go("G", build_launch_G, host_inputs_G(inputs, outB, outD, outE))
    outH = go("H", build_launch_H, host_inputs_H(outG))
    outI = go("I", build_launch_I, host_inputs_I(outH))
    outJ = go("J", build_launch_J, host_inputs_J(inputs, outG, outI))
    out = np.concatenate([outJ[c]["o_out"] for c in cores], axis=0)
    return np.ascontiguousarray(out.reshape(4, 4096, 1024), dtype=np.float32)


def run_A(ins):
    nc = _get_nc("A", build_launch_A)
    maps = host_inputs_A(ins)
    res = run_bass_kernel_spmd(nc, maps, core_ids=list(range(NCORE)))
    return res.results


if __name__ == "__main__":
    build_launch_A()
    print("built A ok")


# version 02
# speedup vs baseline: 1.3641x; 1.3641x over previous
import numpy as np
import ml_dtypes
from contextlib import ExitStack

import concourse.bass as bass
import concourse.mybir as mybir
from concourse.bass_utils import run_bass_kernel_spmd
from concourse.bass_core_rust import IndirectOffsetOnAxis

F32 = mybir.dt.float32
BF16 = mybir.dt.bfloat16
I32 = mybir.dt.int32
ALU = mybir.AluOpType
AF = mybir.ActivationFunctionType
AX = mybir.AxisListType.X

MAGIC = float(np.float32(1.5 * 2 ** 23))
NCORE = 8
TOK = 2048
NT = 16384
D = 1024
LN_EPS = 1e-5
PAD = 512

ENGS = ("sync", "scalar", "vector", "tensor", "gpsimd")


class Prog:
    def __init__(self, nc):
        self.nc = nc
        self.calls = {e: [] for e in ENGS}
        self.cnt = {e: 0 for e in ENGS}
        self.dma_cnt = {e: 0 for e in ENGS}
        self.sems = ExitStack()
        self.sem = {e: self.sems.enter_context(nc.semaphore(f"s_{e}")) for e in ENGS}
        self.NDSEM = 6
        self.dsem = {e: [self.sems.enter_context(nc.semaphore(f"d_{e}{i}"))
                         for i in range(self.NDSEM)] for e in ENGS}
        self.duses = {e: [0] * self.NDSEM for e in ENGS}
        self.drr = {e: 0 for e in ENGS}
        self.last_w = {}
        self.readers = {}
        self.wm = {e: {} for e in ENGS}

    @staticmethod
    def _key(x):
        if isinstance(x, str):
            return x
        t = getattr(x, "tensor", x)
        return getattr(t, "name", str(t))

    def _emit_wait(self, eng, tok):
        kind, e2, val = tok
        if kind == "E":
            sem, name = self.sem[e2], ("E", e2)
        else:
            q, i = e2
            sem, name = self.dsem[q][i], ("D", q, i)
        if self.wm[eng].get(name, 0) >= val:
            return
        self.wm[eng][name] = val
        self.calls[eng].append(lambda E, s=sem, v=val: E.wait_ge(s, v))

    def _deps(self, eng, reads, writes):
        toks = set()
        for r in reads:
            t = self.last_w.get(self._key(r))
            if t is not None:
                toks.add(t)
        for w in writes:
            k = self._key(w)
            t = self.last_w.get(k)
            if t is not None:
                toks.add(t)
            toks.update(self.readers.get(k, ()))
        for t in sorted(toks):
            self._emit_wait(eng, t)

    def _post(self, reads, writes, tok):
        for w in writes:
            k = self._key(w)
            self.last_w[k] = tok
            self.readers[k] = []
        for r in reads:
            self.readers.setdefault(self._key(r), []).append(tok)

    def op(self, eng, fn, reads=(), writes=()):
        self._deps(eng, reads, writes)
        self.cnt[eng] += 1
        tok = ("E", eng, self.cnt[eng])
        sem = self.sem[eng]
        self.calls[eng].append(lambda E, fn=fn, sem=sem: fn(E).then_inc(sem, 1))
        self._post(reads, writes, tok)
        return tok

    def _dma_slot(self, eng):
        i = self.drr[eng] % self.NDSEM
        self.drr[eng] += 1
        if self.duses[eng][i]:
            self._emit_wait(eng, ("D", (eng, i), 16 * self.duses[eng][i]))
        self.duses[eng][i] += 1
        return i, self.dsem[eng][i], 16 * self.duses[eng][i]

    def dma(self, eng, out, in_, reads=(), writes=(), **kw):
        reads = list(reads) + [in_]
        writes = list(writes) + [out]
        self._deps(eng, reads, writes)
        i, sem, val = self._dma_slot(eng)
        tok = ("D", (eng, i), val)
        self.calls[eng].append(
            lambda E, out=out, in_=in_, sem=sem, kw=kw:
            E.dma_start(out=out, in_=in_, **kw).then_inc(sem, 16))
        self._post(reads, writes, tok)
        return tok

    def dma_ind(self, out, out_off, in_, in_off, bounds, reads=(), writes=()):
        eng = "gpsimd"
        extra = [x.ap for x in (out_off, in_off) if x is not None]
        reads = list(reads) + [in_] + extra
        writes = list(writes) + [out]
        self._deps(eng, reads, writes)
        i, sem, val = self._dma_slot(eng)
        tok = ("D", (eng, i), val)
        self.calls[eng].append(
            lambda E, out=out, in_=in_, sem=sem:
            E.indirect_dma_start(out, out_off, in_, in_off, bounds_check=bounds,
                                 oob_is_err=False).then_inc(sem, 16))
        self._post(reads, writes, tok)
        return tok

    def barrier(self):
        toks = []
        for e in ENGS:
            if self.cnt[e]:
                toks.append(("E", e, self.cnt[e]))
            for i in range(self.NDSEM):
                if self.duses[e][i]:
                    toks.append(("D", (e, i), 16 * self.duses[e][i]))
        for eng in ENGS:
            for t in toks:
                if t[0] == "E" and t[1] == eng:
                    continue
                self._emit_wait(eng, t)

    def finish(self):
        nc = self.nc
        for e in ENGS:
            if e != "sync" and self.cnt[e]:
                self.calls["sync"].append(
                    lambda E, s=self.sem[e], v=self.cnt[e]: E.wait_ge(s, v))
            for i in range(self.NDSEM):
                if self.duses[e][i]:
                    self.calls["sync"].append(
                        lambda E, s=self.dsem[e][i], v=16 * self.duses[e][i]:
                        E.wait_ge(s, v))
        with self.sems, nc.Block() as block:
            @block.sync
            def _(E):
                for c in self.calls["sync"]:
                    c(E)

            @block.scalar
            def _(E):
                for c in self.calls["scalar"]:
                    c(E)

            @block.vector
            def _(E):
                for c in self.calls["vector"]:
                    c(E)

            @block.tensor
            def _(E):
                for c in self.calls["tensor"]:
                    c(E)

            @block.gpsimd
            def _(E):
                for c in self.calls["gpsimd"]:
                    c(E)


class Ctx:
    def __init__(self):
        self.nc = bass.Bass("TRN2", target_bir_lowering=False)
        self.es = ExitStack()
        self.p = Prog(self.nc)
        self.banks = [self.es.enter_context(
            self.nc.psum_tensor(f"bank{i}", [128, 512], F32)) for i in range(8)]
        self._bank_i = 0
        self.ones1 = self.sb("ones1", [1, 128])
        self.onescol = self.sb("onescol", [128, 1])
        self.onescolb = self.sb("onescolb", [128, 1], BF16)
        self.p.op("vector", lambda E: E.memset(self.ones1[:], 1.0), writes=[self.ones1])
        self.p.op("vector", lambda E: E.memset(self.onescol[:], 1.0), writes=[self.onescol])
        self.p.op("vector", lambda E: E.memset(self.onescolb[:], 1.0), writes=[self.onescolb])

    def sb(self, name, shape, dtype=F32, es=None):
        return (es or self.es).enter_context(self.nc.sbuf_tensor(name, shape, dtype))

    def bank(self):
        n = len(self.rot) if hasattr(self, "rot") and self.rot else 8
        pool = self.rot if (hasattr(self, "rot") and self.rot) else self.banks
        b = pool[self._bank_i % n]
        self._bank_i += 1
        return b

    def bcast_row(self, row_ap, out_sb, n, out_off=0):
        p = self.p
        for j in range((n + 511) // 512):
            c0, c1 = j * 512, min(n, (j + 1) * 512)
            pt = self.bank()
            p.op("tensor", lambda E, pt=pt, c0=c0, c1=c1: E.matmul(
                pt[:, 0:c1 - c0], self.ones1[0:1, :], row_ap[0:1, c0:c1],
                start=True, stop=True), reads=[row_ap, self.ones1], writes=[pt])
            p.op("scalar", lambda E, pt=pt, c0=c0, c1=c1:
                 E.copy(out_sb[:, out_off + c0:out_off + c1], pt[:, 0:c1 - c0]),
                 reads=[pt], writes=[out_sb])

    def transpose(self, out_sb, in_ap, k, m, ident, out_off=0):
        p = self.p
        pt = self.bank()
        p.op("tensor", lambda E, pt=pt: E.transpose(pt[0:m, 0:k], in_ap, ident[0:k, 0:k]),
             reads=[in_ap, ident], writes=[pt])
        p.op("scalar", lambda E, pt=pt:
             E.copy(out_sb[0:m, out_off:out_off + k], pt[0:m, 0:k]),
             reads=[pt], writes=[out_sb])


def trig_consts(cx, es):
    c = {}
    c["pio2"] = cx.sb("c_pio2", [128, 1], es=es)
    c["neg1"] = cx.sb("c_neg1", [128, 1], es=es)
    cx.p.op("vector", lambda E: E.memset(c["pio2"][:], float(np.pi / 2)), writes=[c["pio2"]])
    cx.p.op("vector", lambda E: E.memset(c["neg1"][:], -1.0), writes=[c["neg1"]])
    return c


def trig_block(cx, phi, r, sin_o, cos_o, c):
    p = cx.p
    inv2pi = float(np.float32(1.0 / (2 * np.pi)))
    twopi = float(np.float32(2 * np.pi))
    p.op("vector", lambda E: E.tensor_scalar(r[:], phi[:], inv2pi, MAGIC, ALU.mult, ALU.add),
         reads=[phi], writes=[r])
    p.op("vector", lambda E: E.tensor_scalar(r[:], r[:], -MAGIC, -twopi, ALU.add, ALU.mult),
         reads=[r], writes=[r])
    p.op("vector", lambda E: E.tensor_add(r[:], r[:], phi[:]), reads=[r, phi], writes=[r])
    p.op("scalar", lambda E: E.activation(sin_o[:], r[:], AF.Sin), reads=[r], writes=[sin_o])
    p.op("scalar", lambda E: E.activation(cos_o[:], r[:], AF.Abs), reads=[r], writes=[cos_o])
    p.op("scalar", lambda E: E.activation(cos_o[:], cos_o[:], AF.Sin,
                                          bias=c["pio2"][:], scale=c["neg1"][:]),
         reads=[cos_o, c["pio2"], c["neg1"]], writes=[cos_o])


def build_launch_A():
    cx = Ctx()
    nc, p = cx.nc, cx.p

    xT = nc.dram_tensor("xT", [D, NT], F32, kind="ExternalInput")
    xTo = nc.dram_tensor("xTo", [D, TOK], F32, kind="ExternalInput")
    Wr16 = nc.dram_tensor("Wr16", [D, 16], F32, kind="ExternalInput")
    WomT = nc.dram_tensor("WomT", [D, D], F32, kind="ExternalInput")
    b_om = nc.dram_tensor("b_om", [128, 8], F32, kind="ExternalInput")
    lsc = nc.dram_tensor("lsc", [128, 8], F32, kind="ExternalInput")
    possc = nc.dram_tensor("possc", [1, TOK], F32, kind="ExternalInput")
    moeW1 = nc.dram_tensor("moeW1", [8, D, 4096], F32, kind="ExternalInput")
    moeW2 = nc.dram_tensor("moeW2", [8, 4096, D], F32, kind="ExternalInput")
    e9iota = nc.dram_tensor("e9iota", [1, 1152], F32, kind="ExternalInput")
    cmask = nc.dram_tensor("cmask", [1, 8], F32, kind="ExternalInput")
    bidx = nc.dram_tensor("bidx", [16, 3], F32, kind="ExternalInput")
    ridx = nc.dram_tensor("ridx", [2, 1], F32, kind="ExternalInput")
    e8colD = nc.dram_tensor("e8colD", [8, 1], F32, kind="ExternalInput")
    identD = nc.dram_tensor("identD", [128, 128], F32, kind="ExternalInput")

    o_phi = nc.dram_tensor("o_phi", [D, TOK], F32, kind="ExternalOutput")
    o_phic = nc.dram_tensor("o_phic", [128, 8], F32, kind="ExternalOutput")
    o_moe = nc.dram_tensor("o_moe", [TOK, D], F32, kind="ExternalOutput")
    o_stat = nc.dram_tensor("o_stat", [2, TOK], F32, kind="ExternalOutput")
    o_rows = nc.dram_tensor("o_rows", [3, NT], F32, kind="ExternalOutput")
    o_cmc = nc.dram_tensor("o_cmc", [128, 16], F32, kind="ExternalOutput")
    scr_row = nc.dram_tensor("scr_row", [5, NT], F32, kind="Internal")
    scr_xe = nc.dram_tensor("scr_xe", [8 * PAD, D], F32, kind="Internal")
    scr_ye = nc.dram_tensor("scr_ye", [8 * PAD, D], F32, kind="Internal")
    scr_b = nc.dram_tensor("scr_b", [8, 1], F32, kind="Internal")

    ident = cx.sb("ident", [128, 128])
    p.dma("sync", ident[:], identD.ap())
    rowview = scr_row.ap().rearrange("r (b j) -> (r b) j", j=128)
    rowview8 = scr_row.ap().rearrange("r (c t) -> (r c) t", t=TOK)

    eT = cx.sb("eT", [128, 16])
    pT = cx.sb("pT", [128, 16])
    sidx_i = cx.sb("sidx_i", [128, 16], I32)
    gidx_i = cx.sb("gidx_i", [128, 16], I32)
    baserow = cx.sb("baserow", [1, 8])
    xn0 = [cx.sb(f"xn0_{k}", [128, TOK]) for k in range(8)]

    esR = ExitStack()
    xt = [cx.sb(f"xt{k}", [128, 512], es=esR) for k in range(8)]
    sq2 = [cx.sb(f"sq{k}", [128, 512], es=esR) for k in range(2)]
    wr16 = cx.sb("wr16", [128, 16 * 8], es=esR)
    logit = cx.sb("logit", [128, 1152], es=esR)
    rb1 = cx.sb("rb1", [1, 512], es=esR)
    rb2 = cx.sb("rb2", [1, 512], es=esR)
    for k in range(8):
        p.dma("scalar", wr16[:, 16 * k:16 * (k + 1)], Wr16.ap()[128 * k:128 * (k + 1), :])
    for ch in range(32):
        for k in range(8):
            p.dma("sync", xt[k][:], xT.ap()[128 * k:128 * (k + 1), 512 * ch:512 * (ch + 1)])
        pS, pQ = cx.bank(), cx.bank()
        for k in range(8):
            sq = sq2[k % 2]
            p.op("scalar", lambda E, k=k, sq=sq: E.activation(sq[:], xt[k][:], AF.Square),
                 reads=[xt[k]], writes=[sq])
            p.op("tensor", lambda E, k=k, pS=pS: E.matmul(
                pS[0:1, :], cx.onescol[:], xt[k][:], start=(k == 0), stop=(k == 7)),
                reads=[xt[k], cx.onescol], writes=[pS])
            p.op("tensor", lambda E, sq=sq, k=k, pQ=pQ: E.matmul(
                pQ[0:1, :], cx.onescol[:], sq[:], start=(k == 0), stop=(k == 7)),
                reads=[sq, cx.onescol], writes=[pQ])
        p.op("vector", lambda E, pS=pS: E.tensor_copy(rb1[:], pS[0:1, :]),
             reads=[pS], writes=[rb1])
        p.op("vector", lambda E, pQ=pQ: E.tensor_copy(rb2[:], pQ[0:1, :]),
             reads=[pQ], writes=[rb2])
        p.dma("sync", scr_row.ap()[3:4, 512 * ch:512 * (ch + 1)], rb1[:])
        p.dma("sync", scr_row.ap()[4:5, 512 * ch:512 * (ch + 1)], rb2[:])
        for w in range(4):
            pL = cx.bank()
            for k in range(8):
                p.op("tensor", lambda E, k=k, w=w, pL=pL: E.matmul(
                    pL[:, 0:16], xt[k][:, 128 * w:128 * (w + 1)],
                    wr16[:, 16 * k:16 * (k + 1)], start=(k == 0), stop=(k == 7)),
                    reads=[xt[k], wr16], writes=[pL])
            W = ch * 4 + w
            p.op("scalar", lambda E, W=W, pL=pL: E.copy(logit[:, 9 * W:9 * W + 9], pL[:, 0:9]),
                 reads=[pL], writes=[logit])

    muw = cx.sb("muw", [128, 128], es=esR)
    rsw = cx.sb("rsw", [128, 128], es=esR)
    vtmp = cx.sb("vtmp", [128, 128], es=esR)
    blk = cx.sb("blk", [128, 128], es=esR)
    p.dma("sync", blk[:], rowview[3 * 128:4 * 128, :])
    cx.transpose(muw, blk[:], 128, 128, ident)
    p.dma("sync", blk[:], rowview[4 * 128:5 * 128, :])
    cx.transpose(vtmp, blk[:], 128, 128, ident)
    p.op("vector", lambda E: E.tensor_scalar(muw[:], muw[:], 1.0 / D, None, ALU.mult),
         reads=[muw], writes=[muw])
    p.op("vector", lambda E: E.tensor_scalar(vtmp[:], vtmp[:], 1.0 / D, None, ALU.mult),
         reads=[vtmp], writes=[vtmp])
    p.op("scalar", lambda E: E.activation(rsw[:], muw[:], AF.Square),
         reads=[muw], writes=[rsw])
    p.op("vector", lambda E: E.tensor_sub(vtmp[:], vtmp[:], rsw[:]),
         reads=[vtmp, rsw], writes=[vtmp])
    p.op("vector", lambda E: E.tensor_scalar(vtmp[:], vtmp[:], float(LN_EPS), None, ALU.add),
         reads=[vtmp], writes=[vtmp])
    varw = cx.sb("varw", [128, 128], es=esR)
    p.op("vector", lambda E: E.tensor_scalar(varw[:], vtmp[:], -float(LN_EPS), None, ALU.add),
         reads=[vtmp], writes=[varw])
    p.op("scalar", lambda E: E.activation(vtmp[:], vtmp[:], AF.Sqrt),
         reads=[vtmp], writes=[vtmp])
    p.op("vector", lambda E: E.reciprocal(rsw[:], vtmp[:]), reads=[vtmp], writes=[rsw])
    sxn2w = cx.sb("sxn2w", [128, 128], es=esR)
    p.op("scalar", lambda E: E.activation(sxn2w[:], rsw[:], AF.Square),
         reads=[rsw], writes=[sxn2w])
    p.op("vector", lambda E: E.tensor_mul(sxn2w[:], sxn2w[:], varw[:]),
         reads=[sxn2w, varw], writes=[sxn2w])
    p.op("vector", lambda E: E.tensor_scalar(sxn2w[:], sxn2w[:], float(D), None, ALU.mult),
         reads=[sxn2w], writes=[sxn2w])

    cw = cx.sb("cw", [1, 16], es=esR)
    cwrep = cx.sb("cwrep", [128, 16], es=esR)
    t1 = cx.sb("t1", [128, 1152], es=esR)
    pC = cx.bank()
    for k in range(8):
        p.op("tensor", lambda E, k=k, pC=pC: E.matmul(
            pC[0:1, 0:16], cx.onescol[:], wr16[:, 16 * k:16 * (k + 1)],
            start=(k == 0), stop=(k == 7)), reads=[wr16, cx.onescol], writes=[pC])
    p.op("vector", lambda E, pC=pC: E.tensor_copy(cw[:], pC[0:1, 0:16]),
         reads=[pC], writes=[cw])
    cx.bcast_row(cw, cwrep, 16)

    def we(x):
        return x.rearrange("p w -> p w ()").broadcast_to((128, 128, 9))

    L3 = logit[:].rearrange("p (w e) -> p w e", e=9)
    T3 = t1[:].rearrange("p (w e) -> p w e", e=9)
    p.op("vector", lambda E: E.tensor_tensor(
        t1[:], we(muw[:]),
        cwrep[:, 0:9].rearrange("p e -> p () e").broadcast_to((128, 128, 9)), ALU.mult),
        reads=[muw, cwrep], writes=[t1])
    p.op("vector", lambda E: E.tensor_sub(logit[:], logit[:], t1[:]),
         reads=[logit, t1], writes=[logit])
    p.op("vector", lambda E: E.tensor_tensor(logit[:], logit[:], we(rsw[:]), ALU.mult),
         reads=[logit, rsw], writes=[logit])

    pmx = cx.sb("pmx", [128, 128], es=esR)
    ptk = cx.sb("ptk", [128, 128], es=esR)
    eix = cx.sb("eix", [128, 128], es=esR)
    e9 = cx.sb("e9", [128, 1152], es=esR)
    e9row = cx.sb("e9row", [1, 1152], es=esR)
    p.dma("scalar", e9row[:], e9iota.ap())
    cx.bcast_row(e9row, e9, 1152)
    p.op("vector", lambda E: E.tensor_reduce(pmx[:], L3[:, :, 0:8], AX, ALU.max),
         reads=[logit], writes=[pmx])
    p.op("vector", lambda E: E.tensor_tensor(t1[:], logit[:], we(pmx[:]), ALU.subtract),
         reads=[logit, pmx], writes=[t1])
    p.op("scalar", lambda E: E.activation(t1[:], t1[:], AF.Exp), reads=[t1], writes=[t1])
    p.op("vector", lambda E: E.tensor_reduce(ptk[:], T3[:, :, 0:8], AX, ALU.add),
         reads=[t1], writes=[ptk])
    p.op("vector", lambda E: E.reciprocal(ptk[:], ptk[:]), reads=[ptk], writes=[ptk])
    p.op("vector", lambda E: E.tensor_tensor(t1[:], logit[:], we(pmx[:]), ALU.is_ge),
         reads=[logit, pmx], writes=[t1])
    p.op("vector", lambda E: E.tensor_mul(t1[:], t1[:], e9[:]), reads=[t1, e9], writes=[t1])
    p.op("vector", lambda E: E.tensor_reduce(eix[:], T3[:, :, 0:8], AX, ALU.add),
         reads=[t1], writes=[eix])

    for rid, t_src in ((0, eix), (1, ptk), (3, muw), (4, rsw)):
        cx.transpose(blk, t_src[:], 128, 128, ident)
        p.dma("sync", rowview[rid * 128:(rid + 1) * 128, :], blk[:])
        if rid == 3:
            p.dma("sync", o_rows.ap()[0:1, :].rearrange("a (w q) -> (a w) q", q=128), blk[:])
        if rid == 4:
            p.dma("sync", o_rows.ap()[1:2, :].rearrange("a (w q) -> (a w) q", q=128), blk[:])
    cx.transpose(blk, sxn2w[:], 128, 128, ident)
    p.dma("sync", o_rows.ap()[2:3, :].rearrange("a (w q) -> (a w) q", q=128), blk[:])

    SEG = 4096
    erow = cx.sb("erow", [1, SEG], es=esR)
    e8 = cx.sb("e8", [8, SEG], es=esR)
    oh = cx.sb("oh", [8, SEG], es=esR)
    rk8 = cx.sb("rk8", [8, SEG], es=esR)
    e8col = cx.sb("e8col", [8, 1], es=esR)
    ones8 = cx.sb("ones8", [8, 1], es=esR)
    carry8 = cx.sb("carry8", [8, 1], es=esR)
    rrow = cx.sb("rrow", [1, SEG], es=esR)
    base8 = cx.sb("base8", [8, 8], es=esR)
    p.op("vector", lambda E: E.memset(ones8[:], 1.0), writes=[ones8])
    p.op("vector", lambda E: E.memset(carry8[:], 0.0), writes=[carry8])
    p.dma("scalar", e8col[:], e8colD.ap())
    for sg in range(NT // SEG):
        p.dma("scalar", erow[:], scr_row.ap()[0:1, SEG * sg:SEG * (sg + 1)])
        for j in range(SEG // 512):
            pt = cx.bank()
            p.op("tensor", lambda E, j=j, pt=pt: E.matmul(
                pt[0:8, :], cx.ones1[0:1, 0:8], erow[0:1, 512 * j:512 * (j + 1)],
                start=True, stop=True), reads=[erow, cx.ones1], writes=[pt])
            p.op("scalar", lambda E, j=j, pt=pt: E.copy(e8[:, 512 * j:512 * (j + 1)], pt[0:8, :]),
                 reads=[pt], writes=[e8])
        p.op("vector", lambda E: E.tensor_scalar(oh[:], e8[:], e8col[:], None, ALU.is_equal),
             reads=[e8, e8col], writes=[oh])
        p.op("vector", lambda E: E.tensor_tensor_scan(
            rk8[:], ones8[:].broadcast_to((8, SEG)), oh[:], carry8[:], ALU.mult, ALU.add),
            reads=[oh, ones8, carry8], writes=[rk8])
        p.op("vector", lambda E: E.tensor_copy(carry8[:], rk8[:, SEG - 1:SEG]),
             reads=[rk8], writes=[carry8])
        p.op("vector", lambda E: E.tensor_sub(rk8[:], rk8[:], oh[:]),
             reads=[rk8, oh], writes=[rk8])
        p.op("vector", lambda E, sg=sg: E.tensor_copy(base8[:, 2 * sg:2 * sg + 1], rk8[:, 0:1]),
             reads=[rk8], writes=[base8])
        p.op("vector", lambda E, sg=sg: E.tensor_copy(
            base8[:, 2 * sg + 1:2 * sg + 2], rk8[:, TOK:TOK + 1]),
            reads=[rk8], writes=[base8])
        p.op("vector", lambda E: E.tensor_mul(oh[:], oh[:], rk8[:]), reads=[oh, rk8], writes=[oh])
        for j in range(SEG // 512):
            pt = cx.bank()
            p.op("tensor", lambda E, j=j, pt=pt: E.matmul(
                pt[0:1, :], ones8[:], oh[:, 512 * j:512 * (j + 1)],
                start=True, stop=True), reads=[oh, ones8], writes=[pt])
            p.op("vector", lambda E, j=j, pt=pt: E.tensor_copy(
                rrow[0:1, 512 * j:512 * (j + 1)], pt[0:1, :]), reads=[pt], writes=[rrow])
        p.dma("sync", scr_row.ap()[2:3, SEG * sg:SEG * (sg + 1)], rrow[:])

    cmk = cx.sb("cmk", [1, 8], es=esR)
    cmk8 = cx.sb("cmk8", [8, 8], es=esR)
    bsel = cx.sb("bsel", [8, 1], es=esR)
    p.dma("scalar", cmk[:], cmask.ap())
    pt = cx.bank()
    p.op("tensor", lambda E, pt=pt: E.matmul(pt[0:8, 0:8], cx.ones1[0:1, 0:8], cmk[0:1, :],
                                             start=True, stop=True),
         reads=[cmk, cx.ones1], writes=[pt])
    p.op("scalar", lambda E, pt=pt: E.copy(cmk8[:], pt[0:8, 0:8]), reads=[pt], writes=[cmk8])
    p.op("vector", lambda E: E.tensor_mul(cmk8[:], cmk8[:], base8[:]),
         reads=[cmk8, base8], writes=[cmk8])
    p.op("vector", lambda E: E.tensor_reduce(bsel[:], cmk8[:], AX, ALU.add),
         reads=[cmk8], writes=[bsel])
    p.dma("sync", scr_b.ap(), bsel[:])
    p.dma("sync", baserow[:], scr_b.ap().rearrange("a b -> b a"))
    esR.close()
    p.barrier()

    esM = ExitStack()
    own3 = [cx.sb(f"own{i}", [16, 128], es=esM) for i in range(3)]
    bidx_sb = cx.sb("bidx_sb", [16, 3], es=esM)
    bidx_i = cx.sb("bidx_i", [16, 3], I32, es=esM)
    p.dma("scalar", bidx_sb[:], bidx.ap())
    p.op("gpsimd", lambda E: E.tensor_copy(bidx_i[:], bidx_sb[:]),
         reads=[bidx_sb], writes=[bidx_i])
    rT = cx.sb("rT", [128, 16], es=esM)
    for i, dst in ((0, eT), (1, pT), (2, rT)):
        p.dma_ind(own3[i][:], None, rowview, IndirectOffsetOnAxis(bidx_i[:, i:i + 1], 0), 5 * 128 - 1)
        cx.transpose(dst, own3[i][:], 16, 128, ident)

    baserep = cx.sb("baserep", [128, 8], es=esM)
    cx.bcast_row(baserow, baserep, 8)
    msk = cx.sb("msk", [128, 16], es=esM)
    btok = cx.sb("btok", [128, 16], es=esM)
    loc = cx.sb("loc", [128, 16], es=esM)
    sidx_f = cx.sb("sidx_f", [128, 16], es=esM)
    gidx_f = cx.sb("gidx_f", [128, 16], es=esM)
    p.op("vector", lambda E: E.memset(btok[:], 0.0), writes=[btok])
    for e in range(8):
        p.op("vector", lambda E, e=e: E.tensor_scalar(msk[:], eT[:], float(e), None, ALU.is_equal),
             reads=[eT], writes=[msk])
        p.op("vector", lambda E, e=e: E.scalar_tensor_tensor(
            btok[:], msk[:], baserep[:, e:e + 1], btok[:], ALU.mult, ALU.add),
            reads=[msk, baserep, btok], writes=[btok])
    p.op("vector", lambda E: E.tensor_sub(loc[:], rT[:], btok[:]), reads=[rT, btok], writes=[loc])
    p.op("vector", lambda E: E.tensor_scalar(sidx_f[:], eT[:], float(PAD), None, ALU.mult),
         reads=[eT], writes=[sidx_f])
    p.op("vector", lambda E: E.tensor_add(sidx_f[:], sidx_f[:], loc[:]),
         reads=[sidx_f, loc], writes=[sidx_f])
    p.op("vector", lambda E: E.tensor_scalar(msk[:], loc[:], float(PAD), None, ALU.is_ge),
         reads=[loc], writes=[msk])
    p.op("vector", lambda E: E.scalar_tensor_tensor(
        sidx_f[:], msk[:], 1.0e6, sidx_f[:], ALU.mult, ALU.add),
        reads=[msk, sidx_f], writes=[sidx_f])
    p.op("vector", lambda E: E.tensor_scalar(msk[:], rT[:], 2048.0, None, ALU.is_ge),
         reads=[rT], writes=[msk])
    p.op("vector", lambda E: E.scalar_tensor_tensor(
        gidx_f[:], msk[:], 1.0e6, sidx_f[:], ALU.mult, ALU.add),
        reads=[msk, sidx_f], writes=[gidx_f])
    p.op("gpsimd", lambda E: E.tensor_copy(sidx_i[:], sidx_f[:]), reads=[sidx_f], writes=[sidx_i])
    p.op("gpsimd", lambda E: E.tensor_copy(gidx_i[:], gidx_f[:]), reads=[gidx_f], writes=[gidx_i])

    ridx_sb = cx.sb("ridx_sb", [2, 1], es=esM)
    ridx_i = cx.sb("ridx_i", [2, 1], I32, es=esM)
    mu_own = cx.sb("mu_own", [1, TOK], es=esM)
    rs_own = cx.sb("rs_own", [1, TOK], es=esM)
    mu_rep = cx.sb("mu_rep", [128, TOK], es=esM)
    rs_rep = cx.sb("rs_rep", [128, TOK], es=esM)
    p.dma("scalar", ridx_sb[:], ridx.ap())
    p.op("gpsimd", lambda E: E.tensor_copy(ridx_i[:], ridx_sb[:]), reads=[ridx_sb], writes=[ridx_i])
    stat2 = cx.sb("stat2", [2, TOK], es=esM)
    p.dma_ind(stat2[:], None, rowview8, IndirectOffsetOnAxis(ridx_i[:], 0), 5 * 8 - 1)
    p.dma("sync", o_stat.ap(), stat2[:])
    p.dma("scalar", mu_own[:], stat2[0:1, :], reads=[stat2])
    p.dma("scalar", rs_own[:], stat2[1:2, :], reads=[stat2])
    cx.bcast_row(mu_own, mu_rep, TOK)
    cx.bcast_row(rs_own, rs_rep, TOK)
    for k in range(8):
        p.dma("sync", xn0[k][:], xTo.ap()[128 * k:128 * (k + 1), :])
        p.op("vector", lambda E, k=k: E.tensor_sub(xn0[k][:], xn0[k][:], mu_rep[:]),
             reads=[xn0[k], mu_rep], writes=[xn0[k]])
        p.op("vector", lambda E, k=k: E.tensor_mul(xn0[k][:], xn0[k][:], rs_rep[:]),
             reads=[xn0[k], rs_rep], writes=[xn0[k]])
    esM.close()
    p.barrier()

    esS = ExitStack()
    zero_t = cx.sb("zero_t", [128, D], es=esS)
    rowt = cx.sb("rowt", [128, D], es=esS)
    p.op("vector", lambda E: E.memset(zero_t[:], 0.0), writes=[zero_t])
    for j in range(32):
        p.dma("scalar", scr_xe.ap()[128 * j:128 * (j + 1), :], zero_t[:])
    for j in range(16):
        for k in range(8):
            cx.transpose(rowt, xn0[k][:, 128 * j:128 * (j + 1)], 128, 128, ident,
                         out_off=128 * k)
        p.dma_ind(scr_xe.ap(), IndirectOffsetOnAxis(sidx_i[:, j:j + 1], 0),
                  rowt[:], None, 8 * PAD - 1)
    esS.close()
    p.barrier()

    esO = ExitStack()
    womk = [cx.sb(f"womk{k}", [128, D], es=esO) for k in range(8)]
    b_om_t = cx.sb("b_om_t", [128, 8], es=esO)
    lsce = cx.sb("lsce", [128, 8], es=esO)
    possr = cx.sb("possr", [1, TOK], es=esO)
    possc_r = cx.sb("possc_r", [128, TOK], es=esO)
    onesT = cx.sb("onesT", [128, TOK], es=esO)
    om_t = cx.sb("om_t", [128, TOK], es=esO)
    phi_t = cx.sb("phi_t", [128, TOK], es=esO)
    phic_t = cx.sb("phic_t", [128, 8], es=esO)
    cmc_t = cx.sb("cmc_t", [128, 16], es=esO)
    trg1 = cx.sb("trg1", [128, TOK], es=esO)
    trg2 = cx.sb("trg2", [128, TOK], es=esO)
    consts = trig_consts(cx, esO)
    p.dma("scalar", b_om_t[:], b_om.ap())
    p.dma("scalar", lsce[:], lsc.ap())
    p.op("scalar", lambda E: E.activation(lsce[:], lsce[:], AF.Exp), reads=[lsce], writes=[lsce])
    p.dma("scalar", possr[:], possc.ap())
    cx.bcast_row(possr, possc_r, TOK)
    p.op("vector", lambda E: E.memset(onesT[:], 1.0), writes=[onesT])
    for k in range(8):
        p.dma("sync", womk[k][:], WomT.ap()[128 * k:128 * (k + 1), :])
    for et in range(8):
        for ch in range(4):
            pt = cx.bank()
            for k in range(8):
                p.op("tensor", lambda E, k=k, et=et, ch=ch, pt=pt: E.matmul(
                    pt[:], womk[k][:, 128 * et:128 * (et + 1)],
                    xn0[k][:, 512 * ch:512 * (ch + 1)], start=(k == 0), stop=(k == 7)),
                    reads=[womk[k], xn0[k]], writes=[pt])
            p.op("vector", lambda E, et=et, ch=ch, pt=pt: E.tensor_scalar(
                om_t[:, 512 * ch:512 * (ch + 1)], pt[:], b_om_t[:, et:et + 1],
                lsce[:, et:et + 1], ALU.add, ALU.mult),
                reads=[pt, b_om_t, lsce], writes=[om_t])
        p.op("vector", lambda E, et=et: E.tensor_mul(om_t[:], om_t[:], possc_r[:]),
             reads=[om_t, possc_r], writes=[om_t])
        p.op("vector", lambda E, et=et: E.tensor_tensor_scan(
            phi_t[:], onesT[:], om_t[:], 0.0, ALU.mult, ALU.add),
            reads=[om_t, onesT], writes=[phi_t])
        p.op("vector", lambda E, et=et: E.tensor_copy(phic_t[:, et:et + 1], phi_t[:, TOK - 1:TOK]),
             reads=[phi_t], writes=[phic_t])
        p.dma("sync", o_phi.ap()[128 * et:128 * (et + 1), :], phi_t[:])
        trig_block(cx, phi_t, om_t, trg1, trg2, consts)
        p.op("vector", lambda E, et=et: E.tensor_mul(trg2[:], trg2[:], xn0[et][:]),
             reads=[trg2, xn0[et]], writes=[trg2])
        p.op("vector", lambda E, et=et: E.tensor_mul(trg1[:], trg1[:], xn0[et][:]),
             reads=[trg1, xn0[et]], writes=[trg1])
        p.op("vector", lambda E, et=et: E.tensor_reduce(
            cmc_t[:, et:et + 1], trg2[:], AX, ALU.add), reads=[trg2], writes=[cmc_t])
        p.op("vector", lambda E, et=et: E.tensor_reduce(
            cmc_t[:, 8 + et:8 + et + 1], trg1[:], AX, ALU.add), reads=[trg1], writes=[cmc_t])
    p.dma("sync", o_phic.ap(), phic_t[:])
    p.dma("sync", o_cmc.ap(), cmc_t[:])
    esO.close()
    p.barrier()

    esE = ExitStack()
    mw = [cx.sb(f"mw{k}", [128, 1024], es=esE) for k in range(8)]
    w2b = [cx.sb(f"w2b{k}", [128, 1024], BF16, es=esE) for k in range(2)]
    mxeT = [cx.sb(f"mxeT{k}", [128, PAD], es=esE) for k in range(8)]
    mxb = [cx.sb(f"mxb{k}", [128, PAD], BF16, es=esE) for k in range(8)]
    mh = [cx.sb(f"mh{f}", [128, PAD], BF16, es=esE) for f in range(32)]
    myeo = [cx.sb(f"myeo{o}", [128, PAD], es=esE) for o in range(8)]
    rowe = cx.sb("rowe", [128, D], es=esE)
    for e in range(8):
        for j in range(4):
            p.dma("sync", rowe[:], scr_xe.ap()[e * PAD + 128 * j:e * PAD + 128 * (j + 1), :])
            for k in range(8):
                cx.transpose(mxeT[k], rowe[:, 128 * k:128 * (k + 1)], 128, 128, ident,
                             out_off=128 * j)
        for k in range(8):
            p.op("vector", lambda E, k=k: E.tensor_copy(mxb[k][:], mxeT[k][:]),
                 reads=[mxeT[k]], writes=[mxb[k]])
        for fb in range(4):
            for k in range(8):
                p.dma("gpsimd", mw[k][:],
                      moeW1.ap()[e, 128 * k:128 * (k + 1), 1024 * fb:1024 * (fb + 1)])
            for k in range(8):
                for fi in range(8):
                    bk = cx.banks[fi]
                    p.op("tensor", lambda E, k=k, fi=fi, bk=bk: E.matmul(
                        bk[:], mw[k][:, 128 * fi:128 * (fi + 1)], mxeT[k][:],
                        start=(k == 0), stop=(k == 7)),
                        reads=[mw[k], mxeT[k]], writes=[bk])
            for fi in range(8):
                f = fb * 8 + fi
                bk = cx.banks[fi]
                p.op("scalar", lambda E, f=f, bk=bk: E.activation(mh[f][:], bk[:], AF.Relu),
                     reads=[bk], writes=[mh[f]])
        for f in range(32):
            p.dma("gpsimd", w2b[f % 2][:], moeW2.ap()[e, 128 * f:128 * (f + 1), :])
            for oi in range(8):
                bk = cx.banks[oi]
                p.op("tensor", lambda E, f=f, oi=oi, bk=bk: E.matmul(
                    bk[:], w2b[f % 2][:, 128 * oi:128 * (oi + 1)], mh[f][:],
                    start=(f == 0), stop=(f == 31)),
                    reads=[w2b[f % 2], mh[f]], writes=[bk])
        for oi in range(8):
            bk = cx.banks[oi]
            p.op("vector", lambda E, oi=oi, bk=bk: E.tensor_copy(myeo[oi][:], bk[:]),
                 reads=[bk], writes=[myeo[oi]])
        for cb in range(4):
            for oi in range(8):
                cx.transpose(rowe, myeo[oi][:, 128 * cb:128 * (cb + 1)], 128, 128, ident,
                             out_off=128 * oi)
            p.dma("sync", scr_ye.ap()[e * PAD + 128 * cb:e * PAD + 128 * (cb + 1), :], rowe[:])
    esE.close()
    p.barrier()

    esC = ExitStack()
    rowc = cx.sb("rowc", [128, D], es=esC)
    for j in range(16):
        p.op("vector", lambda E: E.memset(rowc[:], 0.0), reads=[], writes=[rowc])
        p.dma_ind(rowc[:], None, scr_ye.ap(), IndirectOffsetOnAxis(gidx_i[:, j:j + 1], 0),
                  8 * PAD - 1)
        p.op("vector", lambda E, j=j: E.tensor_scalar(
            rowc[:], rowc[:], pT[:, j:j + 1], None, ALU.mult),
            reads=[rowc, pT], writes=[rowc])
        p.dma("sync", o_moe.ap()[128 * j:128 * (j + 1), :], rowc[:])
    esC.close()
    p.barrier()

    p.finish()
    return nc



def build_launch_B():
    cx = Ctx()
    nc, p = cx.nc, cx.p
    CH = 512
    NCH = TOK // CH

    xTo = nc.dram_tensor("xTo", [D, TOK], F32, kind="ExternalInput")
    stato = nc.dram_tensor("stato", [2, TOK], F32, kind="ExternalInput")
    phiD = nc.dram_tensor("phiD", [D, TOK], F32, kind="ExternalInput")
    phicI = nc.dram_tensor("phicI", [128, 8], F32, kind="ExternalInput")
    cmcI = nc.dram_tensor("cmcI", [128, 16], F32, kind="ExternalInput")
    sxn2o = nc.dram_tensor("sxn2o", [1, TOK], F32, kind="ExternalInput")
    moeT = nc.dram_tensor("moeT", [D, TOK], F32, kind="ExternalInput")
    posinv = nc.dram_tensor("posinv", [1, TOK], F32, kind="ExternalInput")
    W1TD = nc.dram_tensor("W1TD", [4096, 2048], F32, kind="ExternalInput")
    g4 = nc.dram_tensor("g4", [128, 32], F32, kind="ExternalInput")
    b4 = nc.dram_tensor("b4", [128, 32], F32, kind="ExternalInput")
    b1r = nc.dram_tensor("b1r", [1, 2048], F32, kind="ExternalInput")
    W2TD = nc.dram_tensor("W2TD", [2048, 1024], F32, kind="ExternalInput")
    b2c = nc.dram_tensor("b2c", [128, 8], F32, kind="ExternalInput")
    gb1 = nc.dram_tensor("gb1", [128, 16], F32, kind="ExternalInput")
    identD = nc.dram_tensor("identD", [128, 128], F32, kind="ExternalInput")

    o_x1 = nc.dram_tensor("o_x1", [D, TOK], F32, kind="ExternalOutput")
    o_xn1 = nc.dram_tensor("o_xn1", [D, TOK], F32, kind="ExternalOutput")
    o_hln = nc.dram_tensor("o_hln", [D, TOK], F32, kind="ExternalOutput")
    o_mx = nc.dram_tensor("o_mx", [1, 1], F32, kind="ExternalOutput")
    scr = nc.dram_tensor("scrB", [2, 2048], F32, kind="Internal")

    ident = cx.sb("ident", [128, 128])
    p.dma("sync", ident[:], identD.ap())
    consts = trig_consts(cx, cx.es)

    w2r = [cx.sb(f"w2r{k}", [128, 1024], BF16) for k in range(16)]
    for k in range(16):
        p.dma("gpsimd", w2r[k][:], W2TD.ap()[128 * k:128 * (k + 1), :])
    mu_rep = cx.sb("mu_rep", [128, TOK])
    rs_rep = cx.sb("rs_rep", [128, TOK])
    pos_rep = cx.sb("pos_rep", [128, TOK])
    rowT = cx.sb("rowT", [1, TOK])
    p.dma("scalar", rowT[:], stato.ap()[0:1, :])
    cx.bcast_row(rowT, mu_rep, TOK)
    p.dma("scalar", rowT[:], stato.ap()[1:2, :], reads=[mu_rep])
    cx.bcast_row(rowT, rs_rep, TOK)
    p.dma("scalar", rowT[:], posinv.ap(), reads=[rs_rep])
    cx.bcast_row(rowT, pos_rep, TOK)
    g4t = cx.sb("g4t", [128, 32])
    b4t = cx.sb("b4t", [128, 32])
    b2t = cx.sb("b2t", [128, 8])
    gb1t = cx.sb("gb1t", [128, 16])
    phict = cx.sb("phict", [128, 8])
    cmct = cx.sb("cmct", [128, 16])
    p.dma("scalar", g4t[:], g4.ap())
    p.dma("scalar", b4t[:], b4.ap())
    p.dma("scalar", b2t[:], b2c.ap())
    p.dma("scalar", gb1t[:], gb1.ap())
    p.dma("scalar", phict[:], phicI.ap())
    p.dma("scalar", cmct[:], cmcI.ap())
    G1c = cx.sb("G1c", [128, 16])
    Bvc = cx.sb("Bvc", [128, 16])
    mxacc = cx.sb("mxacc", [128, 1])

    esG = ExitStack()
    w1kt = [cx.sb(f"w1kt{i}", [128, 2048], es=esG) for i in range(2)]
    w1s = cx.sb("w1s", [128, 2048], es=esG)
    growp = [cx.banks[i] for i in range(4)]
    browp = [cx.banks[4 + i] for i in range(4)]
    for k in range(32):
        wk = w1kt[k % 2]
        p.dma("sync" if k % 2 == 0 else "scalar", wk[:],
              W1TD.ap()[128 * k:128 * (k + 1), :])
        p.op("vector", lambda E, k=k, wk=wk: E.tensor_scalar(
            w1s[:], wk[:], g4t[:, k:k + 1], None, ALU.mult),
            reads=[wk, g4t], writes=[w1s])
        for j in range(4):
            p.op("tensor", lambda E, k=k, j=j: E.matmul(
                growp[j][0:1, :], cx.onescol[:], w1s[:, 512 * j:512 * (j + 1)],
                start=(k == 0), stop=(k == 31)),
                reads=[w1s, cx.onescol], writes=[growp[j]])
        p.op("vector", lambda E, k=k, wk=wk: E.tensor_scalar(
            w1s[:], wk[:], b4t[:, k:k + 1], None, ALU.mult),
            reads=[wk, b4t], writes=[w1s])
        for j in range(4):
            p.op("tensor", lambda E, k=k, j=j: E.matmul(
                browp[j][0:1, :], cx.onescol[:], w1s[:, 512 * j:512 * (j + 1)],
                start=(k == 0), stop=(k == 31)),
                reads=[w1s, cx.onescol], writes=[browp[j]])
    grow = cx.sb("grow", [1, 2048], es=esG)
    brow = cx.sb("brow", [1, 2048], es=esG)
    b1row = cx.sb("b1row", [1, 2048], es=esG)
    p.dma("scalar", b1row[:], b1r.ap())
    for j in range(4):
        p.op("vector", lambda E, j=j: E.tensor_copy(
            grow[0:1, 512 * j:512 * (j + 1)], growp[j][0:1, :]),
            reads=[growp[j]], writes=[grow])
        p.op("vector", lambda E, j=j: E.tensor_copy(
            brow[0:1, 512 * j:512 * (j + 1)], browp[j][0:1, :]),
            reads=[browp[j]], writes=[brow])
    p.op("vector", lambda E: E.tensor_add(brow[:], brow[:], b1row[:]),
         reads=[brow, b1row], writes=[brow])
    blkg = cx.sb("blkg", [16, 128], es=esG)
    p.dma("sync", scr.ap()[0:1, :], grow[:])
    p.dma("sync", scr.ap()[1:2, :], brow[:])
    p.dma("sync", blkg[:], scr.ap()[0:1, :].rearrange("a (j q) -> (a j) q", q=128))
    cx.transpose(G1c, blkg[:], 16, 128, ident)
    p.dma("sync", blkg[:], scr.ap()[1:2, :].rearrange("a (j q) -> (a j) q", q=128),
          reads=[G1c])
    cx.transpose(Bvc, blkg[:], 16, 128, ident)
    esG.close()
    p.barrier()

    esC = ExitStack()
    xn0c = [cx.sb(f"xn0c{k}", [128, CH], es=esC) for k in range(8)]
    ctx = [cx.sb(f"ctx{k}", [128, CH], BF16, es=esC) for k in range(32)]
    wslab = [cx.sb(f"wslab{i}", [128, 1024], es=esC) for i in range(2)]
    ctxg = [cx.sb(f"ctxg{i}", [128, CH], es=esC) for i in range(2)]
    h1g = [cx.sb(f"h1g{t}", [128, CH], BF16, es=esC) for t in range(16)]
    x1t = [cx.sb(f"x1t{k}", [128, CH], es=esC) for k in range(8)]
    fm = [cx.sb(f"fm{i}", [128, CH], es=esC) for i in range(2)]
    fx = [cx.sb(f"fx{i}", [128, CH], es=esC) for i in range(2)]
    hlb = [cx.sb(f"hlb{i}", [128, CH], es=esC) for i in range(2)]
    phit = [cx.sb(f"phit{i}", [128, CH], es=esC) for i in range(2)]
    sint = cx.sb("sint", [128, CH], es=esC)
    cost = cx.sb("cost", [128, CH], es=esC)
    smr = cx.sb("smr", [128, CH], es=esC)
    smi = cx.sb("smi", [128, CH], es=esC)
    memr = cx.sb("memr", [128, CH], es=esC)
    memi = cx.sb("memi", [128, CH], es=esC)
    sqs = cx.sb("sqs", [128, CH], es=esC)
    tas = cx.sb("tas", [128, CH], es=esC)
    carry = cx.sb("carryB", [128, 16], es=esC)
    statrow = cx.sb("statrow", [1, CH], es=esC)
    muc_rep = cx.sb("muc_rep", [128, CH], es=esC)
    rsc_rep = cx.sb("rsc_rep", [128, CH], es=esC)
    mrow = cx.sb("mrow", [1, CH], es=esC)
    m2row = cx.sb("m2row", [1, CH], es=esC)
    p.op("vector", lambda E: E.memset(mxacc[:], 0.0), writes=[mxacc])
    p.op("vector", lambda E: E.tensor_copy(carry[:], cmct[:]), reads=[cmct], writes=[carry])

    for tch in range(NCH):
        sl = slice(CH * tch, CH * (tch + 1))
        cx.rot = [cx.banks[i] for i in range(6)]
        for k in range(8):
            p.dma("scalar", xn0c[k][:], xTo.ap()[128 * k:128 * (k + 1), sl])
            p.op("vector", lambda E, k=k, sl=sl: E.tensor_sub(xn0c[k][:], xn0c[k][:], mu_rep[:, sl]),
                 reads=[xn0c[k], mu_rep], writes=[xn0c[k]])
            p.op("vector", lambda E, k=k, sl=sl: E.tensor_mul(xn0c[k][:], xn0c[k][:], rs_rep[:, sl]),
                 reads=[xn0c[k], rs_rep], writes=[xn0c[k]])
        pm2 = cx.banks[7]
        for et in range(8):
            ph = phit[et % 2]
            p.dma("sync", ph[:], phiD.ap()[128 * et:128 * (et + 1), sl])
            p.op("vector", lambda E, et=et, ph=ph: E.tensor_scalar(
                ph[:], ph[:], phict[:, et:et + 1], None, ALU.add),
                reads=[ph, phict], writes=[ph])
            trig_block(cx, ph, sqs, sint, cost, consts)
            p.op("vector", lambda E, et=et: E.tensor_tensor(
                ctx[et][:], xn0c[et][:], cost[:], ALU.mult),
                reads=[xn0c[et], cost], writes=[ctx[et]])
            p.op("vector", lambda E, et=et: E.tensor_tensor(
                ctx[8 + et][:], xn0c[et][:], sint[:], ALU.mult),
                reads=[xn0c[et], sint], writes=[ctx[8 + et]])
            p.op("vector", lambda E, et=et: E.tensor_tensor_scan(
                smr[:], cx.onescol[:].broadcast_to((128, CH)), ctx[et][:],
                carry[:, et:et + 1], ALU.mult, ALU.add),
                reads=[ctx[et], carry, cx.onescol], writes=[smr])
            p.op("vector", lambda E, et=et: E.tensor_copy(carry[:, et:et + 1], smr[:, CH - 1:CH]),
                 reads=[smr], writes=[carry])
            p.op("vector", lambda E, et=et: E.tensor_tensor_scan(
                smi[:], cx.onescol[:].broadcast_to((128, CH)), ctx[8 + et][:],
                carry[:, 8 + et:8 + et + 1], ALU.mult, ALU.add),
                reads=[ctx[8 + et], carry, cx.onescol], writes=[smi])
            p.op("vector", lambda E, et=et: E.tensor_copy(
                carry[:, 8 + et:8 + et + 1], smi[:, CH - 1:CH]),
                reads=[smi], writes=[carry])
            p.op("vector", lambda E, sl=sl: E.tensor_mul(memr[:], smr[:], pos_rep[:, sl]),
                 reads=[smr, pos_rep], writes=[memr])
            p.op("vector", lambda E, sl=sl: E.tensor_mul(memi[:], smi[:], pos_rep[:, sl]),
                 reads=[smi, pos_rep], writes=[memi])
            p.op("vector", lambda E: E.tensor_mul(smr[:], memr[:], cost[:]),
                 reads=[memr, cost], writes=[smr])
            p.op("vector", lambda E: E.tensor_mul(smi[:], memi[:], sint[:]),
                 reads=[memi, sint], writes=[smi])
            p.op("vector", lambda E, et=et: E.tensor_add(ctx[16 + et][:], smr[:], smi[:]),
                 reads=[smr, smi], writes=[ctx[16 + et]])
            p.op("vector", lambda E: E.tensor_mul(smr[:], memi[:], cost[:]),
                 reads=[memi, cost], writes=[smr])
            p.op("vector", lambda E: E.tensor_mul(smi[:], memr[:], sint[:]),
                 reads=[memr, sint], writes=[smi])
            p.op("vector", lambda E, et=et: E.tensor_sub(ctx[24 + et][:], smr[:], smi[:]),
                 reads=[smr, smi], writes=[ctx[24 + et]])
            p.op("scalar", lambda E: E.activation(sqs[:], memr[:], AF.Square),
                 reads=[memr], writes=[sqs])
            p.op("tensor", lambda E, et=et: E.matmul(
                pm2[0:1, 0:CH], cx.onescol[:], sqs[:],
                start=(et == 0), stop=False),
                reads=[sqs, cx.onescol], writes=[pm2])
            p.op("scalar", lambda E: E.activation(sqs[:], memi[:], AF.Square),
                 reads=[memi], writes=[sqs])
            p.op("tensor", lambda E, et=et: E.matmul(
                pm2[0:1, 0:CH], cx.onescol[:], sqs[:],
                start=False, stop=(et == 7)),
                reads=[sqs, cx.onescol], writes=[pm2])
        pcs = cx.banks[6]
        for k in range(32):
            p.op("tensor", lambda E, k=k: E.matmul(
                pcs[0:1, 0:CH], cx.onescolb[:], ctx[k][:],
                start=(k == 0), stop=(k == 31)),
                reads=[ctx[k], cx.onescolb], writes=[pcs])
        p.op("vector", lambda E: E.tensor_scalar(
            mrow[:], pcs[0:1, 0:CH], 1.0 / 4096.0, None, ALU.mult),
            reads=[pcs], writes=[mrow])
        p.dma("scalar", statrow[:], sxn2o.ap()[0:1, sl])
        p.op("vector", lambda E: E.tensor_add(m2row[:], pm2[0:1, 0:CH], statrow[:]),
             reads=[pm2, statrow], writes=[m2row])
        p.op("vector", lambda E: E.tensor_scalar(
            m2row[:], m2row[:], 1.0 / 4096.0, None, ALU.mult),
            reads=[m2row], writes=[m2row])
        p.op("scalar", lambda E: E.activation(statrow[:], mrow[:], AF.Square),
             reads=[mrow], writes=[statrow])
        p.op("vector", lambda E: E.tensor_sub(m2row[:], m2row[:], statrow[:]),
             reads=[m2row, statrow], writes=[m2row])
        p.op("vector", lambda E: E.tensor_scalar(
            m2row[:], m2row[:], float(LN_EPS), None, ALU.add),
            reads=[m2row], writes=[m2row])
        p.op("scalar", lambda E: E.activation(m2row[:], m2row[:], AF.Sqrt),
             reads=[m2row], writes=[m2row])
        p.op("vector", lambda E: E.reciprocal(m2row[:], m2row[:]), reads=[m2row], writes=[m2row])
        cx.bcast_row(m2row, rsc_rep, CH)
        p.op("vector", lambda E: E.tensor_mul(mrow[:], mrow[:], m2row[:]),
             reads=[mrow, m2row], writes=[mrow])
        p.op("vector", lambda E: E.tensor_scalar(mrow[:], mrow[:], -1.0, None, ALU.mult),
             reads=[mrow], writes=[mrow])
        cx.bcast_row(mrow, muc_rep, CH)

        for ps in range(2):
            for k in range(32):
                wsl = wslab[k % 2]
                cg = ctxg[k % 2]
                p.dma("sync" if k % 2 == 0 else "scalar", wsl[:],
                      W1TD.ap()[128 * k:128 * (k + 1), 1024 * ps:1024 * (ps + 1)])
                p.op("vector", lambda E, k=k, cg=cg: E.tensor_scalar(
                    cg[:], ctx[k][:], g4t[:, k:k + 1], None, ALU.mult),
                    reads=[ctx[k], g4t], writes=[cg])
                for j in range(8):
                    p.op("tensor", lambda E, k=k, j=j, wsl=wsl, cg=cg: E.matmul(
                        cx.banks[j][:, 0:CH], wsl[:, 128 * j:128 * (j + 1)],
                        cg[:], start=(k == 0), stop=(k == 31)),
                        reads=[wsl, cg], writes=[cx.banks[j]])
            for j in range(8):
                ot = 8 * ps + j
                p.op("vector", lambda E, j=j: E.tensor_tensor(
                    tas[:], cx.banks[j][:, 0:CH], rsc_rep[:], ALU.mult),
                    reads=[cx.banks[j], rsc_rep], writes=[tas])
                p.op("vector", lambda E, ot=ot: E.scalar_tensor_tensor(
                    tas[:], muc_rep[:], G1c[:, ot:ot + 1], tas[:], ALU.mult, ALU.add),
                    reads=[muc_rep, G1c, tas], writes=[tas])
                p.op("scalar", lambda E, ot=ot: E.activation(
                    h1g[ot][:], tas[:], AF.Gelu, bias=Bvc[:, ot:ot + 1]),
                    reads=[tas, Bvc], writes=[h1g[ot]])
        for o2 in range(8):
            pw = cx.banks[o2]
            for k in range(16):
                p.op("tensor", lambda E, k=k, o2=o2, pw=pw: E.matmul(
                    pw[:, 0:CH], w2r[k][:, 128 * o2:128 * (o2 + 1)], h1g[k][:],
                    start=(k == 0), stop=(k == 15)),
                    reads=[w2r[k], h1g[k]], writes=[pw])
            fmb, fxb = fm[o2 % 2], fx[o2 % 2]
            p.dma("gpsimd", fmb[:], moeT.ap()[128 * o2:128 * (o2 + 1), sl])
            p.dma("gpsimd", fxb[:], xTo.ap()[128 * o2:128 * (o2 + 1), sl])
            p.op("vector", lambda E, pw=pw, o2=o2: E.tensor_scalar(
                memr[:], pw[:, 0:CH], b2t[:, o2:o2 + 1], None, ALU.add),
                reads=[pw, b2t], writes=[memr])
            p.op("vector", lambda E, o2=o2: E.tensor_add(memr[:], memr[:], xn0c[o2][:]),
                 reads=[memr, xn0c[o2]], writes=[memr])
            p.op("vector", lambda E, fmb=fmb: E.tensor_add(memr[:], memr[:], fmb[:]),
                 reads=[memr, fmb], writes=[memr])
            p.op("vector", lambda E, o2=o2, fxb=fxb: E.tensor_add(x1t[o2][:], memr[:], fxb[:]),
                 reads=[memr, fxb], writes=[x1t[o2]])
            p.dma("sync", o_x1.ap()[128 * o2:128 * (o2 + 1), sl], x1t[o2][:])
        ps1, ps2 = cx.bank(), cx.bank()
        for k in range(8):
            p.op("tensor", lambda E, k=k, ps1=ps1: E.matmul(
                ps1[0:1, 0:CH], cx.onescol[:], x1t[k][:], start=(k == 0), stop=(k == 7)),
                reads=[x1t[k], cx.onescol], writes=[ps1])
            p.op("scalar", lambda E, k=k: E.activation(sqs[:], x1t[k][:], AF.Square),
                 reads=[x1t[k]], writes=[sqs])
            p.op("tensor", lambda E, k=k, ps2=ps2: E.matmul(
                ps2[0:1, 0:CH], cx.onescol[:], sqs[:], start=(k == 0), stop=(k == 7)),
                reads=[sqs, cx.onescol], writes=[ps2])
        p.op("vector", lambda E, ps1=ps1: E.tensor_scalar(
            mrow[:], ps1[0:1, 0:CH], 1.0 / D, None, ALU.mult), reads=[ps1], writes=[mrow])
        p.op("vector", lambda E, ps2=ps2: E.tensor_scalar(
            m2row[:], ps2[0:1, 0:CH], 1.0 / D, None, ALU.mult), reads=[ps2], writes=[m2row])
        p.op("scalar", lambda E: E.activation(statrow[:], mrow[:], AF.Square),
             reads=[mrow], writes=[statrow])
        p.op("vector", lambda E: E.tensor_sub(m2row[:], m2row[:], statrow[:]),
             reads=[m2row, statrow], writes=[m2row])
        p.op("vector", lambda E: E.tensor_scalar(
            m2row[:], m2row[:], float(LN_EPS), None, ALU.add), reads=[m2row], writes=[m2row])
        p.op("scalar", lambda E: E.activation(m2row[:], m2row[:], AF.Sqrt),
             reads=[m2row], writes=[m2row])
        p.op("vector", lambda E: E.reciprocal(m2row[:], m2row[:]), reads=[m2row], writes=[m2row])
        cx.bcast_row(m2row, rsc_rep, CH)
        cx.bcast_row(mrow, muc_rep, CH)
        for k in range(8):
            p.op("vector", lambda E, k=k: E.tensor_sub(x1t[k][:], x1t[k][:], muc_rep[:]),
                 reads=[x1t[k], muc_rep], writes=[x1t[k]])
            p.op("vector", lambda E, k=k: E.tensor_mul(x1t[k][:], x1t[k][:], rsc_rep[:]),
                 reads=[x1t[k], rsc_rep], writes=[x1t[k]])
            p.dma("sync", o_xn1.ap()[128 * k:128 * (k + 1), sl], x1t[k][:])
            p.op("vector", lambda E, k=k: E.tensor_reduce(
                sqs[:, 0:1], x1t[k][:], AX, ALU.max, apply_absolute_value=True),
                reads=[x1t[k]], writes=[sqs])
            p.op("vector", lambda E, k=k: E.tensor_max(mxacc[:], mxacc[:], sqs[:, 0:1]),
                 reads=[sqs, mxacc], writes=[mxacc])
            hb = hlb[k % 2]
            p.op("vector", lambda E, k=k, hb=hb: E.tensor_scalar(
                hb[:], x1t[k][:], gb1t[:, k:k + 1], gb1t[:, 8 + k:8 + k + 1],
                ALU.mult, ALU.add), reads=[x1t[k], gb1t], writes=[hb])
            p.dma("scalar", o_hln.ap()[128 * k:128 * (k + 1), sl], hb[:])
        cx.rot = None
    esC.close()
    p.barrier()
    pmx = cx.bank()
    p.op("tensor", lambda E, pmx=pmx: E.transpose(pmx[0:1, 0:128], mxacc[:], ident[:]),
         reads=[mxacc, ident], writes=[pmx])
    mrow2 = cx.sb("mrow2", [1, 128])
    p.op("vector", lambda E, pmx=pmx: E.tensor_copy(mrow2[:], pmx[0:1, 0:128]),
         reads=[pmx], writes=[mrow2])
    p.op("vector", lambda E: E.tensor_reduce(mrow2[:, 0:1], mrow2[:], AX, ALU.max),
         reads=[mrow2], writes=[mrow2])
    p.dma("sync", o_mx.ap(), mrow2[0:1, 0:1], reads=[mrow2])

    p.finish()
    return nc


_NC_CACHE = {}


def _get_nc(name, builder):
    if name not in _NC_CACHE:
        _NC_CACHE[name] = builder()
    return _NC_CACHE[name]


def _f32(a):
    return np.ascontiguousarray(a, dtype=np.float32)


def host_inputs_A(ins):
    x = _f32(ins["L0_attn_x"]).reshape(NT, D)
    xT = np.ascontiguousarray(x.T)
    Wr16 = np.zeros((D, 16), np.float32)
    Wr16[:, 0:8] = _f32(ins["L0_ffn_Wr"])
    Wr16[:, 8] = 1.0
    WomT = np.ascontiguousarray(_f32(ins["L0_attn_W_omega"]).T)
    b_om = np.ascontiguousarray(_f32(ins["L0_attn_b_omega"]).reshape(8, 128).T)
    lsc = np.ascontiguousarray(_f32(ins["L0_attn_log_scale"]).reshape(8, 128).T)
    e9 = np.zeros((1, 1152), np.float32)
    e9[0, :] = np.tile(np.arange(9, dtype=np.float32), 128)
    ident = np.eye(128, dtype=np.float32)
    moeW1 = _f32(ins["L0_ffn_W1"])
    moeW2 = _f32(ins["L0_ffn_W2"])
    maps = []
    for c in range(NCORE):
        pos = np.arange(c % 2 * TOK + 1, c % 2 * TOK + TOK + 1, dtype=np.float64)
        possc = (1.0 / np.sqrt(pos)).astype(np.float32).reshape(1, TOK)
        cmask = np.zeros((1, 8), np.float32)
        cmask[0, c] = 1.0
        bidx = np.zeros((16, 3), np.float32)
        for j in range(16):
            for rid in range(3):
                bidx[j, rid] = rid * 128 + c * 16 + j
        ridxv = np.array([[3 * 8 + c], [4 * 8 + c]], np.float32)
        maps.append({
            "xT": xT, "xTo": np.ascontiguousarray(xT[:, c * TOK:(c + 1) * TOK]),
            "Wr16": Wr16, "WomT": WomT, "b_om": b_om, "lsc": lsc,
            "possc": possc, "moeW1": moeW1, "moeW2": moeW2, "e9iota": e9,
            "cmask": cmask, "bidx": bidx, "ridx": ridxv, "identD": ident,
            "e8colD": np.arange(8, dtype=np.float32).reshape(8, 1),
        })
    return maps




def build_launch_C():
    cx = Ctx()
    nc, p = cx.nc, cx.p

    hlnD = nc.dram_tensor("hlnD", [D, TOK], F32, kind="ExternalInput")
    hllD = nc.dram_tensor("hllD", [D, 3], F32, kind="ExternalInput")
    hlrD = nc.dram_tensor("hlrD", [D, 3], F32, kind="ExternalInput")
    inwD = nc.dram_tensor("inwD", [D, 4096], F32, kind="ExternalInput")
    cwf = nc.dram_tensor("cwf", [128, 64], F32, kind="ExternalInput")
    cwb = nc.dram_tensor("cwb", [128, 64], F32, kind="ExternalInput")
    cbf = nc.dram_tensor("cbf", [128, 16], F32, kind="ExternalInput")
    cbb = nc.dram_tensor("cbb", [128, 16], F32, kind="ExternalInput")
    xpf = nc.dram_tensor("xpf", [2048, 96], F32, kind="ExternalInput")
    xpb = nc.dram_tensor("xpb", [2048, 96], F32, kind="ExternalInput")
    dtwf = nc.dram_tensor("dtwf", [64, 2048], F32, kind="ExternalInput")
    dtwb = nc.dram_tensor("dtwb", [64, 2048], F32, kind="ExternalInput")
    dtbf = nc.dram_tensor("dtbf", [128, 16], F32, kind="ExternalInput")
    dtbb = nc.dram_tensor("dtbb", [128, 16], F32, kind="ExternalInput")
    dskf = nc.dram_tensor("dskf", [128, 16], F32, kind="ExternalInput")
    dskb = nc.dram_tensor("dskb", [128, 16], F32, kind="ExternalInput")

    o_dt = nc.dram_tensor("o_dt", [2, 2048, TOK], BF16, kind="ExternalOutput")
    o_dtx = nc.dram_tensor("o_dtx", [2, 2048, TOK], BF16, kind="ExternalOutput")
    o_y0 = nc.dram_tensor("o_y0", [2, 2048, TOK], BF16, kind="ExternalOutput")
    o_bc = nc.dram_tensor("o_bc", [2, 32, TOK], F32, kind="ExternalOutput")
    o_z = nc.dram_tensor("o_z", [2048, TOK], F32, kind="ExternalOutput")

    onebias = cx.sb("onebias", [128, 1])
    p.op("vector", lambda E: E.memset(onebias[:], 1.0), writes=[onebias])
    neg1ap = consts_neg1(cx)

    xsb = [cx.sb(f"xsb{t}", [128, TOK], BF16) for t in range(16)]
    xhal = [cx.sb(f"xhal{t}", [128, 6], BF16) for t in range(16)]
    esP = ExitStack()
    inwr = [cx.sb(f"inwr{k}", [128, 4096], BF16, es=esP) for k in range(8)]
    hlnb = [cx.sb(f"hlnb{k}", [128, TOK], BF16, es=esP) for k in range(8)]
    hlob = [cx.sb(f"hlob{k}", [128, 6], BF16, es=esP) for k in range(8)]
    zrow = [cx.sb(f"zrow{i}", [128, 512], es=esP) for i in range(2)]
    for k in range(8):
        p.dma("gpsimd", inwr[k][:], inwD.ap()[128 * k:128 * (k + 1), :])
        p.dma("gpsimd", hlnb[k][:], hlnD.ap()[128 * k:128 * (k + 1), :])
        p.dma("gpsimd", hlob[k][:, 0:3], hllD.ap()[128 * k:128 * (k + 1), :])
        p.dma("gpsimd", hlob[k][:, 3:6], hlrD.ap()[128 * k:128 * (k + 1), :])
    for ct in range(32):
        for ch in range(4):
            pw = cx.banks[(4 * ct + ch) % 8]
            for k in range(8):
                p.op("tensor", lambda E, ct=ct, ch=ch, k=k, pw=pw: E.matmul(
                    pw[:, 0:512], inwr[k][:, 128 * ct:128 * (ct + 1)],
                    hlnb[k][:, 512 * ch:512 * (ch + 1)],
                    start=(k == 0), stop=(k == 7)),
                    reads=[inwr[k], hlnb[k]], writes=[pw])
            if ct < 16:
                p.op("vector", lambda E, ct=ct, ch=ch, pw=pw: E.tensor_copy(
                    xsb[ct][:, 512 * ch:512 * (ch + 1)], pw[:, 0:512]),
                    reads=[pw], writes=[xsb[ct]])
            else:
                zr = zrow[ch % 2]
                p.op("vector", lambda E, pw=pw, zr=zr: E.tensor_copy(zr[:], pw[:, 0:512]),
                     reads=[pw], writes=[zr])
                p.dma("sync", o_z.ap()[128 * (ct - 16):128 * (ct - 15),
                                       512 * ch:512 * (ch + 1)], zr[:])
        if ct < 16:
            pw = cx.banks[(4 * ct) % 8]
            for k in range(8):
                p.op("tensor", lambda E, ct=ct, k=k, pw=pw: E.matmul(
                    pw[:, 0:6], inwr[k][:, 128 * ct:128 * (ct + 1)], hlob[k][:],
                    start=(k == 0), stop=(k == 7)),
                    reads=[inwr[k], hlob[k]], writes=[pw])
            p.op("vector", lambda E, ct=ct, pw=pw: E.tensor_copy(
                xhal[ct][:], pw[:, 0:6]), reads=[pw], writes=[xhal[ct]])
    esP.close()
    p.barrier()

    cwt = cx.sb("cwt", [128, 64])
    cbt = cx.sb("cbt", [128, 16])
    dtbt = cx.sb("dtbt", [128, 16])
    dskt = cx.sb("dskt", [128, 16])
    ext = cx.sb("ext", [128, TOK + 8], BF16)
    acc = cx.sb("acc", [128, TOK])
    xcb = [cx.sb(f"xcb_{t}", [128, TOK], BF16) for t in range(16)]
    xpt = cx.sb("xpt", [128, 96], BF16)
    xdbl = cx.sb("xdbl", [96, TOK])
    sp1 = cx.sb("sp1", [128, 512])
    sp2 = cx.sb("sp2", [128, 512])
    dtt = cx.sb("dtt", [128, 512], BF16)
    dxt = cx.sb("dxt", [128, 512], BF16)
    bct = cx.sb("bct", [32, TOK])
    dtw_t = cx.sb("dtw_t", [64, 2048])
    for d, (cw, cb, xp, dtw, dtb, dsk) in enumerate(
            (((cwf, cbf, xpf, dtwf, dtbf, dskf)),
             ((cwb, cbb, xpb, dtwb, dtbb, dskb)))):
        p.dma("scalar", cwt[:], cw.ap())
        p.dma("scalar", cbt[:], cb.ap())
        p.dma("scalar", dtbt[:], dtb.ap())
        p.dma("scalar", dskt[:], dsk.ap())
        for ct in range(16):
            p.op("vector", lambda E, ct=ct: E.memset(ext[:], 0.0), writes=[ext])
            p.op("vector", lambda E, ct=ct: E.tensor_copy(ext[:, 3:3 + TOK], xsb[ct][:]),
                 reads=[xsb[ct]], writes=[ext])
            if d == 0:
                p.op("vector", lambda E, ct=ct: E.tensor_copy(ext[:, 0:3], xhal[ct][:, 0:3]),
                     reads=[xhal[ct]], writes=[ext])
            else:
                p.op("vector", lambda E, ct=ct: E.tensor_copy(
                    ext[:, 3 + TOK:6 + TOK], xhal[ct][:, 3:6]),
                    reads=[xhal[ct]], writes=[ext])
            offs = [0, 1, 2, 3] if d == 0 else [6, 5, 4, 3]
            p.op("vector", lambda E, ct=ct, o=offs[0]: E.tensor_scalar(
                acc[:], ext[:, o:o + TOK], cwt[:, 16 * 0 + ct:16 * 0 + ct + 1], None, ALU.mult),
                reads=[ext, cwt], writes=[acc])
            for k in (1, 2, 3):
                p.op("vector", lambda E, ct=ct, k=k, o=offs[k]: E.scalar_tensor_tensor(
                    acc[:], ext[:, o:o + TOK], cwt[:, 16 * k + ct:16 * k + ct + 1],
                    acc[:], ALU.mult, ALU.add),
                    reads=[ext, cwt, acc], writes=[acc])
            p.op("scalar", lambda E, ct=ct: E.activation(
                xcb[ct][:], acc[:], AF.Silu, bias=cbt[:, ct:ct + 1]),
                reads=[acc, cbt], writes=[xcb[ct]])
        for ct in range(16):
            p.op("vector", lambda E, ct=ct: E.tensor_scalar(
                acc[:], xcb[ct][:], dskt[:, ct:ct + 1], None, ALU.mult),
                reads=[xcb[ct], dskt], writes=[acc])
            p.dma("gpsimd", o_y0.ap()[d, 128 * ct:128 * (ct + 1), :], acc[:], reads=[acc])
        for k in range(16):
            p.dma("gpsimd", xpt[:], xp.ap()[128 * k:128 * (k + 1), :])
            for ch in range(4):
                pt = cx.banks[ch]
                p.op("tensor", lambda E, k=k, ch=ch, pt=pt: E.matmul(
                    pt[0:96, :], xpt[:], xcb[k][:, 512 * ch:512 * (ch + 1)],
                    start=(k == 0), stop=(k == 15)),
                    reads=[xpt, xcb[k]], writes=[pt])
        for ch in range(4):
            pt = cx.banks[ch]
            p.op("vector", lambda E, ch=ch, pt=pt: E.tensor_copy(
                xdbl[:, 512 * ch:512 * (ch + 1)], pt[0:96, :]), reads=[pt], writes=[xdbl])
        p.op("vector", lambda E: E.tensor_copy(bct[:], xdbl[64:96, :]),
             reads=[xdbl], writes=[bct])
        p.dma("sync", o_bc.ap()[d], bct[:])
        p.dma("sync", dtw_t[:], dtw.ap())
        for dt_i in range(16):
            for ch in range(4):
                pt = cx.banks[4 + (ch % 2)]
                p.op("tensor", lambda E, dt_i=dt_i, ch=ch, pt=pt: E.matmul(
                    pt[:, 0:512], dtw_t[:, 128 * dt_i:128 * (dt_i + 1)],
                    xdbl[0:64, 512 * ch:512 * (ch + 1)], start=True, stop=True),
                    reads=[dtw_t, xdbl], writes=[pt])
                p.op("vector", lambda E, dt_i=dt_i, pt=pt: E.tensor_scalar(
                    sp1[:], pt[:, 0:512], dtbt[:, dt_i:dt_i + 1], None, ALU.add),
                    reads=[pt, dtbt], writes=[sp1])
                p.op("scalar", lambda E: E.activation(sp2[:], sp1[:], AF.Abs),
                     reads=[sp1], writes=[sp2])
                p.op("scalar", lambda E: E.activation(sp2[:], sp2[:], AF.Exp,
                                                      scale=neg1ap),
                     reads=[sp2, "negone_c"], writes=[sp2])
                p.op("scalar", lambda E: E.activation(sp2[:], sp2[:], AF.Ln,
                                                      bias=onebias[:]),
                     reads=[sp2, onebias], writes=[sp2])
                p.op("scalar", lambda E: E.activation(sp1[:], sp1[:], AF.Relu),
                     reads=[sp1], writes=[sp1])
                p.op("vector", lambda E: E.tensor_add(sp1[:], sp1[:], sp2[:]),
                     reads=[sp1, sp2], writes=[sp1])
                p.op("vector", lambda E: E.tensor_copy(dtt[:], sp1[:]),
                     reads=[sp1], writes=[dtt])
                p.dma("sync", o_dt.ap()[d, 128 * dt_i:128 * (dt_i + 1),
                                        512 * ch:512 * (ch + 1)], dtt[:], reads=[dtt])
                p.op("vector", lambda E, dt_i=dt_i, ch=ch: E.tensor_tensor(
                    dxt[:], dtt[:], xcb[dt_i][:, 512 * ch:512 * (ch + 1)], ALU.mult),
                    reads=[dtt, xcb[dt_i]], writes=[dxt])
                p.dma("sync", o_dtx.ap()[d, 128 * dt_i:128 * (dt_i + 1),
                                         512 * ch:512 * (ch + 1)], dxt[:], reads=[dxt])
    p.finish()
    return nc


_NEG1 = {}


def consts_neg1(cx):
    if id(cx) not in _NEG1:
        t = cx.sb("negone_c", [128, 1])
        cx.p.op("vector", lambda E: E.memset(t[:], -1.0), writes=[t])
        _NEG1[id(cx)] = t
    return _NEG1[id(cx)][:]


def build_launch_scan():
    cx = Ctx()
    nc, p = cx.nc, cx.p
    CH = 512
    NCH = TOK // CH

    dtD = nc.dram_tensor("dtD", [2048, TOK], BF16, kind="ExternalInput")
    dtxD = nc.dram_tensor("dtxD", [2048, TOK], BF16, kind="ExternalInput")
    y0D = nc.dram_tensor("y0D", [2048, TOK], BF16, kind="ExternalInput")
    bcflat = nc.dram_tensor("bcflat", [2, 16 * TOK], F32, kind="ExternalInput")
    alogD = nc.dram_tensor("alogD", [128, 256], F32, kind="ExternalInput")
    hinD = nc.dram_tensor("hinD", [128, 256], F32, kind="ExternalInput")

    o_y = nc.dram_tensor("o_y", [2048, TOK], BF16, kind="ExternalOutput")
    o_hf = nc.dram_tensor("o_hf", [128, 256], F32, kind="ExternalOutput")

    negA = cx.sb("negA", [128, 256])
    hst = cx.sb("hst", [128, 256])
    p.dma("scalar", negA[:], alogD.ap())
    p.op("scalar", lambda E: E.activation(negA[:], negA[:], AF.Exp), reads=[negA], writes=[negA])
    p.op("vector", lambda E: E.tensor_scalar(negA[:], negA[:], -1.0, None, ALU.mult),
         reads=[negA], writes=[negA])
    p.dma("scalar", hst[:], hinD.ap())

    brep = cx.sb("brep", [128, 16 * CH], BF16)
    crep = cx.sb("crep", [128, 16 * CH], BF16)
    bcrow = cx.sb("bcrow", [1, 16 * CH])
    dtt = [cx.sb(f"dtt{i}", [128, CH], BF16) for i in range(2)]
    dxx = [cx.sb(f"dxx{i}", [128, CH], BF16) for i in range(2)]
    y00 = [cx.sb(f"y00{i}", [128, CH], BF16) for i in range(2)]
    dA = cx.sb("dA", [128, CH], BF16)
    dBu = cx.sb("dBu", [128, CH], BF16)
    hh = cx.sb("hh", [128, CH], BF16)
    yC = cx.sb("yC", [128, CH], BF16)
    acc = cx.sb("accS", [128, CH], BF16)

    for ch in range(NCH):
        sl = slice(CH * ch, CH * (ch + 1))
        for row, rep in ((0, brep), (1, crep)):
            src_ap = bcflat.ap()[row:row + 1, :].rearrange(
                "a (n t) -> a n t", t=TOK)[:, :, sl]
            p.dma("scalar", bcrow[:].rearrange("a (n t) -> a n t", t=CH), src_ap,
                  reads=[rep])
            for j in range(16):
                pt = cx.bank()
                p.op("tensor", lambda E, j=j, pt=pt: E.matmul(
                    pt[:, 0:CH], cx.ones1[0:1, :], bcrow[0:1, CH * j:CH * (j + 1)],
                    start=True, stop=True), reads=[bcrow, cx.ones1], writes=[pt])
                p.op("scalar", lambda E, j=j, pt=pt, rep=rep: E.copy(
                    rep[:, CH * j:CH * (j + 1)], pt[:, 0:CH]), reads=[pt], writes=[rep])
        for dt_i in range(16):
            pb = dtt[dt_i % 2]
            px = dxx[dt_i % 2]
            py = y00[dt_i % 2]
            p.dma("sync", pb[:], dtD.ap()[128 * dt_i:128 * (dt_i + 1), sl])
            p.dma("sync", px[:], dtxD.ap()[128 * dt_i:128 * (dt_i + 1), sl])
            p.dma("sync", py[:], y0D.ap()[128 * dt_i:128 * (dt_i + 1), sl])
            p.op("vector", lambda E, py=py: E.tensor_copy(acc[:], py[:]),
                 reads=[py], writes=[acc])
            for n in range(16):
                col = dt_i * 16 + n
                p.op("scalar", lambda E, n=n, col=col, pb=pb: E.activation(
                    dA[:], pb[:], AF.Exp, scale=negA[:, col:col + 1]),
                    reads=[pb, negA], writes=[dA])
                p.op("vector", lambda E, n=n, px=px: E.tensor_tensor(
                    dBu[:], px[:], brep[:, CH * n:CH * (n + 1)], ALU.mult),
                    reads=[px, brep], writes=[dBu])
                p.op("vector", lambda E, col=col: E.tensor_tensor_scan(
                    hh[:], dA[:], dBu[:], hst[:, col:col + 1], ALU.mult, ALU.add),
                    reads=[dA, dBu, hst], writes=[hh])
                p.op("vector", lambda E, col=col: E.tensor_copy(
                    hst[:, col:col + 1], hh[:, CH - 1:CH]), reads=[hh], writes=[hst])
                p.op("vector", lambda E, n=n: E.tensor_tensor(
                    yC[:], hh[:], crep[:, CH * n:CH * (n + 1)], ALU.mult),
                    reads=[hh, crep], writes=[yC])
                p.op("vector", lambda E: E.tensor_add(acc[:], acc[:], yC[:]),
                     reads=[acc, yC], writes=[acc])
            p.dma("sync", o_y.ap()[128 * dt_i:128 * (dt_i + 1), sl], acc[:], reads=[acc])
    p.dma("sync", o_hf.ap(), hst[:])
    p.finish()
    return nc



def emit_lpbq_dequant(cx, wD, rows, cols, outT, ident, nbl):
    nc, p = cx.nc, cx.p
    es = ExitStack()
    wt = cx.sb(f"dq_wt{id(wD)}", [128, cols], es=es)
    bm = cx.sb(f"dq_bm{id(wD)}", [128, nbl], es=es)
    sinv = cx.sb(f"dq_si{id(wD)}", [128, nbl], es=es)
    rr = cx.sb(f"dq_rr{id(wD)}", [128, cols], es=es)
    wq = cx.sb(f"dq_wq{id(wD)}", [128, cols], es=es)
    w3 = wt[:].rearrange("p (b q) -> p b q", q=32)
    for ti in range(rows // 128):
        p.dma("sync", wt[:], wD.ap()[128 * ti:128 * (ti + 1), :])
        p.op("vector", lambda E: E.tensor_reduce(bm[:], w3, AX, ALU.max,
                                                 apply_absolute_value=True),
             reads=[wt], writes=[bm])
        p.op("vector", lambda E: E.tensor_scalar(bm[:], bm[:], 1.0 / 7.0, 1e-12,
                                                 ALU.mult, ALU.max),
             reads=[bm], writes=[bm])
        p.op("vector", lambda E: E.reciprocal(sinv[:], bm[:]), reads=[bm], writes=[sinv])
        p.op("vector", lambda E: E.tensor_tensor(
            rr[:], wt[:], sinv[:].rearrange("p b -> p b ()").broadcast_to((128, nbl, 32)),
            ALU.mult), reads=[wt, sinv], writes=[rr])
        p.op("vector", lambda E: E.tensor_scalar(rr[:], rr[:], MAGIC, -MAGIC,
                                                 ALU.add, ALU.add),
             reads=[rr], writes=[rr])
        p.op("vector", lambda E: E.tensor_tensor(
            wq[:], rr[:], bm[:].rearrange("p b -> p b ()").broadcast_to((128, nbl, 32)),
            ALU.mult), reads=[rr, bm], writes=[wq])
        for k in range(cols // 128):
            cx.transpose(outT[k], wq[:, 128 * k:128 * (k + 1)], 128, 128, ident,
                         out_off=128 * ti)
    es.close()
    p.barrier()


def emit_qdq(cx, src_tiles, sptr, siptr, out_tiles, n):
    p = cx.p
    for t_s, t_o in zip(src_tiles, out_tiles):
        p.op("vector", lambda E, t_s=t_s, t_o=t_o: E.tensor_scalar(
            t_o[:], t_s[:], siptr, MAGIC, ALU.mult, ALU.add),
            reads=[t_s, "qscale"], writes=[t_o])
        p.op("vector", lambda E, t_o=t_o: E.tensor_scalar(
            t_o[:], t_o[:], -MAGIC, sptr, ALU.add, ALU.mult),
            reads=[t_o, "qscale"], writes=[t_o])


def build_launch_G():
    cx = Ctx()
    nc, p = cx.nc, cx.p
    CH = 512

    yfD = nc.dram_tensor("yfD", [2048, TOK], BF16, kind="ExternalInput")
    ybD = nc.dram_tensor("ybD", [2048, TOK], BF16, kind="ExternalInput")
    zD = nc.dram_tensor("zD", [2048, TOK], F32, kind="ExternalInput")
    xn1D = nc.dram_tensor("xn1D", [D, TOK], F32, kind="ExternalInput")
    x1D = nc.dram_tensor("x1D", [D, TOK], F32, kind="ExternalInput")
    mrgD = nc.dram_tensor("mrgD", [4096, 2048], F32, kind="ExternalInput")
    outwD = nc.dram_tensor("outwD", [2048, D], F32, kind="ExternalInput")
    wupD = nc.dram_tensor("wupD", [3072, D], F32, kind="ExternalInput")
    wgtD = nc.dram_tensor("wgtD", [3072, D], F32, kind="ExternalInput")
    gmaxD = nc.dram_tensor("gmaxD", [1, 8], F32, kind="ExternalInput")
    identD = nc.dram_tensor("identD", [128, 128], F32, kind="ExternalInput")

    o_x2p = nc.dram_tensor("o_x2p", [D, TOK], F32, kind="ExternalOutput")
    o_up = nc.dram_tensor("o_up", [3072, TOK], F32, kind="ExternalOutput")
    o_gate = nc.dram_tensor("o_gate", [3072, TOK], F32, kind="ExternalOutput")
    o_mx2 = nc.dram_tensor("o_mx2", [1, 2], F32, kind="ExternalOutput")

    ident = cx.sb("ident", [128, 128])
    p.dma("sync", ident[:], identD.ap())

    esM = ExitStack()
    yk = [cx.sb(f"yk{k}", [128, CH], BF16, es=esM) for k in range(32)]
    mw = cx.sb("mwG", [128, 128], BF16, es=esM)
    gat = [cx.sb(f"gat{t}", [128, CH], BF16, es=esM) for t in range(16)]
    zt = cx.sb("ztG", [128, CH], es=esM)
    szt = cx.sb("sztG", [128, CH], BF16, es=esM)
    mrgb = cx.sb("mrgb", [128, CH], BF16, es=esM)
    t_a = cx.sb("t_aG", [128, CH], es=esM)
    for ch in range(4):
        sl = slice(CH * ch, CH * (ch + 1))
        for k in range(16):
            p.dma("sync", yk[k][:], yfD.ap()[128 * k:128 * (k + 1), sl])
            p.dma("sync", yk[16 + k][:], ybD.ap()[128 * k:128 * (k + 1), sl])
        for ot in range(16):
            pw = cx.bank()
            for k in range(32):
                p.dma("gpsimd", mw[:], mrgD.ap()[128 * k:128 * (k + 1),
                                                 128 * ot:128 * (ot + 1)])
                p.op("tensor", lambda E, k=k, pw=pw: E.matmul(
                    pw[:, 0:CH], mw[:], yk[k][:], start=(k == 0), stop=(k == 31)),
                    reads=[mw, yk[k]], writes=[pw])
            p.dma("sync", zt[:], zD.ap()[128 * ot:128 * (ot + 1), sl])
            p.op("scalar", lambda E: E.activation(szt[:], zt[:], AF.Silu),
                 reads=[zt], writes=[szt])
            p.op("scalar", lambda E, pw=pw: E.copy(mrgb[:], pw[:, 0:CH]),
                 reads=[pw], writes=[mrgb])
            p.op("vector", lambda E, ot=ot: E.tensor_tensor(gat[ot][:], mrgb[:], szt[:],
                                                            ALU.mult),
                 reads=[mrgb, szt], writes=[gat[ot]])
        for o2 in range(8):
            pw = cx.bank()
            for k in range(16):
                p.dma("gpsimd", mw[:], outwD.ap()[128 * k:128 * (k + 1),
                                                  128 * o2:128 * (o2 + 1)])
                p.op("tensor", lambda E, k=k, pw=pw: E.matmul(
                    pw[:, 0:CH], mw[:], gat[k][:], start=(k == 0), stop=(k == 15)),
                    reads=[mw, gat[k]], writes=[pw])
            p.dma("sync", zt[:], x1D.ap()[128 * o2:128 * (o2 + 1), sl])
            p.dma("sync", t_a[:], xn1D.ap()[128 * o2:128 * (o2 + 1), sl])
            p.op("vector", lambda E: E.tensor_add(t_a[:], t_a[:], zt[:]),
                 reads=[t_a, zt], writes=[t_a])
            p.op("vector", lambda E, pw=pw: E.tensor_add(t_a[:], t_a[:], pw[:, 0:CH]),
                 reads=[t_a, pw], writes=[t_a])
            p.dma("sync", o_x2p.ap()[128 * o2:128 * (o2 + 1), sl], t_a[:], reads=[t_a])
    esM.close()
    p.barrier()

    esQ = ExitStack()
    grow = cx.sb("growG", [1, 8], es=esQ)
    srow = cx.sb("srowG", [1, 1], es=esQ)
    sptr = cx.sb("sptrG", [128, 1], es=esQ)
    siptr = cx.sb("siptrG", [128, 1], es=esQ)
    p.dma("scalar", grow[:], gmaxD.ap())
    p.op("vector", lambda E: E.tensor_reduce(srow[:], grow[:], AX, ALU.max),
         reads=[grow], writes=[srow])
    p.op("vector", lambda E: E.tensor_scalar(srow[:], srow[:], 1.0 / 32767.0, 1e-12,
                                             ALU.mult, ALU.max),
         reads=[srow], writes=[srow])
    pt = cx.bank()
    p.op("tensor", lambda E, pt=pt: E.matmul(pt[:, 0:1], cx.ones1[0:1, :], srow[0:1, 0:1],
                                             start=True, stop=True),
         reads=[srow, cx.ones1], writes=[pt])
    p.op("scalar", lambda E, pt=pt: E.copy(sptr[:], pt[:, 0:1]), reads=[pt], writes=[sptr])
    p.op("vector", lambda E: E.reciprocal(siptr[:], sptr[:]), reads=[sptr], writes=[siptr])
    p.op("vector", lambda E: E.engine_nop(), reads=[sptr, siptr], writes=["qscale"])

    xqb = [cx.sb(f"xqb{k}", [128, TOK], BF16, es=esQ) for k in range(8)]
    xqt = cx.sb("xqt", [128, TOK], es=esQ)
    for k in range(8):
        p.dma("sync", xqt[:], xn1D.ap()[128 * k:128 * (k + 1), :])
        p.op("vector", lambda E: E.tensor_scalar(
            xqt[:], xqt[:], siptr[:, 0:1], MAGIC, ALU.mult, ALU.add),
            reads=[xqt, "qscale"], writes=[xqt])
        p.op("vector", lambda E: E.tensor_scalar(
            xqt[:], xqt[:], -MAGIC, sptr[:, 0:1], ALU.add, ALU.mult),
            reads=[xqt, "qscale"], writes=[xqt])
        p.op("vector", lambda E, k=k: E.tensor_copy(xqb[k][:], xqt[:]),
             reads=[xqt], writes=[xqb[k]])

    wT = [cx.sb(f"wTq{k}", [128, 3072], BF16, es=esQ) for k in range(8)]
    upt = cx.sb("uptG", [128, CH], es=esQ)
    mxa = cx.sb("mxaG", [128, 2], es=esQ)
    mrow2 = cx.sb("mrow2G", [1, 128], es=esQ)
    p.op("vector", lambda E: E.memset(mxa[:], 0.0), writes=[mxa])
    for gi, (wD, oD) in enumerate(((wupD, o_up), (wgtD, o_gate))):
        emit_lpbq_dequant(cx, wD, 3072, 1024, wT, ident, 32)
        for ot in range(24):
            for ch in range(4):
                pw = cx.bank()
                for k in range(8):
                    p.op("tensor", lambda E, k=k, ot=ot, ch=ch, pw=pw: E.matmul(
                        pw[:, 0:CH], wT[k][:, 128 * ot:128 * (ot + 1)],
                        xqb[k][:, CH * ch:CH * (ch + 1)], start=(k == 0), stop=(k == 7)),
                        reads=[wT[k], xqb[k]], writes=[pw])
                p.op("vector", lambda E, pw=pw: E.tensor_copy(upt[:], pw[:, 0:CH]),
                     reads=[pw], writes=[upt])
                p.dma("sync", oD.ap()[128 * ot:128 * (ot + 1), CH * ch:CH * (ch + 1)],
                      upt[:], reads=[upt])
                p.op("vector", lambda E: E.tensor_reduce(
                    xqt[:, 0:1], upt[:], AX, ALU.max, apply_absolute_value=True),
                    reads=[upt], writes=[xqt])
                p.op("vector", lambda E, gi=gi: E.tensor_max(
                    mxa[:, gi:gi + 1], mxa[:, gi:gi + 1], xqt[:, 0:1]),
                    reads=[xqt, mxa], writes=[mxa])
    pmx = cx.bank()
    p.op("tensor", lambda E, pmx=pmx: E.transpose(pmx[0:2, 0:128], mxa[:], ident[:]),
         reads=[mxa, ident], writes=[pmx])
    mrow22 = cx.sb("mrow22", [2, 128], es=esQ)
    p.op("vector", lambda E, pmx=pmx: E.tensor_copy(mrow22[:], pmx[0:2, 0:128]),
         reads=[pmx], writes=[mrow22])
    p.op("vector", lambda E: E.tensor_reduce(mrow22[:, 0:1], mrow22[:], AX, ALU.max),
         reads=[mrow22], writes=[mrow22])
    p.dma("sync", o_mx2.ap().rearrange("a b -> b a"), mrow22[0:2, 0:1], reads=[mrow22])
    esQ.close()
    p.finish()
    return nc


def build_launch_H():
    cx = Ctx()
    nc, p = cx.nc, cx.p
    upD = nc.dram_tensor("upD", [3072, TOK], F32, kind="ExternalInput")
    gtD = nc.dram_tensor("gtD", [3072, TOK], F32, kind="ExternalInput")
    gmaxD = nc.dram_tensor("gmaxD", [1, 16], F32, kind="ExternalInput")
    identD = nc.dram_tensor("identD", [128, 128], F32, kind="ExternalInput")
    o_upq = nc.dram_tensor("o_upq", [3072, TOK], F32, kind="ExternalOutput")
    o_t3 = nc.dram_tensor("o_t3", [3072, TOK], F32, kind="ExternalOutput")
    o_mx = nc.dram_tensor("o_mx", [1, 1], F32, kind="ExternalOutput")
    ident = cx.sb("ident", [128, 128])
    p.dma("sync", ident[:], identD.ap())
    grow = cx.sb("growH", [1, 16])
    sr = cx.sb("srH", [1, 2])
    sp_u = cx.sb("spu", [128, 2])
    si_u = cx.sb("siu", [128, 2])
    p.dma("scalar", grow[:], gmaxD.ap())
    p.op("vector", lambda E: E.tensor_reduce(
        sr[:], grow[:].rearrange("a (c q) -> a q c", q=2), AX, ALU.max),
        reads=[grow], writes=[sr])
    p.op("vector", lambda E: E.tensor_scalar(sr[:], sr[:], 1.0 / 32767.0, 1e-12,
                                             ALU.mult, ALU.max), reads=[sr], writes=[sr])
    pt = cx.bank()
    p.op("tensor", lambda E, pt=pt: E.matmul(pt[:, 0:2], cx.ones1[0:1, :], sr[0:1, :],
                                             start=True, stop=True),
         reads=[sr, cx.ones1], writes=[pt])
    p.op("scalar", lambda E, pt=pt: E.copy(sp_u[:], pt[:, 0:2]), reads=[pt], writes=[sp_u])
    p.op("vector", lambda E: E.reciprocal(si_u[:], sp_u[:]), reads=[sp_u], writes=[si_u])
    p.op("vector", lambda E: E.engine_nop(), reads=[sp_u, si_u], writes=["qscale"])

    ut = cx.sb("utH", [128, TOK])
    gt = cx.sb("gtH", [128, TOK])
    sg = cx.sb("sgH", [128, TOK])
    mxa = cx.sb("mxaH", [128, 1])
    p.op("vector", lambda E: E.memset(mxa[:], 0.0), writes=[mxa])
    for t in range(24):
        p.dma("sync", ut[:], upD.ap()[128 * t:128 * (t + 1), :])
        p.dma("sync", gt[:], gtD.ap()[128 * t:128 * (t + 1), :])
        p.op("vector", lambda E: E.tensor_scalar(ut[:], ut[:], si_u[:, 0:1], MAGIC,
                                                 ALU.mult, ALU.add),
             reads=[ut, "qscale"], writes=[ut])
        p.op("vector", lambda E: E.tensor_scalar(ut[:], ut[:], -MAGIC, sp_u[:, 0:1],
                                                 ALU.add, ALU.mult),
             reads=[ut], writes=[ut])
        p.dma("sync", o_upq.ap()[128 * t:128 * (t + 1), :], ut[:], reads=[ut])
        p.op("vector", lambda E: E.tensor_scalar(gt[:], gt[:], si_u[:, 1:2], MAGIC,
                                                 ALU.mult, ALU.add),
             reads=[gt, "qscale"], writes=[gt])
        p.op("vector", lambda E: E.tensor_scalar(gt[:], gt[:], -MAGIC, sp_u[:, 1:2],
                                                 ALU.add, ALU.mult),
             reads=[gt], writes=[gt])
        p.op("scalar", lambda E: E.activation(sg[:], gt[:], AF.Sigmoid),
             reads=[gt], writes=[sg])
        p.op("vector", lambda E: E.tensor_scalar(sg[:], sg[:], 65536.0, MAGIC,
                                                 ALU.mult, ALU.add),
             reads=[sg], writes=[sg])
        p.op("vector", lambda E: E.tensor_scalar(sg[:], sg[:], -MAGIC, 65535.0,
                                                 ALU.add, ALU.min),
             reads=[sg], writes=[sg])
        p.op("vector", lambda E: E.tensor_scalar(sg[:], sg[:], 1.0 / 65536.0, None,
                                                 ALU.mult),
             reads=[sg], writes=[sg])
        p.op("vector", lambda E: E.tensor_mul(gt[:], gt[:], sg[:]),
             reads=[gt, sg], writes=[gt])
        p.dma("sync", o_t3.ap()[128 * t:128 * (t + 1), :], gt[:], reads=[gt])
        p.op("vector", lambda E: E.tensor_reduce(sg[:, 0:1], gt[:], AX, ALU.max,
                                                 apply_absolute_value=True),
             reads=[gt], writes=[sg])
        p.op("vector", lambda E: E.tensor_max(mxa[:], mxa[:], sg[:, 0:1]),
             reads=[sg, mxa], writes=[mxa])
    pmx = cx.bank()
    p.op("tensor", lambda E, pmx=pmx: E.transpose(pmx[0:1, 0:128], mxa[:], ident[:]),
         reads=[mxa, ident], writes=[pmx])
    mr = cx.sb("mrH", [1, 128])
    p.op("vector", lambda E, pmx=pmx: E.tensor_copy(mr[:], pmx[0:1, 0:128]),
         reads=[pmx], writes=[mr])
    p.op("vector", lambda E: E.tensor_reduce(mr[:, 0:1], mr[:], AX, ALU.max),
         reads=[mr], writes=[mr])
    p.dma("sync", o_mx.ap(), mr[0:1, 0:1], reads=[mr])
    p.finish()
    return nc


def build_launch_I():
    cx = Ctx()
    nc, p = cx.nc, cx.p
    t3D = nc.dram_tensor("t3D", [3072, TOK], F32, kind="ExternalInput")
    upqD = nc.dram_tensor("upqD", [3072, TOK], F32, kind="ExternalInput")
    gmaxD = nc.dram_tensor("gmaxD", [1, 8], F32, kind="ExternalInput")
    identD = nc.dram_tensor("identD", [128, 128], F32, kind="ExternalInput")
    o_t4 = nc.dram_tensor("o_t4", [3072, TOK], F32, kind="ExternalOutput")
    o_mx = nc.dram_tensor("o_mx", [1, 1], F32, kind="ExternalOutput")
    ident = cx.sb("ident", [128, 128])
    p.dma("sync", ident[:], identD.ap())
    grow = cx.sb("growI", [1, 8])
    sr = cx.sb("srI", [1, 1])
    sp_u = cx.sb("spuI", [128, 1])
    si_u = cx.sb("siuI", [128, 1])
    p.dma("scalar", grow[:], gmaxD.ap())
    p.op("vector", lambda E: E.tensor_reduce(sr[:], grow[:], AX, ALU.max),
         reads=[grow], writes=[sr])
    p.op("vector", lambda E: E.tensor_scalar(sr[:], sr[:], 1.0 / 32767.0, 1e-12,
                                             ALU.mult, ALU.max), reads=[sr], writes=[sr])
    pt = cx.bank()
    p.op("tensor", lambda E, pt=pt: E.matmul(pt[:, 0:1], cx.ones1[0:1, :], sr[0:1, :],
                                             start=True, stop=True),
         reads=[sr, cx.ones1], writes=[pt])
    p.op("scalar", lambda E, pt=pt: E.copy(sp_u[:], pt[:, 0:1]), reads=[pt], writes=[sp_u])
    p.op("vector", lambda E: E.reciprocal(si_u[:], sp_u[:]), reads=[sp_u], writes=[si_u])
    p.op("vector", lambda E: E.engine_nop(), reads=[sp_u, si_u], writes=["qscale"])
    ut = cx.sb("utI", [128, TOK])
    gt = cx.sb("gtI", [128, TOK])
    mxa = cx.sb("mxaI", [128, 1])
    p.op("vector", lambda E: E.memset(mxa[:], 0.0), writes=[mxa])
    for t in range(24):
        p.dma("sync", gt[:], t3D.ap()[128 * t:128 * (t + 1), :])
        p.dma("sync", ut[:], upqD.ap()[128 * t:128 * (t + 1), :])
        p.op("vector", lambda E: E.tensor_scalar(gt[:], gt[:], si_u[:, 0:1], MAGIC,
                                                 ALU.mult, ALU.add),
             reads=[gt, "qscale"], writes=[gt])
        p.op("vector", lambda E: E.tensor_scalar(gt[:], gt[:], -MAGIC, sp_u[:, 0:1],
                                                 ALU.add, ALU.mult),
             reads=[gt], writes=[gt])
        p.op("vector", lambda E: E.tensor_mul(gt[:], gt[:], ut[:]),
             reads=[gt, ut], writes=[gt])
        p.dma("sync", o_t4.ap()[128 * t:128 * (t + 1), :], gt[:], reads=[gt])
        p.op("vector", lambda E: E.tensor_reduce(ut[:, 0:1], gt[:], AX, ALU.max,
                                                 apply_absolute_value=True),
             reads=[gt], writes=[ut])
        p.op("vector", lambda E: E.tensor_max(mxa[:], mxa[:], ut[:, 0:1]),
             reads=[ut, mxa], writes=[mxa])
    pmx = cx.bank()
    p.op("tensor", lambda E, pmx=pmx: E.transpose(pmx[0:1, 0:128], mxa[:], ident[:]),
         reads=[mxa, ident], writes=[pmx])
    mr = cx.sb("mrI", [1, 128])
    p.op("vector", lambda E, pmx=pmx: E.tensor_copy(mr[:], pmx[0:1, 0:128]),
         reads=[pmx], writes=[mr])
    p.op("vector", lambda E: E.tensor_reduce(mr[:, 0:1], mr[:], AX, ALU.max),
         reads=[mr], writes=[mr])
    p.dma("sync", o_mx.ap(), mr[0:1, 0:1], reads=[mr])
    p.finish()
    return nc


def build_launch_J():
    cx = Ctx()
    nc, p = cx.nc, cx.p
    CH = 512
    t4D = nc.dram_tensor("t4D", [3072, TOK], F32, kind="ExternalInput")
    x2pD = nc.dram_tensor("x2pD", [D, TOK], F32, kind="ExternalInput")
    wdD = nc.dram_tensor("wdD", [D, 3072], F32, kind="ExternalInput")
    gmaxD = nc.dram_tensor("gmaxD", [1, 8], F32, kind="ExternalInput")
    identD = nc.dram_tensor("identD", [128, 128], F32, kind="ExternalInput")
    o_out = nc.dram_tensor("o_out", [TOK, D], F32, kind="ExternalOutput")
    ident = cx.sb("ident", [128, 128])
    p.dma("sync", ident[:], identD.ap())
    grow = cx.sb("growJ", [1, 8])
    sr = cx.sb("srJ", [1, 1])
    sp_u = cx.sb("spuJ", [128, 1])
    si_u = cx.sb("siuJ", [128, 1])
    p.dma("scalar", grow[:], gmaxD.ap())
    p.op("vector", lambda E: E.tensor_reduce(sr[:], grow[:], AX, ALU.max),
         reads=[grow], writes=[sr])
    p.op("vector", lambda E: E.tensor_scalar(sr[:], sr[:], 1.0 / 32767.0, 1e-12,
                                             ALU.mult, ALU.max), reads=[sr], writes=[sr])
    pt = cx.bank()
    p.op("tensor", lambda E, pt=pt: E.matmul(pt[:, 0:1], cx.ones1[0:1, :], sr[0:1, :],
                                             start=True, stop=True),
         reads=[sr, cx.ones1], writes=[pt])
    p.op("scalar", lambda E, pt=pt: E.copy(sp_u[:], pt[:, 0:1]), reads=[pt], writes=[sp_u])
    p.op("vector", lambda E: E.reciprocal(si_u[:], sp_u[:]), reads=[sp_u], writes=[si_u])
    p.op("vector", lambda E: E.engine_nop(), reads=[sp_u, si_u], writes=["qscale"])

    wdT = [cx.sb(f"wdT{k}", [128, 1024], BF16) for k in range(24)]
    emit_lpbq_dequant(cx, wdD, 1024, 3072, wdT, ident, 96)

    oqb = [cx.sb(f"oqb{k}", [128, TOK], BF16) for k in range(24)]
    qt = cx.sb("qtJ", [128, TOK])
    for t in range(24):
        p.dma("sync", qt[:], t4D.ap()[128 * t:128 * (t + 1), :])
        p.op("vector", lambda E: E.tensor_scalar(qt[:], qt[:], si_u[:, 0:1], MAGIC,
                                                 ALU.mult, ALU.add),
             reads=[qt, "qscale"], writes=[qt])
        p.op("vector", lambda E: E.tensor_scalar(qt[:], qt[:], -MAGIC, sp_u[:, 0:1],
                                                 ALU.add, ALU.mult),
             reads=[qt], writes=[qt])
        p.op("vector", lambda E, t=t: E.tensor_copy(oqb[t][:], qt[:]),
             reads=[qt], writes=[oqb[t]])
    xf = cx.sb("xfJ", [128, CH])
    rowo = cx.sb("rowoJ", [128, D])
    for ch in range(4):
        sl = slice(CH * ch, CH * (ch + 1))
        for o2 in range(8):
            pw = cx.bank()
            for k in range(24):
                p.op("tensor", lambda E, k=k, o2=o2, sl=sl, pw=pw: E.matmul(
                    pw[:, 0:CH], wdT[k][:, 128 * o2:128 * (o2 + 1)],
                    oqb[k][:, sl], start=(k == 0), stop=(k == 23)),
                    reads=[wdT[k], oqb[k]], writes=[pw])
            p.dma("sync", xf[:], x2pD.ap()[128 * o2:128 * (o2 + 1), sl])
            p.op("vector", lambda E, pw=pw: E.tensor_add(xf[:], xf[:], pw[:, 0:CH]),
                 reads=[xf, pw], writes=[xf])
            for jj in range(4):
                cx.transpose(rowo, xf[:, 128 * jj:128 * (jj + 1)], 128, 128, ident,
                             out_off=128 * o2)
                p.dma("sync", o_out.ap()[CH * ch + 128 * jj:CH * ch + 128 * (jj + 1),
                                         128 * o2:128 * (o2 + 1)],
                      rowo[:, 128 * o2:128 * (o2 + 1)], reads=[rowo])
    p.finish()
    return nc


def host_inputs_B(ins, outA):
    x = _f32(ins["L0_attn_x"]).reshape(NT, D)
    xT = np.ascontiguousarray(x.T)
    W1T = np.ascontiguousarray(_f32(ins["L0_attn_W1"]).T)
    W2T = np.ascontiguousarray(_f32(ins["L0_attn_W2"]).T)
    g4 = np.ascontiguousarray(_f32(ins["L0_attn_ln_gamma"]).reshape(32, 128).T)
    b4 = np.ascontiguousarray(_f32(ins["L0_attn_ln_beta"]).reshape(32, 128).T)
    b1row = _f32(ins["L0_attn_b1"]).reshape(1, 2048)
    b2c = np.ascontiguousarray(_f32(ins["L0_attn_b2"]).reshape(8, 128).T)
    gb1 = np.zeros((128, 16), np.float32)
    gb1[:, 0:8] = _f32(ins["L1_attn_gamma"]).reshape(8, 128).T
    gb1[:, 8:16] = _f32(ins["L1_attn_beta"]).reshape(8, 128).T
    ident = np.eye(128, dtype=np.float32)
    maps = []
    for c in range(NCORE):
        pos = np.arange(c % 2 * TOK + 1, c % 2 * TOK + TOK + 1, dtype=np.float64)
        pinv = (1.0 / pos).astype(np.float32).reshape(1, TOK)
        if c % 2 == 1:
            phic = outA[c - 1]["o_phic"]
            cmc = outA[c - 1]["o_cmc"]
        else:
            phic = np.zeros((128, 8), np.float32)
            cmc = np.zeros((128, 16), np.float32)
        maps.append({
            "xTo": np.ascontiguousarray(xT[:, c * TOK:(c + 1) * TOK]),
            "stato": outA[c]["o_stat"], "phiD": outA[c]["o_phi"],
            "phicI": np.ascontiguousarray(phic), "cmcI": np.ascontiguousarray(cmc),
            "sxn2o": np.ascontiguousarray(outA[c]["o_rows"][2:3, c * TOK:(c + 1) * TOK]),
            "moeT": np.ascontiguousarray(outA[c]["o_moe"].T), "posinv": pinv,
            "W1TD": W1T, "g4": g4, "b4": b4, "b1r": b1row, "W2TD": W2T,
            "b2c": b2c, "gb1": gb1, "identD": ident,
        })
    return maps


def host_inputs_C(ins, outB):
    def tile16(v):
        return np.ascontiguousarray(_f32(v).reshape(16, 128).T)
    cwf = np.ascontiguousarray(_f32(ins["L1_attn_conv_w"]).reshape(16, 128, 4)
                               .transpose(1, 2, 0).reshape(128, 64))
    cwb = np.ascontiguousarray(_f32(ins["L1_attn_conv_w_b"]).reshape(16, 128, 4)
                               .transpose(1, 2, 0).reshape(128, 64))
    com = {
        "cwf": cwf, "cwb": cwb, "inwD": _f32(ins["L1_attn_in_w"]),
        "cbf": tile16(ins["L1_attn_conv_b"]), "cbb": tile16(ins["L1_attn_conv_b_b"]),
        "xpf": _f32(ins["L1_attn_xproj_w"]), "xpb": _f32(ins["L1_attn_xproj_w_b"]),
        "dtwf": _f32(ins["L1_attn_dt_w"]), "dtwb": _f32(ins["L1_attn_dt_w_b"]),
        "dtbf": tile16(ins["L1_attn_dt_b"]), "dtbb": tile16(ins["L1_attn_dt_b_b"]),
        "dskf": tile16(ins["L1_attn_D_skip"]), "dskb": tile16(ins["L1_attn_D_skip_b"]),
    }
    maps = []
    z3 = np.zeros((D, 3), np.float32)
    for c in range(NCORE):
        hl = outB[c - 1]["o_hln"][:, TOK - 3:TOK] if c % 2 == 1 else z3
        hr = outB[c + 1]["o_hln"][:, 0:3] if c % 2 == 0 else z3
        m = dict(com)
        m["hlnD"] = outB[c]["o_hln"]
        m["hllD"] = np.ascontiguousarray(hl)
        m["hlrD"] = np.ascontiguousarray(hr)
        maps.append(m)
    return maps


def _alog_layout(a):
    return np.ascontiguousarray(_f32(a).reshape(16, 128, 16).transpose(1, 0, 2)
                                .reshape(128, 256))


def host_inputs_scan(ins, outC, outD=None):
    bf = ml_dtypes.bfloat16
    maps = []
    for c in range(NCORE):
        if outD is None:
            di = 0 if c % 2 == 0 else 1
            flip = (c % 2 == 1)
            hin = np.zeros((128, 256), np.float32)
        else:
            di = 1 if c % 2 == 0 else 0
            flip = (c % 2 == 0)
            sib = c + 1 if c % 2 == 0 else c - 1
            hin = outD[sib]["o_hf"]
        dt = outC[c]["o_dt"][di]
        dtx = outC[c]["o_dtx"][di]
        y0 = outC[c]["o_y0"][di]
        bc = _f32(outC[c]["o_bc"][di])
        if flip:
            dt = dt[:, ::-1]; dtx = dtx[:, ::-1]; y0 = y0[:, ::-1]; bc = bc[:, ::-1]
        alog = ins["L1_attn_A_log"] if di == 0 else ins["L1_attn_A_log_b"]
        maps.append({
            "dtD": np.ascontiguousarray(dt), "dtxD": np.ascontiguousarray(dtx),
            "y0D": np.ascontiguousarray(y0),
            "bcflat": np.ascontiguousarray(bc.reshape(2, 16 * TOK)),
            "alogD": _alog_layout(alog),
            "hinD": np.ascontiguousarray(_f32(hin)),
        })
    return maps


def host_inputs_G(ins, outB, outC, outD, outE):
    mrg = _f32(ins["L1_attn_merge_w"])
    outw = _f32(ins["L1_attn_out_w"])
    wup = _f32(ins["L1_ffn_w_up"])
    wgt = _f32(ins["L1_ffn_w_gate"])
    gmax = np.concatenate([outB[c]["o_mx"] for c in range(NCORE)], axis=1)
    ident = np.eye(128, dtype=np.float32)
    maps = []
    for c in range(NCORE):
        if c % 2 == 0:
            yf = outD[c]["o_y"]
            yb = np.ascontiguousarray(outE[c]["o_y"][:, ::-1])
        else:
            yf = outE[c]["o_y"]
            yb = np.ascontiguousarray(outD[c]["o_y"][:, ::-1])
        maps.append({
            "yfD": yf, "ybD": yb,
            "zD": outC[c]["o_z"],
            "xn1D": outB[c]["o_xn1"], "x1D": outB[c]["o_x1"],
            "mrgD": mrg, "outwD": outw, "wupD": wup, "wgtD": wgt,
            "gmaxD": gmax, "identD": ident,
        })
    return maps


def host_inputs_H(outG):
    gmax = np.concatenate([outG[c]["o_mx2"] for c in range(NCORE)], axis=1)
    ident = np.eye(128, dtype=np.float32)
    return [{"upD": outG[c]["o_up"], "gtD": outG[c]["o_gate"],
             "gmaxD": gmax, "identD": ident} for c in range(NCORE)]


def host_inputs_I(outH):
    gmax = np.concatenate([outH[c]["o_mx"] for c in range(NCORE)], axis=1)
    ident = np.eye(128, dtype=np.float32)
    return [{"t3D": outH[c]["o_t3"], "upqD": outH[c]["o_upq"],
             "gmaxD": gmax, "identD": ident} for c in range(NCORE)]


def host_inputs_J(ins, outG, outI):
    gmax = np.concatenate([outI[c]["o_mx"] for c in range(NCORE)], axis=1)
    ident = np.eye(128, dtype=np.float32)
    wd = _f32(ins["L1_ffn_w_down"])
    return [{"t4D": outI[c]["o_t4"], "x2pD": outG[c]["o_x2p"],
             "wdD": wd, "gmaxD": gmax, "identD": ident} for c in range(NCORE)]


def kernel(**inputs):
    cores = list(range(NCORE))

    def go(name, builder, maps):
        nc = _get_nc(name, builder)
        res = run_bass_kernel_spmd(nc, maps, core_ids=cores)
        return [dict(r) for r in res.results]

    outA = go("A", build_launch_A, host_inputs_A(inputs))
    outB = go("B", build_launch_B, host_inputs_B(inputs, outA))
    outC = go("C", build_launch_C, host_inputs_C(inputs, outB))
    outD = go("S", build_launch_scan, host_inputs_scan(inputs, outC))
    outE = go("S", build_launch_scan, host_inputs_scan(inputs, outC, outD))
    outG = go("G", build_launch_G, host_inputs_G(inputs, outB, outC, outD, outE))
    outH = go("H", build_launch_H, host_inputs_H(outG))
    outI = go("I", build_launch_I, host_inputs_I(outH))
    outJ = go("J", build_launch_J, host_inputs_J(inputs, outG, outI))
    out = np.concatenate([outJ[c]["o_out"] for c in cores], axis=0)
    return np.ascontiguousarray(out.reshape(4, 4096, 1024), dtype=np.float32)


def run_A(ins):
    nc = _get_nc("A", build_launch_A)
    maps = host_inputs_A(ins)
    res = run_bass_kernel_spmd(nc, maps, core_ids=list(range(NCORE)))
    return res.results


if __name__ == "__main__":
    build_launch_A()
    print("built A ok")


# version 03
# speedup vs baseline: 1.7802x; 1.3051x over previous
import numpy as np
import ml_dtypes
from contextlib import ExitStack

import concourse.bass as bass
import concourse.mybir as mybir
from concourse.bass_utils import run_bass_kernel_spmd
from concourse.bass_core_rust import IndirectOffsetOnAxis

F32 = mybir.dt.float32
BF16 = mybir.dt.bfloat16
I32 = mybir.dt.int32
ALU = mybir.AluOpType
AF = mybir.ActivationFunctionType
AX = mybir.AxisListType.X

MAGIC = float(np.float32(1.5 * 2 ** 23))
NCORE = 8
TOK = 2048
NT = 16384
D = 1024
LN_EPS = 1e-5
PAD = 512

ENGS = ("sync", "scalar", "vector", "tensor", "gpsimd")


class Prog:
    def __init__(self, nc):
        self.nc = nc
        self.calls = {e: [] for e in ENGS}
        self.cnt = {e: 0 for e in ENGS}
        self.dma_cnt = {e: 0 for e in ENGS}
        self.sems = ExitStack()
        self.sem = {e: self.sems.enter_context(nc.semaphore(f"s_{e}")) for e in ENGS}
        self.NDSEM = 6
        self.dsem = {e: [self.sems.enter_context(nc.semaphore(f"d_{e}{i}"))
                         for i in range(self.NDSEM)] for e in ENGS}
        self.duses = {e: [0] * self.NDSEM for e in ENGS}
        self.drr = {e: 0 for e in ENGS}
        self.last_w = {}
        self.readers = {}
        self.wm = {e: {} for e in ENGS}

    @staticmethod
    def _key(x):
        if isinstance(x, str):
            return x
        t = getattr(x, "tensor", x)
        return getattr(t, "name", str(t))

    def _emit_wait(self, eng, tok):
        kind, e2, val = tok
        if kind == "E":
            sem, name = self.sem[e2], ("E", e2)
        else:
            q, i = e2
            sem, name = self.dsem[q][i], ("D", q, i)
        if self.wm[eng].get(name, 0) >= val:
            return
        self.wm[eng][name] = val
        self.calls[eng].append(lambda E, s=sem, v=val: E.wait_ge(s, v))

    def _deps(self, eng, reads, writes):
        toks = set()
        for r in reads:
            t = self.last_w.get(self._key(r))
            if t is not None:
                toks.add(t)
        for w in writes:
            k = self._key(w)
            t = self.last_w.get(k)
            if t is not None:
                toks.add(t)
            toks.update(self.readers.get(k, ()))
        for t in sorted(toks):
            self._emit_wait(eng, t)

    def _post(self, reads, writes, tok):
        for w in writes:
            k = self._key(w)
            self.last_w[k] = tok
            self.readers[k] = []
        for r in reads:
            self.readers.setdefault(self._key(r), []).append(tok)

    def op(self, eng, fn, reads=(), writes=()):
        self._deps(eng, reads, writes)
        self.cnt[eng] += 1
        tok = ("E", eng, self.cnt[eng])
        sem = self.sem[eng]
        self.calls[eng].append(lambda E, fn=fn, sem=sem: fn(E).then_inc(sem, 1))
        self._post(reads, writes, tok)
        return tok

    def _dma_slot(self, eng):
        i = self.drr[eng] % self.NDSEM
        self.drr[eng] += 1
        if self.duses[eng][i]:
            self._emit_wait(eng, ("D", (eng, i), 16 * self.duses[eng][i]))
        self.duses[eng][i] += 1
        return i, self.dsem[eng][i], 16 * self.duses[eng][i]

    def dma(self, eng, out, in_, reads=(), writes=(), **kw):
        reads = list(reads) + [in_]
        writes = list(writes) + [out]
        self._deps(eng, reads, writes)
        i, sem, val = self._dma_slot(eng)
        tok = ("D", (eng, i), val)
        self.calls[eng].append(
            lambda E, out=out, in_=in_, sem=sem, kw=kw:
            E.dma_start(out=out, in_=in_, **kw).then_inc(sem, 16))
        self._post(reads, writes, tok)
        return tok

    def dma_ind(self, out, out_off, in_, in_off, bounds, reads=(), writes=()):
        eng = "gpsimd"
        extra = [x.ap for x in (out_off, in_off) if x is not None]
        reads = list(reads) + [in_] + extra
        writes = list(writes) + [out]
        self._deps(eng, reads, writes)
        i, sem, val = self._dma_slot(eng)
        tok = ("D", (eng, i), val)
        self.calls[eng].append(
            lambda E, out=out, in_=in_, sem=sem:
            E.indirect_dma_start(out, out_off, in_, in_off, bounds_check=bounds,
                                 oob_is_err=False).then_inc(sem, 16))
        self._post(reads, writes, tok)
        return tok

    def barrier(self):
        toks = []
        for e in ENGS:
            if self.cnt[e]:
                toks.append(("E", e, self.cnt[e]))
            for i in range(self.NDSEM):
                if self.duses[e][i]:
                    toks.append(("D", (e, i), 16 * self.duses[e][i]))
        for eng in ENGS:
            for t in toks:
                if t[0] == "E" and t[1] == eng:
                    continue
                self._emit_wait(eng, t)

    def finish(self):
        nc = self.nc
        for e in ENGS:
            if e != "sync" and self.cnt[e]:
                self.calls["sync"].append(
                    lambda E, s=self.sem[e], v=self.cnt[e]: E.wait_ge(s, v))
            for i in range(self.NDSEM):
                if self.duses[e][i]:
                    self.calls["sync"].append(
                        lambda E, s=self.dsem[e][i], v=16 * self.duses[e][i]:
                        E.wait_ge(s, v))
        with self.sems, nc.Block() as block:
            @block.sync
            def _(E):
                for c in self.calls["sync"]:
                    c(E)

            @block.scalar
            def _(E):
                for c in self.calls["scalar"]:
                    c(E)

            @block.vector
            def _(E):
                for c in self.calls["vector"]:
                    c(E)

            @block.tensor
            def _(E):
                for c in self.calls["tensor"]:
                    c(E)

            @block.gpsimd
            def _(E):
                for c in self.calls["gpsimd"]:
                    c(E)


class Ctx:
    def __init__(self):
        self.nc = bass.Bass("TRN2", target_bir_lowering=False)
        self.es = ExitStack()
        self.p = Prog(self.nc)
        self.banks = [self.es.enter_context(
            self.nc.psum_tensor(f"bank{i}", [128, 512], F32)) for i in range(8)]
        self._bank_i = 0
        self.ones1 = self.sb("ones1", [1, 128])
        self.onescol = self.sb("onescol", [128, 1])
        self.onescolb = self.sb("onescolb", [128, 1], BF16)
        self.p.op("vector", lambda E: E.memset(self.ones1[:], 1.0), writes=[self.ones1])
        self.p.op("vector", lambda E: E.memset(self.onescol[:], 1.0), writes=[self.onescol])
        self.p.op("vector", lambda E: E.memset(self.onescolb[:], 1.0), writes=[self.onescolb])

    def sb(self, name, shape, dtype=F32, es=None):
        return (es or self.es).enter_context(self.nc.sbuf_tensor(name, shape, dtype))

    def bank(self):
        n = len(self.rot) if hasattr(self, "rot") and self.rot else 8
        pool = self.rot if (hasattr(self, "rot") and self.rot) else self.banks
        b = pool[self._bank_i % n]
        self._bank_i += 1
        return b

    def bcast_row(self, row_ap, out_sb, n, out_off=0):
        p = self.p
        for j in range((n + 511) // 512):
            c0, c1 = j * 512, min(n, (j + 1) * 512)
            pt = self.bank()
            p.op("tensor", lambda E, pt=pt, c0=c0, c1=c1: E.matmul(
                pt[:, 0:c1 - c0], self.ones1[0:1, :], row_ap[0:1, c0:c1],
                start=True, stop=True), reads=[row_ap, self.ones1], writes=[pt])
            p.op("scalar", lambda E, pt=pt, c0=c0, c1=c1:
                 E.copy(out_sb[:, out_off + c0:out_off + c1], pt[:, 0:c1 - c0]),
                 reads=[pt], writes=[out_sb])

    def transpose(self, out_sb, in_ap, k, m, ident, out_off=0):
        p = self.p
        pt = self.bank()
        p.op("tensor", lambda E, pt=pt: E.transpose(pt[0:m, 0:k], in_ap, ident[0:k, 0:k]),
             reads=[in_ap, ident], writes=[pt])
        p.op("scalar", lambda E, pt=pt:
             E.copy(out_sb[0:m, out_off:out_off + k], pt[0:m, 0:k]),
             reads=[pt], writes=[out_sb])


def trig_consts(cx, es):
    c = {}
    c["pio2"] = cx.sb("c_pio2", [128, 1], es=es)
    c["neg1"] = cx.sb("c_neg1", [128, 1], es=es)
    cx.p.op("vector", lambda E: E.memset(c["pio2"][:], float(np.pi / 2)), writes=[c["pio2"]])
    cx.p.op("vector", lambda E: E.memset(c["neg1"][:], -1.0), writes=[c["neg1"]])
    return c


def trig_block(cx, phi, r, sin_o, cos_o, c):
    p = cx.p
    inv2pi = float(np.float32(1.0 / (2 * np.pi)))
    twopi = float(np.float32(2 * np.pi))
    p.op("vector", lambda E: E.tensor_scalar(r[:], phi[:], inv2pi, MAGIC, ALU.mult, ALU.add),
         reads=[phi], writes=[r])
    p.op("vector", lambda E: E.tensor_scalar(r[:], r[:], -MAGIC, -twopi, ALU.add, ALU.mult),
         reads=[r], writes=[r])
    p.op("vector", lambda E: E.tensor_add(r[:], r[:], phi[:]), reads=[r, phi], writes=[r])
    p.op("scalar", lambda E: E.activation(sin_o[:], r[:], AF.Sin), reads=[r], writes=[sin_o])
    p.op("scalar", lambda E: E.activation(cos_o[:], r[:], AF.Abs), reads=[r], writes=[cos_o])
    p.op("scalar", lambda E: E.activation(cos_o[:], cos_o[:], AF.Sin,
                                          bias=c["pio2"][:], scale=c["neg1"][:]),
         reads=[cos_o, c["pio2"], c["neg1"]], writes=[cos_o])


def build_launch_A():
    cx = Ctx()
    nc, p = cx.nc, cx.p

    xT = nc.dram_tensor("xT", [D, NT], F32, kind="ExternalInput")
    xTo = nc.dram_tensor("xTo", [D, TOK], F32, kind="ExternalInput")
    Wr16 = nc.dram_tensor("Wr16", [D, 16], F32, kind="ExternalInput")
    WomT = nc.dram_tensor("WomT", [D, D], F32, kind="ExternalInput")
    b_om = nc.dram_tensor("b_om", [128, 8], F32, kind="ExternalInput")
    lsc = nc.dram_tensor("lsc", [128, 8], F32, kind="ExternalInput")
    possc = nc.dram_tensor("possc", [1, TOK], F32, kind="ExternalInput")
    moeW1 = nc.dram_tensor("moeW1", [8, D, 4096], F32, kind="ExternalInput")
    moeW2 = nc.dram_tensor("moeW2", [8, 4096, D], F32, kind="ExternalInput")
    e9iota = nc.dram_tensor("e9iota", [1, 1152], F32, kind="ExternalInput")
    cmask = nc.dram_tensor("cmask", [1, 8], F32, kind="ExternalInput")
    bidx = nc.dram_tensor("bidx", [16, 3], F32, kind="ExternalInput")
    ridx = nc.dram_tensor("ridx", [2, 1], F32, kind="ExternalInput")
    e8colD = nc.dram_tensor("e8colD", [8, 1], F32, kind="ExternalInput")
    identD = nc.dram_tensor("identD", [128, 128], F32, kind="ExternalInput")

    o_phi = nc.dram_tensor("o_phi", [D, TOK], F32, kind="ExternalOutput")
    o_phic = nc.dram_tensor("o_phic", [128, 8], F32, kind="ExternalOutput")
    o_moe = nc.dram_tensor("o_moe", [TOK, D], F32, kind="ExternalOutput")
    o_stat = nc.dram_tensor("o_stat", [2, TOK], F32, kind="ExternalOutput")
    o_rows = nc.dram_tensor("o_rows", [3, NT], F32, kind="ExternalOutput")
    o_cmc = nc.dram_tensor("o_cmc", [128, 16], F32, kind="ExternalOutput")
    scr_row = nc.dram_tensor("scr_row", [5, NT], F32, kind="Internal")
    scr_xe = nc.dram_tensor("scr_xe", [8 * PAD, D], F32, kind="Internal")
    scr_ye = nc.dram_tensor("scr_ye", [8 * PAD, D], F32, kind="Internal")
    scr_b = nc.dram_tensor("scr_b", [8, 1], F32, kind="Internal")

    ident = cx.sb("ident", [128, 128])
    p.dma("sync", ident[:], identD.ap())
    rowview = scr_row.ap().rearrange("r (b j) -> (r b) j", j=128)
    rowview8 = scr_row.ap().rearrange("r (c t) -> (r c) t", t=TOK)

    eT = cx.sb("eT", [128, 16])
    pT = cx.sb("pT", [128, 16])
    sidx_i = cx.sb("sidx_i", [128, 16], I32)
    gidx_i = cx.sb("gidx_i", [128, 16], I32)
    baserow = cx.sb("baserow", [1, 8])
    xn0 = [cx.sb(f"xn0_{k}", [128, TOK]) for k in range(8)]

    esR = ExitStack()
    xt = [cx.sb(f"xt{k}", [128, 512], es=esR) for k in range(8)]
    sq2 = [cx.sb(f"sq{k}", [128, 512], es=esR) for k in range(2)]
    wr16 = cx.sb("wr16", [128, 16 * 8], es=esR)
    logit = cx.sb("logit", [128, 1152], es=esR)
    rb1 = cx.sb("rb1", [1, 512], es=esR)
    rb2 = cx.sb("rb2", [1, 512], es=esR)
    for k in range(8):
        p.dma("scalar", wr16[:, 16 * k:16 * (k + 1)], Wr16.ap()[128 * k:128 * (k + 1), :])
    for ch in range(32):
        for k in range(8):
            p.dma("sync", xt[k][:], xT.ap()[128 * k:128 * (k + 1), 512 * ch:512 * (ch + 1)])
        pS, pQ = cx.bank(), cx.bank()
        for k in range(8):
            sq = sq2[k % 2]
            p.op("scalar", lambda E, k=k, sq=sq: E.activation(sq[:], xt[k][:], AF.Square),
                 reads=[xt[k]], writes=[sq])
            p.op("tensor", lambda E, k=k, pS=pS: E.matmul(
                pS[0:1, :], cx.onescol[:], xt[k][:], start=(k == 0), stop=(k == 7)),
                reads=[xt[k], cx.onescol], writes=[pS])
            p.op("tensor", lambda E, sq=sq, k=k, pQ=pQ: E.matmul(
                pQ[0:1, :], cx.onescol[:], sq[:], start=(k == 0), stop=(k == 7)),
                reads=[sq, cx.onescol], writes=[pQ])
        p.op("vector", lambda E, pS=pS: E.tensor_copy(rb1[:], pS[0:1, :]),
             reads=[pS], writes=[rb1])
        p.op("vector", lambda E, pQ=pQ: E.tensor_copy(rb2[:], pQ[0:1, :]),
             reads=[pQ], writes=[rb2])
        p.dma("sync", scr_row.ap()[3:4, 512 * ch:512 * (ch + 1)], rb1[:])
        p.dma("sync", scr_row.ap()[4:5, 512 * ch:512 * (ch + 1)], rb2[:])
        for w in range(4):
            pL = cx.bank()
            for k in range(8):
                p.op("tensor", lambda E, k=k, w=w, pL=pL: E.matmul(
                    pL[:, 0:16], xt[k][:, 128 * w:128 * (w + 1)],
                    wr16[:, 16 * k:16 * (k + 1)], start=(k == 0), stop=(k == 7)),
                    reads=[xt[k], wr16], writes=[pL])
            W = ch * 4 + w
            p.op("scalar", lambda E, W=W, pL=pL: E.copy(logit[:, 9 * W:9 * W + 9], pL[:, 0:9]),
                 reads=[pL], writes=[logit])

    muw = cx.sb("muw", [128, 128], es=esR)
    rsw = cx.sb("rsw", [128, 128], es=esR)
    vtmp = cx.sb("vtmp", [128, 128], es=esR)
    blk = cx.sb("blk", [128, 128], es=esR)
    p.dma("sync", blk[:], rowview[3 * 128:4 * 128, :])
    cx.transpose(muw, blk[:], 128, 128, ident)
    p.dma("sync", blk[:], rowview[4 * 128:5 * 128, :])
    cx.transpose(vtmp, blk[:], 128, 128, ident)
    p.op("vector", lambda E: E.tensor_scalar(muw[:], muw[:], 1.0 / D, None, ALU.mult),
         reads=[muw], writes=[muw])
    p.op("vector", lambda E: E.tensor_scalar(vtmp[:], vtmp[:], 1.0 / D, None, ALU.mult),
         reads=[vtmp], writes=[vtmp])
    p.op("scalar", lambda E: E.activation(rsw[:], muw[:], AF.Square),
         reads=[muw], writes=[rsw])
    p.op("vector", lambda E: E.tensor_sub(vtmp[:], vtmp[:], rsw[:]),
         reads=[vtmp, rsw], writes=[vtmp])
    p.op("vector", lambda E: E.tensor_scalar(vtmp[:], vtmp[:], float(LN_EPS), None, ALU.add),
         reads=[vtmp], writes=[vtmp])
    varw = cx.sb("varw", [128, 128], es=esR)
    p.op("vector", lambda E: E.tensor_scalar(varw[:], vtmp[:], -float(LN_EPS), None, ALU.add),
         reads=[vtmp], writes=[varw])
    p.op("scalar", lambda E: E.activation(vtmp[:], vtmp[:], AF.Sqrt),
         reads=[vtmp], writes=[vtmp])
    p.op("vector", lambda E: E.reciprocal(rsw[:], vtmp[:]), reads=[vtmp], writes=[rsw])
    sxn2w = cx.sb("sxn2w", [128, 128], es=esR)
    p.op("scalar", lambda E: E.activation(sxn2w[:], rsw[:], AF.Square),
         reads=[rsw], writes=[sxn2w])
    p.op("vector", lambda E: E.tensor_mul(sxn2w[:], sxn2w[:], varw[:]),
         reads=[sxn2w, varw], writes=[sxn2w])
    p.op("vector", lambda E: E.tensor_scalar(sxn2w[:], sxn2w[:], float(D), None, ALU.mult),
         reads=[sxn2w], writes=[sxn2w])

    cw = cx.sb("cw", [1, 16], es=esR)
    cwrep = cx.sb("cwrep", [128, 16], es=esR)
    t1 = cx.sb("t1", [128, 1152], es=esR)
    pC = cx.bank()
    for k in range(8):
        p.op("tensor", lambda E, k=k, pC=pC: E.matmul(
            pC[0:1, 0:16], cx.onescol[:], wr16[:, 16 * k:16 * (k + 1)],
            start=(k == 0), stop=(k == 7)), reads=[wr16, cx.onescol], writes=[pC])
    p.op("vector", lambda E, pC=pC: E.tensor_copy(cw[:], pC[0:1, 0:16]),
         reads=[pC], writes=[cw])
    cx.bcast_row(cw, cwrep, 16)

    def we(x):
        return x.rearrange("p w -> p w ()").broadcast_to((128, 128, 9))

    L3 = logit[:].rearrange("p (w e) -> p w e", e=9)
    T3 = t1[:].rearrange("p (w e) -> p w e", e=9)
    p.op("vector", lambda E: E.tensor_tensor(
        t1[:], we(muw[:]),
        cwrep[:, 0:9].rearrange("p e -> p () e").broadcast_to((128, 128, 9)), ALU.mult),
        reads=[muw, cwrep], writes=[t1])
    p.op("vector", lambda E: E.tensor_sub(logit[:], logit[:], t1[:]),
         reads=[logit, t1], writes=[logit])
    p.op("vector", lambda E: E.tensor_tensor(logit[:], logit[:], we(rsw[:]), ALU.mult),
         reads=[logit, rsw], writes=[logit])

    pmx = cx.sb("pmx", [128, 128], es=esR)
    ptk = cx.sb("ptk", [128, 128], es=esR)
    eix = cx.sb("eix", [128, 128], es=esR)
    e9 = cx.sb("e9", [128, 1152], es=esR)
    e9row = cx.sb("e9row", [1, 1152], es=esR)
    p.dma("scalar", e9row[:], e9iota.ap())
    cx.bcast_row(e9row, e9, 1152)
    p.op("vector", lambda E: E.tensor_reduce(pmx[:], L3[:, :, 0:8], AX, ALU.max),
         reads=[logit], writes=[pmx])
    p.op("vector", lambda E: E.tensor_tensor(t1[:], logit[:], we(pmx[:]), ALU.subtract),
         reads=[logit, pmx], writes=[t1])
    p.op("scalar", lambda E: E.activation(t1[:], t1[:], AF.Exp), reads=[t1], writes=[t1])
    p.op("vector", lambda E: E.tensor_reduce(ptk[:], T3[:, :, 0:8], AX, ALU.add),
         reads=[t1], writes=[ptk])
    p.op("vector", lambda E: E.reciprocal(ptk[:], ptk[:]), reads=[ptk], writes=[ptk])
    p.op("vector", lambda E: E.tensor_tensor(t1[:], logit[:], we(pmx[:]), ALU.is_ge),
         reads=[logit, pmx], writes=[t1])
    p.op("vector", lambda E: E.tensor_mul(t1[:], t1[:], e9[:]), reads=[t1, e9], writes=[t1])
    p.op("vector", lambda E: E.tensor_reduce(eix[:], T3[:, :, 0:8], AX, ALU.add),
         reads=[t1], writes=[eix])

    for rid, t_src in ((0, eix), (1, ptk), (3, muw), (4, rsw)):
        cx.transpose(blk, t_src[:], 128, 128, ident)
        p.dma("sync", rowview[rid * 128:(rid + 1) * 128, :], blk[:])
        if rid == 3:
            p.dma("sync", o_rows.ap()[0:1, :].rearrange("a (w q) -> (a w) q", q=128), blk[:])
        if rid == 4:
            p.dma("sync", o_rows.ap()[1:2, :].rearrange("a (w q) -> (a w) q", q=128), blk[:])
    cx.transpose(blk, sxn2w[:], 128, 128, ident)
    p.dma("sync", o_rows.ap()[2:3, :].rearrange("a (w q) -> (a w) q", q=128), blk[:])

    SEG = 4096
    erow = cx.sb("erow", [1, SEG], es=esR)
    e8 = cx.sb("e8", [8, SEG], es=esR)
    oh = cx.sb("oh", [8, SEG], es=esR)
    rk8 = cx.sb("rk8", [8, SEG], es=esR)
    e8col = cx.sb("e8col", [8, 1], es=esR)
    ones8 = cx.sb("ones8", [8, 1], es=esR)
    carry8 = cx.sb("carry8", [8, 1], es=esR)
    rrow = cx.sb("rrow", [1, SEG], es=esR)
    base8 = cx.sb("base8", [8, 8], es=esR)
    p.op("vector", lambda E: E.memset(ones8[:], 1.0), writes=[ones8])
    p.op("vector", lambda E: E.memset(carry8[:], 0.0), writes=[carry8])
    p.dma("scalar", e8col[:], e8colD.ap())
    for sg in range(NT // SEG):
        p.dma("scalar", erow[:], scr_row.ap()[0:1, SEG * sg:SEG * (sg + 1)])
        for j in range(SEG // 512):
            pt = cx.bank()
            p.op("tensor", lambda E, j=j, pt=pt: E.matmul(
                pt[0:8, :], cx.ones1[0:1, 0:8], erow[0:1, 512 * j:512 * (j + 1)],
                start=True, stop=True), reads=[erow, cx.ones1], writes=[pt])
            p.op("scalar", lambda E, j=j, pt=pt: E.copy(e8[:, 512 * j:512 * (j + 1)], pt[0:8, :]),
                 reads=[pt], writes=[e8])
        p.op("vector", lambda E: E.tensor_scalar(oh[:], e8[:], e8col[:], None, ALU.is_equal),
             reads=[e8, e8col], writes=[oh])
        p.op("vector", lambda E: E.tensor_tensor_scan(
            rk8[:], ones8[:].broadcast_to((8, SEG)), oh[:], carry8[:], ALU.mult, ALU.add),
            reads=[oh, ones8, carry8], writes=[rk8])
        p.op("vector", lambda E: E.tensor_copy(carry8[:], rk8[:, SEG - 1:SEG]),
             reads=[rk8], writes=[carry8])
        p.op("vector", lambda E: E.tensor_sub(rk8[:], rk8[:], oh[:]),
             reads=[rk8, oh], writes=[rk8])
        p.op("vector", lambda E, sg=sg: E.tensor_copy(base8[:, 2 * sg:2 * sg + 1], rk8[:, 0:1]),
             reads=[rk8], writes=[base8])
        p.op("vector", lambda E, sg=sg: E.tensor_copy(
            base8[:, 2 * sg + 1:2 * sg + 2], rk8[:, TOK:TOK + 1]),
            reads=[rk8], writes=[base8])
        p.op("vector", lambda E: E.tensor_mul(oh[:], oh[:], rk8[:]), reads=[oh, rk8], writes=[oh])
        for j in range(SEG // 512):
            pt = cx.bank()
            p.op("tensor", lambda E, j=j, pt=pt: E.matmul(
                pt[0:1, :], ones8[:], oh[:, 512 * j:512 * (j + 1)],
                start=True, stop=True), reads=[oh, ones8], writes=[pt])
            p.op("vector", lambda E, j=j, pt=pt: E.tensor_copy(
                rrow[0:1, 512 * j:512 * (j + 1)], pt[0:1, :]), reads=[pt], writes=[rrow])
        p.dma("sync", scr_row.ap()[2:3, SEG * sg:SEG * (sg + 1)], rrow[:])

    cmk = cx.sb("cmk", [1, 8], es=esR)
    cmk8 = cx.sb("cmk8", [8, 8], es=esR)
    bsel = cx.sb("bsel", [8, 1], es=esR)
    p.dma("scalar", cmk[:], cmask.ap())
    pt = cx.bank()
    p.op("tensor", lambda E, pt=pt: E.matmul(pt[0:8, 0:8], cx.ones1[0:1, 0:8], cmk[0:1, :],
                                             start=True, stop=True),
         reads=[cmk, cx.ones1], writes=[pt])
    p.op("scalar", lambda E, pt=pt: E.copy(cmk8[:], pt[0:8, 0:8]), reads=[pt], writes=[cmk8])
    p.op("vector", lambda E: E.tensor_mul(cmk8[:], cmk8[:], base8[:]),
         reads=[cmk8, base8], writes=[cmk8])
    p.op("vector", lambda E: E.tensor_reduce(bsel[:], cmk8[:], AX, ALU.add),
         reads=[cmk8], writes=[bsel])
    p.dma("sync", scr_b.ap(), bsel[:])
    p.dma("sync", baserow[:], scr_b.ap().rearrange("a b -> b a"))
    esR.close()
    p.barrier()

    esM = ExitStack()
    own3 = [cx.sb(f"own{i}", [16, 128], es=esM) for i in range(3)]
    bidx_sb = cx.sb("bidx_sb", [16, 3], es=esM)
    bidx_i = cx.sb("bidx_i", [16, 3], I32, es=esM)
    p.dma("scalar", bidx_sb[:], bidx.ap())
    p.op("gpsimd", lambda E: E.tensor_copy(bidx_i[:], bidx_sb[:]),
         reads=[bidx_sb], writes=[bidx_i])
    rT = cx.sb("rT", [128, 16], es=esM)
    for i, dst in ((0, eT), (1, pT), (2, rT)):
        p.dma_ind(own3[i][:], None, rowview, IndirectOffsetOnAxis(bidx_i[:, i:i + 1], 0), 5 * 128 - 1)
        cx.transpose(dst, own3[i][:], 16, 128, ident)

    baserep = cx.sb("baserep", [128, 8], es=esM)
    cx.bcast_row(baserow, baserep, 8)
    msk = cx.sb("msk", [128, 16], es=esM)
    btok = cx.sb("btok", [128, 16], es=esM)
    loc = cx.sb("loc", [128, 16], es=esM)
    sidx_f = cx.sb("sidx_f", [128, 16], es=esM)
    gidx_f = cx.sb("gidx_f", [128, 16], es=esM)
    p.op("vector", lambda E: E.memset(btok[:], 0.0), writes=[btok])
    for e in range(8):
        p.op("vector", lambda E, e=e: E.tensor_scalar(msk[:], eT[:], float(e), None, ALU.is_equal),
             reads=[eT], writes=[msk])
        p.op("vector", lambda E, e=e: E.scalar_tensor_tensor(
            btok[:], msk[:], baserep[:, e:e + 1], btok[:], ALU.mult, ALU.add),
            reads=[msk, baserep, btok], writes=[btok])
    p.op("vector", lambda E: E.tensor_sub(loc[:], rT[:], btok[:]), reads=[rT, btok], writes=[loc])
    p.op("vector", lambda E: E.tensor_scalar(sidx_f[:], eT[:], float(PAD), None, ALU.mult),
         reads=[eT], writes=[sidx_f])
    p.op("vector", lambda E: E.tensor_add(sidx_f[:], sidx_f[:], loc[:]),
         reads=[sidx_f, loc], writes=[sidx_f])
    p.op("vector", lambda E: E.tensor_scalar(msk[:], loc[:], float(PAD), None, ALU.is_ge),
         reads=[loc], writes=[msk])
    p.op("vector", lambda E: E.scalar_tensor_tensor(
        sidx_f[:], msk[:], 1.0e6, sidx_f[:], ALU.mult, ALU.add),
        reads=[msk, sidx_f], writes=[sidx_f])
    p.op("vector", lambda E: E.tensor_scalar(msk[:], rT[:], 2048.0, None, ALU.is_ge),
         reads=[rT], writes=[msk])
    p.op("vector", lambda E: E.scalar_tensor_tensor(
        gidx_f[:], msk[:], 1.0e6, sidx_f[:], ALU.mult, ALU.add),
        reads=[msk, sidx_f], writes=[gidx_f])
    p.op("gpsimd", lambda E: E.tensor_copy(sidx_i[:], sidx_f[:]), reads=[sidx_f], writes=[sidx_i])
    p.op("gpsimd", lambda E: E.tensor_copy(gidx_i[:], gidx_f[:]), reads=[gidx_f], writes=[gidx_i])

    ridx_sb = cx.sb("ridx_sb", [2, 1], es=esM)
    ridx_i = cx.sb("ridx_i", [2, 1], I32, es=esM)
    mu_own = cx.sb("mu_own", [1, TOK], es=esM)
    rs_own = cx.sb("rs_own", [1, TOK], es=esM)
    mu_rep = cx.sb("mu_rep", [128, TOK], es=esM)
    rs_rep = cx.sb("rs_rep", [128, TOK], es=esM)
    p.dma("scalar", ridx_sb[:], ridx.ap())
    p.op("gpsimd", lambda E: E.tensor_copy(ridx_i[:], ridx_sb[:]), reads=[ridx_sb], writes=[ridx_i])
    stat2 = cx.sb("stat2", [2, TOK], es=esM)
    p.dma_ind(stat2[:], None, rowview8, IndirectOffsetOnAxis(ridx_i[:], 0), 5 * 8 - 1)
    p.dma("sync", o_stat.ap(), stat2[:])
    p.dma("scalar", mu_own[:], stat2[0:1, :], reads=[stat2])
    p.dma("scalar", rs_own[:], stat2[1:2, :], reads=[stat2])
    cx.bcast_row(mu_own, mu_rep, TOK)
    cx.bcast_row(rs_own, rs_rep, TOK)
    for k in range(8):
        p.dma("sync", xn0[k][:], xTo.ap()[128 * k:128 * (k + 1), :])
        p.op("vector", lambda E, k=k: E.tensor_sub(xn0[k][:], xn0[k][:], mu_rep[:]),
             reads=[xn0[k], mu_rep], writes=[xn0[k]])
        p.op("vector", lambda E, k=k: E.tensor_mul(xn0[k][:], xn0[k][:], rs_rep[:]),
             reads=[xn0[k], rs_rep], writes=[xn0[k]])
    esM.close()
    p.barrier()

    esS = ExitStack()
    zero_t = cx.sb("zero_t", [128, D], es=esS)
    rowt = cx.sb("rowt", [128, D], es=esS)
    p.op("vector", lambda E: E.memset(zero_t[:], 0.0), writes=[zero_t])
    for j in range(32):
        p.dma("scalar", scr_xe.ap()[128 * j:128 * (j + 1), :], zero_t[:])
    for j in range(16):
        for k in range(8):
            cx.transpose(rowt, xn0[k][:, 128 * j:128 * (j + 1)], 128, 128, ident,
                         out_off=128 * k)
        p.dma_ind(scr_xe.ap(), IndirectOffsetOnAxis(sidx_i[:, j:j + 1], 0),
                  rowt[:], None, 8 * PAD - 1)
    esS.close()
    p.barrier()

    esO = ExitStack()
    womk = [cx.sb(f"womk{k}", [128, D], es=esO) for k in range(8)]
    b_om_t = cx.sb("b_om_t", [128, 8], es=esO)
    lsce = cx.sb("lsce", [128, 8], es=esO)
    possr = cx.sb("possr", [1, TOK], es=esO)
    possc_r = cx.sb("possc_r", [128, TOK], es=esO)
    onesT = cx.sb("onesT", [128, TOK], es=esO)
    om_t = cx.sb("om_t", [128, TOK], es=esO)
    phi_t = cx.sb("phi_t", [128, TOK], es=esO)
    phic_t = cx.sb("phic_t", [128, 8], es=esO)
    cmc_t = cx.sb("cmc_t", [128, 16], es=esO)
    trg1 = cx.sb("trg1", [128, TOK], es=esO)
    trg2 = cx.sb("trg2", [128, TOK], es=esO)
    consts = trig_consts(cx, esO)
    p.dma("scalar", b_om_t[:], b_om.ap())
    p.dma("scalar", lsce[:], lsc.ap())
    p.op("scalar", lambda E: E.activation(lsce[:], lsce[:], AF.Exp), reads=[lsce], writes=[lsce])
    p.dma("scalar", possr[:], possc.ap())
    cx.bcast_row(possr, possc_r, TOK)
    p.op("vector", lambda E: E.memset(onesT[:], 1.0), writes=[onesT])
    for k in range(8):
        p.dma("sync", womk[k][:], WomT.ap()[128 * k:128 * (k + 1), :])
    for et in range(8):
        for ch in range(4):
            pt = cx.bank()
            for k in range(8):
                p.op("tensor", lambda E, k=k, et=et, ch=ch, pt=pt: E.matmul(
                    pt[:], womk[k][:, 128 * et:128 * (et + 1)],
                    xn0[k][:, 512 * ch:512 * (ch + 1)], start=(k == 0), stop=(k == 7)),
                    reads=[womk[k], xn0[k]], writes=[pt])
            p.op("vector", lambda E, et=et, ch=ch, pt=pt: E.tensor_scalar(
                om_t[:, 512 * ch:512 * (ch + 1)], pt[:], b_om_t[:, et:et + 1],
                lsce[:, et:et + 1], ALU.add, ALU.mult),
                reads=[pt, b_om_t, lsce], writes=[om_t])
        p.op("vector", lambda E, et=et: E.tensor_mul(om_t[:], om_t[:], possc_r[:]),
             reads=[om_t, possc_r], writes=[om_t])
        p.op("vector", lambda E, et=et: E.tensor_tensor_scan(
            phi_t[:], onesT[:], om_t[:], 0.0, ALU.mult, ALU.add),
            reads=[om_t, onesT], writes=[phi_t])
        p.op("vector", lambda E, et=et: E.tensor_copy(phic_t[:, et:et + 1], phi_t[:, TOK - 1:TOK]),
             reads=[phi_t], writes=[phic_t])
        p.dma("sync", o_phi.ap()[128 * et:128 * (et + 1), :], phi_t[:])
        trig_block(cx, phi_t, om_t, trg1, trg2, consts)
        p.op("vector", lambda E, et=et: E.tensor_mul(trg2[:], trg2[:], xn0[et][:]),
             reads=[trg2, xn0[et]], writes=[trg2])
        p.op("vector", lambda E, et=et: E.tensor_mul(trg1[:], trg1[:], xn0[et][:]),
             reads=[trg1, xn0[et]], writes=[trg1])
        p.op("vector", lambda E, et=et: E.tensor_reduce(
            cmc_t[:, et:et + 1], trg2[:], AX, ALU.add), reads=[trg2], writes=[cmc_t])
        p.op("vector", lambda E, et=et: E.tensor_reduce(
            cmc_t[:, 8 + et:8 + et + 1], trg1[:], AX, ALU.add), reads=[trg1], writes=[cmc_t])
    p.dma("sync", o_phic.ap(), phic_t[:])
    p.dma("sync", o_cmc.ap(), cmc_t[:])
    esO.close()
    p.barrier()

    esE = ExitStack()
    mw = [cx.sb(f"mw{k}", [128, 1024], es=esE) for k in range(8)]
    w2b = [cx.sb(f"w2b{k}", [128, 1024], BF16, es=esE) for k in range(2)]
    mxeT = [cx.sb(f"mxeT{k}", [128, PAD], es=esE) for k in range(8)]
    mxb = [cx.sb(f"mxb{k}", [128, PAD], BF16, es=esE) for k in range(8)]
    mh = [cx.sb(f"mh{f}", [128, PAD], BF16, es=esE) for f in range(32)]
    myeo = [cx.sb(f"myeo{o}", [128, PAD], es=esE) for o in range(8)]
    rowe = cx.sb("rowe", [128, D], es=esE)
    for e in range(8):
        for j in range(4):
            p.dma("sync", rowe[:], scr_xe.ap()[e * PAD + 128 * j:e * PAD + 128 * (j + 1), :])
            for k in range(8):
                cx.transpose(mxeT[k], rowe[:, 128 * k:128 * (k + 1)], 128, 128, ident,
                             out_off=128 * j)
        for k in range(8):
            p.op("vector", lambda E, k=k: E.tensor_copy(mxb[k][:], mxeT[k][:]),
                 reads=[mxeT[k]], writes=[mxb[k]])
        for fb in range(4):
            for k in range(8):
                p.dma("gpsimd", mw[k][:],
                      moeW1.ap()[e, 128 * k:128 * (k + 1), 1024 * fb:1024 * (fb + 1)])
            for k in range(8):
                for fi in range(8):
                    bk = cx.banks[fi]
                    p.op("tensor", lambda E, k=k, fi=fi, bk=bk: E.matmul(
                        bk[:], mw[k][:, 128 * fi:128 * (fi + 1)], mxeT[k][:],
                        start=(k == 0), stop=(k == 7)),
                        reads=[mw[k], mxeT[k]], writes=[bk])
            for fi in range(8):
                f = fb * 8 + fi
                bk = cx.banks[fi]
                p.op("scalar", lambda E, f=f, bk=bk: E.activation(mh[f][:], bk[:], AF.Relu),
                     reads=[bk], writes=[mh[f]])
        for f in range(32):
            p.dma("gpsimd", w2b[f % 2][:], moeW2.ap()[e, 128 * f:128 * (f + 1), :])
            for oi in range(8):
                bk = cx.banks[oi]
                p.op("tensor", lambda E, f=f, oi=oi, bk=bk: E.matmul(
                    bk[:], w2b[f % 2][:, 128 * oi:128 * (oi + 1)], mh[f][:],
                    start=(f == 0), stop=(f == 31)),
                    reads=[w2b[f % 2], mh[f]], writes=[bk])
        for oi in range(8):
            bk = cx.banks[oi]
            p.op("vector", lambda E, oi=oi, bk=bk: E.tensor_copy(myeo[oi][:], bk[:]),
                 reads=[bk], writes=[myeo[oi]])
        for cb in range(4):
            for oi in range(8):
                cx.transpose(rowe, myeo[oi][:, 128 * cb:128 * (cb + 1)], 128, 128, ident,
                             out_off=128 * oi)
            p.dma("sync", scr_ye.ap()[e * PAD + 128 * cb:e * PAD + 128 * (cb + 1), :], rowe[:])
    esE.close()
    p.barrier()

    esC = ExitStack()
    rowc = cx.sb("rowc", [128, D], es=esC)
    for j in range(16):
        p.op("vector", lambda E: E.memset(rowc[:], 0.0), reads=[], writes=[rowc])
        p.dma_ind(rowc[:], None, scr_ye.ap(), IndirectOffsetOnAxis(gidx_i[:, j:j + 1], 0),
                  8 * PAD - 1)
        p.op("vector", lambda E, j=j: E.tensor_scalar(
            rowc[:], rowc[:], pT[:, j:j + 1], None, ALU.mult),
            reads=[rowc, pT], writes=[rowc])
        p.dma("sync", o_moe.ap()[128 * j:128 * (j + 1), :], rowc[:])
    esC.close()
    p.barrier()

    p.finish()
    return nc



def build_launch_B():
    cx = Ctx()
    nc, p = cx.nc, cx.p
    CH = 512
    NCH = TOK // CH

    xTo = nc.dram_tensor("xTo", [D, TOK], F32, kind="ExternalInput")
    stato = nc.dram_tensor("stato", [2, TOK], F32, kind="ExternalInput")
    phiD = nc.dram_tensor("phiD", [D, TOK], F32, kind="ExternalInput")
    phicI = nc.dram_tensor("phicI", [128, 8], F32, kind="ExternalInput")
    cmcI = nc.dram_tensor("cmcI", [128, 16], F32, kind="ExternalInput")
    sxn2o = nc.dram_tensor("sxn2o", [1, TOK], F32, kind="ExternalInput")
    moeT = nc.dram_tensor("moeT", [D, TOK], F32, kind="ExternalInput")
    posinv = nc.dram_tensor("posinv", [1, TOK], F32, kind="ExternalInput")
    W1TD = nc.dram_tensor("W1TD", [4096, 2048], F32, kind="ExternalInput")
    g4 = nc.dram_tensor("g4", [128, 32], F32, kind="ExternalInput")
    b4 = nc.dram_tensor("b4", [128, 32], F32, kind="ExternalInput")
    b1r = nc.dram_tensor("b1r", [1, 2048], F32, kind="ExternalInput")
    W2TD = nc.dram_tensor("W2TD", [2048, 1024], F32, kind="ExternalInput")
    b2c = nc.dram_tensor("b2c", [128, 8], F32, kind="ExternalInput")
    gb1 = nc.dram_tensor("gb1", [128, 16], F32, kind="ExternalInput")
    identD = nc.dram_tensor("identD", [128, 128], F32, kind="ExternalInput")

    o_x1 = nc.dram_tensor("o_x1", [D, TOK], F32, kind="ExternalOutput")
    o_xn1 = nc.dram_tensor("o_xn1", [D, TOK], F32, kind="ExternalOutput")
    o_hln = nc.dram_tensor("o_hln", [D, TOK], F32, kind="ExternalOutput")
    o_mx = nc.dram_tensor("o_mx", [1, 1], F32, kind="ExternalOutput")
    scr = nc.dram_tensor("scrB", [2, 2048], F32, kind="Internal")

    ident = cx.sb("ident", [128, 128])
    p.dma("sync", ident[:], identD.ap())
    consts = trig_consts(cx, cx.es)

    w2r = [cx.sb(f"w2r{k}", [128, 1024], BF16) for k in range(16)]
    for k in range(16):
        p.dma("gpsimd", w2r[k][:], W2TD.ap()[128 * k:128 * (k + 1), :])
    mu_rep = cx.sb("mu_rep", [128, TOK])
    rs_rep = cx.sb("rs_rep", [128, TOK])
    pos_rep = cx.sb("pos_rep", [128, TOK])
    rowT = cx.sb("rowT", [1, TOK])
    p.dma("scalar", rowT[:], stato.ap()[0:1, :])
    cx.bcast_row(rowT, mu_rep, TOK)
    p.dma("scalar", rowT[:], stato.ap()[1:2, :], reads=[mu_rep])
    cx.bcast_row(rowT, rs_rep, TOK)
    p.dma("scalar", rowT[:], posinv.ap(), reads=[rs_rep])
    cx.bcast_row(rowT, pos_rep, TOK)
    g4t = cx.sb("g4t", [128, 32])
    b4t = cx.sb("b4t", [128, 32])
    b2t = cx.sb("b2t", [128, 8])
    gb1t = cx.sb("gb1t", [128, 16])
    phict = cx.sb("phict", [128, 8])
    cmct = cx.sb("cmct", [128, 16])
    p.dma("scalar", g4t[:], g4.ap())
    p.dma("scalar", b4t[:], b4.ap())
    p.dma("scalar", b2t[:], b2c.ap())
    p.dma("scalar", gb1t[:], gb1.ap())
    p.dma("scalar", phict[:], phicI.ap())
    p.dma("scalar", cmct[:], cmcI.ap())
    G1c = cx.sb("G1c", [128, 16])
    Bvc = cx.sb("Bvc", [128, 16])
    mxacc = cx.sb("mxacc", [128, 1])

    esG = ExitStack()
    w1kt = [cx.sb(f"w1kt{i}", [128, 2048], es=esG) for i in range(2)]
    w1s = cx.sb("w1s", [128, 2048], es=esG)
    growp = [cx.banks[i] for i in range(4)]
    browp = [cx.banks[4 + i] for i in range(4)]
    for k in range(32):
        wk = w1kt[k % 2]
        p.dma("sync" if k % 2 == 0 else "scalar", wk[:],
              W1TD.ap()[128 * k:128 * (k + 1), :])
        p.op("vector", lambda E, k=k, wk=wk: E.tensor_scalar(
            w1s[:], wk[:], g4t[:, k:k + 1], None, ALU.mult),
            reads=[wk, g4t], writes=[w1s])
        for j in range(4):
            p.op("tensor", lambda E, k=k, j=j: E.matmul(
                growp[j][0:1, :], cx.onescol[:], w1s[:, 512 * j:512 * (j + 1)],
                start=(k == 0), stop=(k == 31)),
                reads=[w1s, cx.onescol], writes=[growp[j]])
        p.op("vector", lambda E, k=k, wk=wk: E.tensor_scalar(
            w1s[:], wk[:], b4t[:, k:k + 1], None, ALU.mult),
            reads=[wk, b4t], writes=[w1s])
        for j in range(4):
            p.op("tensor", lambda E, k=k, j=j: E.matmul(
                browp[j][0:1, :], cx.onescol[:], w1s[:, 512 * j:512 * (j + 1)],
                start=(k == 0), stop=(k == 31)),
                reads=[w1s, cx.onescol], writes=[browp[j]])
    grow = cx.sb("grow", [1, 2048], es=esG)
    brow = cx.sb("brow", [1, 2048], es=esG)
    b1row = cx.sb("b1row", [1, 2048], es=esG)
    p.dma("scalar", b1row[:], b1r.ap())
    for j in range(4):
        p.op("vector", lambda E, j=j: E.tensor_copy(
            grow[0:1, 512 * j:512 * (j + 1)], growp[j][0:1, :]),
            reads=[growp[j]], writes=[grow])
        p.op("vector", lambda E, j=j: E.tensor_copy(
            brow[0:1, 512 * j:512 * (j + 1)], browp[j][0:1, :]),
            reads=[browp[j]], writes=[brow])
    p.op("vector", lambda E: E.tensor_add(brow[:], brow[:], b1row[:]),
         reads=[brow, b1row], writes=[brow])
    blkg = cx.sb("blkg", [16, 128], es=esG)
    p.dma("sync", scr.ap()[0:1, :], grow[:])
    p.dma("sync", scr.ap()[1:2, :], brow[:])
    p.dma("sync", blkg[:], scr.ap()[0:1, :].rearrange("a (j q) -> (a j) q", q=128))
    cx.transpose(G1c, blkg[:], 16, 128, ident)
    p.dma("sync", blkg[:], scr.ap()[1:2, :].rearrange("a (j q) -> (a j) q", q=128),
          reads=[G1c])
    cx.transpose(Bvc, blkg[:], 16, 128, ident)
    esG.close()
    p.barrier()

    esC = ExitStack()
    xn0c = [cx.sb(f"xn0c{k}", [128, CH], es=esC) for k in range(8)]
    ctx = [cx.sb(f"ctx{k}", [128, CH], BF16, es=esC) for k in range(32)]
    wslab = [cx.sb(f"wslab{i}", [128, 1024], es=esC) for i in range(2)]
    ctxg = [cx.sb(f"ctxg{i}", [128, CH], es=esC) for i in range(2)]
    h1g = [cx.sb(f"h1g{t}", [128, CH], BF16, es=esC) for t in range(16)]
    x1t = [cx.sb(f"x1t{k}", [128, CH], es=esC) for k in range(8)]
    fm = [cx.sb(f"fm{i}", [128, CH], es=esC) for i in range(2)]
    fx = [cx.sb(f"fx{i}", [128, CH], es=esC) for i in range(2)]
    hlb = [cx.sb(f"hlb{i}", [128, CH], es=esC) for i in range(2)]
    phit = [cx.sb(f"phit{i}", [128, CH], es=esC) for i in range(2)]
    sint = cx.sb("sint", [128, CH], es=esC)
    cost = cx.sb("cost", [128, CH], es=esC)
    smr = cx.sb("smr", [128, CH], es=esC)
    smi = cx.sb("smi", [128, CH], es=esC)
    memr = cx.sb("memr", [128, CH], es=esC)
    memi = cx.sb("memi", [128, CH], es=esC)
    sqs = cx.sb("sqs", [128, CH], es=esC)
    tas = cx.sb("tas", [128, CH], es=esC)
    carry = cx.sb("carryB", [128, 16], es=esC)
    statrow = cx.sb("statrow", [1, CH], es=esC)
    muc_rep = cx.sb("muc_rep", [128, CH], es=esC)
    rsc_rep = cx.sb("rsc_rep", [128, CH], es=esC)
    mrow = cx.sb("mrow", [1, CH], es=esC)
    m2row = cx.sb("m2row", [1, CH], es=esC)
    p.op("vector", lambda E: E.memset(mxacc[:], 0.0), writes=[mxacc])
    p.op("vector", lambda E: E.tensor_copy(carry[:], cmct[:]), reads=[cmct], writes=[carry])

    for tch in range(NCH):
        sl = slice(CH * tch, CH * (tch + 1))
        cx.rot = [cx.banks[i] for i in range(6)]
        for k in range(8):
            p.dma("scalar", xn0c[k][:], xTo.ap()[128 * k:128 * (k + 1), sl])
            p.op("vector", lambda E, k=k, sl=sl: E.tensor_sub(xn0c[k][:], xn0c[k][:], mu_rep[:, sl]),
                 reads=[xn0c[k], mu_rep], writes=[xn0c[k]])
            p.op("vector", lambda E, k=k, sl=sl: E.tensor_mul(xn0c[k][:], xn0c[k][:], rs_rep[:, sl]),
                 reads=[xn0c[k], rs_rep], writes=[xn0c[k]])
        pm2 = cx.banks[7]
        for et in range(8):
            ph = phit[et % 2]
            p.dma("sync", ph[:], phiD.ap()[128 * et:128 * (et + 1), sl])
            p.op("vector", lambda E, et=et, ph=ph: E.tensor_scalar(
                ph[:], ph[:], phict[:, et:et + 1], None, ALU.add),
                reads=[ph, phict], writes=[ph])
            trig_block(cx, ph, sqs, sint, cost, consts)
            p.op("vector", lambda E, et=et: E.tensor_tensor(
                ctx[et][:], xn0c[et][:], cost[:], ALU.mult),
                reads=[xn0c[et], cost], writes=[ctx[et]])
            p.op("vector", lambda E, et=et: E.tensor_tensor(
                ctx[8 + et][:], xn0c[et][:], sint[:], ALU.mult),
                reads=[xn0c[et], sint], writes=[ctx[8 + et]])
            p.op("vector", lambda E, et=et: E.tensor_tensor_scan(
                smr[:], cx.onescol[:].broadcast_to((128, CH)), ctx[et][:],
                carry[:, et:et + 1], ALU.mult, ALU.add),
                reads=[ctx[et], carry, cx.onescol], writes=[smr])
            p.op("vector", lambda E, et=et: E.tensor_copy(carry[:, et:et + 1], smr[:, CH - 1:CH]),
                 reads=[smr], writes=[carry])
            p.op("vector", lambda E, et=et: E.tensor_tensor_scan(
                smi[:], cx.onescol[:].broadcast_to((128, CH)), ctx[8 + et][:],
                carry[:, 8 + et:8 + et + 1], ALU.mult, ALU.add),
                reads=[ctx[8 + et], carry, cx.onescol], writes=[smi])
            p.op("vector", lambda E, et=et: E.tensor_copy(
                carry[:, 8 + et:8 + et + 1], smi[:, CH - 1:CH]),
                reads=[smi], writes=[carry])
            p.op("vector", lambda E, sl=sl: E.tensor_mul(memr[:], smr[:], pos_rep[:, sl]),
                 reads=[smr, pos_rep], writes=[memr])
            p.op("vector", lambda E, sl=sl: E.tensor_mul(memi[:], smi[:], pos_rep[:, sl]),
                 reads=[smi, pos_rep], writes=[memi])
            p.op("vector", lambda E: E.tensor_mul(smr[:], memr[:], cost[:]),
                 reads=[memr, cost], writes=[smr])
            p.op("vector", lambda E: E.tensor_mul(smi[:], memi[:], sint[:]),
                 reads=[memi, sint], writes=[smi])
            p.op("vector", lambda E, et=et: E.tensor_add(ctx[16 + et][:], smr[:], smi[:]),
                 reads=[smr, smi], writes=[ctx[16 + et]])
            p.op("vector", lambda E: E.tensor_mul(smr[:], memi[:], cost[:]),
                 reads=[memi, cost], writes=[smr])
            p.op("vector", lambda E: E.tensor_mul(smi[:], memr[:], sint[:]),
                 reads=[memr, sint], writes=[smi])
            p.op("vector", lambda E, et=et: E.tensor_sub(ctx[24 + et][:], smr[:], smi[:]),
                 reads=[smr, smi], writes=[ctx[24 + et]])
            p.op("scalar", lambda E: E.activation(sqs[:], memr[:], AF.Square),
                 reads=[memr], writes=[sqs])
            p.op("tensor", lambda E, et=et: E.matmul(
                pm2[0:1, 0:CH], cx.onescol[:], sqs[:],
                start=(et == 0), stop=False),
                reads=[sqs, cx.onescol], writes=[pm2])
            p.op("scalar", lambda E: E.activation(sqs[:], memi[:], AF.Square),
                 reads=[memi], writes=[sqs])
            p.op("tensor", lambda E, et=et: E.matmul(
                pm2[0:1, 0:CH], cx.onescol[:], sqs[:],
                start=False, stop=(et == 7)),
                reads=[sqs, cx.onescol], writes=[pm2])
        pcs = cx.banks[6]
        for k in range(32):
            p.op("tensor", lambda E, k=k: E.matmul(
                pcs[0:1, 0:CH], cx.onescolb[:], ctx[k][:],
                start=(k == 0), stop=(k == 31)),
                reads=[ctx[k], cx.onescolb], writes=[pcs])
        p.op("vector", lambda E: E.tensor_scalar(
            mrow[:], pcs[0:1, 0:CH], 1.0 / 4096.0, None, ALU.mult),
            reads=[pcs], writes=[mrow])
        p.dma("scalar", statrow[:], sxn2o.ap()[0:1, sl])
        p.op("vector", lambda E: E.tensor_add(m2row[:], pm2[0:1, 0:CH], statrow[:]),
             reads=[pm2, statrow], writes=[m2row])
        p.op("vector", lambda E: E.tensor_scalar(
            m2row[:], m2row[:], 1.0 / 4096.0, None, ALU.mult),
            reads=[m2row], writes=[m2row])
        p.op("scalar", lambda E: E.activation(statrow[:], mrow[:], AF.Square),
             reads=[mrow], writes=[statrow])
        p.op("vector", lambda E: E.tensor_sub(m2row[:], m2row[:], statrow[:]),
             reads=[m2row, statrow], writes=[m2row])
        p.op("vector", lambda E: E.tensor_scalar(
            m2row[:], m2row[:], float(LN_EPS), None, ALU.add),
            reads=[m2row], writes=[m2row])
        p.op("scalar", lambda E: E.activation(m2row[:], m2row[:], AF.Sqrt),
             reads=[m2row], writes=[m2row])
        p.op("vector", lambda E: E.reciprocal(m2row[:], m2row[:]), reads=[m2row], writes=[m2row])
        cx.bcast_row(m2row, rsc_rep, CH)
        p.op("vector", lambda E: E.tensor_mul(mrow[:], mrow[:], m2row[:]),
             reads=[mrow, m2row], writes=[mrow])
        p.op("vector", lambda E: E.tensor_scalar(mrow[:], mrow[:], -1.0, None, ALU.mult),
             reads=[mrow], writes=[mrow])
        cx.bcast_row(mrow, muc_rep, CH)

        for ps in range(2):
            for k in range(32):
                wsl = wslab[k % 2]
                cg = ctxg[k % 2]
                p.dma("sync" if k % 2 == 0 else "scalar", wsl[:],
                      W1TD.ap()[128 * k:128 * (k + 1), 1024 * ps:1024 * (ps + 1)])
                p.op("vector", lambda E, k=k, cg=cg: E.tensor_scalar(
                    cg[:], ctx[k][:], g4t[:, k:k + 1], None, ALU.mult),
                    reads=[ctx[k], g4t], writes=[cg])
                for j in range(8):
                    p.op("tensor", lambda E, k=k, j=j, wsl=wsl, cg=cg: E.matmul(
                        cx.banks[j][:, 0:CH], wsl[:, 128 * j:128 * (j + 1)],
                        cg[:], start=(k == 0), stop=(k == 31)),
                        reads=[wsl, cg], writes=[cx.banks[j]])
            for j in range(8):
                ot = 8 * ps + j
                p.op("vector", lambda E, j=j: E.tensor_tensor(
                    tas[:], cx.banks[j][:, 0:CH], rsc_rep[:], ALU.mult),
                    reads=[cx.banks[j], rsc_rep], writes=[tas])
                p.op("vector", lambda E, ot=ot: E.scalar_tensor_tensor(
                    tas[:], muc_rep[:], G1c[:, ot:ot + 1], tas[:], ALU.mult, ALU.add),
                    reads=[muc_rep, G1c, tas], writes=[tas])
                p.op("scalar", lambda E, ot=ot: E.activation(
                    h1g[ot][:], tas[:], AF.Gelu, bias=Bvc[:, ot:ot + 1]),
                    reads=[tas, Bvc], writes=[h1g[ot]])
        for o2 in range(8):
            pw = cx.banks[o2]
            for k in range(16):
                p.op("tensor", lambda E, k=k, o2=o2, pw=pw: E.matmul(
                    pw[:, 0:CH], w2r[k][:, 128 * o2:128 * (o2 + 1)], h1g[k][:],
                    start=(k == 0), stop=(k == 15)),
                    reads=[w2r[k], h1g[k]], writes=[pw])
            fmb, fxb = fm[o2 % 2], fx[o2 % 2]
            p.dma("gpsimd", fmb[:], moeT.ap()[128 * o2:128 * (o2 + 1), sl])
            p.dma("gpsimd", fxb[:], xTo.ap()[128 * o2:128 * (o2 + 1), sl])
            p.op("vector", lambda E, pw=pw, o2=o2: E.tensor_scalar(
                memr[:], pw[:, 0:CH], b2t[:, o2:o2 + 1], None, ALU.add),
                reads=[pw, b2t], writes=[memr])
            p.op("vector", lambda E, o2=o2: E.tensor_add(memr[:], memr[:], xn0c[o2][:]),
                 reads=[memr, xn0c[o2]], writes=[memr])
            p.op("vector", lambda E, fmb=fmb: E.tensor_add(memr[:], memr[:], fmb[:]),
                 reads=[memr, fmb], writes=[memr])
            p.op("vector", lambda E, o2=o2, fxb=fxb: E.tensor_add(x1t[o2][:], memr[:], fxb[:]),
                 reads=[memr, fxb], writes=[x1t[o2]])
            p.dma("sync", o_x1.ap()[128 * o2:128 * (o2 + 1), sl], x1t[o2][:])
        ps1, ps2 = cx.bank(), cx.bank()
        for k in range(8):
            p.op("tensor", lambda E, k=k, ps1=ps1: E.matmul(
                ps1[0:1, 0:CH], cx.onescol[:], x1t[k][:], start=(k == 0), stop=(k == 7)),
                reads=[x1t[k], cx.onescol], writes=[ps1])
            p.op("scalar", lambda E, k=k: E.activation(sqs[:], x1t[k][:], AF.Square),
                 reads=[x1t[k]], writes=[sqs])
            p.op("tensor", lambda E, k=k, ps2=ps2: E.matmul(
                ps2[0:1, 0:CH], cx.onescol[:], sqs[:], start=(k == 0), stop=(k == 7)),
                reads=[sqs, cx.onescol], writes=[ps2])
        p.op("vector", lambda E, ps1=ps1: E.tensor_scalar(
            mrow[:], ps1[0:1, 0:CH], 1.0 / D, None, ALU.mult), reads=[ps1], writes=[mrow])
        p.op("vector", lambda E, ps2=ps2: E.tensor_scalar(
            m2row[:], ps2[0:1, 0:CH], 1.0 / D, None, ALU.mult), reads=[ps2], writes=[m2row])
        p.op("scalar", lambda E: E.activation(statrow[:], mrow[:], AF.Square),
             reads=[mrow], writes=[statrow])
        p.op("vector", lambda E: E.tensor_sub(m2row[:], m2row[:], statrow[:]),
             reads=[m2row, statrow], writes=[m2row])
        p.op("vector", lambda E: E.tensor_scalar(
            m2row[:], m2row[:], float(LN_EPS), None, ALU.add), reads=[m2row], writes=[m2row])
        p.op("scalar", lambda E: E.activation(m2row[:], m2row[:], AF.Sqrt),
             reads=[m2row], writes=[m2row])
        p.op("vector", lambda E: E.reciprocal(m2row[:], m2row[:]), reads=[m2row], writes=[m2row])
        cx.bcast_row(m2row, rsc_rep, CH)
        cx.bcast_row(mrow, muc_rep, CH)
        for k in range(8):
            p.op("vector", lambda E, k=k: E.tensor_sub(x1t[k][:], x1t[k][:], muc_rep[:]),
                 reads=[x1t[k], muc_rep], writes=[x1t[k]])
            p.op("vector", lambda E, k=k: E.tensor_mul(x1t[k][:], x1t[k][:], rsc_rep[:]),
                 reads=[x1t[k], rsc_rep], writes=[x1t[k]])
            p.dma("sync", o_xn1.ap()[128 * k:128 * (k + 1), sl], x1t[k][:])
            p.op("vector", lambda E, k=k: E.tensor_reduce(
                sqs[:, 0:1], x1t[k][:], AX, ALU.max, apply_absolute_value=True),
                reads=[x1t[k]], writes=[sqs])
            p.op("vector", lambda E, k=k: E.tensor_max(mxacc[:], mxacc[:], sqs[:, 0:1]),
                 reads=[sqs, mxacc], writes=[mxacc])
            hb = hlb[k % 2]
            p.op("vector", lambda E, k=k, hb=hb: E.tensor_scalar(
                hb[:], x1t[k][:], gb1t[:, k:k + 1], gb1t[:, 8 + k:8 + k + 1],
                ALU.mult, ALU.add), reads=[x1t[k], gb1t], writes=[hb])
            p.dma("scalar", o_hln.ap()[128 * k:128 * (k + 1), sl], hb[:])
        cx.rot = None
    esC.close()
    p.barrier()
    pmx = cx.bank()
    p.op("tensor", lambda E, pmx=pmx: E.transpose(pmx[0:1, 0:128], mxacc[:], ident[:]),
         reads=[mxacc, ident], writes=[pmx])
    mrow2 = cx.sb("mrow2", [1, 128])
    p.op("vector", lambda E, pmx=pmx: E.tensor_copy(mrow2[:], pmx[0:1, 0:128]),
         reads=[pmx], writes=[mrow2])
    p.op("vector", lambda E: E.tensor_reduce(mrow2[:, 0:1], mrow2[:], AX, ALU.max),
         reads=[mrow2], writes=[mrow2])
    p.dma("sync", o_mx.ap(), mrow2[0:1, 0:1], reads=[mrow2])

    p.finish()
    return nc


_NC_CACHE = {}


def _get_nc(name, builder):
    if name not in _NC_CACHE:
        _NC_CACHE[name] = builder()
    return _NC_CACHE[name]


def _f32(a):
    return np.ascontiguousarray(a, dtype=np.float32)


def host_inputs_A(ins):
    x = _f32(ins["L0_attn_x"]).reshape(NT, D)
    xT = np.ascontiguousarray(x.T)
    Wr16 = np.zeros((D, 16), np.float32)
    Wr16[:, 0:8] = _f32(ins["L0_ffn_Wr"])
    Wr16[:, 8] = 1.0
    WomT = np.ascontiguousarray(_f32(ins["L0_attn_W_omega"]).T)
    b_om = np.ascontiguousarray(_f32(ins["L0_attn_b_omega"]).reshape(8, 128).T)
    lsc = np.ascontiguousarray(_f32(ins["L0_attn_log_scale"]).reshape(8, 128).T)
    e9 = np.zeros((1, 1152), np.float32)
    e9[0, :] = np.tile(np.arange(9, dtype=np.float32), 128)
    ident = np.eye(128, dtype=np.float32)
    moeW1 = _f32(ins["L0_ffn_W1"])
    moeW2 = _f32(ins["L0_ffn_W2"])
    maps = []
    for c in range(NCORE):
        pos = np.arange(c % 2 * TOK + 1, c % 2 * TOK + TOK + 1, dtype=np.float64)
        possc = (1.0 / np.sqrt(pos)).astype(np.float32).reshape(1, TOK)
        cmask = np.zeros((1, 8), np.float32)
        cmask[0, c] = 1.0
        bidx = np.zeros((16, 3), np.float32)
        for j in range(16):
            for rid in range(3):
                bidx[j, rid] = rid * 128 + c * 16 + j
        ridxv = np.array([[3 * 8 + c], [4 * 8 + c]], np.float32)
        maps.append({
            "xT": xT, "xTo": np.ascontiguousarray(xT[:, c * TOK:(c + 1) * TOK]),
            "Wr16": Wr16, "WomT": WomT, "b_om": b_om, "lsc": lsc,
            "possc": possc, "moeW1": moeW1, "moeW2": moeW2, "e9iota": e9,
            "cmask": cmask, "bidx": bidx, "ridx": ridxv, "identD": ident,
            "e8colD": np.arange(8, dtype=np.float32).reshape(8, 1),
        })
    return maps




def build_launch_C():
    cx = Ctx()
    nc, p = cx.nc, cx.p

    hlnD = nc.dram_tensor("hlnD", [D, TOK], F32, kind="ExternalInput")
    hllD = nc.dram_tensor("hllD", [D, 3], F32, kind="ExternalInput")
    hlrD = nc.dram_tensor("hlrD", [D, 3], F32, kind="ExternalInput")
    inwD = nc.dram_tensor("inwD", [D, 4096], F32, kind="ExternalInput")
    cwf = nc.dram_tensor("cwf", [128, 64], F32, kind="ExternalInput")
    cwb = nc.dram_tensor("cwb", [128, 64], F32, kind="ExternalInput")
    cbf = nc.dram_tensor("cbf", [128, 16], F32, kind="ExternalInput")
    cbb = nc.dram_tensor("cbb", [128, 16], F32, kind="ExternalInput")
    xpf = nc.dram_tensor("xpf", [2048, 96], F32, kind="ExternalInput")
    xpb = nc.dram_tensor("xpb", [2048, 96], F32, kind="ExternalInput")
    dtwf = nc.dram_tensor("dtwf", [64, 2048], F32, kind="ExternalInput")
    dtwb = nc.dram_tensor("dtwb", [64, 2048], F32, kind="ExternalInput")
    dtbf = nc.dram_tensor("dtbf", [128, 16], F32, kind="ExternalInput")
    dtbb = nc.dram_tensor("dtbb", [128, 16], F32, kind="ExternalInput")
    dskf = nc.dram_tensor("dskf", [128, 16], F32, kind="ExternalInput")
    dskb = nc.dram_tensor("dskb", [128, 16], F32, kind="ExternalInput")

    o_dt = nc.dram_tensor("o_dt", [2, 2048, TOK], BF16, kind="ExternalOutput")
    o_dtx = nc.dram_tensor("o_dtx", [2, 2048, TOK], BF16, kind="ExternalOutput")
    o_y0 = nc.dram_tensor("o_y0", [2, 2048, TOK], BF16, kind="ExternalOutput")
    o_bc = nc.dram_tensor("o_bc", [2, 32, TOK], F32, kind="ExternalOutput")
    o_z = nc.dram_tensor("o_z", [2048, TOK], F32, kind="ExternalOutput")

    onebias = cx.sb("onebias", [128, 1])
    p.op("vector", lambda E: E.memset(onebias[:], 1.0), writes=[onebias])
    neg1ap = consts_neg1(cx)

    xsb = [cx.sb(f"xsb{t}", [128, TOK], BF16) for t in range(16)]
    xhal = [cx.sb(f"xhal{t}", [128, 6], BF16) for t in range(16)]
    esP = ExitStack()
    inwr = [cx.sb(f"inwr{k}", [128, 4096], BF16, es=esP) for k in range(8)]
    hlnb = [cx.sb(f"hlnb{k}", [128, TOK], BF16, es=esP) for k in range(8)]
    hlob = [cx.sb(f"hlob{k}", [128, 6], BF16, es=esP) for k in range(8)]
    zrow = [cx.sb(f"zrow{i}", [128, 512], es=esP) for i in range(2)]
    for k in range(8):
        p.dma("gpsimd", inwr[k][:], inwD.ap()[128 * k:128 * (k + 1), :])
        p.dma("gpsimd", hlnb[k][:], hlnD.ap()[128 * k:128 * (k + 1), :])
        p.dma("gpsimd", hlob[k][:, 0:3], hllD.ap()[128 * k:128 * (k + 1), :])
        p.dma("gpsimd", hlob[k][:, 3:6], hlrD.ap()[128 * k:128 * (k + 1), :])
    for ct in range(32):
        for ch in range(4):
            pw = cx.banks[(4 * ct + ch) % 8]
            for k in range(8):
                p.op("tensor", lambda E, ct=ct, ch=ch, k=k, pw=pw: E.matmul(
                    pw[:, 0:512], inwr[k][:, 128 * ct:128 * (ct + 1)],
                    hlnb[k][:, 512 * ch:512 * (ch + 1)],
                    start=(k == 0), stop=(k == 7)),
                    reads=[inwr[k], hlnb[k]], writes=[pw])
            if ct < 16:
                p.op("vector", lambda E, ct=ct, ch=ch, pw=pw: E.tensor_copy(
                    xsb[ct][:, 512 * ch:512 * (ch + 1)], pw[:, 0:512]),
                    reads=[pw], writes=[xsb[ct]])
            else:
                zr = zrow[ch % 2]
                p.op("vector", lambda E, pw=pw, zr=zr: E.tensor_copy(zr[:], pw[:, 0:512]),
                     reads=[pw], writes=[zr])
                p.dma("sync", o_z.ap()[128 * (ct - 16):128 * (ct - 15),
                                       512 * ch:512 * (ch + 1)], zr[:])
        if ct < 16:
            pw = cx.banks[(4 * ct) % 8]
            for k in range(8):
                p.op("tensor", lambda E, ct=ct, k=k, pw=pw: E.matmul(
                    pw[:, 0:6], inwr[k][:, 128 * ct:128 * (ct + 1)], hlob[k][:],
                    start=(k == 0), stop=(k == 7)),
                    reads=[inwr[k], hlob[k]], writes=[pw])
            p.op("vector", lambda E, ct=ct, pw=pw: E.tensor_copy(
                xhal[ct][:], pw[:, 0:6]), reads=[pw], writes=[xhal[ct]])
    esP.close()
    p.barrier()

    cwt = cx.sb("cwt", [128, 64])
    cbt = cx.sb("cbt", [128, 16])
    dtbt = cx.sb("dtbt", [128, 16])
    dskt = cx.sb("dskt", [128, 16])
    ext = cx.sb("ext", [128, TOK + 8], BF16)
    acc = cx.sb("acc", [128, TOK])
    xcb = [cx.sb(f"xcb_{t}", [128, TOK], BF16) for t in range(16)]
    xpt = cx.sb("xpt", [128, 96], BF16)
    xdbl = cx.sb("xdbl", [96, TOK])
    sp1 = cx.sb("sp1", [128, 512])
    sp2 = cx.sb("sp2", [128, 512])
    dtt = cx.sb("dtt", [128, 512], BF16)
    dxt = cx.sb("dxt", [128, 512], BF16)
    bct = cx.sb("bct", [32, TOK])
    dtw_t = cx.sb("dtw_t", [64, 2048])
    for d, (cw, cb, xp, dtw, dtb, dsk) in enumerate(
            (((cwf, cbf, xpf, dtwf, dtbf, dskf)),
             ((cwb, cbb, xpb, dtwb, dtbb, dskb)))):
        p.dma("scalar", cwt[:], cw.ap())
        p.dma("scalar", cbt[:], cb.ap())
        p.dma("scalar", dtbt[:], dtb.ap())
        p.dma("scalar", dskt[:], dsk.ap())
        for ct in range(16):
            p.op("vector", lambda E, ct=ct: E.memset(ext[:], 0.0), writes=[ext])
            p.op("vector", lambda E, ct=ct: E.tensor_copy(ext[:, 3:3 + TOK], xsb[ct][:]),
                 reads=[xsb[ct]], writes=[ext])
            if d == 0:
                p.op("vector", lambda E, ct=ct: E.tensor_copy(ext[:, 0:3], xhal[ct][:, 0:3]),
                     reads=[xhal[ct]], writes=[ext])
            else:
                p.op("vector", lambda E, ct=ct: E.tensor_copy(
                    ext[:, 3 + TOK:6 + TOK], xhal[ct][:, 3:6]),
                    reads=[xhal[ct]], writes=[ext])
            offs = [0, 1, 2, 3] if d == 0 else [6, 5, 4, 3]
            p.op("vector", lambda E, ct=ct, o=offs[0]: E.tensor_scalar(
                acc[:], ext[:, o:o + TOK], cwt[:, 16 * 0 + ct:16 * 0 + ct + 1], None, ALU.mult),
                reads=[ext, cwt], writes=[acc])
            for k in (1, 2, 3):
                p.op("vector", lambda E, ct=ct, k=k, o=offs[k]: E.scalar_tensor_tensor(
                    acc[:], ext[:, o:o + TOK], cwt[:, 16 * k + ct:16 * k + ct + 1],
                    acc[:], ALU.mult, ALU.add),
                    reads=[ext, cwt, acc], writes=[acc])
            p.op("scalar", lambda E, ct=ct: E.activation(
                xcb[ct][:], acc[:], AF.Silu, bias=cbt[:, ct:ct + 1]),
                reads=[acc, cbt], writes=[xcb[ct]])
        for ct in range(16):
            p.op("vector", lambda E, ct=ct: E.tensor_scalar(
                acc[:], xcb[ct][:], dskt[:, ct:ct + 1], None, ALU.mult),
                reads=[xcb[ct], dskt], writes=[acc])
            p.dma("gpsimd", o_y0.ap()[d, 128 * ct:128 * (ct + 1), :], acc[:], reads=[acc])
        for k in range(16):
            p.dma("gpsimd", xpt[:], xp.ap()[128 * k:128 * (k + 1), :])
            for ch in range(4):
                pt = cx.banks[ch]
                p.op("tensor", lambda E, k=k, ch=ch, pt=pt: E.matmul(
                    pt[0:96, :], xpt[:], xcb[k][:, 512 * ch:512 * (ch + 1)],
                    start=(k == 0), stop=(k == 15)),
                    reads=[xpt, xcb[k]], writes=[pt])
        for ch in range(4):
            pt = cx.banks[ch]
            p.op("vector", lambda E, ch=ch, pt=pt: E.tensor_copy(
                xdbl[:, 512 * ch:512 * (ch + 1)], pt[0:96, :]), reads=[pt], writes=[xdbl])
        p.op("vector", lambda E: E.tensor_copy(bct[:], xdbl[64:96, :]),
             reads=[xdbl], writes=[bct])
        p.dma("sync", o_bc.ap()[d], bct[:])
        p.dma("sync", dtw_t[:], dtw.ap())
        for dt_i in range(16):
            for ch in range(4):
                pt = cx.banks[4 + (ch % 2)]
                p.op("tensor", lambda E, dt_i=dt_i, ch=ch, pt=pt: E.matmul(
                    pt[:, 0:512], dtw_t[:, 128 * dt_i:128 * (dt_i + 1)],
                    xdbl[0:64, 512 * ch:512 * (ch + 1)], start=True, stop=True),
                    reads=[dtw_t, xdbl], writes=[pt])
                p.op("vector", lambda E, dt_i=dt_i, pt=pt: E.tensor_scalar(
                    sp1[:], pt[:, 0:512], dtbt[:, dt_i:dt_i + 1], None, ALU.add),
                    reads=[pt, dtbt], writes=[sp1])
                p.op("scalar", lambda E: E.activation(sp2[:], sp1[:], AF.Abs),
                     reads=[sp1], writes=[sp2])
                p.op("scalar", lambda E: E.activation(sp2[:], sp2[:], AF.Exp,
                                                      scale=neg1ap),
                     reads=[sp2, "negone_c"], writes=[sp2])
                p.op("scalar", lambda E: E.activation(sp2[:], sp2[:], AF.Ln,
                                                      bias=onebias[:]),
                     reads=[sp2, onebias], writes=[sp2])
                p.op("scalar", lambda E: E.activation(sp1[:], sp1[:], AF.Relu),
                     reads=[sp1], writes=[sp1])
                p.op("vector", lambda E: E.tensor_add(sp1[:], sp1[:], sp2[:]),
                     reads=[sp1, sp2], writes=[sp1])
                p.op("vector", lambda E: E.tensor_copy(dtt[:], sp1[:]),
                     reads=[sp1], writes=[dtt])
                p.dma("sync", o_dt.ap()[d, 128 * dt_i:128 * (dt_i + 1),
                                        512 * ch:512 * (ch + 1)], dtt[:], reads=[dtt])
                p.op("vector", lambda E, dt_i=dt_i, ch=ch: E.tensor_tensor(
                    dxt[:], dtt[:], xcb[dt_i][:, 512 * ch:512 * (ch + 1)], ALU.mult),
                    reads=[dtt, xcb[dt_i]], writes=[dxt])
                p.dma("sync", o_dtx.ap()[d, 128 * dt_i:128 * (dt_i + 1),
                                         512 * ch:512 * (ch + 1)], dxt[:], reads=[dxt])
    p.finish()
    return nc


_NEG1 = {}


def consts_neg1(cx):
    if id(cx) not in _NEG1:
        t = cx.sb("negone_c", [128, 1])
        cx.p.op("vector", lambda E: E.memset(t[:], -1.0), writes=[t])
        _NEG1[id(cx)] = t
    return _NEG1[id(cx)][:]


def build_launch_scan():
    cx = Ctx()
    nc, p = cx.nc, cx.p
    CH = 512
    NCH = TOK // CH

    dtD = nc.dram_tensor("dtD", [2048, TOK], BF16, kind="ExternalInput")
    dtxD = nc.dram_tensor("dtxD", [2048, TOK], BF16, kind="ExternalInput")
    y0D = nc.dram_tensor("y0D", [2048, TOK], BF16, kind="ExternalInput")
    bcflat = nc.dram_tensor("bcflat", [2, 16 * TOK], F32, kind="ExternalInput")
    alogD = nc.dram_tensor("alogD", [128, 256], F32, kind="ExternalInput")
    hinD = nc.dram_tensor("hinD", [128, 256], F32, kind="ExternalInput")

    o_y = nc.dram_tensor("o_y", [2048, TOK], BF16, kind="ExternalOutput")
    o_hf = nc.dram_tensor("o_hf", [128, 256], F32, kind="ExternalOutput")

    negA = cx.sb("negA", [128, 256])
    hst = cx.sb("hst", [128, 256])
    p.dma("scalar", negA[:], alogD.ap())
    p.op("scalar", lambda E: E.activation(negA[:], negA[:], AF.Exp), reads=[negA], writes=[negA])
    p.op("vector", lambda E: E.tensor_scalar(negA[:], negA[:], -1.0, None, ALU.mult),
         reads=[negA], writes=[negA])
    p.dma("scalar", hst[:], hinD.ap())

    brep = cx.sb("brep", [128, 16 * CH], BF16)
    crep = cx.sb("crep", [128, 16 * CH], BF16)
    bcrow = cx.sb("bcrow", [1, 16 * CH])
    dtt = [cx.sb(f"dtt{i}", [128, CH], BF16) for i in range(2)]
    dxx = [cx.sb(f"dxx{i}", [128, CH], BF16) for i in range(2)]
    y00 = [cx.sb(f"y00{i}", [128, CH], BF16) for i in range(2)]
    dA = cx.sb("dA", [128, CH], BF16)
    dBu = cx.sb("dBu", [128, CH], BF16)
    hh = cx.sb("hh", [128, CH], BF16)
    yC = cx.sb("yC", [128, CH], BF16)
    acc = cx.sb("accS", [128, CH], BF16)

    for ch in range(NCH):
        sl = slice(CH * ch, CH * (ch + 1))
        for row, rep in ((0, brep), (1, crep)):
            src_ap = bcflat.ap()[row:row + 1, :].rearrange(
                "a (n t) -> a n t", t=TOK)[:, :, sl]
            p.dma("scalar", bcrow[:].rearrange("a (n t) -> a n t", t=CH), src_ap,
                  reads=[rep])
            for j in range(16):
                pt = cx.bank()
                p.op("tensor", lambda E, j=j, pt=pt: E.matmul(
                    pt[:, 0:CH], cx.ones1[0:1, :], bcrow[0:1, CH * j:CH * (j + 1)],
                    start=True, stop=True), reads=[bcrow, cx.ones1], writes=[pt])
                p.op("scalar", lambda E, j=j, pt=pt, rep=rep: E.copy(
                    rep[:, CH * j:CH * (j + 1)], pt[:, 0:CH]), reads=[pt], writes=[rep])
        for dt_i in range(16):
            pb = dtt[dt_i % 2]
            px = dxx[dt_i % 2]
            py = y00[dt_i % 2]
            p.dma("sync", pb[:], dtD.ap()[128 * dt_i:128 * (dt_i + 1), sl])
            p.dma("sync", px[:], dtxD.ap()[128 * dt_i:128 * (dt_i + 1), sl])
            p.dma("sync", py[:], y0D.ap()[128 * dt_i:128 * (dt_i + 1), sl])
            p.op("vector", lambda E, py=py: E.tensor_copy(acc[:], py[:]),
                 reads=[py], writes=[acc])
            for n in range(16):
                col = dt_i * 16 + n
                p.op("scalar", lambda E, n=n, col=col, pb=pb: E.activation(
                    dA[:], pb[:], AF.Exp, scale=negA[:, col:col + 1]),
                    reads=[pb, negA], writes=[dA])
                p.op("vector", lambda E, n=n, px=px: E.tensor_tensor(
                    dBu[:], px[:], brep[:, CH * n:CH * (n + 1)], ALU.mult),
                    reads=[px, brep], writes=[dBu])
                p.op("vector", lambda E, col=col: E.tensor_tensor_scan(
                    hh[:], dA[:], dBu[:], hst[:, col:col + 1], ALU.mult, ALU.add),
                    reads=[dA, dBu, hst], writes=[hh])
                p.op("vector", lambda E, col=col: E.tensor_copy(
                    hst[:, col:col + 1], hh[:, CH - 1:CH]), reads=[hh], writes=[hst])
                p.op("vector", lambda E, n=n: E.tensor_tensor(
                    yC[:], hh[:], crep[:, CH * n:CH * (n + 1)], ALU.mult),
                    reads=[hh, crep], writes=[yC])
                p.op("vector", lambda E: E.tensor_add(acc[:], acc[:], yC[:]),
                     reads=[acc, yC], writes=[acc])
            p.dma("sync", o_y.ap()[128 * dt_i:128 * (dt_i + 1), sl], acc[:], reads=[acc])
    p.dma("sync", o_hf.ap(), hst[:])
    p.finish()
    return nc



def emit_lpbq_dequant(cx, wD, rows, cols, outT, ident, nbl):
    nc, p = cx.nc, cx.p
    es = ExitStack()
    wt = [cx.sb(f"dq_wt{i}{id(wD)}", [128, cols], es=es) for i in range(2)]
    bm = cx.sb(f"dq_bm{id(wD)}", [128, nbl], es=es)
    sinv = cx.sb(f"dq_si{id(wD)}", [128, nbl], es=es)
    rr = cx.sb(f"dq_rr{id(wD)}", [128, cols], es=es)
    wq = [cx.sb(f"dq_wq{i}{id(wD)}", [128, cols], es=es) for i in range(2)]
    for ti in range(rows // 128):
        wti, wqi = wt[ti % 2], wq[ti % 2]
        p.dma("sync" if ti % 2 == 0 else "scalar", wti[:],
              wD.ap()[128 * ti:128 * (ti + 1), :])
        p.op("vector", lambda E, wti=wti: E.tensor_reduce(
            bm[:], wti[:].rearrange("p (b q) -> p b q", q=32), AX, ALU.max,
            apply_absolute_value=True),
            reads=[wti], writes=[bm])
        p.op("vector", lambda E: E.tensor_scalar(bm[:], bm[:], 1.0 / 7.0, 1e-12,
                                                 ALU.mult, ALU.max),
             reads=[bm], writes=[bm])
        p.op("vector", lambda E: E.reciprocal(sinv[:], bm[:]), reads=[bm], writes=[sinv])
        p.op("vector", lambda E, wti=wti: E.tensor_tensor(
            rr[:], wti[:], sinv[:].rearrange("p b -> p b ()").broadcast_to((128, nbl, 32)),
            ALU.mult), reads=[wti, sinv], writes=[rr])
        p.op("vector", lambda E: E.tensor_scalar(rr[:], rr[:], MAGIC, -MAGIC,
                                                 ALU.add, ALU.add),
             reads=[rr], writes=[rr])
        p.op("vector", lambda E, wqi=wqi: E.tensor_tensor(
            wqi[:], rr[:], bm[:].rearrange("p b -> p b ()").broadcast_to((128, nbl, 32)),
            ALU.mult), reads=[rr, bm], writes=[wqi])
        for k in range(cols // 128):
            cx.transpose(outT[k], wqi[:, 128 * k:128 * (k + 1)], 128, 128, ident,
                         out_off=128 * ti)
    es.close()
    p.barrier()


def emit_qdq(cx, src_tiles, sptr, siptr, out_tiles, n):
    p = cx.p
    for t_s, t_o in zip(src_tiles, out_tiles):
        p.op("vector", lambda E, t_s=t_s, t_o=t_o: E.tensor_scalar(
            t_o[:], t_s[:], siptr, MAGIC, ALU.mult, ALU.add),
            reads=[t_s, "qscale"], writes=[t_o])
        p.op("vector", lambda E, t_o=t_o: E.tensor_scalar(
            t_o[:], t_o[:], -MAGIC, sptr, ALU.add, ALU.mult),
            reads=[t_o, "qscale"], writes=[t_o])


def build_launch_G():
    cx = Ctx()
    nc, p = cx.nc, cx.p
    CH = 512

    yfD = nc.dram_tensor("yfD", [2048, TOK], BF16, kind="ExternalInput")
    ybD = nc.dram_tensor("ybD", [2048, TOK], BF16, kind="ExternalInput")
    zD = nc.dram_tensor("zD", [2048, TOK], F32, kind="ExternalInput")
    xn1D = nc.dram_tensor("xn1D", [D, TOK], F32, kind="ExternalInput")
    x1D = nc.dram_tensor("x1D", [D, TOK], F32, kind="ExternalInput")
    mrgD = nc.dram_tensor("mrgD", [4096, 2048], F32, kind="ExternalInput")
    outwD = nc.dram_tensor("outwD", [2048, D], F32, kind="ExternalInput")
    wupD = nc.dram_tensor("wupD", [3072, D], F32, kind="ExternalInput")
    wgtD = nc.dram_tensor("wgtD", [3072, D], F32, kind="ExternalInput")
    gmaxD = nc.dram_tensor("gmaxD", [1, 8], F32, kind="ExternalInput")
    identD = nc.dram_tensor("identD", [128, 128], F32, kind="ExternalInput")

    o_x2p = nc.dram_tensor("o_x2p", [D, TOK], F32, kind="ExternalOutput")
    o_up = nc.dram_tensor("o_up", [3072, TOK], F32, kind="ExternalOutput")
    o_gate = nc.dram_tensor("o_gate", [3072, TOK], F32, kind="ExternalOutput")
    o_mx2 = nc.dram_tensor("o_mx2", [1, 2], F32, kind="ExternalOutput")

    ident = cx.sb("ident", [128, 128])
    p.dma("sync", ident[:], identD.ap())

    esM = ExitStack()
    outwr = [cx.sb(f"outwr{k}", [128, 1024], BF16, es=esM) for k in range(16)]
    for k in range(16):
        p.dma("gpsimd", outwr[k][:], outwD.ap()[128 * k:128 * (k + 1), :])
    yk = [cx.sb(f"yk{k}", [128, CH], es=esM) for k in range(32)]
    mslab = [cx.sb(f"mslab{i}", [128, 1024], es=esM) for i in range(2)]
    gat = [cx.sb(f"gat{t}", [128, CH], BF16, es=esM) for t in range(16)]
    zt = [cx.sb(f"ztG{i}", [128, CH], es=esM) for i in range(2)]
    szt = [cx.sb(f"sztG{i}", [128, CH], es=esM) for i in range(2)]
    xat = [cx.sb(f"xatG{i}", [128, CH], es=esM) for i in range(2)]
    t_a = [cx.sb(f"t_aG{i}", [128, CH], es=esM) for i in range(2)]
    for ch in range(4):
        sl = slice(CH * ch, CH * (ch + 1))
        for k in range(16):
            p.dma("gpsimd", yk[k][:], yfD.ap()[128 * k:128 * (k + 1), sl])
            p.dma("gpsimd", yk[16 + k][:], ybD.ap()[128 * k:128 * (k + 1), sl])
        for ps in range(2):
            for k in range(32):
                msl = mslab[k % 2]
                p.dma("sync" if k % 2 == 0 else "scalar", msl[:],
                      mrgD.ap()[128 * k:128 * (k + 1), 1024 * ps:1024 * (ps + 1)])
                for j in range(8):
                    p.op("tensor", lambda E, k=k, j=j, msl=msl: E.matmul(
                        cx.banks[j][:, 0:CH], msl[:, 128 * j:128 * (j + 1)],
                        yk[k][:], start=(k == 0), stop=(k == 31)),
                        reads=[msl, yk[k]], writes=[cx.banks[j]])
            for j in range(8):
                ot = 8 * ps + j
                ztj, sztj = zt[j % 2], szt[j % 2]
                p.dma("gpsimd", ztj[:], zD.ap()[128 * ot:128 * (ot + 1), sl])
                p.op("scalar", lambda E, ztj=ztj, sztj=sztj: E.activation(
                    sztj[:], ztj[:], AF.Silu), reads=[ztj], writes=[sztj])
                p.op("vector", lambda E, j=j, ot=ot, sztj=sztj: E.tensor_tensor(
                    gat[ot][:], cx.banks[j][:, 0:CH], sztj[:], ALU.mult),
                    reads=[cx.banks[j], sztj], writes=[gat[ot]])
        for o2 in range(8):
            pw = cx.banks[o2]
            for k in range(16):
                p.op("tensor", lambda E, k=k, o2=o2, pw=pw: E.matmul(
                    pw[:, 0:CH], outwr[k][:, 128 * o2:128 * (o2 + 1)], gat[k][:],
                    start=(k == 0), stop=(k == 15)),
                    reads=[outwr[k], gat[k]], writes=[pw])
            ztj, xaj, taj = zt[o2 % 2], xat[o2 % 2], t_a[o2 % 2]
            p.dma("gpsimd", ztj[:], x1D.ap()[128 * o2:128 * (o2 + 1), sl])
            p.dma("gpsimd", xaj[:], xn1D.ap()[128 * o2:128 * (o2 + 1), sl])
            p.op("vector", lambda E, ztj=ztj, xaj=xaj, taj=taj: E.tensor_add(
                taj[:], xaj[:], ztj[:]), reads=[xaj, ztj], writes=[taj])
            p.op("vector", lambda E, pw=pw, taj=taj: E.tensor_add(
                taj[:], taj[:], pw[:, 0:CH]), reads=[taj, pw], writes=[taj])
            p.dma("sync", o_x2p.ap()[128 * o2:128 * (o2 + 1), sl], taj[:], reads=[taj])
    esM.close()
    p.barrier()

    esQ = ExitStack()
    grow = cx.sb("growG", [1, 8], es=esQ)
    srow = cx.sb("srowG", [1, 1], es=esQ)
    sptr = cx.sb("sptrG", [128, 1], es=esQ)
    siptr = cx.sb("siptrG", [128, 1], es=esQ)
    p.dma("scalar", grow[:], gmaxD.ap())
    p.op("vector", lambda E: E.tensor_reduce(srow[:], grow[:], AX, ALU.max),
         reads=[grow], writes=[srow])
    p.op("vector", lambda E: E.tensor_scalar(srow[:], srow[:], 1.0 / 32767.0, 1e-12,
                                             ALU.mult, ALU.max),
         reads=[srow], writes=[srow])
    pt = cx.bank()
    p.op("tensor", lambda E, pt=pt: E.matmul(pt[:, 0:1], cx.ones1[0:1, :], srow[0:1, 0:1],
                                             start=True, stop=True),
         reads=[srow, cx.ones1], writes=[pt])
    p.op("scalar", lambda E, pt=pt: E.copy(sptr[:], pt[:, 0:1]), reads=[pt], writes=[sptr])
    p.op("vector", lambda E: E.reciprocal(siptr[:], sptr[:]), reads=[sptr], writes=[siptr])
    p.op("vector", lambda E: E.engine_nop(), reads=[sptr, siptr], writes=["qscale"])

    xqb = [cx.sb(f"xqb{k}", [128, TOK], BF16, es=esQ) for k in range(8)]
    xqt = cx.sb("xqt", [128, TOK], es=esQ)
    for k in range(8):
        p.dma("sync", xqt[:], xn1D.ap()[128 * k:128 * (k + 1), :])
        p.op("vector", lambda E: E.tensor_scalar(
            xqt[:], xqt[:], siptr[:, 0:1], MAGIC, ALU.mult, ALU.add),
            reads=[xqt, "qscale"], writes=[xqt])
        p.op("vector", lambda E: E.tensor_scalar(
            xqt[:], xqt[:], -MAGIC, sptr[:, 0:1], ALU.add, ALU.mult),
            reads=[xqt, "qscale"], writes=[xqt])
        p.op("vector", lambda E, k=k: E.tensor_copy(xqb[k][:], xqt[:]),
             reads=[xqt], writes=[xqb[k]])

    wT = [cx.sb(f"wTq{k}", [128, 3072], BF16, es=esQ) for k in range(8)]
    upt2 = [cx.sb(f"uptG{i}", [128, CH], es=esQ) for i in range(2)]
    mxt = cx.sb("mxtG", [128, 1], es=esQ)
    mxa = cx.sb("mxaG", [128, 2], es=esQ)
    mrow2 = cx.sb("mrow2G", [1, 128], es=esQ)
    p.op("vector", lambda E: E.memset(mxa[:], 0.0), writes=[mxa])
    for gi, (wD, oD) in enumerate(((wupD, o_up), (wgtD, o_gate))):
        emit_lpbq_dequant(cx, wD, 3072, 1024, wT, ident, 32)
        for ot in range(24):
            for ch in range(4):
                pw = cx.bank()
                for k in range(8):
                    p.op("tensor", lambda E, k=k, ot=ot, ch=ch, pw=pw: E.matmul(
                        pw[:, 0:CH], wT[k][:, 128 * ot:128 * (ot + 1)],
                        xqb[k][:, CH * ch:CH * (ch + 1)], start=(k == 0), stop=(k == 7)),
                        reads=[wT[k], xqb[k]], writes=[pw])
                upt = upt2[(4 * ot + ch) % 2]
                p.op("vector", lambda E, pw=pw, upt=upt: E.tensor_copy(upt[:], pw[:, 0:CH]),
                     reads=[pw], writes=[upt])
                p.dma("sync" if ch % 2 == 0 else "scalar",
                      oD.ap()[128 * ot:128 * (ot + 1), CH * ch:CH * (ch + 1)],
                      upt[:], reads=[upt])
                p.op("vector", lambda E, upt=upt: E.tensor_reduce(
                    mxt[:, 0:1], upt[:], AX, ALU.max, apply_absolute_value=True),
                    reads=[upt], writes=[mxt])
                p.op("vector", lambda E, gi=gi: E.tensor_max(
                    mxa[:, gi:gi + 1], mxa[:, gi:gi + 1], mxt[:, 0:1]),
                    reads=[mxt, mxa], writes=[mxa])
    pmx = cx.bank()
    p.op("tensor", lambda E, pmx=pmx: E.transpose(pmx[0:2, 0:128], mxa[:], ident[:]),
         reads=[mxa, ident], writes=[pmx])
    mrow22 = cx.sb("mrow22", [2, 128], es=esQ)
    p.op("vector", lambda E, pmx=pmx: E.tensor_copy(mrow22[:], pmx[0:2, 0:128]),
         reads=[pmx], writes=[mrow22])
    p.op("vector", lambda E: E.tensor_reduce(mrow22[:, 0:1], mrow22[:], AX, ALU.max),
         reads=[mrow22], writes=[mrow22])
    p.dma("sync", o_mx2.ap().rearrange("a b -> b a"), mrow22[0:2, 0:1], reads=[mrow22])
    esQ.close()
    p.finish()
    return nc


def build_launch_H():
    cx = Ctx()
    nc, p = cx.nc, cx.p
    upD = nc.dram_tensor("upD", [3072, TOK], F32, kind="ExternalInput")
    gtD = nc.dram_tensor("gtD", [3072, TOK], F32, kind="ExternalInput")
    gmaxD = nc.dram_tensor("gmaxD", [1, 16], F32, kind="ExternalInput")
    identD = nc.dram_tensor("identD", [128, 128], F32, kind="ExternalInput")
    o_upq = nc.dram_tensor("o_upq", [3072, TOK], F32, kind="ExternalOutput")
    o_t3 = nc.dram_tensor("o_t3", [3072, TOK], F32, kind="ExternalOutput")
    o_mx = nc.dram_tensor("o_mx", [1, 1], F32, kind="ExternalOutput")
    ident = cx.sb("ident", [128, 128])
    p.dma("sync", ident[:], identD.ap())
    grow = cx.sb("growH", [1, 16])
    sr = cx.sb("srH", [1, 2])
    sp_u = cx.sb("spu", [128, 2])
    si_u = cx.sb("siu", [128, 2])
    p.dma("scalar", grow[:], gmaxD.ap())
    p.op("vector", lambda E: E.tensor_reduce(
        sr[:], grow[:].rearrange("a (c q) -> a q c", q=2), AX, ALU.max),
        reads=[grow], writes=[sr])
    p.op("vector", lambda E: E.tensor_scalar(sr[:], sr[:], 1.0 / 32767.0, 1e-12,
                                             ALU.mult, ALU.max), reads=[sr], writes=[sr])
    pt = cx.bank()
    p.op("tensor", lambda E, pt=pt: E.matmul(pt[:, 0:2], cx.ones1[0:1, :], sr[0:1, :],
                                             start=True, stop=True),
         reads=[sr, cx.ones1], writes=[pt])
    p.op("scalar", lambda E, pt=pt: E.copy(sp_u[:], pt[:, 0:2]), reads=[pt], writes=[sp_u])
    p.op("vector", lambda E: E.reciprocal(si_u[:], sp_u[:]), reads=[sp_u], writes=[si_u])
    p.op("vector", lambda E: E.engine_nop(), reads=[sp_u, si_u], writes=["qscale"])

    ut = cx.sb("utH", [128, TOK])
    gt = cx.sb("gtH", [128, TOK])
    sg = cx.sb("sgH", [128, TOK])
    mxa = cx.sb("mxaH", [128, 1])
    p.op("vector", lambda E: E.memset(mxa[:], 0.0), writes=[mxa])
    for t in range(24):
        p.dma("sync", ut[:], upD.ap()[128 * t:128 * (t + 1), :])
        p.dma("sync", gt[:], gtD.ap()[128 * t:128 * (t + 1), :])
        p.op("vector", lambda E: E.tensor_scalar(ut[:], ut[:], si_u[:, 0:1], MAGIC,
                                                 ALU.mult, ALU.add),
             reads=[ut, "qscale"], writes=[ut])
        p.op("vector", lambda E: E.tensor_scalar(ut[:], ut[:], -MAGIC, sp_u[:, 0:1],
                                                 ALU.add, ALU.mult),
             reads=[ut], writes=[ut])
        p.dma("sync", o_upq.ap()[128 * t:128 * (t + 1), :], ut[:], reads=[ut])
        p.op("vector", lambda E: E.tensor_scalar(gt[:], gt[:], si_u[:, 1:2], MAGIC,
                                                 ALU.mult, ALU.add),
             reads=[gt, "qscale"], writes=[gt])
        p.op("vector", lambda E: E.tensor_scalar(gt[:], gt[:], -MAGIC, sp_u[:, 1:2],
                                                 ALU.add, ALU.mult),
             reads=[gt], writes=[gt])
        p.op("scalar", lambda E: E.activation(sg[:], gt[:], AF.Sigmoid),
             reads=[gt], writes=[sg])
        p.op("vector", lambda E: E.tensor_scalar(sg[:], sg[:], 65536.0, MAGIC,
                                                 ALU.mult, ALU.add),
             reads=[sg], writes=[sg])
        p.op("vector", lambda E: E.tensor_scalar(sg[:], sg[:], -MAGIC, 65535.0,
                                                 ALU.add, ALU.min),
             reads=[sg], writes=[sg])
        p.op("vector", lambda E: E.tensor_scalar(sg[:], sg[:], 1.0 / 65536.0, None,
                                                 ALU.mult),
             reads=[sg], writes=[sg])
        p.op("vector", lambda E: E.tensor_mul(gt[:], gt[:], sg[:]),
             reads=[gt, sg], writes=[gt])
        p.dma("sync", o_t3.ap()[128 * t:128 * (t + 1), :], gt[:], reads=[gt])
        p.op("vector", lambda E: E.tensor_reduce(sg[:, 0:1], gt[:], AX, ALU.max,
                                                 apply_absolute_value=True),
             reads=[gt], writes=[sg])
        p.op("vector", lambda E: E.tensor_max(mxa[:], mxa[:], sg[:, 0:1]),
             reads=[sg, mxa], writes=[mxa])
    pmx = cx.bank()
    p.op("tensor", lambda E, pmx=pmx: E.transpose(pmx[0:1, 0:128], mxa[:], ident[:]),
         reads=[mxa, ident], writes=[pmx])
    mr = cx.sb("mrH", [1, 128])
    p.op("vector", lambda E, pmx=pmx: E.tensor_copy(mr[:], pmx[0:1, 0:128]),
         reads=[pmx], writes=[mr])
    p.op("vector", lambda E: E.tensor_reduce(mr[:, 0:1], mr[:], AX, ALU.max),
         reads=[mr], writes=[mr])
    p.dma("sync", o_mx.ap(), mr[0:1, 0:1], reads=[mr])
    p.finish()
    return nc


def build_launch_I():
    cx = Ctx()
    nc, p = cx.nc, cx.p
    t3D = nc.dram_tensor("t3D", [3072, TOK], F32, kind="ExternalInput")
    upqD = nc.dram_tensor("upqD", [3072, TOK], F32, kind="ExternalInput")
    gmaxD = nc.dram_tensor("gmaxD", [1, 8], F32, kind="ExternalInput")
    identD = nc.dram_tensor("identD", [128, 128], F32, kind="ExternalInput")
    o_t4 = nc.dram_tensor("o_t4", [3072, TOK], F32, kind="ExternalOutput")
    o_mx = nc.dram_tensor("o_mx", [1, 1], F32, kind="ExternalOutput")
    ident = cx.sb("ident", [128, 128])
    p.dma("sync", ident[:], identD.ap())
    grow = cx.sb("growI", [1, 8])
    sr = cx.sb("srI", [1, 1])
    sp_u = cx.sb("spuI", [128, 1])
    si_u = cx.sb("siuI", [128, 1])
    p.dma("scalar", grow[:], gmaxD.ap())
    p.op("vector", lambda E: E.tensor_reduce(sr[:], grow[:], AX, ALU.max),
         reads=[grow], writes=[sr])
    p.op("vector", lambda E: E.tensor_scalar(sr[:], sr[:], 1.0 / 32767.0, 1e-12,
                                             ALU.mult, ALU.max), reads=[sr], writes=[sr])
    pt = cx.bank()
    p.op("tensor", lambda E, pt=pt: E.matmul(pt[:, 0:1], cx.ones1[0:1, :], sr[0:1, :],
                                             start=True, stop=True),
         reads=[sr, cx.ones1], writes=[pt])
    p.op("scalar", lambda E, pt=pt: E.copy(sp_u[:], pt[:, 0:1]), reads=[pt], writes=[sp_u])
    p.op("vector", lambda E: E.reciprocal(si_u[:], sp_u[:]), reads=[sp_u], writes=[si_u])
    p.op("vector", lambda E: E.engine_nop(), reads=[sp_u, si_u], writes=["qscale"])
    ut = cx.sb("utI", [128, TOK])
    gt = cx.sb("gtI", [128, TOK])
    mxa = cx.sb("mxaI", [128, 1])
    p.op("vector", lambda E: E.memset(mxa[:], 0.0), writes=[mxa])
    for t in range(24):
        p.dma("sync", gt[:], t3D.ap()[128 * t:128 * (t + 1), :])
        p.dma("sync", ut[:], upqD.ap()[128 * t:128 * (t + 1), :])
        p.op("vector", lambda E: E.tensor_scalar(gt[:], gt[:], si_u[:, 0:1], MAGIC,
                                                 ALU.mult, ALU.add),
             reads=[gt, "qscale"], writes=[gt])
        p.op("vector", lambda E: E.tensor_scalar(gt[:], gt[:], -MAGIC, sp_u[:, 0:1],
                                                 ALU.add, ALU.mult),
             reads=[gt], writes=[gt])
        p.op("vector", lambda E: E.tensor_mul(gt[:], gt[:], ut[:]),
             reads=[gt, ut], writes=[gt])
        p.dma("sync", o_t4.ap()[128 * t:128 * (t + 1), :], gt[:], reads=[gt])
        p.op("vector", lambda E: E.tensor_reduce(ut[:, 0:1], gt[:], AX, ALU.max,
                                                 apply_absolute_value=True),
             reads=[gt], writes=[ut])
        p.op("vector", lambda E: E.tensor_max(mxa[:], mxa[:], ut[:, 0:1]),
             reads=[ut, mxa], writes=[mxa])
    pmx = cx.bank()
    p.op("tensor", lambda E, pmx=pmx: E.transpose(pmx[0:1, 0:128], mxa[:], ident[:]),
         reads=[mxa, ident], writes=[pmx])
    mr = cx.sb("mrI", [1, 128])
    p.op("vector", lambda E, pmx=pmx: E.tensor_copy(mr[:], pmx[0:1, 0:128]),
         reads=[pmx], writes=[mr])
    p.op("vector", lambda E: E.tensor_reduce(mr[:, 0:1], mr[:], AX, ALU.max),
         reads=[mr], writes=[mr])
    p.dma("sync", o_mx.ap(), mr[0:1, 0:1], reads=[mr])
    p.finish()
    return nc


def build_launch_J():
    cx = Ctx()
    nc, p = cx.nc, cx.p
    CH = 512
    t4D = nc.dram_tensor("t4D", [3072, TOK], F32, kind="ExternalInput")
    x2pD = nc.dram_tensor("x2pD", [D, TOK], F32, kind="ExternalInput")
    wdD = nc.dram_tensor("wdD", [D, 3072], F32, kind="ExternalInput")
    gmaxD = nc.dram_tensor("gmaxD", [1, 8], F32, kind="ExternalInput")
    identD = nc.dram_tensor("identD", [128, 128], F32, kind="ExternalInput")
    o_out = nc.dram_tensor("o_out", [TOK, D], F32, kind="ExternalOutput")
    ident = cx.sb("ident", [128, 128])
    p.dma("sync", ident[:], identD.ap())
    grow = cx.sb("growJ", [1, 8])
    sr = cx.sb("srJ", [1, 1])
    sp_u = cx.sb("spuJ", [128, 1])
    si_u = cx.sb("siuJ", [128, 1])
    p.dma("scalar", grow[:], gmaxD.ap())
    p.op("vector", lambda E: E.tensor_reduce(sr[:], grow[:], AX, ALU.max),
         reads=[grow], writes=[sr])
    p.op("vector", lambda E: E.tensor_scalar(sr[:], sr[:], 1.0 / 32767.0, 1e-12,
                                             ALU.mult, ALU.max), reads=[sr], writes=[sr])
    pt = cx.bank()
    p.op("tensor", lambda E, pt=pt: E.matmul(pt[:, 0:1], cx.ones1[0:1, :], sr[0:1, :],
                                             start=True, stop=True),
         reads=[sr, cx.ones1], writes=[pt])
    p.op("scalar", lambda E, pt=pt: E.copy(sp_u[:], pt[:, 0:1]), reads=[pt], writes=[sp_u])
    p.op("vector", lambda E: E.reciprocal(si_u[:], sp_u[:]), reads=[sp_u], writes=[si_u])
    p.op("vector", lambda E: E.engine_nop(), reads=[sp_u, si_u], writes=["qscale"])

    wdT = [cx.sb(f"wdT{k}", [128, 1024], BF16) for k in range(24)]
    emit_lpbq_dequant(cx, wdD, 1024, 3072, wdT, ident, 96)

    oqb = [cx.sb(f"oqb{k}", [128, TOK], BF16) for k in range(24)]
    qt = cx.sb("qtJ", [128, TOK])
    for t in range(24):
        p.dma("sync", qt[:], t4D.ap()[128 * t:128 * (t + 1), :])
        p.op("vector", lambda E: E.tensor_scalar(qt[:], qt[:], si_u[:, 0:1], MAGIC,
                                                 ALU.mult, ALU.add),
             reads=[qt, "qscale"], writes=[qt])
        p.op("vector", lambda E: E.tensor_scalar(qt[:], qt[:], -MAGIC, sp_u[:, 0:1],
                                                 ALU.add, ALU.mult),
             reads=[qt], writes=[qt])
        p.op("vector", lambda E, t=t: E.tensor_copy(oqb[t][:], qt[:]),
             reads=[qt], writes=[oqb[t]])
    xf = cx.sb("xfJ", [128, CH])
    rowo = cx.sb("rowoJ", [128, D])
    for ch in range(4):
        sl = slice(CH * ch, CH * (ch + 1))
        for o2 in range(8):
            pw = cx.bank()
            for k in range(24):
                p.op("tensor", lambda E, k=k, o2=o2, sl=sl, pw=pw: E.matmul(
                    pw[:, 0:CH], wdT[k][:, 128 * o2:128 * (o2 + 1)],
                    oqb[k][:, sl], start=(k == 0), stop=(k == 23)),
                    reads=[wdT[k], oqb[k]], writes=[pw])
            p.dma("sync", xf[:], x2pD.ap()[128 * o2:128 * (o2 + 1), sl])
            p.op("vector", lambda E, pw=pw: E.tensor_add(xf[:], xf[:], pw[:, 0:CH]),
                 reads=[xf, pw], writes=[xf])
            for jj in range(4):
                cx.transpose(rowo, xf[:, 128 * jj:128 * (jj + 1)], 128, 128, ident,
                             out_off=128 * o2)
                p.dma("sync", o_out.ap()[CH * ch + 128 * jj:CH * ch + 128 * (jj + 1),
                                         128 * o2:128 * (o2 + 1)],
                      rowo[:, 128 * o2:128 * (o2 + 1)], reads=[rowo])
    p.finish()
    return nc


def host_inputs_B(ins, outA):
    x = _f32(ins["L0_attn_x"]).reshape(NT, D)
    xT = np.ascontiguousarray(x.T)
    W1T = np.ascontiguousarray(_f32(ins["L0_attn_W1"]).T)
    W2T = np.ascontiguousarray(_f32(ins["L0_attn_W2"]).T)
    g4 = np.ascontiguousarray(_f32(ins["L0_attn_ln_gamma"]).reshape(32, 128).T)
    b4 = np.ascontiguousarray(_f32(ins["L0_attn_ln_beta"]).reshape(32, 128).T)
    b1row = _f32(ins["L0_attn_b1"]).reshape(1, 2048)
    b2c = np.ascontiguousarray(_f32(ins["L0_attn_b2"]).reshape(8, 128).T)
    gb1 = np.zeros((128, 16), np.float32)
    gb1[:, 0:8] = _f32(ins["L1_attn_gamma"]).reshape(8, 128).T
    gb1[:, 8:16] = _f32(ins["L1_attn_beta"]).reshape(8, 128).T
    ident = np.eye(128, dtype=np.float32)
    maps = []
    for c in range(NCORE):
        pos = np.arange(c % 2 * TOK + 1, c % 2 * TOK + TOK + 1, dtype=np.float64)
        pinv = (1.0 / pos).astype(np.float32).reshape(1, TOK)
        if c % 2 == 1:
            phic = outA[c - 1]["o_phic"]
            cmc = outA[c - 1]["o_cmc"]
        else:
            phic = np.zeros((128, 8), np.float32)
            cmc = np.zeros((128, 16), np.float32)
        maps.append({
            "xTo": np.ascontiguousarray(xT[:, c * TOK:(c + 1) * TOK]),
            "stato": outA[c]["o_stat"], "phiD": outA[c]["o_phi"],
            "phicI": np.ascontiguousarray(phic), "cmcI": np.ascontiguousarray(cmc),
            "sxn2o": np.ascontiguousarray(outA[c]["o_rows"][2:3, c * TOK:(c + 1) * TOK]),
            "moeT": np.ascontiguousarray(outA[c]["o_moe"].T), "posinv": pinv,
            "W1TD": W1T, "g4": g4, "b4": b4, "b1r": b1row, "W2TD": W2T,
            "b2c": b2c, "gb1": gb1, "identD": ident,
        })
    return maps


def host_inputs_C(ins, outB):
    def tile16(v):
        return np.ascontiguousarray(_f32(v).reshape(16, 128).T)
    cwf = np.ascontiguousarray(_f32(ins["L1_attn_conv_w"]).reshape(16, 128, 4)
                               .transpose(1, 2, 0).reshape(128, 64))
    cwb = np.ascontiguousarray(_f32(ins["L1_attn_conv_w_b"]).reshape(16, 128, 4)
                               .transpose(1, 2, 0).reshape(128, 64))
    com = {
        "cwf": cwf, "cwb": cwb, "inwD": _f32(ins["L1_attn_in_w"]),
        "cbf": tile16(ins["L1_attn_conv_b"]), "cbb": tile16(ins["L1_attn_conv_b_b"]),
        "xpf": _f32(ins["L1_attn_xproj_w"]), "xpb": _f32(ins["L1_attn_xproj_w_b"]),
        "dtwf": _f32(ins["L1_attn_dt_w"]), "dtwb": _f32(ins["L1_attn_dt_w_b"]),
        "dtbf": tile16(ins["L1_attn_dt_b"]), "dtbb": tile16(ins["L1_attn_dt_b_b"]),
        "dskf": tile16(ins["L1_attn_D_skip"]), "dskb": tile16(ins["L1_attn_D_skip_b"]),
    }
    maps = []
    z3 = np.zeros((D, 3), np.float32)
    for c in range(NCORE):
        hl = outB[c - 1]["o_hln"][:, TOK - 3:TOK] if c % 2 == 1 else z3
        hr = outB[c + 1]["o_hln"][:, 0:3] if c % 2 == 0 else z3
        m = dict(com)
        m["hlnD"] = outB[c]["o_hln"]
        m["hllD"] = np.ascontiguousarray(hl)
        m["hlrD"] = np.ascontiguousarray(hr)
        maps.append(m)
    return maps


def _alog_layout(a):
    return np.ascontiguousarray(_f32(a).reshape(16, 128, 16).transpose(1, 0, 2)
                                .reshape(128, 256))


def host_inputs_scan(ins, outC, outD=None):
    bf = ml_dtypes.bfloat16
    maps = []
    for c in range(NCORE):
        if outD is None:
            di = 0 if c % 2 == 0 else 1
            flip = (c % 2 == 1)
            hin = np.zeros((128, 256), np.float32)
        else:
            di = 1 if c % 2 == 0 else 0
            flip = (c % 2 == 0)
            sib = c + 1 if c % 2 == 0 else c - 1
            hin = outD[sib]["o_hf"]
        dt = outC[c]["o_dt"][di]
        dtx = outC[c]["o_dtx"][di]
        y0 = outC[c]["o_y0"][di]
        bc = _f32(outC[c]["o_bc"][di])
        if flip:
            dt = dt[:, ::-1]; dtx = dtx[:, ::-1]; y0 = y0[:, ::-1]; bc = bc[:, ::-1]
        alog = ins["L1_attn_A_log"] if di == 0 else ins["L1_attn_A_log_b"]
        maps.append({
            "dtD": np.ascontiguousarray(dt), "dtxD": np.ascontiguousarray(dtx),
            "y0D": np.ascontiguousarray(y0),
            "bcflat": np.ascontiguousarray(bc.reshape(2, 16 * TOK)),
            "alogD": _alog_layout(alog),
            "hinD": np.ascontiguousarray(_f32(hin)),
        })
    return maps


def host_inputs_G(ins, outB, outC, outD, outE):
    mrg = _f32(ins["L1_attn_merge_w"])
    outw = _f32(ins["L1_attn_out_w"])
    wup = _f32(ins["L1_ffn_w_up"])
    wgt = _f32(ins["L1_ffn_w_gate"])
    gmax = np.concatenate([outB[c]["o_mx"] for c in range(NCORE)], axis=1)
    ident = np.eye(128, dtype=np.float32)
    maps = []
    for c in range(NCORE):
        if c % 2 == 0:
            yf = outD[c]["o_y"]
            yb = np.ascontiguousarray(outE[c]["o_y"][:, ::-1])
        else:
            yf = outE[c]["o_y"]
            yb = np.ascontiguousarray(outD[c]["o_y"][:, ::-1])
        maps.append({
            "yfD": yf, "ybD": yb,
            "zD": outC[c]["o_z"],
            "xn1D": outB[c]["o_xn1"], "x1D": outB[c]["o_x1"],
            "mrgD": mrg, "outwD": outw, "wupD": wup, "wgtD": wgt,
            "gmaxD": gmax, "identD": ident,
        })
    return maps


def host_inputs_H(outG):
    gmax = np.concatenate([outG[c]["o_mx2"] for c in range(NCORE)], axis=1)
    ident = np.eye(128, dtype=np.float32)
    return [{"upD": outG[c]["o_up"], "gtD": outG[c]["o_gate"],
             "gmaxD": gmax, "identD": ident} for c in range(NCORE)]


def host_inputs_I(outH):
    gmax = np.concatenate([outH[c]["o_mx"] for c in range(NCORE)], axis=1)
    ident = np.eye(128, dtype=np.float32)
    return [{"t3D": outH[c]["o_t3"], "upqD": outH[c]["o_upq"],
             "gmaxD": gmax, "identD": ident} for c in range(NCORE)]


def host_inputs_J(ins, outG, outI):
    gmax = np.concatenate([outI[c]["o_mx"] for c in range(NCORE)], axis=1)
    ident = np.eye(128, dtype=np.float32)
    wd = _f32(ins["L1_ffn_w_down"])
    return [{"t4D": outI[c]["o_t4"], "x2pD": outG[c]["o_x2p"],
             "wdD": wd, "gmaxD": gmax, "identD": ident} for c in range(NCORE)]


def kernel(**inputs):
    cores = list(range(NCORE))

    def go(name, builder, maps):
        nc = _get_nc(name, builder)
        res = run_bass_kernel_spmd(nc, maps, core_ids=cores)
        return [dict(r) for r in res.results]

    outA = go("A", build_launch_A, host_inputs_A(inputs))
    outB = go("B", build_launch_B, host_inputs_B(inputs, outA))
    outC = go("C", build_launch_C, host_inputs_C(inputs, outB))
    outD = go("S", build_launch_scan, host_inputs_scan(inputs, outC))
    outE = go("S", build_launch_scan, host_inputs_scan(inputs, outC, outD))
    outG = go("G", build_launch_G, host_inputs_G(inputs, outB, outC, outD, outE))
    outH = go("H", build_launch_H, host_inputs_H(outG))
    outI = go("I", build_launch_I, host_inputs_I(outH))
    outJ = go("J", build_launch_J, host_inputs_J(inputs, outG, outI))
    out = np.concatenate([outJ[c]["o_out"] for c in cores], axis=0)
    return np.ascontiguousarray(out.reshape(4, 4096, 1024), dtype=np.float32)


def run_A(ins):
    nc = _get_nc("A", build_launch_A)
    maps = host_inputs_A(ins)
    res = run_bass_kernel_spmd(nc, maps, core_ids=list(range(NCORE)))
    return res.results


if __name__ == "__main__":
    build_launch_A()
    print("built A ok")


# version 04
# speedup vs baseline: 1.8388x; 1.0329x over previous
import numpy as np
import ml_dtypes
from contextlib import ExitStack

import concourse.bass as bass
import concourse.mybir as mybir
from concourse.bass_utils import run_bass_kernel_spmd
from concourse.bass_core_rust import IndirectOffsetOnAxis

F32 = mybir.dt.float32
BF16 = mybir.dt.bfloat16
I32 = mybir.dt.int32
ALU = mybir.AluOpType
AF = mybir.ActivationFunctionType
AX = mybir.AxisListType.X

MAGIC = float(np.float32(1.5 * 2 ** 23))
NCORE = 8
TOK = 2048
NT = 16384
D = 1024
LN_EPS = 1e-5
PAD = 512

ENGS = ("sync", "scalar", "vector", "tensor", "gpsimd")


class Prog:
    def __init__(self, nc):
        self.nc = nc
        self.calls = {e: [] for e in ENGS}
        self.cnt = {e: 0 for e in ENGS}
        self.dma_cnt = {e: 0 for e in ENGS}
        self.sems = ExitStack()
        self.sem = {e: self.sems.enter_context(nc.semaphore(f"s_{e}")) for e in ENGS}
        self.NDSEM = 6
        self.dsem = {e: [self.sems.enter_context(nc.semaphore(f"d_{e}{i}"))
                         for i in range(self.NDSEM)] for e in ENGS}
        self.duses = {e: [0] * self.NDSEM for e in ENGS}
        self.drr = {e: 0 for e in ENGS}
        self.last_w = {}
        self.readers = {}
        self.wm = {e: {} for e in ENGS}

    @staticmethod
    def _key(x):
        if isinstance(x, str):
            return x
        t = getattr(x, "tensor", x)
        return getattr(t, "name", str(t))

    def _emit_wait(self, eng, tok):
        kind, e2, val = tok
        if kind == "E":
            sem, name = self.sem[e2], ("E", e2)
        else:
            q, i = e2
            sem, name = self.dsem[q][i], ("D", q, i)
        if self.wm[eng].get(name, 0) >= val:
            return
        self.wm[eng][name] = val
        self.calls[eng].append(lambda E, s=sem, v=val: E.wait_ge(s, v))

    def _deps(self, eng, reads, writes):
        toks = set()
        for r in reads:
            t = self.last_w.get(self._key(r))
            if t is not None:
                toks.add(t)
        for w in writes:
            k = self._key(w)
            t = self.last_w.get(k)
            if t is not None:
                toks.add(t)
            toks.update(self.readers.get(k, ()))
        for t in sorted(toks):
            self._emit_wait(eng, t)

    def _post(self, reads, writes, tok):
        for w in writes:
            k = self._key(w)
            self.last_w[k] = tok
            self.readers[k] = []
        for r in reads:
            self.readers.setdefault(self._key(r), []).append(tok)

    def op(self, eng, fn, reads=(), writes=()):
        self._deps(eng, reads, writes)
        self.cnt[eng] += 1
        tok = ("E", eng, self.cnt[eng])
        sem = self.sem[eng]
        self.calls[eng].append(lambda E, fn=fn, sem=sem: fn(E).then_inc(sem, 1))
        self._post(reads, writes, tok)
        return tok

    def _dma_slot(self, eng):
        i = self.drr[eng] % self.NDSEM
        self.drr[eng] += 1
        if self.duses[eng][i]:
            self._emit_wait(eng, ("D", (eng, i), 16 * self.duses[eng][i]))
        self.duses[eng][i] += 1
        return i, self.dsem[eng][i], 16 * self.duses[eng][i]

    def dma(self, eng, out, in_, reads=(), writes=(), **kw):
        reads = list(reads) + [in_]
        writes = list(writes) + [out]
        self._deps(eng, reads, writes)
        i, sem, val = self._dma_slot(eng)
        tok = ("D", (eng, i), val)
        self.calls[eng].append(
            lambda E, out=out, in_=in_, sem=sem, kw=kw:
            E.dma_start(out=out, in_=in_, **kw).then_inc(sem, 16))
        self._post(reads, writes, tok)
        return tok

    def dma_ind(self, out, out_off, in_, in_off, bounds, reads=(), writes=()):
        eng = "gpsimd"
        extra = [x.ap for x in (out_off, in_off) if x is not None]
        reads = list(reads) + [in_] + extra
        writes = list(writes) + [out]
        self._deps(eng, reads, writes)
        i, sem, val = self._dma_slot(eng)
        tok = ("D", (eng, i), val)
        self.calls[eng].append(
            lambda E, out=out, in_=in_, sem=sem:
            E.indirect_dma_start(out, out_off, in_, in_off, bounds_check=bounds,
                                 oob_is_err=False).then_inc(sem, 16))
        self._post(reads, writes, tok)
        return tok

    def barrier(self):
        toks = []
        for e in ENGS:
            if self.cnt[e]:
                toks.append(("E", e, self.cnt[e]))
            for i in range(self.NDSEM):
                if self.duses[e][i]:
                    toks.append(("D", (e, i), 16 * self.duses[e][i]))
        for eng in ENGS:
            for t in toks:
                if t[0] == "E" and t[1] == eng:
                    continue
                self._emit_wait(eng, t)

    def finish(self):
        nc = self.nc
        for e in ENGS:
            if e != "sync" and self.cnt[e]:
                self.calls["sync"].append(
                    lambda E, s=self.sem[e], v=self.cnt[e]: E.wait_ge(s, v))
            for i in range(self.NDSEM):
                if self.duses[e][i]:
                    self.calls["sync"].append(
                        lambda E, s=self.dsem[e][i], v=16 * self.duses[e][i]:
                        E.wait_ge(s, v))
        with self.sems, nc.Block() as block:
            @block.sync
            def _(E):
                for c in self.calls["sync"]:
                    c(E)

            @block.scalar
            def _(E):
                for c in self.calls["scalar"]:
                    c(E)

            @block.vector
            def _(E):
                for c in self.calls["vector"]:
                    c(E)

            @block.tensor
            def _(E):
                for c in self.calls["tensor"]:
                    c(E)

            @block.gpsimd
            def _(E):
                for c in self.calls["gpsimd"]:
                    c(E)


class Ctx:
    def __init__(self):
        self.nc = bass.Bass("TRN2", target_bir_lowering=False)
        self.es = ExitStack()
        self.p = Prog(self.nc)
        self.banks = [self.es.enter_context(
            self.nc.psum_tensor(f"bank{i}", [128, 512], F32)) for i in range(8)]
        self._bank_i = 0
        self.ones1 = self.sb("ones1", [1, 128])
        self.onescol = self.sb("onescol", [128, 1])
        self.onescolb = self.sb("onescolb", [128, 1], BF16)
        self.p.op("vector", lambda E: E.memset(self.ones1[:], 1.0), writes=[self.ones1])
        self.p.op("vector", lambda E: E.memset(self.onescol[:], 1.0), writes=[self.onescol])
        self.p.op("vector", lambda E: E.memset(self.onescolb[:], 1.0), writes=[self.onescolb])

    def sb(self, name, shape, dtype=F32, es=None):
        return (es or self.es).enter_context(self.nc.sbuf_tensor(name, shape, dtype))

    def bank(self):
        n = len(self.rot) if hasattr(self, "rot") and self.rot else 8
        pool = self.rot if (hasattr(self, "rot") and self.rot) else self.banks
        b = pool[self._bank_i % n]
        self._bank_i += 1
        return b

    def bcast_row(self, row_ap, out_sb, n, out_off=0):
        p = self.p
        for j in range((n + 511) // 512):
            c0, c1 = j * 512, min(n, (j + 1) * 512)
            pt = self.bank()
            p.op("tensor", lambda E, pt=pt, c0=c0, c1=c1: E.matmul(
                pt[:, 0:c1 - c0], self.ones1[0:1, :], row_ap[0:1, c0:c1],
                start=True, stop=True), reads=[row_ap, self.ones1], writes=[pt])
            p.op("scalar", lambda E, pt=pt, c0=c0, c1=c1:
                 E.copy(out_sb[:, out_off + c0:out_off + c1], pt[:, 0:c1 - c0]),
                 reads=[pt], writes=[out_sb])

    def transpose(self, out_sb, in_ap, k, m, ident, out_off=0):
        p = self.p
        pt = self.bank()
        p.op("tensor", lambda E, pt=pt: E.transpose(pt[0:m, 0:k], in_ap, ident[0:k, 0:k]),
             reads=[in_ap, ident], writes=[pt])
        p.op("scalar", lambda E, pt=pt:
             E.copy(out_sb[0:m, out_off:out_off + k], pt[0:m, 0:k]),
             reads=[pt], writes=[out_sb])


def trig_consts(cx, es):
    c = {}
    c["pio2"] = cx.sb("c_pio2", [128, 1], es=es)
    c["neg1"] = cx.sb("c_neg1", [128, 1], es=es)
    cx.p.op("vector", lambda E: E.memset(c["pio2"][:], float(np.pi / 2)), writes=[c["pio2"]])
    cx.p.op("vector", lambda E: E.memset(c["neg1"][:], -1.0), writes=[c["neg1"]])
    return c


def trig_block(cx, phi, r, sin_o, cos_o, c):
    p = cx.p
    inv2pi = float(np.float32(1.0 / (2 * np.pi)))
    twopi = float(np.float32(2 * np.pi))
    p.op("vector", lambda E: E.tensor_scalar(r[:], phi[:], inv2pi, MAGIC, ALU.mult, ALU.add),
         reads=[phi], writes=[r])
    p.op("vector", lambda E: E.tensor_scalar(r[:], r[:], -MAGIC, -twopi, ALU.add, ALU.mult),
         reads=[r], writes=[r])
    p.op("vector", lambda E: E.tensor_add(r[:], r[:], phi[:]), reads=[r, phi], writes=[r])
    p.op("scalar", lambda E: E.activation(sin_o[:], r[:], AF.Sin), reads=[r], writes=[sin_o])
    p.op("scalar", lambda E: E.activation(cos_o[:], r[:], AF.Abs), reads=[r], writes=[cos_o])
    p.op("scalar", lambda E: E.activation(cos_o[:], cos_o[:], AF.Sin,
                                          bias=c["pio2"][:], scale=c["neg1"][:]),
         reads=[cos_o, c["pio2"], c["neg1"]], writes=[cos_o])


def build_launch_A():
    cx = Ctx()
    nc, p = cx.nc, cx.p

    xT = nc.dram_tensor("xT", [D, NT], F32, kind="ExternalInput")
    xTo = nc.dram_tensor("xTo", [D, TOK], F32, kind="ExternalInput")
    Wr16 = nc.dram_tensor("Wr16", [D, 16], F32, kind="ExternalInput")
    WomT = nc.dram_tensor("WomT", [D, D], F32, kind="ExternalInput")
    b_om = nc.dram_tensor("b_om", [128, 8], F32, kind="ExternalInput")
    lsc = nc.dram_tensor("lsc", [128, 8], F32, kind="ExternalInput")
    possc = nc.dram_tensor("possc", [1, TOK], F32, kind="ExternalInput")
    moeW1 = nc.dram_tensor("moeW1", [8, D, 4096], F32, kind="ExternalInput")
    moeW2 = nc.dram_tensor("moeW2", [8, 4096, D], F32, kind="ExternalInput")
    e9iota = nc.dram_tensor("e9iota", [1, 1152], F32, kind="ExternalInput")
    cmask = nc.dram_tensor("cmask", [1, 8], F32, kind="ExternalInput")
    bidx = nc.dram_tensor("bidx", [16, 3], F32, kind="ExternalInput")
    ridx = nc.dram_tensor("ridx", [2, 1], F32, kind="ExternalInput")
    e8colD = nc.dram_tensor("e8colD", [8, 1], F32, kind="ExternalInput")
    identD = nc.dram_tensor("identD", [128, 128], F32, kind="ExternalInput")

    o_phi = nc.dram_tensor("o_phi", [D, TOK], F32, kind="ExternalOutput")
    o_phic = nc.dram_tensor("o_phic", [128, 8], F32, kind="ExternalOutput")
    o_moe = nc.dram_tensor("o_moe", [TOK, D], F32, kind="ExternalOutput")
    o_stat = nc.dram_tensor("o_stat", [2, TOK], F32, kind="ExternalOutput")
    o_rows = nc.dram_tensor("o_rows", [3, NT], F32, kind="ExternalOutput")
    o_cmc = nc.dram_tensor("o_cmc", [128, 16], F32, kind="ExternalOutput")
    scr_row = nc.dram_tensor("scr_row", [5, NT], F32, kind="Internal")
    scr_xe = nc.dram_tensor("scr_xe", [8 * PAD, D], F32, kind="Internal")
    scr_ye = nc.dram_tensor("scr_ye", [8 * PAD, D], F32, kind="Internal")
    scr_b = nc.dram_tensor("scr_b", [8, 1], F32, kind="Internal")

    ident = cx.sb("ident", [128, 128])
    p.dma("sync", ident[:], identD.ap())
    rowview = scr_row.ap().rearrange("r (b j) -> (r b) j", j=128)
    rowview8 = scr_row.ap().rearrange("r (c t) -> (r c) t", t=TOK)

    eT = cx.sb("eT", [128, 16])
    pT = cx.sb("pT", [128, 16])
    sidx_i = cx.sb("sidx_i", [128, 16], I32)
    gidx_i = cx.sb("gidx_i", [128, 16], I32)
    baserow = cx.sb("baserow", [1, 8])
    xn0 = [cx.sb(f"xn0_{k}", [128, TOK]) for k in range(8)]

    esR = ExitStack()
    xt = [cx.sb(f"xt{k}", [128, 512], es=esR) for k in range(8)]
    sq2 = [cx.sb(f"sq{k}", [128, 512], es=esR) for k in range(2)]
    wr16 = cx.sb("wr16", [128, 16 * 8], es=esR)
    logit = cx.sb("logit", [128, 1152], es=esR)
    rb1 = cx.sb("rb1", [1, 512], es=esR)
    rb2 = cx.sb("rb2", [1, 512], es=esR)
    for k in range(8):
        p.dma("scalar", wr16[:, 16 * k:16 * (k + 1)], Wr16.ap()[128 * k:128 * (k + 1), :])
    for ch in range(32):
        for k in range(8):
            p.dma("sync", xt[k][:], xT.ap()[128 * k:128 * (k + 1), 512 * ch:512 * (ch + 1)])
        pS, pQ = cx.bank(), cx.bank()
        for k in range(8):
            sq = sq2[k % 2]
            p.op("scalar", lambda E, k=k, sq=sq: E.activation(sq[:], xt[k][:], AF.Square),
                 reads=[xt[k]], writes=[sq])
            p.op("tensor", lambda E, k=k, pS=pS: E.matmul(
                pS[0:1, :], cx.onescol[:], xt[k][:], start=(k == 0), stop=(k == 7)),
                reads=[xt[k], cx.onescol], writes=[pS])
            p.op("tensor", lambda E, sq=sq, k=k, pQ=pQ: E.matmul(
                pQ[0:1, :], cx.onescol[:], sq[:], start=(k == 0), stop=(k == 7)),
                reads=[sq, cx.onescol], writes=[pQ])
        p.op("vector", lambda E, pS=pS: E.tensor_copy(rb1[:], pS[0:1, :]),
             reads=[pS], writes=[rb1])
        p.op("vector", lambda E, pQ=pQ: E.tensor_copy(rb2[:], pQ[0:1, :]),
             reads=[pQ], writes=[rb2])
        p.dma("sync", scr_row.ap()[3:4, 512 * ch:512 * (ch + 1)], rb1[:])
        p.dma("sync", scr_row.ap()[4:5, 512 * ch:512 * (ch + 1)], rb2[:])
        for w in range(4):
            pL = cx.bank()
            for k in range(8):
                p.op("tensor", lambda E, k=k, w=w, pL=pL: E.matmul(
                    pL[:, 0:16], xt[k][:, 128 * w:128 * (w + 1)],
                    wr16[:, 16 * k:16 * (k + 1)], start=(k == 0), stop=(k == 7)),
                    reads=[xt[k], wr16], writes=[pL])
            W = ch * 4 + w
            p.op("scalar", lambda E, W=W, pL=pL: E.copy(logit[:, 9 * W:9 * W + 9], pL[:, 0:9]),
                 reads=[pL], writes=[logit])

    muw = cx.sb("muw", [128, 128], es=esR)
    rsw = cx.sb("rsw", [128, 128], es=esR)
    vtmp = cx.sb("vtmp", [128, 128], es=esR)
    blk = cx.sb("blk", [128, 128], es=esR)
    p.dma("sync", blk[:], rowview[3 * 128:4 * 128, :])
    cx.transpose(muw, blk[:], 128, 128, ident)
    p.dma("sync", blk[:], rowview[4 * 128:5 * 128, :])
    cx.transpose(vtmp, blk[:], 128, 128, ident)
    p.op("vector", lambda E: E.tensor_scalar(muw[:], muw[:], 1.0 / D, None, ALU.mult),
         reads=[muw], writes=[muw])
    p.op("vector", lambda E: E.tensor_scalar(vtmp[:], vtmp[:], 1.0 / D, None, ALU.mult),
         reads=[vtmp], writes=[vtmp])
    p.op("scalar", lambda E: E.activation(rsw[:], muw[:], AF.Square),
         reads=[muw], writes=[rsw])
    p.op("vector", lambda E: E.tensor_sub(vtmp[:], vtmp[:], rsw[:]),
         reads=[vtmp, rsw], writes=[vtmp])
    p.op("vector", lambda E: E.tensor_scalar(vtmp[:], vtmp[:], float(LN_EPS), None, ALU.add),
         reads=[vtmp], writes=[vtmp])
    varw = cx.sb("varw", [128, 128], es=esR)
    p.op("vector", lambda E: E.tensor_scalar(varw[:], vtmp[:], -float(LN_EPS), None, ALU.add),
         reads=[vtmp], writes=[varw])
    p.op("scalar", lambda E: E.activation(vtmp[:], vtmp[:], AF.Sqrt),
         reads=[vtmp], writes=[vtmp])
    p.op("vector", lambda E: E.reciprocal(rsw[:], vtmp[:]), reads=[vtmp], writes=[rsw])
    sxn2w = cx.sb("sxn2w", [128, 128], es=esR)
    p.op("scalar", lambda E: E.activation(sxn2w[:], rsw[:], AF.Square),
         reads=[rsw], writes=[sxn2w])
    p.op("vector", lambda E: E.tensor_mul(sxn2w[:], sxn2w[:], varw[:]),
         reads=[sxn2w, varw], writes=[sxn2w])
    p.op("vector", lambda E: E.tensor_scalar(sxn2w[:], sxn2w[:], float(D), None, ALU.mult),
         reads=[sxn2w], writes=[sxn2w])

    cw = cx.sb("cw", [1, 16], es=esR)
    cwrep = cx.sb("cwrep", [128, 16], es=esR)
    t1 = cx.sb("t1", [128, 1152], es=esR)
    pC = cx.bank()
    for k in range(8):
        p.op("tensor", lambda E, k=k, pC=pC: E.matmul(
            pC[0:1, 0:16], cx.onescol[:], wr16[:, 16 * k:16 * (k + 1)],
            start=(k == 0), stop=(k == 7)), reads=[wr16, cx.onescol], writes=[pC])
    p.op("vector", lambda E, pC=pC: E.tensor_copy(cw[:], pC[0:1, 0:16]),
         reads=[pC], writes=[cw])
    cx.bcast_row(cw, cwrep, 16)

    def we(x):
        return x.rearrange("p w -> p w ()").broadcast_to((128, 128, 9))

    L3 = logit[:].rearrange("p (w e) -> p w e", e=9)
    T3 = t1[:].rearrange("p (w e) -> p w e", e=9)
    p.op("vector", lambda E: E.tensor_tensor(
        t1[:], we(muw[:]),
        cwrep[:, 0:9].rearrange("p e -> p () e").broadcast_to((128, 128, 9)), ALU.mult),
        reads=[muw, cwrep], writes=[t1])
    p.op("vector", lambda E: E.tensor_sub(logit[:], logit[:], t1[:]),
         reads=[logit, t1], writes=[logit])
    p.op("vector", lambda E: E.tensor_tensor(logit[:], logit[:], we(rsw[:]), ALU.mult),
         reads=[logit, rsw], writes=[logit])

    pmx = cx.sb("pmx", [128, 128], es=esR)
    ptk = cx.sb("ptk", [128, 128], es=esR)
    eix = cx.sb("eix", [128, 128], es=esR)
    e9 = cx.sb("e9", [128, 1152], es=esR)
    e9row = cx.sb("e9row", [1, 1152], es=esR)
    p.dma("scalar", e9row[:], e9iota.ap())
    cx.bcast_row(e9row, e9, 1152)
    p.op("vector", lambda E: E.tensor_reduce(pmx[:], L3[:, :, 0:8], AX, ALU.max),
         reads=[logit], writes=[pmx])
    p.op("vector", lambda E: E.tensor_tensor(t1[:], logit[:], we(pmx[:]), ALU.subtract),
         reads=[logit, pmx], writes=[t1])
    p.op("scalar", lambda E: E.activation(t1[:], t1[:], AF.Exp), reads=[t1], writes=[t1])
    p.op("vector", lambda E: E.tensor_reduce(ptk[:], T3[:, :, 0:8], AX, ALU.add),
         reads=[t1], writes=[ptk])
    p.op("vector", lambda E: E.reciprocal(ptk[:], ptk[:]), reads=[ptk], writes=[ptk])
    p.op("vector", lambda E: E.tensor_tensor(t1[:], logit[:], we(pmx[:]), ALU.is_ge),
         reads=[logit, pmx], writes=[t1])
    p.op("vector", lambda E: E.tensor_mul(t1[:], t1[:], e9[:]), reads=[t1, e9], writes=[t1])
    p.op("vector", lambda E: E.tensor_reduce(eix[:], T3[:, :, 0:8], AX, ALU.add),
         reads=[t1], writes=[eix])

    for rid, t_src in ((0, eix), (1, ptk), (3, muw), (4, rsw)):
        cx.transpose(blk, t_src[:], 128, 128, ident)
        p.dma("sync", rowview[rid * 128:(rid + 1) * 128, :], blk[:])
        if rid == 3:
            p.dma("sync", o_rows.ap()[0:1, :].rearrange("a (w q) -> (a w) q", q=128), blk[:])
        if rid == 4:
            p.dma("sync", o_rows.ap()[1:2, :].rearrange("a (w q) -> (a w) q", q=128), blk[:])
    cx.transpose(blk, sxn2w[:], 128, 128, ident)
    p.dma("sync", o_rows.ap()[2:3, :].rearrange("a (w q) -> (a w) q", q=128), blk[:])

    SEG = 4096
    erow = cx.sb("erow", [1, SEG], es=esR)
    e8 = cx.sb("e8", [8, SEG], es=esR)
    oh = cx.sb("oh", [8, SEG], es=esR)
    rk8 = cx.sb("rk8", [8, SEG], es=esR)
    e8col = cx.sb("e8col", [8, 1], es=esR)
    ones8 = cx.sb("ones8", [8, 1], es=esR)
    carry8 = cx.sb("carry8", [8, 1], es=esR)
    rrow = cx.sb("rrow", [1, SEG], es=esR)
    base8 = cx.sb("base8", [8, 8], es=esR)
    p.op("vector", lambda E: E.memset(ones8[:], 1.0), writes=[ones8])
    p.op("vector", lambda E: E.memset(carry8[:], 0.0), writes=[carry8])
    p.dma("scalar", e8col[:], e8colD.ap())
    for sg in range(NT // SEG):
        p.dma("scalar", erow[:], scr_row.ap()[0:1, SEG * sg:SEG * (sg + 1)])
        for j in range(SEG // 512):
            pt = cx.bank()
            p.op("tensor", lambda E, j=j, pt=pt: E.matmul(
                pt[0:8, :], cx.ones1[0:1, 0:8], erow[0:1, 512 * j:512 * (j + 1)],
                start=True, stop=True), reads=[erow, cx.ones1], writes=[pt])
            p.op("scalar", lambda E, j=j, pt=pt: E.copy(e8[:, 512 * j:512 * (j + 1)], pt[0:8, :]),
                 reads=[pt], writes=[e8])
        p.op("vector", lambda E: E.tensor_scalar(oh[:], e8[:], e8col[:], None, ALU.is_equal),
             reads=[e8, e8col], writes=[oh])
        p.op("vector", lambda E: E.tensor_tensor_scan(
            rk8[:], ones8[:].broadcast_to((8, SEG)), oh[:], carry8[:], ALU.mult, ALU.add),
            reads=[oh, ones8, carry8], writes=[rk8])
        p.op("vector", lambda E: E.tensor_copy(carry8[:], rk8[:, SEG - 1:SEG]),
             reads=[rk8], writes=[carry8])
        p.op("vector", lambda E: E.tensor_sub(rk8[:], rk8[:], oh[:]),
             reads=[rk8, oh], writes=[rk8])
        p.op("vector", lambda E, sg=sg: E.tensor_copy(base8[:, 2 * sg:2 * sg + 1], rk8[:, 0:1]),
             reads=[rk8], writes=[base8])
        p.op("vector", lambda E, sg=sg: E.tensor_copy(
            base8[:, 2 * sg + 1:2 * sg + 2], rk8[:, TOK:TOK + 1]),
            reads=[rk8], writes=[base8])
        p.op("vector", lambda E: E.tensor_mul(oh[:], oh[:], rk8[:]), reads=[oh, rk8], writes=[oh])
        for j in range(SEG // 512):
            pt = cx.bank()
            p.op("tensor", lambda E, j=j, pt=pt: E.matmul(
                pt[0:1, :], ones8[:], oh[:, 512 * j:512 * (j + 1)],
                start=True, stop=True), reads=[oh, ones8], writes=[pt])
            p.op("vector", lambda E, j=j, pt=pt: E.tensor_copy(
                rrow[0:1, 512 * j:512 * (j + 1)], pt[0:1, :]), reads=[pt], writes=[rrow])
        p.dma("sync", scr_row.ap()[2:3, SEG * sg:SEG * (sg + 1)], rrow[:])

    cmk = cx.sb("cmk", [1, 8], es=esR)
    cmk8 = cx.sb("cmk8", [8, 8], es=esR)
    bsel = cx.sb("bsel", [8, 1], es=esR)
    p.dma("scalar", cmk[:], cmask.ap())
    pt = cx.bank()
    p.op("tensor", lambda E, pt=pt: E.matmul(pt[0:8, 0:8], cx.ones1[0:1, 0:8], cmk[0:1, :],
                                             start=True, stop=True),
         reads=[cmk, cx.ones1], writes=[pt])
    p.op("scalar", lambda E, pt=pt: E.copy(cmk8[:], pt[0:8, 0:8]), reads=[pt], writes=[cmk8])
    p.op("vector", lambda E: E.tensor_mul(cmk8[:], cmk8[:], base8[:]),
         reads=[cmk8, base8], writes=[cmk8])
    p.op("vector", lambda E: E.tensor_reduce(bsel[:], cmk8[:], AX, ALU.add),
         reads=[cmk8], writes=[bsel])
    p.dma("sync", scr_b.ap(), bsel[:])
    p.dma("sync", baserow[:], scr_b.ap().rearrange("a b -> b a"))
    esR.close()
    p.barrier()

    esM = ExitStack()
    own3 = [cx.sb(f"own{i}", [16, 128], es=esM) for i in range(3)]
    bidx_sb = cx.sb("bidx_sb", [16, 3], es=esM)
    bidx_i = cx.sb("bidx_i", [16, 3], I32, es=esM)
    p.dma("scalar", bidx_sb[:], bidx.ap())
    p.op("gpsimd", lambda E: E.tensor_copy(bidx_i[:], bidx_sb[:]),
         reads=[bidx_sb], writes=[bidx_i])
    rT = cx.sb("rT", [128, 16], es=esM)
    for i, dst in ((0, eT), (1, pT), (2, rT)):
        p.dma_ind(own3[i][:], None, rowview, IndirectOffsetOnAxis(bidx_i[:, i:i + 1], 0), 5 * 128 - 1)
        cx.transpose(dst, own3[i][:], 16, 128, ident)

    baserep = cx.sb("baserep", [128, 8], es=esM)
    cx.bcast_row(baserow, baserep, 8)
    msk = cx.sb("msk", [128, 16], es=esM)
    btok = cx.sb("btok", [128, 16], es=esM)
    loc = cx.sb("loc", [128, 16], es=esM)
    sidx_f = cx.sb("sidx_f", [128, 16], es=esM)
    gidx_f = cx.sb("gidx_f", [128, 16], es=esM)
    p.op("vector", lambda E: E.memset(btok[:], 0.0), writes=[btok])
    for e in range(8):
        p.op("vector", lambda E, e=e: E.tensor_scalar(msk[:], eT[:], float(e), None, ALU.is_equal),
             reads=[eT], writes=[msk])
        p.op("vector", lambda E, e=e: E.scalar_tensor_tensor(
            btok[:], msk[:], baserep[:, e:e + 1], btok[:], ALU.mult, ALU.add),
            reads=[msk, baserep, btok], writes=[btok])
    p.op("vector", lambda E: E.tensor_sub(loc[:], rT[:], btok[:]), reads=[rT, btok], writes=[loc])
    p.op("vector", lambda E: E.tensor_scalar(sidx_f[:], eT[:], float(PAD), None, ALU.mult),
         reads=[eT], writes=[sidx_f])
    p.op("vector", lambda E: E.tensor_add(sidx_f[:], sidx_f[:], loc[:]),
         reads=[sidx_f, loc], writes=[sidx_f])
    p.op("vector", lambda E: E.tensor_scalar(msk[:], loc[:], float(PAD), None, ALU.is_ge),
         reads=[loc], writes=[msk])
    p.op("vector", lambda E: E.scalar_tensor_tensor(
        sidx_f[:], msk[:], 1.0e6, sidx_f[:], ALU.mult, ALU.add),
        reads=[msk, sidx_f], writes=[sidx_f])
    p.op("vector", lambda E: E.tensor_scalar(msk[:], rT[:], 2048.0, None, ALU.is_ge),
         reads=[rT], writes=[msk])
    p.op("vector", lambda E: E.scalar_tensor_tensor(
        gidx_f[:], msk[:], 1.0e6, sidx_f[:], ALU.mult, ALU.add),
        reads=[msk, sidx_f], writes=[gidx_f])
    p.op("gpsimd", lambda E: E.tensor_copy(sidx_i[:], sidx_f[:]), reads=[sidx_f], writes=[sidx_i])
    p.op("gpsimd", lambda E: E.tensor_copy(gidx_i[:], gidx_f[:]), reads=[gidx_f], writes=[gidx_i])

    ridx_sb = cx.sb("ridx_sb", [2, 1], es=esM)
    ridx_i = cx.sb("ridx_i", [2, 1], I32, es=esM)
    mu_own = cx.sb("mu_own", [1, TOK], es=esM)
    rs_own = cx.sb("rs_own", [1, TOK], es=esM)
    mu_rep = cx.sb("mu_rep", [128, TOK], es=esM)
    rs_rep = cx.sb("rs_rep", [128, TOK], es=esM)
    p.dma("scalar", ridx_sb[:], ridx.ap())
    p.op("gpsimd", lambda E: E.tensor_copy(ridx_i[:], ridx_sb[:]), reads=[ridx_sb], writes=[ridx_i])
    stat2 = cx.sb("stat2", [2, TOK], es=esM)
    p.dma_ind(stat2[:], None, rowview8, IndirectOffsetOnAxis(ridx_i[:], 0), 5 * 8 - 1)
    p.dma("sync", o_stat.ap(), stat2[:])
    p.dma("scalar", mu_own[:], stat2[0:1, :], reads=[stat2])
    p.dma("scalar", rs_own[:], stat2[1:2, :], reads=[stat2])
    cx.bcast_row(mu_own, mu_rep, TOK)
    cx.bcast_row(rs_own, rs_rep, TOK)
    for k in range(8):
        p.dma("sync", xn0[k][:], xTo.ap()[128 * k:128 * (k + 1), :])
        p.op("vector", lambda E, k=k: E.tensor_sub(xn0[k][:], xn0[k][:], mu_rep[:]),
             reads=[xn0[k], mu_rep], writes=[xn0[k]])
        p.op("vector", lambda E, k=k: E.tensor_mul(xn0[k][:], xn0[k][:], rs_rep[:]),
             reads=[xn0[k], rs_rep], writes=[xn0[k]])
    esM.close()
    p.barrier()

    esS = ExitStack()
    zero_t = cx.sb("zero_t", [128, D], es=esS)
    rowt = cx.sb("rowt", [128, D], es=esS)
    p.op("vector", lambda E: E.memset(zero_t[:], 0.0), writes=[zero_t])
    for j in range(32):
        p.dma("scalar", scr_xe.ap()[128 * j:128 * (j + 1), :], zero_t[:])
    for j in range(16):
        for k in range(8):
            cx.transpose(rowt, xn0[k][:, 128 * j:128 * (j + 1)], 128, 128, ident,
                         out_off=128 * k)
        p.dma_ind(scr_xe.ap(), IndirectOffsetOnAxis(sidx_i[:, j:j + 1], 0),
                  rowt[:], None, 8 * PAD - 1)
    esS.close()
    p.barrier()

    esO = ExitStack()
    womk = [cx.sb(f"womk{k}", [128, D], es=esO) for k in range(8)]
    b_om_t = cx.sb("b_om_t", [128, 8], es=esO)
    lsce = cx.sb("lsce", [128, 8], es=esO)
    possr = cx.sb("possr", [1, TOK], es=esO)
    possc_r = cx.sb("possc_r", [128, TOK], es=esO)
    onesT = cx.sb("onesT", [128, TOK], es=esO)
    om_t = cx.sb("om_t", [128, TOK], es=esO)
    phi_t = cx.sb("phi_t", [128, TOK], es=esO)
    phic_t = cx.sb("phic_t", [128, 8], es=esO)
    cmc_t = cx.sb("cmc_t", [128, 16], es=esO)
    trg1 = cx.sb("trg1", [128, TOK], es=esO)
    trg2 = cx.sb("trg2", [128, TOK], es=esO)
    consts = trig_consts(cx, esO)
    p.dma("scalar", b_om_t[:], b_om.ap())
    p.dma("scalar", lsce[:], lsc.ap())
    p.op("scalar", lambda E: E.activation(lsce[:], lsce[:], AF.Exp), reads=[lsce], writes=[lsce])
    p.dma("scalar", possr[:], possc.ap())
    cx.bcast_row(possr, possc_r, TOK)
    p.op("vector", lambda E: E.memset(onesT[:], 1.0), writes=[onesT])
    for k in range(8):
        p.dma("sync", womk[k][:], WomT.ap()[128 * k:128 * (k + 1), :])
    for et in range(8):
        for ch in range(4):
            pt = cx.bank()
            for k in range(8):
                p.op("tensor", lambda E, k=k, et=et, ch=ch, pt=pt: E.matmul(
                    pt[:], womk[k][:, 128 * et:128 * (et + 1)],
                    xn0[k][:, 512 * ch:512 * (ch + 1)], start=(k == 0), stop=(k == 7)),
                    reads=[womk[k], xn0[k]], writes=[pt])
            p.op("vector", lambda E, et=et, ch=ch, pt=pt: E.tensor_scalar(
                om_t[:, 512 * ch:512 * (ch + 1)], pt[:], b_om_t[:, et:et + 1],
                lsce[:, et:et + 1], ALU.add, ALU.mult),
                reads=[pt, b_om_t, lsce], writes=[om_t])
        p.op("vector", lambda E, et=et: E.tensor_mul(om_t[:], om_t[:], possc_r[:]),
             reads=[om_t, possc_r], writes=[om_t])
        p.op("vector", lambda E, et=et: E.tensor_tensor_scan(
            phi_t[:], onesT[:], om_t[:], 0.0, ALU.mult, ALU.add),
            reads=[om_t, onesT], writes=[phi_t])
        p.op("vector", lambda E, et=et: E.tensor_copy(phic_t[:, et:et + 1], phi_t[:, TOK - 1:TOK]),
             reads=[phi_t], writes=[phic_t])
        p.dma("sync", o_phi.ap()[128 * et:128 * (et + 1), :], phi_t[:])
        trig_block(cx, phi_t, om_t, trg1, trg2, consts)
        p.op("vector", lambda E, et=et: E.tensor_mul(trg2[:], trg2[:], xn0[et][:]),
             reads=[trg2, xn0[et]], writes=[trg2])
        p.op("vector", lambda E, et=et: E.tensor_mul(trg1[:], trg1[:], xn0[et][:]),
             reads=[trg1, xn0[et]], writes=[trg1])
        p.op("vector", lambda E, et=et: E.tensor_reduce(
            cmc_t[:, et:et + 1], trg2[:], AX, ALU.add), reads=[trg2], writes=[cmc_t])
        p.op("vector", lambda E, et=et: E.tensor_reduce(
            cmc_t[:, 8 + et:8 + et + 1], trg1[:], AX, ALU.add), reads=[trg1], writes=[cmc_t])
    p.dma("sync", o_phic.ap(), phic_t[:])
    p.dma("sync", o_cmc.ap(), cmc_t[:])
    esO.close()
    p.barrier()

    esE = ExitStack()
    mw = [cx.sb(f"mw{k}", [128, 1024], es=esE) for k in range(8)]
    w2b = [cx.sb(f"w2b{k}", [128, 1024], BF16, es=esE) for k in range(2)]
    mxeT = [cx.sb(f"mxeT{k}", [128, PAD], es=esE) for k in range(8)]
    mxb = [cx.sb(f"mxb{k}", [128, PAD], BF16, es=esE) for k in range(8)]
    mh = [cx.sb(f"mh{f}", [128, PAD], BF16, es=esE) for f in range(32)]
    myeo = [cx.sb(f"myeo{o}", [128, PAD], es=esE) for o in range(8)]
    rowe = cx.sb("rowe", [128, D], es=esE)
    for e in range(8):
        for j in range(4):
            p.dma("sync", rowe[:], scr_xe.ap()[e * PAD + 128 * j:e * PAD + 128 * (j + 1), :])
            for k in range(8):
                cx.transpose(mxeT[k], rowe[:, 128 * k:128 * (k + 1)], 128, 128, ident,
                             out_off=128 * j)
        for k in range(8):
            p.op("vector", lambda E, k=k: E.tensor_copy(mxb[k][:], mxeT[k][:]),
                 reads=[mxeT[k]], writes=[mxb[k]])
        for fb in range(4):
            for k in range(8):
                p.dma("gpsimd", mw[k][:],
                      moeW1.ap()[e, 128 * k:128 * (k + 1), 1024 * fb:1024 * (fb + 1)])
            for k in range(8):
                for fi in range(8):
                    bk = cx.banks[fi]
                    p.op("tensor", lambda E, k=k, fi=fi, bk=bk: E.matmul(
                        bk[:], mw[k][:, 128 * fi:128 * (fi + 1)], mxeT[k][:],
                        start=(k == 0), stop=(k == 7)),
                        reads=[mw[k], mxeT[k]], writes=[bk])
            for fi in range(8):
                f = fb * 8 + fi
                bk = cx.banks[fi]
                p.op("scalar", lambda E, f=f, bk=bk: E.activation(mh[f][:], bk[:], AF.Relu),
                     reads=[bk], writes=[mh[f]])
        for f in range(32):
            p.dma("gpsimd", w2b[f % 2][:], moeW2.ap()[e, 128 * f:128 * (f + 1), :])
            for oi in range(8):
                bk = cx.banks[oi]
                p.op("tensor", lambda E, f=f, oi=oi, bk=bk: E.matmul(
                    bk[:], w2b[f % 2][:, 128 * oi:128 * (oi + 1)], mh[f][:],
                    start=(f == 0), stop=(f == 31)),
                    reads=[w2b[f % 2], mh[f]], writes=[bk])
        for oi in range(8):
            bk = cx.banks[oi]
            p.op("vector", lambda E, oi=oi, bk=bk: E.tensor_copy(myeo[oi][:], bk[:]),
                 reads=[bk], writes=[myeo[oi]])
        for cb in range(4):
            for oi in range(8):
                cx.transpose(rowe, myeo[oi][:, 128 * cb:128 * (cb + 1)], 128, 128, ident,
                             out_off=128 * oi)
            p.dma("sync", scr_ye.ap()[e * PAD + 128 * cb:e * PAD + 128 * (cb + 1), :], rowe[:])
    esE.close()
    p.barrier()

    esC = ExitStack()
    rowc = cx.sb("rowc", [128, D], es=esC)
    for j in range(16):
        p.op("vector", lambda E: E.memset(rowc[:], 0.0), reads=[], writes=[rowc])
        p.dma_ind(rowc[:], None, scr_ye.ap(), IndirectOffsetOnAxis(gidx_i[:, j:j + 1], 0),
                  8 * PAD - 1)
        p.op("vector", lambda E, j=j: E.tensor_scalar(
            rowc[:], rowc[:], pT[:, j:j + 1], None, ALU.mult),
            reads=[rowc, pT], writes=[rowc])
        p.dma("sync", o_moe.ap()[128 * j:128 * (j + 1), :], rowc[:])
    esC.close()
    p.barrier()

    p.finish()
    return nc



def build_launch_B():
    cx = Ctx()
    nc, p = cx.nc, cx.p
    CH = 512
    NCH = TOK // CH

    xTo = nc.dram_tensor("xTo", [D, TOK], F32, kind="ExternalInput")
    stato = nc.dram_tensor("stato", [2, TOK], F32, kind="ExternalInput")
    phiD = nc.dram_tensor("phiD", [D, TOK], F32, kind="ExternalInput")
    phicI = nc.dram_tensor("phicI", [128, 8], F32, kind="ExternalInput")
    cmcI = nc.dram_tensor("cmcI", [128, 16], F32, kind="ExternalInput")
    sxn2o = nc.dram_tensor("sxn2o", [1, TOK], F32, kind="ExternalInput")
    moeT = nc.dram_tensor("moeT", [D, TOK], F32, kind="ExternalInput")
    posinv = nc.dram_tensor("posinv", [1, TOK], F32, kind="ExternalInput")
    W1TD = nc.dram_tensor("W1TD", [4096, 2048], F32, kind="ExternalInput")
    g4 = nc.dram_tensor("g4", [128, 32], F32, kind="ExternalInput")
    b4 = nc.dram_tensor("b4", [128, 32], F32, kind="ExternalInput")
    b1r = nc.dram_tensor("b1r", [1, 2048], F32, kind="ExternalInput")
    W2TD = nc.dram_tensor("W2TD", [2048, 1024], F32, kind="ExternalInput")
    b2c = nc.dram_tensor("b2c", [128, 8], F32, kind="ExternalInput")
    gb1 = nc.dram_tensor("gb1", [128, 16], F32, kind="ExternalInput")
    identD = nc.dram_tensor("identD", [128, 128], F32, kind="ExternalInput")

    o_x1 = nc.dram_tensor("o_x1", [D, TOK], F32, kind="ExternalOutput")
    o_xn1 = nc.dram_tensor("o_xn1", [D, TOK], F32, kind="ExternalOutput")
    o_hln = nc.dram_tensor("o_hln", [D, TOK], F32, kind="ExternalOutput")
    o_mx = nc.dram_tensor("o_mx", [1, 1], F32, kind="ExternalOutput")
    scr = nc.dram_tensor("scrB", [2, 2048], F32, kind="Internal")

    ident = cx.sb("ident", [128, 128])
    p.dma("sync", ident[:], identD.ap())
    consts = trig_consts(cx, cx.es)

    w2r = [cx.sb(f"w2r{k}", [128, 1024], BF16) for k in range(16)]
    for k in range(16):
        p.dma("gpsimd", w2r[k][:], W2TD.ap()[128 * k:128 * (k + 1), :])
    mu_rep = cx.sb("mu_rep", [128, TOK])
    rs_rep = cx.sb("rs_rep", [128, TOK])
    pos_rep = cx.sb("pos_rep", [128, TOK])
    rowT = cx.sb("rowT", [1, TOK])
    p.dma("scalar", rowT[:], stato.ap()[0:1, :])
    cx.bcast_row(rowT, mu_rep, TOK)
    p.dma("scalar", rowT[:], stato.ap()[1:2, :], reads=[mu_rep])
    cx.bcast_row(rowT, rs_rep, TOK)
    p.dma("scalar", rowT[:], posinv.ap(), reads=[rs_rep])
    cx.bcast_row(rowT, pos_rep, TOK)
    g4t = cx.sb("g4t", [128, 32])
    b4t = cx.sb("b4t", [128, 32])
    b2t = cx.sb("b2t", [128, 8])
    gb1t = cx.sb("gb1t", [128, 16])
    phict = cx.sb("phict", [128, 8])
    cmct = cx.sb("cmct", [128, 16])
    p.dma("scalar", g4t[:], g4.ap())
    p.dma("scalar", b4t[:], b4.ap())
    p.dma("scalar", b2t[:], b2c.ap())
    p.dma("scalar", gb1t[:], gb1.ap())
    p.dma("scalar", phict[:], phicI.ap())
    p.dma("scalar", cmct[:], cmcI.ap())
    G1c = cx.sb("G1c", [128, 16])
    Bvc = cx.sb("Bvc", [128, 16])
    mxacc = cx.sb("mxacc", [128, 1])

    esG = ExitStack()
    w1kt = [cx.sb(f"w1kt{i}", [128, 2048], es=esG) for i in range(2)]
    w1s = cx.sb("w1s", [128, 2048], es=esG)
    growp = [cx.banks[i] for i in range(4)]
    browp = [cx.banks[4 + i] for i in range(4)]
    for k in range(32):
        wk = w1kt[k % 2]
        p.dma("sync" if k % 2 == 0 else "scalar", wk[:],
              W1TD.ap()[128 * k:128 * (k + 1), :])
        p.op("vector", lambda E, k=k, wk=wk: E.tensor_scalar(
            w1s[:], wk[:], g4t[:, k:k + 1], None, ALU.mult),
            reads=[wk, g4t], writes=[w1s])
        for j in range(4):
            p.op("tensor", lambda E, k=k, j=j: E.matmul(
                growp[j][0:1, :], cx.onescol[:], w1s[:, 512 * j:512 * (j + 1)],
                start=(k == 0), stop=(k == 31)),
                reads=[w1s, cx.onescol], writes=[growp[j]])
        p.op("vector", lambda E, k=k, wk=wk: E.tensor_scalar(
            w1s[:], wk[:], b4t[:, k:k + 1], None, ALU.mult),
            reads=[wk, b4t], writes=[w1s])
        for j in range(4):
            p.op("tensor", lambda E, k=k, j=j: E.matmul(
                browp[j][0:1, :], cx.onescol[:], w1s[:, 512 * j:512 * (j + 1)],
                start=(k == 0), stop=(k == 31)),
                reads=[w1s, cx.onescol], writes=[browp[j]])
    grow = cx.sb("grow", [1, 2048], es=esG)
    brow = cx.sb("brow", [1, 2048], es=esG)
    b1row = cx.sb("b1row", [1, 2048], es=esG)
    p.dma("scalar", b1row[:], b1r.ap())
    for j in range(4):
        p.op("vector", lambda E, j=j: E.tensor_copy(
            grow[0:1, 512 * j:512 * (j + 1)], growp[j][0:1, :]),
            reads=[growp[j]], writes=[grow])
        p.op("vector", lambda E, j=j: E.tensor_copy(
            brow[0:1, 512 * j:512 * (j + 1)], browp[j][0:1, :]),
            reads=[browp[j]], writes=[brow])
    p.op("vector", lambda E: E.tensor_add(brow[:], brow[:], b1row[:]),
         reads=[brow, b1row], writes=[brow])
    blkg = cx.sb("blkg", [16, 128], es=esG)
    p.dma("sync", scr.ap()[0:1, :], grow[:])
    p.dma("sync", scr.ap()[1:2, :], brow[:])
    p.dma("sync", blkg[:], scr.ap()[0:1, :].rearrange("a (j q) -> (a j) q", q=128))
    cx.transpose(G1c, blkg[:], 16, 128, ident)
    p.dma("sync", blkg[:], scr.ap()[1:2, :].rearrange("a (j q) -> (a j) q", q=128),
          reads=[G1c])
    cx.transpose(Bvc, blkg[:], 16, 128, ident)
    esG.close()
    p.barrier()

    esC = ExitStack()
    xn0c = [cx.sb(f"xn0c{k}", [128, CH], es=esC) for k in range(8)]
    ctx = [cx.sb(f"ctx{k}", [128, CH], BF16, es=esC) for k in range(32)]
    wslab = [cx.sb(f"wslab{i}", [128, 1024], es=esC) for i in range(2)]
    ctxg = [cx.sb(f"ctxg{i}", [128, CH], es=esC) for i in range(2)]
    h1g = [cx.sb(f"h1g{t}", [128, CH], BF16, es=esC) for t in range(16)]
    x1t = [cx.sb(f"x1t{k}", [128, CH], es=esC) for k in range(8)]
    fm = [cx.sb(f"fm{i}", [128, CH], es=esC) for i in range(2)]
    fx = [cx.sb(f"fx{i}", [128, CH], es=esC) for i in range(2)]
    hlb = [cx.sb(f"hlb{i}", [128, CH], es=esC) for i in range(2)]
    phit = [cx.sb(f"phit{i}", [128, CH], es=esC) for i in range(2)]
    sint = cx.sb("sint", [128, CH], es=esC)
    cost = cx.sb("cost", [128, CH], es=esC)
    smr = cx.sb("smr", [128, CH], es=esC)
    smi = cx.sb("smi", [128, CH], es=esC)
    memr = cx.sb("memr", [128, CH], es=esC)
    memi = cx.sb("memi", [128, CH], es=esC)
    sqs = cx.sb("sqs", [128, CH], es=esC)
    tas = cx.sb("tas", [128, CH], es=esC)
    carry = cx.sb("carryB", [128, 16], es=esC)
    statrow = cx.sb("statrow", [1, CH], es=esC)
    muc_rep = cx.sb("muc_rep", [128, CH], es=esC)
    rsc_rep = cx.sb("rsc_rep", [128, CH], es=esC)
    mrow = cx.sb("mrow", [1, CH], es=esC)
    m2row = cx.sb("m2row", [1, CH], es=esC)
    p.op("vector", lambda E: E.memset(mxacc[:], 0.0), writes=[mxacc])
    p.op("vector", lambda E: E.tensor_copy(carry[:], cmct[:]), reads=[cmct], writes=[carry])

    for tch in range(NCH):
        sl = slice(CH * tch, CH * (tch + 1))
        cx.rot = [cx.banks[i] for i in range(6)]
        for k in range(8):
            p.dma("scalar", xn0c[k][:], xTo.ap()[128 * k:128 * (k + 1), sl])
            p.op("vector", lambda E, k=k, sl=sl: E.tensor_sub(xn0c[k][:], xn0c[k][:], mu_rep[:, sl]),
                 reads=[xn0c[k], mu_rep], writes=[xn0c[k]])
            p.op("vector", lambda E, k=k, sl=sl: E.tensor_mul(xn0c[k][:], xn0c[k][:], rs_rep[:, sl]),
                 reads=[xn0c[k], rs_rep], writes=[xn0c[k]])
        pm2 = cx.banks[7]
        for et in range(8):
            ph = phit[et % 2]
            p.dma("sync", ph[:], phiD.ap()[128 * et:128 * (et + 1), sl])
            p.op("vector", lambda E, et=et, ph=ph: E.tensor_scalar(
                ph[:], ph[:], phict[:, et:et + 1], None, ALU.add),
                reads=[ph, phict], writes=[ph])
            trig_block(cx, ph, sqs, sint, cost, consts)
            p.op("vector", lambda E, et=et: E.tensor_tensor(
                ctx[et][:], xn0c[et][:], cost[:], ALU.mult),
                reads=[xn0c[et], cost], writes=[ctx[et]])
            p.op("vector", lambda E, et=et: E.tensor_tensor(
                ctx[8 + et][:], xn0c[et][:], sint[:], ALU.mult),
                reads=[xn0c[et], sint], writes=[ctx[8 + et]])
            p.op("vector", lambda E, et=et: E.tensor_tensor_scan(
                smr[:], cx.onescol[:].broadcast_to((128, CH)), ctx[et][:],
                carry[:, et:et + 1], ALU.mult, ALU.add),
                reads=[ctx[et], carry, cx.onescol], writes=[smr])
            p.op("vector", lambda E, et=et: E.tensor_copy(carry[:, et:et + 1], smr[:, CH - 1:CH]),
                 reads=[smr], writes=[carry])
            p.op("vector", lambda E, et=et: E.tensor_tensor_scan(
                smi[:], cx.onescol[:].broadcast_to((128, CH)), ctx[8 + et][:],
                carry[:, 8 + et:8 + et + 1], ALU.mult, ALU.add),
                reads=[ctx[8 + et], carry, cx.onescol], writes=[smi])
            p.op("vector", lambda E, et=et: E.tensor_copy(
                carry[:, 8 + et:8 + et + 1], smi[:, CH - 1:CH]),
                reads=[smi], writes=[carry])
            p.op("vector", lambda E, sl=sl: E.tensor_mul(memr[:], smr[:], pos_rep[:, sl]),
                 reads=[smr, pos_rep], writes=[memr])
            p.op("vector", lambda E, sl=sl: E.tensor_mul(memi[:], smi[:], pos_rep[:, sl]),
                 reads=[smi, pos_rep], writes=[memi])
            p.op("vector", lambda E: E.tensor_mul(smr[:], memr[:], cost[:]),
                 reads=[memr, cost], writes=[smr])
            p.op("vector", lambda E: E.tensor_mul(smi[:], memi[:], sint[:]),
                 reads=[memi, sint], writes=[smi])
            p.op("vector", lambda E, et=et: E.tensor_add(ctx[16 + et][:], smr[:], smi[:]),
                 reads=[smr, smi], writes=[ctx[16 + et]])
            p.op("vector", lambda E: E.tensor_mul(smr[:], memi[:], cost[:]),
                 reads=[memi, cost], writes=[smr])
            p.op("vector", lambda E: E.tensor_mul(smi[:], memr[:], sint[:]),
                 reads=[memr, sint], writes=[smi])
            p.op("vector", lambda E, et=et: E.tensor_sub(ctx[24 + et][:], smr[:], smi[:]),
                 reads=[smr, smi], writes=[ctx[24 + et]])
            p.op("scalar", lambda E: E.activation(sqs[:], memr[:], AF.Square),
                 reads=[memr], writes=[sqs])
            p.op("tensor", lambda E, et=et: E.matmul(
                pm2[0:1, 0:CH], cx.onescol[:], sqs[:],
                start=(et == 0), stop=False),
                reads=[sqs, cx.onescol], writes=[pm2])
            p.op("scalar", lambda E: E.activation(sqs[:], memi[:], AF.Square),
                 reads=[memi], writes=[sqs])
            p.op("tensor", lambda E, et=et: E.matmul(
                pm2[0:1, 0:CH], cx.onescol[:], sqs[:],
                start=False, stop=(et == 7)),
                reads=[sqs, cx.onescol], writes=[pm2])
        pcs = cx.banks[6]
        for k in range(32):
            p.op("tensor", lambda E, k=k: E.matmul(
                pcs[0:1, 0:CH], cx.onescolb[:], ctx[k][:],
                start=(k == 0), stop=(k == 31)),
                reads=[ctx[k], cx.onescolb], writes=[pcs])
        p.op("vector", lambda E: E.tensor_scalar(
            mrow[:], pcs[0:1, 0:CH], 1.0 / 4096.0, None, ALU.mult),
            reads=[pcs], writes=[mrow])
        p.dma("scalar", statrow[:], sxn2o.ap()[0:1, sl])
        p.op("vector", lambda E: E.tensor_add(m2row[:], pm2[0:1, 0:CH], statrow[:]),
             reads=[pm2, statrow], writes=[m2row])
        p.op("vector", lambda E: E.tensor_scalar(
            m2row[:], m2row[:], 1.0 / 4096.0, None, ALU.mult),
            reads=[m2row], writes=[m2row])
        p.op("scalar", lambda E: E.activation(statrow[:], mrow[:], AF.Square),
             reads=[mrow], writes=[statrow])
        p.op("vector", lambda E: E.tensor_sub(m2row[:], m2row[:], statrow[:]),
             reads=[m2row, statrow], writes=[m2row])
        p.op("vector", lambda E: E.tensor_scalar(
            m2row[:], m2row[:], float(LN_EPS), None, ALU.add),
            reads=[m2row], writes=[m2row])
        p.op("scalar", lambda E: E.activation(m2row[:], m2row[:], AF.Sqrt),
             reads=[m2row], writes=[m2row])
        p.op("vector", lambda E: E.reciprocal(m2row[:], m2row[:]), reads=[m2row], writes=[m2row])
        cx.bcast_row(m2row, rsc_rep, CH)
        p.op("vector", lambda E: E.tensor_mul(mrow[:], mrow[:], m2row[:]),
             reads=[mrow, m2row], writes=[mrow])
        p.op("vector", lambda E: E.tensor_scalar(mrow[:], mrow[:], -1.0, None, ALU.mult),
             reads=[mrow], writes=[mrow])
        cx.bcast_row(mrow, muc_rep, CH)

        for ps in range(2):
            for k in range(32):
                wsl = wslab[k % 2]
                cg = ctxg[k % 2]
                p.dma("sync" if k % 2 == 0 else "scalar", wsl[:],
                      W1TD.ap()[128 * k:128 * (k + 1), 1024 * ps:1024 * (ps + 1)])
                p.op("vector", lambda E, k=k, cg=cg: E.tensor_scalar(
                    cg[:], ctx[k][:], g4t[:, k:k + 1], None, ALU.mult),
                    reads=[ctx[k], g4t], writes=[cg])
                for j in range(8):
                    p.op("tensor", lambda E, k=k, j=j, wsl=wsl, cg=cg: E.matmul(
                        cx.banks[j][:, 0:CH], wsl[:, 128 * j:128 * (j + 1)],
                        cg[:], start=(k == 0), stop=(k == 31)),
                        reads=[wsl, cg], writes=[cx.banks[j]])
            for j in range(8):
                ot = 8 * ps + j
                p.op("vector", lambda E, j=j: E.tensor_tensor(
                    tas[:], cx.banks[j][:, 0:CH], rsc_rep[:], ALU.mult),
                    reads=[cx.banks[j], rsc_rep], writes=[tas])
                p.op("vector", lambda E, ot=ot: E.scalar_tensor_tensor(
                    tas[:], muc_rep[:], G1c[:, ot:ot + 1], tas[:], ALU.mult, ALU.add),
                    reads=[muc_rep, G1c, tas], writes=[tas])
                p.op("scalar", lambda E, ot=ot: E.activation(
                    h1g[ot][:], tas[:], AF.Gelu, bias=Bvc[:, ot:ot + 1]),
                    reads=[tas, Bvc], writes=[h1g[ot]])
        for o2 in range(8):
            pw = cx.banks[o2]
            for k in range(16):
                p.op("tensor", lambda E, k=k, o2=o2, pw=pw: E.matmul(
                    pw[:, 0:CH], w2r[k][:, 128 * o2:128 * (o2 + 1)], h1g[k][:],
                    start=(k == 0), stop=(k == 15)),
                    reads=[w2r[k], h1g[k]], writes=[pw])
            fmb, fxb = fm[o2 % 2], fx[o2 % 2]
            p.dma("gpsimd", fmb[:], moeT.ap()[128 * o2:128 * (o2 + 1), sl])
            p.dma("gpsimd", fxb[:], xTo.ap()[128 * o2:128 * (o2 + 1), sl])
            p.op("vector", lambda E, pw=pw, o2=o2: E.tensor_scalar(
                memr[:], pw[:, 0:CH], b2t[:, o2:o2 + 1], None, ALU.add),
                reads=[pw, b2t], writes=[memr])
            p.op("vector", lambda E, o2=o2: E.tensor_add(memr[:], memr[:], xn0c[o2][:]),
                 reads=[memr, xn0c[o2]], writes=[memr])
            p.op("vector", lambda E, fmb=fmb: E.tensor_add(memr[:], memr[:], fmb[:]),
                 reads=[memr, fmb], writes=[memr])
            p.op("vector", lambda E, o2=o2, fxb=fxb: E.tensor_add(x1t[o2][:], memr[:], fxb[:]),
                 reads=[memr, fxb], writes=[x1t[o2]])
            p.dma("sync", o_x1.ap()[128 * o2:128 * (o2 + 1), sl], x1t[o2][:])
        ps1, ps2 = cx.bank(), cx.bank()
        for k in range(8):
            p.op("tensor", lambda E, k=k, ps1=ps1: E.matmul(
                ps1[0:1, 0:CH], cx.onescol[:], x1t[k][:], start=(k == 0), stop=(k == 7)),
                reads=[x1t[k], cx.onescol], writes=[ps1])
            p.op("scalar", lambda E, k=k: E.activation(sqs[:], x1t[k][:], AF.Square),
                 reads=[x1t[k]], writes=[sqs])
            p.op("tensor", lambda E, k=k, ps2=ps2: E.matmul(
                ps2[0:1, 0:CH], cx.onescol[:], sqs[:], start=(k == 0), stop=(k == 7)),
                reads=[sqs, cx.onescol], writes=[ps2])
        p.op("vector", lambda E, ps1=ps1: E.tensor_scalar(
            mrow[:], ps1[0:1, 0:CH], 1.0 / D, None, ALU.mult), reads=[ps1], writes=[mrow])
        p.op("vector", lambda E, ps2=ps2: E.tensor_scalar(
            m2row[:], ps2[0:1, 0:CH], 1.0 / D, None, ALU.mult), reads=[ps2], writes=[m2row])
        p.op("scalar", lambda E: E.activation(statrow[:], mrow[:], AF.Square),
             reads=[mrow], writes=[statrow])
        p.op("vector", lambda E: E.tensor_sub(m2row[:], m2row[:], statrow[:]),
             reads=[m2row, statrow], writes=[m2row])
        p.op("vector", lambda E: E.tensor_scalar(
            m2row[:], m2row[:], float(LN_EPS), None, ALU.add), reads=[m2row], writes=[m2row])
        p.op("scalar", lambda E: E.activation(m2row[:], m2row[:], AF.Sqrt),
             reads=[m2row], writes=[m2row])
        p.op("vector", lambda E: E.reciprocal(m2row[:], m2row[:]), reads=[m2row], writes=[m2row])
        cx.bcast_row(m2row, rsc_rep, CH)
        cx.bcast_row(mrow, muc_rep, CH)
        for k in range(8):
            p.op("vector", lambda E, k=k: E.tensor_sub(x1t[k][:], x1t[k][:], muc_rep[:]),
                 reads=[x1t[k], muc_rep], writes=[x1t[k]])
            p.op("vector", lambda E, k=k: E.tensor_mul(x1t[k][:], x1t[k][:], rsc_rep[:]),
                 reads=[x1t[k], rsc_rep], writes=[x1t[k]])
            p.dma("sync", o_xn1.ap()[128 * k:128 * (k + 1), sl], x1t[k][:])
            p.op("vector", lambda E, k=k: E.tensor_reduce(
                sqs[:, 0:1], x1t[k][:], AX, ALU.max, apply_absolute_value=True),
                reads=[x1t[k]], writes=[sqs])
            p.op("vector", lambda E, k=k: E.tensor_max(mxacc[:], mxacc[:], sqs[:, 0:1]),
                 reads=[sqs, mxacc], writes=[mxacc])
            hb = hlb[k % 2]
            p.op("vector", lambda E, k=k, hb=hb: E.tensor_scalar(
                hb[:], x1t[k][:], gb1t[:, k:k + 1], gb1t[:, 8 + k:8 + k + 1],
                ALU.mult, ALU.add), reads=[x1t[k], gb1t], writes=[hb])
            p.dma("scalar", o_hln.ap()[128 * k:128 * (k + 1), sl], hb[:])
        cx.rot = None
    esC.close()
    p.barrier()
    pmx = cx.bank()
    p.op("tensor", lambda E, pmx=pmx: E.transpose(pmx[0:1, 0:128], mxacc[:], ident[:]),
         reads=[mxacc, ident], writes=[pmx])
    mrow2 = cx.sb("mrow2", [1, 128])
    p.op("vector", lambda E, pmx=pmx: E.tensor_copy(mrow2[:], pmx[0:1, 0:128]),
         reads=[pmx], writes=[mrow2])
    p.op("vector", lambda E: E.tensor_reduce(mrow2[:, 0:1], mrow2[:], AX, ALU.max),
         reads=[mrow2], writes=[mrow2])
    p.dma("sync", o_mx.ap(), mrow2[0:1, 0:1], reads=[mrow2])

    p.finish()
    return nc


_NC_CACHE = {}


def _get_nc(name, builder):
    if name not in _NC_CACHE:
        _NC_CACHE[name] = builder()
    return _NC_CACHE[name]


def _f32(a):
    return np.ascontiguousarray(a, dtype=np.float32)


def host_inputs_A(ins):
    x = _f32(ins["L0_attn_x"]).reshape(NT, D)
    xT = np.ascontiguousarray(x.T)
    Wr16 = np.zeros((D, 16), np.float32)
    Wr16[:, 0:8] = _f32(ins["L0_ffn_Wr"])
    Wr16[:, 8] = 1.0
    WomT = np.ascontiguousarray(_f32(ins["L0_attn_W_omega"]).T)
    b_om = np.ascontiguousarray(_f32(ins["L0_attn_b_omega"]).reshape(8, 128).T)
    lsc = np.ascontiguousarray(_f32(ins["L0_attn_log_scale"]).reshape(8, 128).T)
    e9 = np.zeros((1, 1152), np.float32)
    e9[0, :] = np.tile(np.arange(9, dtype=np.float32), 128)
    ident = np.eye(128, dtype=np.float32)
    moeW1 = _f32(ins["L0_ffn_W1"])
    moeW2 = _f32(ins["L0_ffn_W2"])
    maps = []
    for c in range(NCORE):
        pos = np.arange(c % 2 * TOK + 1, c % 2 * TOK + TOK + 1, dtype=np.float64)
        possc = (1.0 / np.sqrt(pos)).astype(np.float32).reshape(1, TOK)
        cmask = np.zeros((1, 8), np.float32)
        cmask[0, c] = 1.0
        bidx = np.zeros((16, 3), np.float32)
        for j in range(16):
            for rid in range(3):
                bidx[j, rid] = rid * 128 + c * 16 + j
        ridxv = np.array([[3 * 8 + c], [4 * 8 + c]], np.float32)
        maps.append({
            "xT": xT, "xTo": np.ascontiguousarray(xT[:, c * TOK:(c + 1) * TOK]),
            "Wr16": Wr16, "WomT": WomT, "b_om": b_om, "lsc": lsc,
            "possc": possc, "moeW1": moeW1, "moeW2": moeW2, "e9iota": e9,
            "cmask": cmask, "bidx": bidx, "ridx": ridxv, "identD": ident,
            "e8colD": np.arange(8, dtype=np.float32).reshape(8, 1),
        })
    return maps




def build_launch_C():
    cx = Ctx()
    nc, p = cx.nc, cx.p

    hlnD = nc.dram_tensor("hlnD", [D, TOK], F32, kind="ExternalInput")
    hllD = nc.dram_tensor("hllD", [D, 3], F32, kind="ExternalInput")
    hlrD = nc.dram_tensor("hlrD", [D, 3], F32, kind="ExternalInput")
    inwD = nc.dram_tensor("inwD", [D, 4096], F32, kind="ExternalInput")
    cwf = nc.dram_tensor("cwf", [128, 64], F32, kind="ExternalInput")
    cwb = nc.dram_tensor("cwb", [128, 64], F32, kind="ExternalInput")
    cbf = nc.dram_tensor("cbf", [128, 16], F32, kind="ExternalInput")
    cbb = nc.dram_tensor("cbb", [128, 16], F32, kind="ExternalInput")
    xpf = nc.dram_tensor("xpf", [2048, 96], F32, kind="ExternalInput")
    xpb = nc.dram_tensor("xpb", [2048, 96], F32, kind="ExternalInput")
    dtwf = nc.dram_tensor("dtwf", [64, 2048], F32, kind="ExternalInput")
    dtwb = nc.dram_tensor("dtwb", [64, 2048], F32, kind="ExternalInput")
    dtbf = nc.dram_tensor("dtbf", [128, 16], F32, kind="ExternalInput")
    dtbb = nc.dram_tensor("dtbb", [128, 16], F32, kind="ExternalInput")
    dskf = nc.dram_tensor("dskf", [128, 16], F32, kind="ExternalInput")
    dskb = nc.dram_tensor("dskb", [128, 16], F32, kind="ExternalInput")

    o_dt = nc.dram_tensor("o_dt", [2, 2048, TOK], BF16, kind="ExternalOutput")
    o_dtx = nc.dram_tensor("o_dtx", [2, 2048, TOK], BF16, kind="ExternalOutput")
    o_y0 = nc.dram_tensor("o_y0", [2, 2048, TOK], BF16, kind="ExternalOutput")
    o_bc = nc.dram_tensor("o_bc", [2, 32, TOK], F32, kind="ExternalOutput")
    o_z = nc.dram_tensor("o_z", [2048, TOK], F32, kind="ExternalOutput")

    onebias = cx.sb("onebias", [128, 1])
    p.op("vector", lambda E: E.memset(onebias[:], 1.0), writes=[onebias])
    neg1ap = consts_neg1(cx)

    xsb = [cx.sb(f"xsb{t}", [128, TOK], BF16) for t in range(16)]
    xhal = [cx.sb(f"xhal{t}", [128, 6], BF16) for t in range(16)]
    esP = ExitStack()
    inwr = [cx.sb(f"inwr{k}", [128, 4096], BF16, es=esP) for k in range(8)]
    hlnb = [cx.sb(f"hlnb{k}", [128, TOK], BF16, es=esP) for k in range(8)]
    hlob = [cx.sb(f"hlob{k}", [128, 6], BF16, es=esP) for k in range(8)]
    zrow = [cx.sb(f"zrow{i}", [128, 512], es=esP) for i in range(2)]
    for k in range(8):
        p.dma("gpsimd", inwr[k][:], inwD.ap()[128 * k:128 * (k + 1), :])
        p.dma("gpsimd", hlnb[k][:], hlnD.ap()[128 * k:128 * (k + 1), :])
        p.dma("gpsimd", hlob[k][:, 0:3], hllD.ap()[128 * k:128 * (k + 1), :])
        p.dma("gpsimd", hlob[k][:, 3:6], hlrD.ap()[128 * k:128 * (k + 1), :])
    for ct in range(32):
        for ch in range(4):
            pw = cx.banks[(4 * ct + ch) % 8]
            for k in range(8):
                p.op("tensor", lambda E, ct=ct, ch=ch, k=k, pw=pw: E.matmul(
                    pw[:, 0:512], inwr[k][:, 128 * ct:128 * (ct + 1)],
                    hlnb[k][:, 512 * ch:512 * (ch + 1)],
                    start=(k == 0), stop=(k == 7)),
                    reads=[inwr[k], hlnb[k]], writes=[pw])
            if ct < 16:
                p.op("vector", lambda E, ct=ct, ch=ch, pw=pw: E.tensor_copy(
                    xsb[ct][:, 512 * ch:512 * (ch + 1)], pw[:, 0:512]),
                    reads=[pw], writes=[xsb[ct]])
            else:
                zr = zrow[ch % 2]
                p.op("vector", lambda E, pw=pw, zr=zr: E.tensor_copy(zr[:], pw[:, 0:512]),
                     reads=[pw], writes=[zr])
                p.dma("sync", o_z.ap()[128 * (ct - 16):128 * (ct - 15),
                                       512 * ch:512 * (ch + 1)], zr[:])
        if ct < 16:
            pw = cx.banks[(4 * ct) % 8]
            for k in range(8):
                p.op("tensor", lambda E, ct=ct, k=k, pw=pw: E.matmul(
                    pw[:, 0:6], inwr[k][:, 128 * ct:128 * (ct + 1)], hlob[k][:],
                    start=(k == 0), stop=(k == 7)),
                    reads=[inwr[k], hlob[k]], writes=[pw])
            p.op("vector", lambda E, ct=ct, pw=pw: E.tensor_copy(
                xhal[ct][:], pw[:, 0:6]), reads=[pw], writes=[xhal[ct]])
    esP.close()
    p.barrier()

    cwt = cx.sb("cwt", [128, 64])
    cbt = cx.sb("cbt", [128, 16])
    dtbt = cx.sb("dtbt", [128, 16])
    dskt = cx.sb("dskt", [128, 16])
    ext = cx.sb("ext", [128, TOK + 8], BF16)
    acc = cx.sb("acc", [128, TOK])
    xcb = [cx.sb(f"xcb_{t}", [128, TOK], BF16) for t in range(16)]
    xpt = cx.sb("xpt", [128, 96], BF16)
    xdbl = cx.sb("xdbl", [96, TOK])
    sp1 = cx.sb("sp1", [128, 512])
    sp2 = cx.sb("sp2", [128, 512])
    dtt = cx.sb("dtt", [128, 512], BF16)
    dxt = cx.sb("dxt", [128, 512], BF16)
    bct = cx.sb("bct", [32, TOK])
    dtw_t = cx.sb("dtw_t", [64, 2048])
    for d, (cw, cb, xp, dtw, dtb, dsk) in enumerate(
            (((cwf, cbf, xpf, dtwf, dtbf, dskf)),
             ((cwb, cbb, xpb, dtwb, dtbb, dskb)))):
        p.dma("scalar", cwt[:], cw.ap())
        p.dma("scalar", cbt[:], cb.ap())
        p.dma("scalar", dtbt[:], dtb.ap())
        p.dma("scalar", dskt[:], dsk.ap())
        for ct in range(16):
            p.op("vector", lambda E, ct=ct: E.memset(ext[:], 0.0), writes=[ext])
            p.op("vector", lambda E, ct=ct: E.tensor_copy(ext[:, 3:3 + TOK], xsb[ct][:]),
                 reads=[xsb[ct]], writes=[ext])
            if d == 0:
                p.op("vector", lambda E, ct=ct: E.tensor_copy(ext[:, 0:3], xhal[ct][:, 0:3]),
                     reads=[xhal[ct]], writes=[ext])
            else:
                p.op("vector", lambda E, ct=ct: E.tensor_copy(
                    ext[:, 3 + TOK:6 + TOK], xhal[ct][:, 3:6]),
                    reads=[xhal[ct]], writes=[ext])
            offs = [0, 1, 2, 3] if d == 0 else [6, 5, 4, 3]
            p.op("vector", lambda E, ct=ct, o=offs[0]: E.tensor_scalar(
                acc[:], ext[:, o:o + TOK], cwt[:, 16 * 0 + ct:16 * 0 + ct + 1], None, ALU.mult),
                reads=[ext, cwt], writes=[acc])
            for k in (1, 2, 3):
                p.op("vector", lambda E, ct=ct, k=k, o=offs[k]: E.scalar_tensor_tensor(
                    acc[:], ext[:, o:o + TOK], cwt[:, 16 * k + ct:16 * k + ct + 1],
                    acc[:], ALU.mult, ALU.add),
                    reads=[ext, cwt, acc], writes=[acc])
            p.op("scalar", lambda E, ct=ct: E.activation(
                xcb[ct][:], acc[:], AF.Silu, bias=cbt[:, ct:ct + 1]),
                reads=[acc, cbt], writes=[xcb[ct]])
        for ct in range(16):
            p.op("vector", lambda E, ct=ct: E.tensor_scalar(
                acc[:], xcb[ct][:], dskt[:, ct:ct + 1], None, ALU.mult),
                reads=[xcb[ct], dskt], writes=[acc])
            p.dma("gpsimd", o_y0.ap()[d, 128 * ct:128 * (ct + 1), :], acc[:], reads=[acc])
        for k in range(16):
            p.dma("gpsimd", xpt[:], xp.ap()[128 * k:128 * (k + 1), :])
            for ch in range(4):
                pt = cx.banks[ch]
                p.op("tensor", lambda E, k=k, ch=ch, pt=pt: E.matmul(
                    pt[0:96, :], xpt[:], xcb[k][:, 512 * ch:512 * (ch + 1)],
                    start=(k == 0), stop=(k == 15)),
                    reads=[xpt, xcb[k]], writes=[pt])
        for ch in range(4):
            pt = cx.banks[ch]
            p.op("vector", lambda E, ch=ch, pt=pt: E.tensor_copy(
                xdbl[:, 512 * ch:512 * (ch + 1)], pt[0:96, :]), reads=[pt], writes=[xdbl])
        p.op("vector", lambda E: E.tensor_copy(bct[:], xdbl[64:96, :]),
             reads=[xdbl], writes=[bct])
        p.dma("sync", o_bc.ap()[d], bct[:])
        p.dma("sync", dtw_t[:], dtw.ap())
        for dt_i in range(16):
            for ch in range(4):
                pt = cx.banks[4 + (ch % 2)]
                p.op("tensor", lambda E, dt_i=dt_i, ch=ch, pt=pt: E.matmul(
                    pt[:, 0:512], dtw_t[:, 128 * dt_i:128 * (dt_i + 1)],
                    xdbl[0:64, 512 * ch:512 * (ch + 1)], start=True, stop=True),
                    reads=[dtw_t, xdbl], writes=[pt])
                p.op("vector", lambda E, dt_i=dt_i, pt=pt: E.tensor_scalar(
                    sp1[:], pt[:, 0:512], dtbt[:, dt_i:dt_i + 1], None, ALU.add),
                    reads=[pt, dtbt], writes=[sp1])
                p.op("scalar", lambda E: E.activation(sp2[:], sp1[:], AF.Abs),
                     reads=[sp1], writes=[sp2])
                p.op("scalar", lambda E: E.activation(sp2[:], sp2[:], AF.Exp,
                                                      scale=neg1ap),
                     reads=[sp2, "negone_c"], writes=[sp2])
                p.op("scalar", lambda E: E.activation(sp2[:], sp2[:], AF.Ln,
                                                      bias=onebias[:]),
                     reads=[sp2, onebias], writes=[sp2])
                p.op("scalar", lambda E: E.activation(sp1[:], sp1[:], AF.Relu),
                     reads=[sp1], writes=[sp1])
                p.op("vector", lambda E: E.tensor_add(sp1[:], sp1[:], sp2[:]),
                     reads=[sp1, sp2], writes=[sp1])
                p.op("vector", lambda E: E.tensor_copy(dtt[:], sp1[:]),
                     reads=[sp1], writes=[dtt])
                p.dma("sync", o_dt.ap()[d, 128 * dt_i:128 * (dt_i + 1),
                                        512 * ch:512 * (ch + 1)], dtt[:], reads=[dtt])
                p.op("vector", lambda E, dt_i=dt_i, ch=ch: E.tensor_tensor(
                    dxt[:], dtt[:], xcb[dt_i][:, 512 * ch:512 * (ch + 1)], ALU.mult),
                    reads=[dtt, xcb[dt_i]], writes=[dxt])
                p.dma("sync", o_dtx.ap()[d, 128 * dt_i:128 * (dt_i + 1),
                                         512 * ch:512 * (ch + 1)], dxt[:], reads=[dxt])
    p.finish()
    return nc


_NEG1 = {}


def consts_neg1(cx):
    if id(cx) not in _NEG1:
        t = cx.sb("negone_c", [128, 1])
        cx.p.op("vector", lambda E: E.memset(t[:], -1.0), writes=[t])
        _NEG1[id(cx)] = t
    return _NEG1[id(cx)][:]


def build_launch_scan():
    cx = Ctx()
    nc, p = cx.nc, cx.p
    CH = 1024
    NCH = TOK // CH

    dtD = nc.dram_tensor("dtD", [2048, TOK], BF16, kind="ExternalInput")
    dtxD = nc.dram_tensor("dtxD", [2048, TOK], BF16, kind="ExternalInput")
    y0D = nc.dram_tensor("y0D", [2048, TOK], BF16, kind="ExternalInput")
    bcflat = nc.dram_tensor("bcflat", [2, 16 * TOK], F32, kind="ExternalInput")
    alogD = nc.dram_tensor("alogD", [128, 256], F32, kind="ExternalInput")
    hinD = nc.dram_tensor("hinD", [128, 256], F32, kind="ExternalInput")

    o_y = nc.dram_tensor("o_y", [2048, TOK], BF16, kind="ExternalOutput")
    o_hf = nc.dram_tensor("o_hf", [128, 256], F32, kind="ExternalOutput")

    negA = cx.sb("negA", [128, 256])
    hst = cx.sb("hst", [128, 256])
    p.dma("scalar", negA[:], alogD.ap())
    p.op("scalar", lambda E: E.activation(negA[:], negA[:], AF.Exp), reads=[negA], writes=[negA])
    p.op("vector", lambda E: E.tensor_scalar(negA[:], negA[:], -1.0, None, ALU.mult),
         reads=[negA], writes=[negA])
    p.dma("scalar", hst[:], hinD.ap(),
          writes=[f"hstc{c}" for c in range(256)])

    dtt16 = [cx.sb(f"dtt16_{i}", [128, CH], BF16) for i in range(16)]
    dtx16 = [cx.sb(f"dtx16_{i}", [128, CH], BF16) for i in range(16)]
    acc16 = [cx.sb(f"acc16_{i}", [128, CH], BF16) for i in range(16)]
    brep = [cx.sb(f"brepS{i}", [128, CH], BF16) for i in range(2)]
    crep = [cx.sb(f"crepS{i}", [128, CH], BF16) for i in range(2)]
    dA = [cx.sb(f"dAS{i}", [128, CH], BF16) for i in range(2)]
    dBu = cx.sb("dBuS", [128, CH], BF16)
    yCt = [cx.sb(f"yCtS{i}", [128, CH], BF16) for i in range(2)]

    for ch in range(NCH):
        sl = slice(CH * ch, CH * (ch + 1))
        for i in range(16):
            p.dma("sync", dtt16[i][:], dtD.ap()[128 * i:128 * (i + 1), sl])
            p.dma("scalar", dtx16[i][:], dtxD.ap()[128 * i:128 * (i + 1), sl])
            p.dma("gpsimd", acc16[i][:], y0D.ap()[128 * i:128 * (i + 1), sl])
        for n in range(16):
            bb, cc = brep[n % 2], crep[n % 2]
            p.dma("gpsimd", bb[:],
                  bcflat.ap()[0:1, TOK * n + CH * ch:TOK * n + CH * (ch + 1)]
                  .broadcast_to((128, CH)), reads=[bb])
            p.dma("gpsimd", cc[:],
                  bcflat.ap()[1:2, TOK * n + CH * ch:TOK * n + CH * (ch + 1)]
                  .broadcast_to((128, CH)), reads=[cc])
            for dt_i in range(16):
                col = dt_i * 16 + n
                dA_, yC_ = dA[dt_i % 2], yCt[dt_i % 2]
                p.op("scalar", lambda E, col=col, dt_i=dt_i, dA_=dA_: E.activation(
                    dA_[:], dtt16[dt_i][:], AF.Exp, scale=negA[:, col:col + 1]),
                    reads=[dtt16[dt_i], negA], writes=[dA_])
                p.op("vector", lambda E, dt_i=dt_i, bb=bb: E.tensor_tensor(
                    dBu[:], dtx16[dt_i][:], bb[:], ALU.mult),
                    reads=[dtx16[dt_i], bb], writes=[dBu])
                p.op("vector", lambda E, col=col, dA_=dA_: E.tensor_tensor_scan(
                    dBu[:], dA_[:], dBu[:], hst[:, col:col + 1], ALU.mult, ALU.add),
                    reads=[dA_, dBu, f"hstc{col}"], writes=[dBu])
                p.op("scalar", lambda E, col=col: E.copy(
                    hst[:, col:col + 1], dBu[:, CH - 1:CH]),
                    reads=[dBu], writes=[f"hstc{col}"])
                p.op("vector", lambda E, cc=cc, yC_=yC_: E.tensor_tensor(
                    yC_[:], dBu[:], cc[:], ALU.mult),
                    reads=[dBu, cc], writes=[yC_])
                p.op("gpsimd", lambda E, dt_i=dt_i, yC_=yC_: E.tensor_add(
                    acc16[dt_i][:], acc16[dt_i][:], yC_[:]),
                    reads=[acc16[dt_i], yC_], writes=[acc16[dt_i]])
        for i in range(16):
            p.dma("sync", o_y.ap()[128 * i:128 * (i + 1), sl], acc16[i][:])
    p.dma("sync", o_hf.ap(), hst[:],
          reads=[hst] + [f"hstc{c}" for c in range(256)])
    p.finish()
    return nc


def emit_lpbq_dequant(cx, wD, rows, cols, outT, ident, nbl):
    nc, p = cx.nc, cx.p
    es = ExitStack()
    wt = [cx.sb(f"dq_wt{i}{id(wD)}", [128, cols], es=es) for i in range(2)]
    bm = cx.sb(f"dq_bm{id(wD)}", [128, nbl], es=es)
    sinv = cx.sb(f"dq_si{id(wD)}", [128, nbl], es=es)
    rr = cx.sb(f"dq_rr{id(wD)}", [128, cols], es=es)
    wq = [cx.sb(f"dq_wq{i}{id(wD)}", [128, cols], es=es) for i in range(2)]
    for ti in range(rows // 128):
        wti, wqi = wt[ti % 2], wq[ti % 2]
        p.dma("sync" if ti % 2 == 0 else "scalar", wti[:],
              wD.ap()[128 * ti:128 * (ti + 1), :])
        p.op("vector", lambda E, wti=wti: E.tensor_reduce(
            bm[:], wti[:].rearrange("p (b q) -> p b q", q=32), AX, ALU.max,
            apply_absolute_value=True),
            reads=[wti], writes=[bm])
        p.op("vector", lambda E: E.tensor_scalar(bm[:], bm[:], 1.0 / 7.0, 1e-12,
                                                 ALU.mult, ALU.max),
             reads=[bm], writes=[bm])
        p.op("vector", lambda E: E.reciprocal(sinv[:], bm[:]), reads=[bm], writes=[sinv])
        p.op("vector", lambda E, wti=wti: E.tensor_tensor(
            rr[:], wti[:], sinv[:].rearrange("p b -> p b ()").broadcast_to((128, nbl, 32)),
            ALU.mult), reads=[wti, sinv], writes=[rr])
        p.op("vector", lambda E: E.tensor_scalar(rr[:], rr[:], MAGIC, -MAGIC,
                                                 ALU.add, ALU.add),
             reads=[rr], writes=[rr])
        p.op("vector", lambda E, wqi=wqi: E.tensor_tensor(
            wqi[:], rr[:], bm[:].rearrange("p b -> p b ()").broadcast_to((128, nbl, 32)),
            ALU.mult), reads=[rr, bm], writes=[wqi])
        for k in range(cols // 128):
            cx.transpose(outT[k], wqi[:, 128 * k:128 * (k + 1)], 128, 128, ident,
                         out_off=128 * ti)
    es.close()
    p.barrier()


def emit_qdq(cx, src_tiles, sptr, siptr, out_tiles, n):
    p = cx.p
    for t_s, t_o in zip(src_tiles, out_tiles):
        p.op("vector", lambda E, t_s=t_s, t_o=t_o: E.tensor_scalar(
            t_o[:], t_s[:], siptr, MAGIC, ALU.mult, ALU.add),
            reads=[t_s, "qscale"], writes=[t_o])
        p.op("vector", lambda E, t_o=t_o: E.tensor_scalar(
            t_o[:], t_o[:], -MAGIC, sptr, ALU.add, ALU.mult),
            reads=[t_o, "qscale"], writes=[t_o])


def build_launch_G():
    cx = Ctx()
    nc, p = cx.nc, cx.p
    CH = 512

    yfD = nc.dram_tensor("yfD", [2048, TOK], BF16, kind="ExternalInput")
    ybD = nc.dram_tensor("ybD", [2048, TOK], BF16, kind="ExternalInput")
    zD = nc.dram_tensor("zD", [2048, TOK], F32, kind="ExternalInput")
    xn1D = nc.dram_tensor("xn1D", [D, TOK], F32, kind="ExternalInput")
    x1D = nc.dram_tensor("x1D", [D, TOK], F32, kind="ExternalInput")
    mrgD = nc.dram_tensor("mrgD", [4096, 2048], F32, kind="ExternalInput")
    outwD = nc.dram_tensor("outwD", [2048, D], F32, kind="ExternalInput")
    wupD = nc.dram_tensor("wupD", [3072, D], F32, kind="ExternalInput")
    wgtD = nc.dram_tensor("wgtD", [3072, D], F32, kind="ExternalInput")
    gmaxD = nc.dram_tensor("gmaxD", [1, 8], F32, kind="ExternalInput")
    identD = nc.dram_tensor("identD", [128, 128], F32, kind="ExternalInput")

    o_x2p = nc.dram_tensor("o_x2p", [D, TOK], F32, kind="ExternalOutput")
    o_up = nc.dram_tensor("o_up", [3072, TOK], F32, kind="ExternalOutput")
    o_gate = nc.dram_tensor("o_gate", [3072, TOK], F32, kind="ExternalOutput")
    o_mx2 = nc.dram_tensor("o_mx2", [1, 2], F32, kind="ExternalOutput")

    ident = cx.sb("ident", [128, 128])
    p.dma("sync", ident[:], identD.ap())

    esM = ExitStack()
    outwr = [cx.sb(f"outwr{k}", [128, 1024], BF16, es=esM) for k in range(16)]
    for k in range(16):
        p.dma("gpsimd", outwr[k][:], outwD.ap()[128 * k:128 * (k + 1), :])
    yk = [cx.sb(f"yk{k}", [128, CH], es=esM) for k in range(32)]
    mslab = [cx.sb(f"mslab{i}", [128, 1024], es=esM) for i in range(2)]
    gat = [cx.sb(f"gat{t}", [128, CH], BF16, es=esM) for t in range(16)]
    zt = [cx.sb(f"ztG{i}", [128, CH], es=esM) for i in range(2)]
    szt = [cx.sb(f"sztG{i}", [128, CH], es=esM) for i in range(2)]
    xat = [cx.sb(f"xatG{i}", [128, CH], es=esM) for i in range(2)]
    t_a = [cx.sb(f"t_aG{i}", [128, CH], es=esM) for i in range(2)]
    for ch in range(4):
        sl = slice(CH * ch, CH * (ch + 1))
        for k in range(16):
            p.dma("gpsimd", yk[k][:], yfD.ap()[128 * k:128 * (k + 1), sl])
            p.dma("gpsimd", yk[16 + k][:], ybD.ap()[128 * k:128 * (k + 1), sl])
        for ps in range(2):
            for k in range(32):
                msl = mslab[k % 2]
                p.dma("sync" if k % 2 == 0 else "scalar", msl[:],
                      mrgD.ap()[128 * k:128 * (k + 1), 1024 * ps:1024 * (ps + 1)])
                for j in range(8):
                    p.op("tensor", lambda E, k=k, j=j, msl=msl: E.matmul(
                        cx.banks[j][:, 0:CH], msl[:, 128 * j:128 * (j + 1)],
                        yk[k][:], start=(k == 0), stop=(k == 31)),
                        reads=[msl, yk[k]], writes=[cx.banks[j]])
            for j in range(8):
                ot = 8 * ps + j
                ztj, sztj = zt[j % 2], szt[j % 2]
                p.dma("gpsimd", ztj[:], zD.ap()[128 * ot:128 * (ot + 1), sl])
                p.op("scalar", lambda E, ztj=ztj, sztj=sztj: E.activation(
                    sztj[:], ztj[:], AF.Silu), reads=[ztj], writes=[sztj])
                p.op("vector", lambda E, j=j, ot=ot, sztj=sztj: E.tensor_tensor(
                    gat[ot][:], cx.banks[j][:, 0:CH], sztj[:], ALU.mult),
                    reads=[cx.banks[j], sztj], writes=[gat[ot]])
        for o2 in range(8):
            pw = cx.banks[o2]
            for k in range(16):
                p.op("tensor", lambda E, k=k, o2=o2, pw=pw: E.matmul(
                    pw[:, 0:CH], outwr[k][:, 128 * o2:128 * (o2 + 1)], gat[k][:],
                    start=(k == 0), stop=(k == 15)),
                    reads=[outwr[k], gat[k]], writes=[pw])
            ztj, xaj, taj = zt[o2 % 2], xat[o2 % 2], t_a[o2 % 2]
            p.dma("gpsimd", ztj[:], x1D.ap()[128 * o2:128 * (o2 + 1), sl])
            p.dma("gpsimd", xaj[:], xn1D.ap()[128 * o2:128 * (o2 + 1), sl])
            p.op("vector", lambda E, ztj=ztj, xaj=xaj, taj=taj: E.tensor_add(
                taj[:], xaj[:], ztj[:]), reads=[xaj, ztj], writes=[taj])
            p.op("vector", lambda E, pw=pw, taj=taj: E.tensor_add(
                taj[:], taj[:], pw[:, 0:CH]), reads=[taj, pw], writes=[taj])
            p.dma("sync", o_x2p.ap()[128 * o2:128 * (o2 + 1), sl], taj[:], reads=[taj])
    esM.close()
    p.barrier()

    esQ = ExitStack()
    grow = cx.sb("growG", [1, 8], es=esQ)
    srow = cx.sb("srowG", [1, 1], es=esQ)
    sptr = cx.sb("sptrG", [128, 1], es=esQ)
    siptr = cx.sb("siptrG", [128, 1], es=esQ)
    p.dma("scalar", grow[:], gmaxD.ap())
    p.op("vector", lambda E: E.tensor_reduce(srow[:], grow[:], AX, ALU.max),
         reads=[grow], writes=[srow])
    p.op("vector", lambda E: E.tensor_scalar(srow[:], srow[:], 1.0 / 32767.0, 1e-12,
                                             ALU.mult, ALU.max),
         reads=[srow], writes=[srow])
    pt = cx.bank()
    p.op("tensor", lambda E, pt=pt: E.matmul(pt[:, 0:1], cx.ones1[0:1, :], srow[0:1, 0:1],
                                             start=True, stop=True),
         reads=[srow, cx.ones1], writes=[pt])
    p.op("scalar", lambda E, pt=pt: E.copy(sptr[:], pt[:, 0:1]), reads=[pt], writes=[sptr])
    p.op("vector", lambda E: E.reciprocal(siptr[:], sptr[:]), reads=[sptr], writes=[siptr])
    p.op("vector", lambda E: E.engine_nop(), reads=[sptr, siptr], writes=["qscale"])

    xqb = [cx.sb(f"xqb{k}", [128, TOK], BF16, es=esQ) for k in range(8)]
    xqt = cx.sb("xqt", [128, TOK], es=esQ)
    for k in range(8):
        p.dma("sync", xqt[:], xn1D.ap()[128 * k:128 * (k + 1), :])
        p.op("vector", lambda E: E.tensor_scalar(
            xqt[:], xqt[:], siptr[:, 0:1], MAGIC, ALU.mult, ALU.add),
            reads=[xqt, "qscale"], writes=[xqt])
        p.op("vector", lambda E: E.tensor_scalar(
            xqt[:], xqt[:], -MAGIC, sptr[:, 0:1], ALU.add, ALU.mult),
            reads=[xqt, "qscale"], writes=[xqt])
        p.op("vector", lambda E, k=k: E.tensor_copy(xqb[k][:], xqt[:]),
             reads=[xqt], writes=[xqb[k]])

    wT = [cx.sb(f"wTq{k}", [128, 3072], BF16, es=esQ) for k in range(8)]
    upt2 = [cx.sb(f"uptG{i}", [128, CH], es=esQ) for i in range(2)]
    mxt = cx.sb("mxtG", [128, 1], es=esQ)
    mxa = cx.sb("mxaG", [128, 2], es=esQ)
    mrow2 = cx.sb("mrow2G", [1, 128], es=esQ)
    p.op("vector", lambda E: E.memset(mxa[:], 0.0), writes=[mxa])
    for gi, (wD, oD) in enumerate(((wupD, o_up), (wgtD, o_gate))):
        emit_lpbq_dequant(cx, wD, 3072, 1024, wT, ident, 32)
        for ot in range(24):
            for ch in range(4):
                pw = cx.bank()
                for k in range(8):
                    p.op("tensor", lambda E, k=k, ot=ot, ch=ch, pw=pw: E.matmul(
                        pw[:, 0:CH], wT[k][:, 128 * ot:128 * (ot + 1)],
                        xqb[k][:, CH * ch:CH * (ch + 1)], start=(k == 0), stop=(k == 7)),
                        reads=[wT[k], xqb[k]], writes=[pw])
                upt = upt2[(4 * ot + ch) % 2]
                p.op("vector", lambda E, pw=pw, upt=upt: E.tensor_copy(upt[:], pw[:, 0:CH]),
                     reads=[pw], writes=[upt])
                p.dma("sync" if ch % 2 == 0 else "scalar",
                      oD.ap()[128 * ot:128 * (ot + 1), CH * ch:CH * (ch + 1)],
                      upt[:], reads=[upt])
                p.op("vector", lambda E, upt=upt: E.tensor_reduce(
                    mxt[:, 0:1], upt[:], AX, ALU.max, apply_absolute_value=True),
                    reads=[upt], writes=[mxt])
                p.op("vector", lambda E, gi=gi: E.tensor_max(
                    mxa[:, gi:gi + 1], mxa[:, gi:gi + 1], mxt[:, 0:1]),
                    reads=[mxt, mxa], writes=[mxa])
    pmx = cx.bank()
    p.op("tensor", lambda E, pmx=pmx: E.transpose(pmx[0:2, 0:128], mxa[:], ident[:]),
         reads=[mxa, ident], writes=[pmx])
    mrow22 = cx.sb("mrow22", [2, 128], es=esQ)
    p.op("vector", lambda E, pmx=pmx: E.tensor_copy(mrow22[:], pmx[0:2, 0:128]),
         reads=[pmx], writes=[mrow22])
    p.op("vector", lambda E: E.tensor_reduce(mrow22[:, 0:1], mrow22[:], AX, ALU.max),
         reads=[mrow22], writes=[mrow22])
    p.dma("sync", o_mx2.ap().rearrange("a b -> b a"), mrow22[0:2, 0:1], reads=[mrow22])
    esQ.close()
    p.finish()
    return nc


def build_launch_H():
    cx = Ctx()
    nc, p = cx.nc, cx.p
    upD = nc.dram_tensor("upD", [3072, TOK], F32, kind="ExternalInput")
    gtD = nc.dram_tensor("gtD", [3072, TOK], F32, kind="ExternalInput")
    gmaxD = nc.dram_tensor("gmaxD", [1, 16], F32, kind="ExternalInput")
    identD = nc.dram_tensor("identD", [128, 128], F32, kind="ExternalInput")
    o_upq = nc.dram_tensor("o_upq", [3072, TOK], F32, kind="ExternalOutput")
    o_t3 = nc.dram_tensor("o_t3", [3072, TOK], F32, kind="ExternalOutput")
    o_mx = nc.dram_tensor("o_mx", [1, 1], F32, kind="ExternalOutput")
    ident = cx.sb("ident", [128, 128])
    p.dma("sync", ident[:], identD.ap())
    grow = cx.sb("growH", [1, 16])
    sr = cx.sb("srH", [1, 2])
    sp_u = cx.sb("spu", [128, 2])
    si_u = cx.sb("siu", [128, 2])
    p.dma("scalar", grow[:], gmaxD.ap())
    p.op("vector", lambda E: E.tensor_reduce(
        sr[:], grow[:].rearrange("a (c q) -> a q c", q=2), AX, ALU.max),
        reads=[grow], writes=[sr])
    p.op("vector", lambda E: E.tensor_scalar(sr[:], sr[:], 1.0 / 32767.0, 1e-12,
                                             ALU.mult, ALU.max), reads=[sr], writes=[sr])
    pt = cx.bank()
    p.op("tensor", lambda E, pt=pt: E.matmul(pt[:, 0:2], cx.ones1[0:1, :], sr[0:1, :],
                                             start=True, stop=True),
         reads=[sr, cx.ones1], writes=[pt])
    p.op("scalar", lambda E, pt=pt: E.copy(sp_u[:], pt[:, 0:2]), reads=[pt], writes=[sp_u])
    p.op("vector", lambda E: E.reciprocal(si_u[:], sp_u[:]), reads=[sp_u], writes=[si_u])
    p.op("vector", lambda E: E.engine_nop(), reads=[sp_u, si_u], writes=["qscale"])

    ut = cx.sb("utH", [128, TOK])
    gt = cx.sb("gtH", [128, TOK])
    sg = cx.sb("sgH", [128, TOK])
    mxa = cx.sb("mxaH", [128, 1])
    p.op("vector", lambda E: E.memset(mxa[:], 0.0), writes=[mxa])
    for t in range(24):
        p.dma("sync", ut[:], upD.ap()[128 * t:128 * (t + 1), :])
        p.dma("sync", gt[:], gtD.ap()[128 * t:128 * (t + 1), :])
        p.op("vector", lambda E: E.tensor_scalar(ut[:], ut[:], si_u[:, 0:1], MAGIC,
                                                 ALU.mult, ALU.add),
             reads=[ut, "qscale"], writes=[ut])
        p.op("vector", lambda E: E.tensor_scalar(ut[:], ut[:], -MAGIC, sp_u[:, 0:1],
                                                 ALU.add, ALU.mult),
             reads=[ut], writes=[ut])
        p.dma("sync", o_upq.ap()[128 * t:128 * (t + 1), :], ut[:], reads=[ut])
        p.op("vector", lambda E: E.tensor_scalar(gt[:], gt[:], si_u[:, 1:2], MAGIC,
                                                 ALU.mult, ALU.add),
             reads=[gt, "qscale"], writes=[gt])
        p.op("vector", lambda E: E.tensor_scalar(gt[:], gt[:], -MAGIC, sp_u[:, 1:2],
                                                 ALU.add, ALU.mult),
             reads=[gt], writes=[gt])
        p.op("scalar", lambda E: E.activation(sg[:], gt[:], AF.Sigmoid),
             reads=[gt], writes=[sg])
        p.op("vector", lambda E: E.tensor_scalar(sg[:], sg[:], 65536.0, MAGIC,
                                                 ALU.mult, ALU.add),
             reads=[sg], writes=[sg])
        p.op("vector", lambda E: E.tensor_scalar(sg[:], sg[:], -MAGIC, 65535.0,
                                                 ALU.add, ALU.min),
             reads=[sg], writes=[sg])
        p.op("vector", lambda E: E.tensor_scalar(sg[:], sg[:], 1.0 / 65536.0, None,
                                                 ALU.mult),
             reads=[sg], writes=[sg])
        p.op("vector", lambda E: E.tensor_mul(gt[:], gt[:], sg[:]),
             reads=[gt, sg], writes=[gt])
        p.dma("sync", o_t3.ap()[128 * t:128 * (t + 1), :], gt[:], reads=[gt])
        p.op("vector", lambda E: E.tensor_reduce(sg[:, 0:1], gt[:], AX, ALU.max,
                                                 apply_absolute_value=True),
             reads=[gt], writes=[sg])
        p.op("vector", lambda E: E.tensor_max(mxa[:], mxa[:], sg[:, 0:1]),
             reads=[sg, mxa], writes=[mxa])
    pmx = cx.bank()
    p.op("tensor", lambda E, pmx=pmx: E.transpose(pmx[0:1, 0:128], mxa[:], ident[:]),
         reads=[mxa, ident], writes=[pmx])
    mr = cx.sb("mrH", [1, 128])
    p.op("vector", lambda E, pmx=pmx: E.tensor_copy(mr[:], pmx[0:1, 0:128]),
         reads=[pmx], writes=[mr])
    p.op("vector", lambda E: E.tensor_reduce(mr[:, 0:1], mr[:], AX, ALU.max),
         reads=[mr], writes=[mr])
    p.dma("sync", o_mx.ap(), mr[0:1, 0:1], reads=[mr])
    p.finish()
    return nc


def build_launch_I():
    cx = Ctx()
    nc, p = cx.nc, cx.p
    t3D = nc.dram_tensor("t3D", [3072, TOK], F32, kind="ExternalInput")
    upqD = nc.dram_tensor("upqD", [3072, TOK], F32, kind="ExternalInput")
    gmaxD = nc.dram_tensor("gmaxD", [1, 8], F32, kind="ExternalInput")
    identD = nc.dram_tensor("identD", [128, 128], F32, kind="ExternalInput")
    o_t4 = nc.dram_tensor("o_t4", [3072, TOK], F32, kind="ExternalOutput")
    o_mx = nc.dram_tensor("o_mx", [1, 1], F32, kind="ExternalOutput")
    ident = cx.sb("ident", [128, 128])
    p.dma("sync", ident[:], identD.ap())
    grow = cx.sb("growI", [1, 8])
    sr = cx.sb("srI", [1, 1])
    sp_u = cx.sb("spuI", [128, 1])
    si_u = cx.sb("siuI", [128, 1])
    p.dma("scalar", grow[:], gmaxD.ap())
    p.op("vector", lambda E: E.tensor_reduce(sr[:], grow[:], AX, ALU.max),
         reads=[grow], writes=[sr])
    p.op("vector", lambda E: E.tensor_scalar(sr[:], sr[:], 1.0 / 32767.0, 1e-12,
                                             ALU.mult, ALU.max), reads=[sr], writes=[sr])
    pt = cx.bank()
    p.op("tensor", lambda E, pt=pt: E.matmul(pt[:, 0:1], cx.ones1[0:1, :], sr[0:1, :],
                                             start=True, stop=True),
         reads=[sr, cx.ones1], writes=[pt])
    p.op("scalar", lambda E, pt=pt: E.copy(sp_u[:], pt[:, 0:1]), reads=[pt], writes=[sp_u])
    p.op("vector", lambda E: E.reciprocal(si_u[:], sp_u[:]), reads=[sp_u], writes=[si_u])
    p.op("vector", lambda E: E.engine_nop(), reads=[sp_u, si_u], writes=["qscale"])
    ut = cx.sb("utI", [128, TOK])
    gt = cx.sb("gtI", [128, TOK])
    mxa = cx.sb("mxaI", [128, 1])
    p.op("vector", lambda E: E.memset(mxa[:], 0.0), writes=[mxa])
    for t in range(24):
        p.dma("sync", gt[:], t3D.ap()[128 * t:128 * (t + 1), :])
        p.dma("sync", ut[:], upqD.ap()[128 * t:128 * (t + 1), :])
        p.op("vector", lambda E: E.tensor_scalar(gt[:], gt[:], si_u[:, 0:1], MAGIC,
                                                 ALU.mult, ALU.add),
             reads=[gt, "qscale"], writes=[gt])
        p.op("vector", lambda E: E.tensor_scalar(gt[:], gt[:], -MAGIC, sp_u[:, 0:1],
                                                 ALU.add, ALU.mult),
             reads=[gt], writes=[gt])
        p.op("vector", lambda E: E.tensor_mul(gt[:], gt[:], ut[:]),
             reads=[gt, ut], writes=[gt])
        p.dma("sync", o_t4.ap()[128 * t:128 * (t + 1), :], gt[:], reads=[gt])
        p.op("vector", lambda E: E.tensor_reduce(ut[:, 0:1], gt[:], AX, ALU.max,
                                                 apply_absolute_value=True),
             reads=[gt], writes=[ut])
        p.op("vector", lambda E: E.tensor_max(mxa[:], mxa[:], ut[:, 0:1]),
             reads=[ut, mxa], writes=[mxa])
    pmx = cx.bank()
    p.op("tensor", lambda E, pmx=pmx: E.transpose(pmx[0:1, 0:128], mxa[:], ident[:]),
         reads=[mxa, ident], writes=[pmx])
    mr = cx.sb("mrI", [1, 128])
    p.op("vector", lambda E, pmx=pmx: E.tensor_copy(mr[:], pmx[0:1, 0:128]),
         reads=[pmx], writes=[mr])
    p.op("vector", lambda E: E.tensor_reduce(mr[:, 0:1], mr[:], AX, ALU.max),
         reads=[mr], writes=[mr])
    p.dma("sync", o_mx.ap(), mr[0:1, 0:1], reads=[mr])
    p.finish()
    return nc


def build_launch_J():
    cx = Ctx()
    nc, p = cx.nc, cx.p
    CH = 512
    t4D = nc.dram_tensor("t4D", [3072, TOK], F32, kind="ExternalInput")
    x2pD = nc.dram_tensor("x2pD", [D, TOK], F32, kind="ExternalInput")
    wdD = nc.dram_tensor("wdD", [D, 3072], F32, kind="ExternalInput")
    gmaxD = nc.dram_tensor("gmaxD", [1, 8], F32, kind="ExternalInput")
    identD = nc.dram_tensor("identD", [128, 128], F32, kind="ExternalInput")
    o_out = nc.dram_tensor("o_out", [TOK, D], F32, kind="ExternalOutput")
    ident = cx.sb("ident", [128, 128])
    p.dma("sync", ident[:], identD.ap())
    grow = cx.sb("growJ", [1, 8])
    sr = cx.sb("srJ", [1, 1])
    sp_u = cx.sb("spuJ", [128, 1])
    si_u = cx.sb("siuJ", [128, 1])
    p.dma("scalar", grow[:], gmaxD.ap())
    p.op("vector", lambda E: E.tensor_reduce(sr[:], grow[:], AX, ALU.max),
         reads=[grow], writes=[sr])
    p.op("vector", lambda E: E.tensor_scalar(sr[:], sr[:], 1.0 / 32767.0, 1e-12,
                                             ALU.mult, ALU.max), reads=[sr], writes=[sr])
    pt = cx.bank()
    p.op("tensor", lambda E, pt=pt: E.matmul(pt[:, 0:1], cx.ones1[0:1, :], sr[0:1, :],
                                             start=True, stop=True),
         reads=[sr, cx.ones1], writes=[pt])
    p.op("scalar", lambda E, pt=pt: E.copy(sp_u[:], pt[:, 0:1]), reads=[pt], writes=[sp_u])
    p.op("vector", lambda E: E.reciprocal(si_u[:], sp_u[:]), reads=[sp_u], writes=[si_u])
    p.op("vector", lambda E: E.engine_nop(), reads=[sp_u, si_u], writes=["qscale"])

    wdT = [cx.sb(f"wdT{k}", [128, 1024], BF16) for k in range(24)]
    emit_lpbq_dequant(cx, wdD, 1024, 3072, wdT, ident, 96)

    oqb = [cx.sb(f"oqb{k}", [128, TOK], BF16) for k in range(24)]
    qt = cx.sb("qtJ", [128, TOK])
    for t in range(24):
        p.dma("sync", qt[:], t4D.ap()[128 * t:128 * (t + 1), :])
        p.op("vector", lambda E: E.tensor_scalar(qt[:], qt[:], si_u[:, 0:1], MAGIC,
                                                 ALU.mult, ALU.add),
             reads=[qt, "qscale"], writes=[qt])
        p.op("vector", lambda E: E.tensor_scalar(qt[:], qt[:], -MAGIC, sp_u[:, 0:1],
                                                 ALU.add, ALU.mult),
             reads=[qt], writes=[qt])
        p.op("vector", lambda E, t=t: E.tensor_copy(oqb[t][:], qt[:]),
             reads=[qt], writes=[oqb[t]])
    xf = cx.sb("xfJ", [128, CH])
    rowo = cx.sb("rowoJ", [128, D])
    for ch in range(4):
        sl = slice(CH * ch, CH * (ch + 1))
        for o2 in range(8):
            pw = cx.bank()
            for k in range(24):
                p.op("tensor", lambda E, k=k, o2=o2, sl=sl, pw=pw: E.matmul(
                    pw[:, 0:CH], wdT[k][:, 128 * o2:128 * (o2 + 1)],
                    oqb[k][:, sl], start=(k == 0), stop=(k == 23)),
                    reads=[wdT[k], oqb[k]], writes=[pw])
            p.dma("sync", xf[:], x2pD.ap()[128 * o2:128 * (o2 + 1), sl])
            p.op("vector", lambda E, pw=pw: E.tensor_add(xf[:], xf[:], pw[:, 0:CH]),
                 reads=[xf, pw], writes=[xf])
            for jj in range(4):
                cx.transpose(rowo, xf[:, 128 * jj:128 * (jj + 1)], 128, 128, ident,
                             out_off=128 * o2)
                p.dma("sync", o_out.ap()[CH * ch + 128 * jj:CH * ch + 128 * (jj + 1),
                                         128 * o2:128 * (o2 + 1)],
                      rowo[:, 128 * o2:128 * (o2 + 1)], reads=[rowo])
    p.finish()
    return nc


def host_inputs_B(ins, outA):
    x = _f32(ins["L0_attn_x"]).reshape(NT, D)
    xT = np.ascontiguousarray(x.T)
    W1T = np.ascontiguousarray(_f32(ins["L0_attn_W1"]).T)
    W2T = np.ascontiguousarray(_f32(ins["L0_attn_W2"]).T)
    g4 = np.ascontiguousarray(_f32(ins["L0_attn_ln_gamma"]).reshape(32, 128).T)
    b4 = np.ascontiguousarray(_f32(ins["L0_attn_ln_beta"]).reshape(32, 128).T)
    b1row = _f32(ins["L0_attn_b1"]).reshape(1, 2048)
    b2c = np.ascontiguousarray(_f32(ins["L0_attn_b2"]).reshape(8, 128).T)
    gb1 = np.zeros((128, 16), np.float32)
    gb1[:, 0:8] = _f32(ins["L1_attn_gamma"]).reshape(8, 128).T
    gb1[:, 8:16] = _f32(ins["L1_attn_beta"]).reshape(8, 128).T
    ident = np.eye(128, dtype=np.float32)
    maps = []
    for c in range(NCORE):
        pos = np.arange(c % 2 * TOK + 1, c % 2 * TOK + TOK + 1, dtype=np.float64)
        pinv = (1.0 / pos).astype(np.float32).reshape(1, TOK)
        if c % 2 == 1:
            phic = outA[c - 1]["o_phic"]
            cmc = outA[c - 1]["o_cmc"]
        else:
            phic = np.zeros((128, 8), np.float32)
            cmc = np.zeros((128, 16), np.float32)
        maps.append({
            "xTo": np.ascontiguousarray(xT[:, c * TOK:(c + 1) * TOK]),
            "stato": outA[c]["o_stat"], "phiD": outA[c]["o_phi"],
            "phicI": np.ascontiguousarray(phic), "cmcI": np.ascontiguousarray(cmc),
            "sxn2o": np.ascontiguousarray(outA[c]["o_rows"][2:3, c * TOK:(c + 1) * TOK]),
            "moeT": np.ascontiguousarray(outA[c]["o_moe"].T), "posinv": pinv,
            "W1TD": W1T, "g4": g4, "b4": b4, "b1r": b1row, "W2TD": W2T,
            "b2c": b2c, "gb1": gb1, "identD": ident,
        })
    return maps


def host_inputs_C(ins, outB):
    def tile16(v):
        return np.ascontiguousarray(_f32(v).reshape(16, 128).T)
    cwf = np.ascontiguousarray(_f32(ins["L1_attn_conv_w"]).reshape(16, 128, 4)
                               .transpose(1, 2, 0).reshape(128, 64))
    cwb = np.ascontiguousarray(_f32(ins["L1_attn_conv_w_b"]).reshape(16, 128, 4)
                               .transpose(1, 2, 0).reshape(128, 64))
    com = {
        "cwf": cwf, "cwb": cwb, "inwD": _f32(ins["L1_attn_in_w"]),
        "cbf": tile16(ins["L1_attn_conv_b"]), "cbb": tile16(ins["L1_attn_conv_b_b"]),
        "xpf": _f32(ins["L1_attn_xproj_w"]), "xpb": _f32(ins["L1_attn_xproj_w_b"]),
        "dtwf": _f32(ins["L1_attn_dt_w"]), "dtwb": _f32(ins["L1_attn_dt_w_b"]),
        "dtbf": tile16(ins["L1_attn_dt_b"]), "dtbb": tile16(ins["L1_attn_dt_b_b"]),
        "dskf": tile16(ins["L1_attn_D_skip"]), "dskb": tile16(ins["L1_attn_D_skip_b"]),
    }
    maps = []
    z3 = np.zeros((D, 3), np.float32)
    for c in range(NCORE):
        hl = outB[c - 1]["o_hln"][:, TOK - 3:TOK] if c % 2 == 1 else z3
        hr = outB[c + 1]["o_hln"][:, 0:3] if c % 2 == 0 else z3
        m = dict(com)
        m["hlnD"] = outB[c]["o_hln"]
        m["hllD"] = np.ascontiguousarray(hl)
        m["hlrD"] = np.ascontiguousarray(hr)
        maps.append(m)
    return maps


def _alog_layout(a):
    return np.ascontiguousarray(_f32(a).reshape(16, 128, 16).transpose(1, 0, 2)
                                .reshape(128, 256))


def host_inputs_scan(ins, outC, outD=None):
    bf = ml_dtypes.bfloat16
    maps = []
    for c in range(NCORE):
        if outD is None:
            di = 0 if c % 2 == 0 else 1
            flip = (c % 2 == 1)
            hin = np.zeros((128, 256), np.float32)
        else:
            di = 1 if c % 2 == 0 else 0
            flip = (c % 2 == 0)
            sib = c + 1 if c % 2 == 0 else c - 1
            hin = outD[sib]["o_hf"]
        dt = outC[c]["o_dt"][di]
        dtx = outC[c]["o_dtx"][di]
        y0 = outC[c]["o_y0"][di]
        bc = _f32(outC[c]["o_bc"][di])
        if flip:
            dt = dt[:, ::-1]; dtx = dtx[:, ::-1]; y0 = y0[:, ::-1]; bc = bc[:, ::-1]
        alog = ins["L1_attn_A_log"] if di == 0 else ins["L1_attn_A_log_b"]
        maps.append({
            "dtD": np.ascontiguousarray(dt), "dtxD": np.ascontiguousarray(dtx),
            "y0D": np.ascontiguousarray(y0),
            "bcflat": np.ascontiguousarray(bc.reshape(2, 16 * TOK)),
            "alogD": _alog_layout(alog),
            "hinD": np.ascontiguousarray(_f32(hin)),
        })
    return maps


def host_inputs_G(ins, outB, outC, outD, outE):
    mrg = _f32(ins["L1_attn_merge_w"])
    outw = _f32(ins["L1_attn_out_w"])
    wup = _f32(ins["L1_ffn_w_up"])
    wgt = _f32(ins["L1_ffn_w_gate"])
    gmax = np.concatenate([outB[c]["o_mx"] for c in range(NCORE)], axis=1)
    ident = np.eye(128, dtype=np.float32)
    maps = []
    for c in range(NCORE):
        if c % 2 == 0:
            yf = outD[c]["o_y"]
            yb = np.ascontiguousarray(outE[c]["o_y"][:, ::-1])
        else:
            yf = outE[c]["o_y"]
            yb = np.ascontiguousarray(outD[c]["o_y"][:, ::-1])
        maps.append({
            "yfD": yf, "ybD": yb,
            "zD": outC[c]["o_z"],
            "xn1D": outB[c]["o_xn1"], "x1D": outB[c]["o_x1"],
            "mrgD": mrg, "outwD": outw, "wupD": wup, "wgtD": wgt,
            "gmaxD": gmax, "identD": ident,
        })
    return maps


def host_inputs_H(outG):
    gmax = np.concatenate([outG[c]["o_mx2"] for c in range(NCORE)], axis=1)
    ident = np.eye(128, dtype=np.float32)
    return [{"upD": outG[c]["o_up"], "gtD": outG[c]["o_gate"],
             "gmaxD": gmax, "identD": ident} for c in range(NCORE)]


def host_inputs_I(outH):
    gmax = np.concatenate([outH[c]["o_mx"] for c in range(NCORE)], axis=1)
    ident = np.eye(128, dtype=np.float32)
    return [{"t3D": outH[c]["o_t3"], "upqD": outH[c]["o_upq"],
             "gmaxD": gmax, "identD": ident} for c in range(NCORE)]


def host_inputs_J(ins, outG, outI):
    gmax = np.concatenate([outI[c]["o_mx"] for c in range(NCORE)], axis=1)
    ident = np.eye(128, dtype=np.float32)
    wd = _f32(ins["L1_ffn_w_down"])
    return [{"t4D": outI[c]["o_t4"], "x2pD": outG[c]["o_x2p"],
             "wdD": wd, "gmaxD": gmax, "identD": ident} for c in range(NCORE)]


def kernel(**inputs):
    cores = list(range(NCORE))

    def go(name, builder, maps):
        nc = _get_nc(name, builder)
        res = run_bass_kernel_spmd(nc, maps, core_ids=cores)
        return [dict(r) for r in res.results]

    outA = go("A", build_launch_A, host_inputs_A(inputs))
    outB = go("B", build_launch_B, host_inputs_B(inputs, outA))
    outC = go("C", build_launch_C, host_inputs_C(inputs, outB))
    outD = go("S", build_launch_scan, host_inputs_scan(inputs, outC))
    outE = go("S", build_launch_scan, host_inputs_scan(inputs, outC, outD))
    outG = go("G", build_launch_G, host_inputs_G(inputs, outB, outC, outD, outE))
    outH = go("H", build_launch_H, host_inputs_H(outG))
    outI = go("I", build_launch_I, host_inputs_I(outH))
    outJ = go("J", build_launch_J, host_inputs_J(inputs, outG, outI))
    out = np.concatenate([outJ[c]["o_out"] for c in cores], axis=0)
    return np.ascontiguousarray(out.reshape(4, 4096, 1024), dtype=np.float32)


def run_A(ins):
    nc = _get_nc("A", build_launch_A)
    maps = host_inputs_A(ins)
    res = run_bass_kernel_spmd(nc, maps, core_ids=list(range(NCORE)))
    return res.results


if __name__ == "__main__":
    build_launch_A()
    print("built A ok")
